# version 08
# baseline (speedup 1.0000x reference)
import numpy as np
import ml_dtypes
from contextlib import ExitStack

import concourse.bass as bass
import concourse.mybir as mybir
from concourse.ap import AP
from concourse.bass import IndirectOffsetOnAxis
from concourse.bass_utils import run_bass_kernel_spmd

F32 = mybir.dt.float32
BF16 = mybir.dt.bfloat16
I32 = mybir.dt.int32
AF = mybir.ActivationFunctionType
ALU = mybir.AluOpType
AX = mybir.AxisListType
BF16NP = ml_dtypes.bfloat16

B, L, D, NCORES = 4, 4096, 1024, 8
LC = L // NCORES
NH, HD = 16, 64
NBH = B * NH

ENGINES = ["tensor", "vector", "scalar", "gpsimd", "sync"]


class G:

    def __init__(self, num_devices=None):
        self.nc = bass.Bass("TRN2", target_bir_lowering=False, debug=False,
                            num_devices=num_devices)
        self.stack = ExitStack()
        self.ctx = self.stack.__enter__()
        self._stage = {e: [] for e in ENGINES}
        self._stages = []

    def io(self, name, shape, dtype=F32, out=False):
        return self.nc.dram_tensor(name, list(shape), dtype,
                                   kind="ExternalOutput" if out else "ExternalInput").ap()

    def dram(self, name, shape, dtype=F32):
        return self.nc.dram_tensor(name, list(shape), dtype, kind="Internal").ap()

    def sb(self, name, shape, dtype=F32):
        return self.ctx.enter_context(self.nc.sbuf_tensor(name, list(shape), dtype))

    def ps(self, name, shape):
        return self.ctx.enter_context(self.nc.psum_tensor(name, list(shape), F32))

    def add(self, eng, fn, dma=False, defer=0):
        self._stage[eng].append((fn, dma, defer))

    def bar(self):
        if any(self._stage[e] for e in ENGINES):
            self._stages.append(self._stage)
            self._stage = {e: [] for e in ENGINES}

    def build(self):
        self.bar()
        sems = {e: self.ctx.enter_context(self.nc.semaphore("S_" + e)) for e in ENGINES}
        nst = len(self._stages)
        adds = {e: [0] * (nst + 1) for e in ENGINES}
        for s, st in enumerate(self._stages):
            for e in ENGINES:
                for fn, dma, defer in st[e]:
                    acct = min(s + defer, nst - 1)
                    adds[e][acct] += 16 if dma else 1
        totals = {e: [0] * (nst + 1) for e in ENGINES}
        for e in ENGINES:
            for s in range(nst):
                totals[e][s + 1] = totals[e][s] + adds[e][s]
        block = self.ctx.enter_context(self.nc.Block())
        for e in ENGINES:
            def stream(E, e=e):
                last_wait = {oe: 0 for oe in ENGINES}
                for s in range(nst):
                    ops = self._stages[s][e]
                    if not ops:
                        continue
                    for oe in ENGINES:
                        need = totals[oe][s]
                        if need > last_wait[oe]:
                            E.wait_ge(sems[oe], need)
                            last_wait[oe] = need
                    for fn, dma, defer in ops:
                        instr = fn(E)
                        assert instr is not None, f"op on {e} returned None at stage {s}"
                        instr.then_inc(sems[e], 16 if dma else 1)
                if e == "sync":
                    for oe in ENGINES:
                        E.wait_ge(sems[oe], totals[oe][nst])
            getattr(block, e)(stream)
        self.stack.__exit__(None, None, None)
        return self.nc


def bfc(x):
    return np.ascontiguousarray(np.asarray(x).astype(np.float32))


def f32c(x):
    return np.ascontiguousarray(np.asarray(x), dtype=np.float32)


IDENT = np.eye(128, dtype=np.float32)



_BUILDERS = {}


def emit_ln(g, sXb, sXn, C):
    sOnes32, sSq, sStat, sEps = C["sOnes32"], C["sSq"], C["sStat"], C["sEps"]
    sMB, sRB = C["sMB"], C["sRB"]
    pS1, pS2, pBC = C["pS1"], C["pS2"], C["pBC"]

    def s1(t):
        ii = None
        for fc in range(8):
            ii = t.matmul(pS1[:], sOnes32[:], sXb[:, fc], start=(fc == 0), stop=(fc == 7))
        return ii
    g.add("tensor", s1)
    g.bar()
    for grp in range(4):
        def sq(sc, grp=grp):
            sc.activation(out=sSq[:, 0], in_=sXb[:, 2 * grp], func=AF.Square)
            return sc.activation(out=sSq[:, 1], in_=sXb[:, 2 * grp + 1], func=AF.Square)
        g.add("scalar", sq)
        g.bar()

        def mm2(t, grp=grp):
            t.matmul(pS2[:], sOnes32[:], sSq[:, 0], start=(grp == 0), stop=False)
            return t.matmul(pS2[:], sOnes32[:], sSq[:, 1], start=False, stop=(grp == 3))
        g.add("tensor", mm2)
        g.bar()

    def mv(v):
        v.tensor_scalar(sStat[:, 0], pS1[:], 1.0 / D, None, ALU.mult)
        v.tensor_scalar(sStat[:, 1], pS2[:], 1.0 / D, None, ALU.mult)
        v.tensor_tensor(sStat[:, 2], sStat[:, 0], sStat[:, 0], ALU.mult)
        return v.tensor_tensor(sStat[:, 2], sStat[:, 1], sStat[:, 2], ALU.subtract)
    g.add("vector", mv)
    g.bar()

    def rstd(sc):
        sc.activation(out=sStat[:, 3], in_=sStat[:, 2], func=AF.Ln, bias=sEps[:])
        return sc.activation(out=sStat[:, 4], in_=sStat[:, 3], func=AF.Exp, scale=-0.5)
    g.add("scalar", rstd)
    g.bar()

    g.add("tensor", lambda t: t.matmul(pBC[:], C["sOnesRow"][:], sStat[:, 0], start=True, stop=True))
    g.bar()
    g.add("vector", lambda v: v.tensor_copy(sMB[:], pBC[:]))
    g.bar()
    g.add("tensor", lambda t: t.matmul(pBC[:], C["sOnesRow"][:], sStat[:, 4], start=True, stop=True))
    g.bar()
    g.add("vector", lambda v: v.tensor_copy(sRB[:], pBC[:]))
    g.bar()

    def norm(v):
        ii = None
        for fc in range(8):
            v.tensor_tensor(sSq[:, fc % 2], sXb[:, fc], sMB[:], ALU.subtract)
            ii = v.tensor_tensor(sXn[:, fc], sSq[:, fc % 2], sRB[:], ALU.mult)
        return ii
    g.add("vector", norm)
    g.bar()


def common_bufs(g, psum=True):
    C = {}
    C["sOnes32"] = g.sb("c_ones32", [128, 1])
    C["sOnesRow"] = g.sb("c_onesrow", [1, 128])
    C["sOnesB"] = g.sb("c_onesb", [128, 1], BF16)
    C["sSq"] = g.sb("c_sq", [128, 2, LC])
    C["sStat"] = g.sb("c_stat", [1, 8, LC])
    C["sEps"] = g.sb("c_eps", [1, 1])
    C["sMB"] = g.sb("c_mb", [128, LC])
    C["sRB"] = g.sb("c_rb", [128, LC])
    if psum:
        C["pS1"] = g.ps("c_ps1", [1, 512])
        C["pS2"] = g.ps("c_ps2", [1, 512])
        C["pBC"] = g.ps("c_pbc", [128, 512])

    def cst(v):
        v.memset(C["sOnes32"][:], 1.0)
        v.memset(C["sOnesRow"][:], 1.0)
        v.memset(C["sOnesB"][:], 1.0)
        return v.memset(C["sEps"][:], 1e-5)
    g.add("vector", cst)
    return C



def build_attn_prep(qscale):
    g = G()
    xsl = g.io("xsl", [B, 8, 128, LC])
    WqT = g.io("WqT", [D, D]); WkT = g.io("WkT", [D, D]); WvT = g.io("WvT", [D, D])
    bcol = g.io("bcol", [128, 3, 8])
    ident = g.io("ident", [128, 128])
    q8 = g.io("q8", [NBH, HD, LC], BF16, out=True)
    kf = g.io("kf", [NBH, HD, LC], BF16, out=True)
    vt = g.io("vt", [NBH, LC, HD], BF16, out=True)
    nrm = g.io("nrm", [1, 2, NBH], out=True)

    sX = g.sb("sX", [128, B, 8, LC])
    sXn = g.sb("sXn", [128, 8, LC], BF16)
    sW = g.sb("sW", [128, 3, 8, 8, 128], BF16)
    sBia = g.sb("sBia", [128, 3, 8])
    sOut = g.sb("sOut", [128, 2, LC], BF16)
    sVT = g.sb("sVT", [128, 512], BF16)
    sQsq = g.sb("sQsq", [128, 2, LC], BF16)
    sNrm = g.sb("sNrm", [1, 2, NBH])
    sIdent = g.sb("sIdent", [128, 128], BF16)

    pP = g.ps("pP", [128, 2, 512])
    pT = g.ps("pT", [128, 512])
    pN = g.ps("pN", [1, 1024])

    C = common_bufs(g)
    sOnes32, sOnesB = C["sOnes32"], C["sOnesB"]

    for w, W in enumerate([WqT, WkT, WvT]):
        def ld(gp, w=w, W=W):
            return gp.dma_start(out=sW[:, w], in_=W.rearrange("(ec e) (fc f) -> e ec fc f", e=128, f=128))
        g.add("gpsimd", ld, dma=True, defer=10)
    g.add("gpsimd", lambda gp: gp.dma_start(out=sBia[:], in_=bcol[:]), dma=True)
    g.add("gpsimd", lambda gp: gp.dma_start(out=sIdent[:], in_=ident[:]), dma=True)
    g.add("sync", lambda s: s.dma_start(out=sX[:], in_=xsl.rearrange("b c f l -> f b c l")), dma=True)
    g.add("vector", lambda v: v.memset(sNrm[:], 0.0))
    g.bar()

    for b in range(B):
        emit_ln(g, sX[:, b], sXn, C)
        slots = [(w, fc) for w in range(3) for fc in range(8)]
        NS = len(slots)

        def proj(t, i):
            w, fc = slots[i]
            ii = None
            for ec in range(8):
                ii = t.matmul(pP[:, i % 2], sW[:, w, ec, fc], sXn[:, ec],
                              start=(ec == 0), stop=(ec == 7))
            return ii

        def ev(v, i):
            w, fc = slots[i]
            if w == 0:
                return v.tensor_scalar(sOut[:, i % 2], pP[:, i % 2],
                                       sBia[:, 0, fc:fc + 1], qscale, ALU.add, ALU.mult)
            return v.tensor_scalar(sOut[:, i % 2], pP[:, i % 2],
                                   sBia[:, w, fc:fc + 1], None, ALU.add)

        def nmm(t, i):
            return (t.matmul(pN[:, 0:512], sOnesB[0:64], sQsq[0:64, i % 2, :], start=True, stop=True,
                             tile_position=(0, 0)),
                    t.matmul(pN[:, 512:1024], sOnesB[64:128], sQsq[64:128, i % 2, :], start=True,
                             stop=True, tile_position=(64, 0)))[1]

        def vtr(t, i):
            ii = None
            for q in range(4):
                ii = t.matmul(pT[:, q * 128:(q + 1) * 128],
                              sOut[:, i % 2, q * 128:(q + 1) * 128],
                              sIdent[:], start=(q == 0), stop=(q == 3))
            return ii

        for i in range(NS + 2):
            def stA(t, i=i, b=b):
                ii = None
                if i < NS:
                    ii = proj(t, i)
                if 0 <= i - 2 < NS and slots[i - 2][0] < 2:
                    ii = nmm(t, i - 2)
                return ii
            if i < NS or (0 <= i - 2 < NS and slots[i - 2][0] < 2):
                g.add("tensor", stA)

            def stAv(v, i=i):
                ii = None
                if 0 <= i - 1 < NS:
                    ii = ev(v, i - 1)
                if 0 <= i - 2 < NS and slots[i - 2][0] == 2:
                    ii = v.tensor_copy(sVT[:], pT[:])
                return ii
            if 0 <= i - 1 < NS or (0 <= i - 2 < NS and slots[i - 2][0] == 2):
                g.add("vector", stAv)
            g.bar()
            j = i - 1
            if 0 <= j < NS:
                w, fc = slots[j]
                h0 = fc * 2
                bh0 = b * NH + h0
                if w < 2:
                    g.add("scalar", lambda sc, j=j: sc.activation(
                        out=sQsq[:, j % 2, :], in_=sOut[:, j % 2], func=AF.Square))
                    tgt = q8 if w == 0 else kf
                    g.add("sync", lambda s, j=j, tgt=tgt, bh0=bh0: s.dma_start(
                        out=tgt[bh0:bh0 + 2].rearrange("t d l -> (t d) l"), in_=sOut[:, j % 2]), dma=True)
                else:
                    g.add("tensor", lambda t, j=j: vtr(t, j))
            j2 = i - 2
            if 0 <= j2 < NS:
                w2, fc2 = slots[j2]
                bh0b = b * NH + fc2 * 2
                if w2 < 2:
                    g.add("vector", lambda v, j2=j2, w2=w2, bh0b=bh0b: (
                        v.reduce_max(sNrm[:, w2, bh0b:bh0b + 1], pN[:, 0:512], AX.X),
                        v.reduce_max(sNrm[:, w2, bh0b + 1:bh0b + 2], pN[:, 512:1024], AX.X))[1])
                else:
                    g.add("sync", lambda s, bh0b=bh0b: s.dma_start(
                        out=vt[bh0b].rearrange("(q p) d -> p q d", p=128),
                        in_=sVT[:].rearrange("p (q d) -> p q d", d=128)[:, :, 0:64]), dma=True)
                    g.add("sync", lambda s, bh0b=bh0b: s.dma_start(
                        out=vt[bh0b + 1].rearrange("(q p) d -> p q d", p=128),
                        in_=sVT[:].rearrange("p (q d) -> p q d", d=128)[:, :, 64:128]), dma=True)
            g.bar()

    g.add("sync", lambda s: s.dma_start(out=nrm[:], in_=sNrm[:]), dma=True)
    g.bar()
    return g.build()



def to_fm(x):
    return np.ascontiguousarray(x.transpose(0, 2, 1).reshape(B, 8, 128, L))


def from_fm(x_fm):
    return np.ascontiguousarray(x_fm.reshape(B, D, L).transpose(0, 2, 1))


def run_attn_prep(x_fm, Wq, bq, Wk, bk, Wv, bv, qscale):
    key = f"a1_{qscale}"
    if key not in _BUILDERS:
        _BUILDERS[key] = build_attn_prep(qscale)
    nc = _BUILDERS[key]
    bcol = np.stack([b_.reshape(8, 128).T for b_ in (bq, bk, bv)], axis=1)
    ins = []
    for c in range(NCORES):
        ins.append({
            "xsl": f32c(x_fm[:, :, :, LC * c:LC * (c + 1)]),
            "WqT": f32c(Wq.T), "WkT": f32c(Wk.T), "WvT": f32c(Wv.T),
            "bcol": f32c(bcol), "ident": IDENT,
        })
    res = run_bass_kernel_spmd(nc, ins, core_ids=list(range(NCORES)))
    q8 = np.concatenate([r["q8"][None] for r in res.results], axis=0)
    kf = np.concatenate([r["kf"][None] for r in res.results], axis=0)
    vtt = np.concatenate([r["vt"][None] for r in res.results], axis=0)
    nrm = np.stack([r["nrm"][0] for r in res.results], axis=0)
    q8f = np.transpose(q8, (1, 2, 0, 3)).reshape(NBH, HD, L)
    kff = np.transpose(kf, (1, 2, 0, 3)).reshape(NBH, HD, L)
    vtf = np.transpose(vtt, (1, 0, 2, 3)).reshape(NBH, L, HD)
    return q8f, kff, vtf, nrm





PADA = 512
WRELA = 3072
VPAST = 513
VFUT = 2559
BHC = NBH // NCORES


def _slab_window(s):
    return max(0, 4 * s - 8), min(31, 4 * s + 11)


def build_attn_core():
    g = G()
    q8 = g.io("q8", [BHC, HD, L], BF16)
    kf = g.io("kf", [BHC, HD, L], BF16)
    vt = g.io("vt", [BHC, L, HD], BF16)
    relT = g.io("relT", [HD, WRELA])
    nrm8 = g.io("nrm8", [BHC, 2, NCORES])
    ident = g.io("ident", [128, 128])
    ao = g.io("ao", [BHC, HD, L], BF16, out=True)

    sQ8 = g.sb("sQ8", [64, L], BF16)
    sKf = g.sb("sKf", [64, L], BF16)
    sVa = g.sb("sVa", [128, 32, 65], BF16)
    sRel = g.sb("sRel", [64, WRELA], BF16)
    sRelSq = g.sb("sRelSq", [64, 512], BF16)
    sP = g.sb("sP", [128, 2, WRELA], BF16)
    sSkew = g.sb("sSkew", [128, 4, 2560], BF16)
    sAT = g.sb("sAT", [128, 4, 512], BF16)
    sEdge = g.sb("sEdge", [128, 4, 2], BF16)
    sERow = g.sb("sERow", [1, 2, 512], BF16)
    sIdentB = g.sb("sIdentB", [128, 128], BF16)
    sNrm8 = g.sb("sNrm8", [BHC, 2, NCORES])
    sMQK = g.sb("sMQK", [BHC, 6])
    sMcolB = g.sb("sMcolB", [BHC, 1], BF16)
    sMrowB = g.sb("sMrowB", [1, BHC], BF16)
    sMB128 = g.sb("sMB128", [128, BHC])
    sRn = g.sb("sRn", [1, WRELA])
    sRnMax = g.sb("sRnMax", [1, 2])
    sComb = g.sb("sComb", [65, 512])
    sFar = g.sb("sFar", [65, 512])
    sBCf = g.sb("sBCf", [65, 512])
    sDrowB = g.sb("sDrowB", [1, 512], BF16)
    sDrow = g.sb("sDrow", [1, 512])
    sDB = g.sb("sDB", [64, 512])
    sAO = g.sb("sAO", [64, 512], BF16)
    sOnesRow = g.sb("sOnesRow", [1, 128])
    sOnesRowB = g.sb("sOnesRowB", [1, 128], BF16)
    sOnes64B = g.sb("sOnes64B", [64, 1], BF16)

    pSc4 = g.ps("pSc4", [128, 4, 512])
    pPV = g.ps("pPV", [65, 3, 512])
    pBC = g.ps("pBC", [128, 512])

    g.add("gpsimd", lambda gp: gp.dma_start(out=sRel[:], in_=relT[:]), dma=True)
    g.add("gpsimd", lambda gp: gp.dma_start(out=sIdentB[:], in_=ident[:]), dma=True)
    g.add("sync", lambda s: s.dma_start(out=sNrm8[:], in_=nrm8[:]), dma=True)

    def cst(v):
        v.memset(sOnesRow[:], 1.0)
        v.memset(sOnesRowB[:], 1.0)
        v.memset(sOnes64B[:], 1.0)
        return v.memset(sVa[:, :, 64:65], 1.0)
    g.add("vector", cst)
    g.bar()

    for k in range(WRELA // 512):
        def rsq(sc, k=k):
            return sc.activation(out=sRelSq[:], in_=sRel[:, 512 * k:512 * (k + 1)], func=AF.Square)
        g.add("scalar", rsq)
        g.bar()

        def rnmm(t, k=k):
            return t.matmul(pSc4[0:1, 0, :], sOnes64B[:], sRelSq[:], start=True, stop=True)
        g.add("tensor", rnmm)
        g.bar()

        def rncp(v, k=k):
            return v.tensor_copy(sRn[:, 512 * k:512 * (k + 1)], pSc4[0:1, 0, :])
        g.add("vector", rncp)
        g.bar()
    g.add("vector", lambda v: v.reduce_max(sRnMax[:, 0:1], sRn[:], AX.X))
    g.bar()
    g.add("scalar", lambda sc: sc.activation(out=sRnMax[:, 1:2], in_=sRnMax[:, 0:1], func=AF.Sqrt))
    g.bar()

    def mred(v):
        v.reduce_max(sMQK[:, 0:1], sNrm8[:, 0], AX.X)
        return v.reduce_max(sMQK[:, 1:2], sNrm8[:, 1], AX.X)
    g.add("vector", mred)
    g.bar()
    g.add("scalar", lambda sc: sc.activation(out=sMQK[:, 2:4], in_=sMQK[:, 0:2], func=AF.Sqrt))
    g.bar()
    g.add("tensor", lambda t: t.matmul(pBC[0:BHC, 0:1], sOnesRow[:, 0:BHC], sRnMax[:, 1:2],
                                       start=True, stop=True))
    g.bar()

    g.add("vector", lambda v: v.tensor_copy(sMQK[:, 4:5], pBC[0:BHC, 0:1]))
    g.bar()
    g.add("vector", lambda v: v.tensor_tensor(sMQK[:, 4:5], sMQK[:, 3:4], sMQK[:, 4:5], ALU.add))
    g.bar()
    g.add("vector", lambda v: v.tensor_tensor(sMQK[:, 5:6], sMQK[:, 2:3], sMQK[:, 4:5], ALU.mult))
    g.bar()
    g.add("vector", lambda v: v.tensor_scalar(sMcolB[:], sMQK[:, 5:6], -1.0, None, ALU.mult))
    g.bar()
    g.add("tensor", lambda t: t.matmul(pBC[0:1, 0:BHC], sMcolB[:], sIdentB[0:BHC, 0:BHC],
                                       start=True, stop=True))
    g.bar()
    g.add("vector", lambda v: v.tensor_copy(sMrowB[:], pBC[0:1, 0:BHC]))
    g.bar()
    g.add("tensor", lambda t: t.matmul(pBC[:, 0:BHC], sOnesRowB[:], sMrowB[:], start=True, stop=True))
    g.bar()
    g.add("vector", lambda v: v.tensor_copy(sMB128[:], pBC[:, 0:BHC]))
    g.bar()

    for bhi in range(BHC):
        g.add("sync", lambda s, bhi=bhi: s.dma_start(out=sQ8[:], in_=q8[bhi]), dma=True)
        g.add("sync", lambda s, bhi=bhi: s.dma_start(out=sKf[:], in_=kf[bhi]), dma=True)
        g.add("sync", lambda s, bhi=bhi: s.dma_start(
            out=sVa[:, :, 0:64], in_=vt[bhi].rearrange("(jc p) d -> p jc d", p=128)), dma=True)
        g.bar()

        for s_ in range(8):
            jclo, jchi = _slab_window(s_)
            nch = jchi - jclo + 1
            width = nch * 128
            has_past = jclo > 0
            has_fut = jchi < 31

            def eext(v, qt):
                v.tensor_copy(sEdge[:, qt, 0:1], sP[:, qt % 2, VPAST:VPAST + 1])
                return v.tensor_copy(sEdge[:, qt, 1:2], sP[:, qt % 2, VFUT:VFUT + 1])

            def skdma(s, qt, s_, jclo, width):
                S0 = (qt % 2) * WRELA + 1536 + jclo * 128 - 512 * s_ - 128 * qt
                src = AP(sP, S0, [[2 * WRELA - 1, 128], [1, width]])
                return s.dma_start(out=sSkew[:, qt, 0:width], in_=src)

            for qt in range(5):
                for k in range(7 if qt < 4 else 1):
                    if k < 6 and qt < 4:
                        def pmm(t, qt=qt, k=k, s_=s_):
                            return t.matmul(pSc4[:, k % 2],
                                            sQ8[:, 512 * s_ + 128 * qt: 512 * s_ + 128 * (qt + 1)],
                                            sRel[:, 512 * k:512 * (k + 1)], start=True, stop=True)
                        g.add("tensor", pmm)
                    if k >= 1 and qt < 4:
                        g.add("scalar", lambda sc, qt=qt, k=k: sc.activation(
                            out=sP[:, qt % 2, 512 * (k - 1):512 * k], in_=pSc4[:, (k - 1) % 2],
                            func=AF.Copy))
                    if k == 0 and qt >= 1:
                        g.add("vector", lambda v, qt=qt: eext(v, qt - 1))
                        g.add("sync",
                              lambda s, qt=qt, s_=s_, jclo=jclo, width=width: skdma(
                                  s, qt - 1, s_, jclo, width), dma=True,
                              defer=6 if qt < 4 else 0)
                    g.bar()

            cls_of = {}
            for jc in range(32):
                cls_of[jc] = 0 if jclo <= jc <= jchi else (1 if jc < jclo else 2)
            firsts = {c: min((j for j in range(32) if cls_of[j] == c), default=None) for c in range(3)}
            lasts = {c: max((j for j in range(32) if cls_of[j] == c), default=None) for c in range(3)}

            farp = [p for p in range(16) if cls_of[2 * p] != 0 and cls_of[2 * p + 1] != 0]
            rest = [p for p in range(16) if p not in farp]
            order = list(range(16))
            exjc = [jc for p in order for jc in (2 * p, 2 * p + 1)]
            efirst = {c: next((jc for jc in exjc if cls_of[jc] == c), None) for c in range(3)}
            elast = {c: next((jc for jc in reversed(exjc) if cls_of[jc] == c), None) for c in range(3)}

            def content2(t, k, s_=s_, jclo=jclo, cls_of=cls_of, order=order):
                ii = None
                for i, jc in enumerate((2 * order[k], 2 * order[k] + 1)):
                    band = cls_of[jc] == 0
                    bk = 2 * (k % 2) + i
                    ii = t.matmul(pSc4[:, bk], sKf[:, 128 * jc:128 * (jc + 1)],
                                  sQ8[:, 512 * s_:512 * (s_ + 1)], start=True, stop=not band)
                    if band:
                        for qt in range(4):
                            ii = t.matmul(pSc4[:, bk, 128 * qt:128 * (qt + 1)],
                                          sSkew[:, qt, (jc - jclo) * 128:(jc - jclo + 1) * 128],
                                          sIdentB[:], start=False, stop=(qt == 3))
                return ii

            def exp2(sc, k, bhi=bhi):
                base = 2 * (k % 2)
                return sc.activation(out=sAT[:, base:base + 2, :],
                                     in_=pSc4[:, base:base + 2, :],
                                     func=AF.Exp, bias=sMB128[:, bhi:bhi + 1])

            def pv2(t, k, cls_of=cls_of, order=order, efirst=efirst, elast=elast):
                ii = None
                base = 2 * (k % 2)
                for i, jc in enumerate((2 * order[k], 2 * order[k] + 1)):
                    c = cls_of[jc]
                    ii = t.matmul(pPV[:, c], sVa[:, jc], sAT[:, base + i, :],
                                  start=(jc == efirst[c]), stop=(jc == elast[c]))
                return ii

            for st in range(18):
                if st < 16:
                    g.add("tensor", lambda t, st=st, f=content2: f(t, st))
                if 1 <= st <= 16:
                    g.add("scalar", lambda sc, st=st, f=exp2: f(sc, st - 1))
                if st >= 2:
                    g.add("tensor", lambda t, st=st, f=pv2: f(t, st - 2))
                g.bar()

            def etr(t, col, tgt):
                ii = None
                for qt in range(4):
                    ii = t.matmul(tgt[0:1, 128 * qt:128 * (qt + 1)], sEdge[:, qt, col:col + 1],
                                  sIdentB[:], start=(qt == 0), stop=(qt == 3))
                return ii

            if has_past:
                g.add("tensor", lambda t: etr(t, 0, pBC))
            g.add("vector", lambda v: v.tensor_copy(sComb[:], pPV[:, 0]))
            g.bar()
            if has_fut:
                g.add("tensor", lambda t: etr(t, 1, pSc4[:, 0, :]))
            if has_past:
                g.add("scalar", lambda sc: sc.activation(out=sERow[:, 0], in_=pBC[0:1, :], func=AF.Exp))
            g.bar()
            if has_past:
                g.add("tensor", lambda t: t.matmul(pBC[0:65, :], sOnesRowB[:, 0:65], sERow[:, 0],
                                                   start=True, stop=True))
            if has_fut:
                g.add("scalar", lambda sc: sc.activation(out=sERow[:, 1], in_=pSc4[0:1, 0, :], func=AF.Exp))
            g.bar()
            if has_fut:
                g.add("tensor", lambda t: t.matmul(pSc4[0:65, 0, :], sOnesRowB[:, 0:65], sERow[:, 1],
                                                   start=True, stop=True))
            if has_past:
                def fm0(v):
                    v.tensor_copy(sBCf[:], pBC[0:65, :])
                    v.tensor_tensor(sFar[:], pPV[:, 1], sBCf[:], ALU.mult)
                    return v.tensor_tensor(sComb[:], sComb[:], sFar[:], ALU.add)
                g.add("vector", fm0)
            g.bar()
            def fm1(v, has_fut=has_fut):
                if has_fut:
                    v.tensor_copy(sBCf[:], pSc4[0:65, 0, :])
                    v.tensor_tensor(sFar[:], pPV[:, 2], sBCf[:], ALU.mult)
                    v.tensor_tensor(sComb[:], sComb[:], sFar[:], ALU.add)
                v.reciprocal(sDrow[:], sComb[64:65, :])
                return v.tensor_copy(sDrowB[:], sDrow[:])
            g.add("vector", fm1)
            g.bar()
            g.add("tensor", lambda t: t.matmul(pBC[0:64, :], sOnesRowB[:, 0:64], sDrowB[:],
                                               start=True, stop=True))
            g.bar()

            def fin(v):
                v.tensor_copy(sDB[:], pBC[0:64, :])
                return v.tensor_tensor(sAO[:], sComb[0:64, :], sDB[:], ALU.mult)
            g.add("vector", fin)
            g.bar()
            g.add("sync", lambda s, bhi=bhi, s_=s_: s.dma_start(
                out=ao[bhi, :, 512 * s_:512 * (s_ + 1)], in_=sAO[:]), dma=True)
            g.bar()

    return g.build()


def build_attn_out():
    g = G()
    xsl = g.io("xsl", [B, 8, 128, LC])
    aosl = g.io("aosl", [NBH, HD, LC], BF16)
    WoT = g.io("WoT", [D, D])
    bocol = g.io("bocol", [128, 8])
    xout = g.io("xout", [B, 8, 128, LC], out=True)

    sX = g.sb("sX", [128, B, 8, LC])
    sAo = g.sb("sAo", [64, NBH, LC], BF16)
    sWo = g.sb("sWo", [64, 16, 8, 128], BF16)
    sBo = g.sb("sBo", [128, 8])
    sY = g.sb("sY", [128, 2, LC])
    pY = g.ps("pY", [128, 2, 512])

    g.add("sync", lambda s: s.dma_start(out=sX[:], in_=xsl.rearrange("b c f l -> f b c l")), dma=True)
    g.add("sync", lambda s: s.dma_start(out=sAo[:], in_=aosl.rearrange("t d l -> d t l")), dma=True)
    g.add("gpsimd", lambda gp: gp.dma_start(
        out=sWo[:], in_=WoT.rearrange("(h e) (fc f) -> e h fc f", e=64, f=128)), dma=True)
    g.add("gpsimd", lambda gp: gp.dma_start(out=sBo[:], in_=bocol[:]), dma=True)
    g.bar()

    for b in range(B):
        for fc in range(8):
            bank = fc % 2

            def ymm(t, b=b, fc=fc, bank=bank):
                ii = None
                for h in range(16):
                    ii = t.matmul(pY[:, bank], sWo[:, h, fc], sAo[:, b * 16 + h],
                                  start=(h == 0), stop=(h == 15))
                return ii
            g.add("tensor", ymm)
            g.bar()

            def yev(v, b=b, fc=fc, bank=bank):
                return v.scalar_tensor_tensor(sY[:, bank], pY[:, bank], sBo[:, fc:fc + 1],
                                              sX[:, b, fc], ALU.add, ALU.add)
            g.add("vector", yev)
            g.bar()
            g.add("sync", lambda s, b=b, fc=fc, bank=bank: s.dma_start(
                out=xout[b, fc], in_=sY[:, bank]), dma=True)
    g.bar()
    return g.build()


def make_relT_l0(rel_table):
    pu = np.arange(WRELA)
    idx = np.clip(pu - PADA, 0, 2046)
    rel_ext = rel_table[idx]
    return np.ascontiguousarray(rel_ext[::-1].T)


def run_attn_core(q8f, kff, vtf, nrm, rel_table):
    if "a2l0" not in _BUILDERS:
        _BUILDERS["a2l0"] = build_attn_core()
    nc = _BUILDERS["a2l0"]
    relT = f32c(make_relT_l0(rel_table))
    ins = []
    for c in range(NCORES):
        sl = slice(BHC * c, BHC * (c + 1))
        ins.append({
            "q8": q8f[sl], "kf": kff[sl], "vt": vtf[sl],
            "relT": relT,
            "nrm8": f32c(np.transpose(nrm[:, :, sl], (2, 1, 0))),
            "ident": IDENT,
        })
    res = run_bass_kernel_spmd(nc, ins, core_ids=list(range(NCORES)))
    return np.concatenate([r["ao"] for r in res.results], axis=0)


def run_attn_out(x_fm, ao, Wo, bo):
    if "a3" not in _BUILDERS:
        _BUILDERS["a3"] = build_attn_out()
    nc = _BUILDERS["a3"]
    bocol = bo.reshape(8, 128).T
    ins = []
    for c in range(NCORES):
        ins.append({
            "xsl": f32c(x_fm[:, :, :, LC * c:LC * (c + 1)]),
            "aosl": np.ascontiguousarray(ao[:, :, LC * c:LC * (c + 1)]),
            "WoT": f32c(Wo.T), "bocol": f32c(bocol),
        })
    res = run_bass_kernel_spmd(nc, ins, core_ids=list(range(NCORES)))
    out = np.zeros((B, 8, 128, L), np.float32)
    for c in range(NCORES):
        out[:, :, :, LC * c:LC * (c + 1)] = res.results[c]["xout"]
    return out








NTOK = B * L
KSEL = 2048
FF0 = 2816
XGW = 1032


def build_ffn0_route():
    g = G()
    xsl = g.io("xsl", [B, 8, 128, LC])
    Wr = g.io("Wr", [D, 8])
    br = g.io("br", [1, 8])
    ident = g.io("ident", [128, 128])
    smslc = g.io("smslc", [8, B, LC], out=True)
    xnslc = g.io("xnslc", [B * LC, D], BF16, out=True)

    sX = g.sb("sX", [128, B, 8, LC])
    sXn = g.sb("sXn", [128, 8, LC], BF16)
    sWr = g.sb("sWr", [128, 8, 8], BF16)
    sBrRow = g.sb("sBrRow", [1, 8])
    sBr = g.sb("sBr", [128, 8])
    sLg = g.sb("sLg", [128, 8])
    sSm = g.sb("sSm", [128, 8])
    sSmT = g.sb("sSmT", [8, 128])
    sTmp = g.sb("sTmp", [128, 2])
    sXT = g.sb("sXT", [128, 512], BF16)
    sIdentB = g.sb("sIdentB", [128, 128], BF16)
    sIdent32 = g.sb("sIdent32", [128, 128])
    pL = g.ps("pL", [128, 2, 512])
    pT = g.ps("pT", [128, 512])

    C = common_bufs(g)
    g.add("gpsimd", lambda gp: gp.dma_start(
        out=sWr[:], in_=Wr.rearrange("(ec e) o -> e ec o", e=128)), dma=True)
    g.add("gpsimd", lambda gp: gp.dma_start(out=sIdentB[:], in_=ident[:]), dma=True)
    g.add("sync", lambda s: s.dma_start(out=sBrRow[:], in_=br[:]), dma=True)
    g.add("sync", lambda s: s.dma_start(out=sIdent32[:], in_=ident[:]), dma=True)
    g.add("sync", lambda s: s.dma_start(out=sX[:], in_=xsl.rearrange("b c f l -> f b c l")), dma=True)
    g.bar()
    g.add("tensor", lambda t: t.matmul(pT[:, 0:8], C["sOnesRow"][:], sBrRow[:], start=True, stop=True))
    g.bar()
    g.add("vector", lambda v: v.tensor_copy(sBr[:], pT[:, 0:8]))
    g.bar()

    for b in range(B):
        emit_ln(g, sX[:, b], sXn, C)
        for fc in range(8):
            def xtr(t, fc=fc):
                ii = None
                for q in range(4):
                    ii = t.matmul(pT[:, q * 128:(q + 1) * 128], sXn[:, fc, q * 128:(q + 1) * 128],
                                  sIdentB[:], start=(q == 0), stop=(q == 3))
                return ii
            g.add("tensor", xtr)
            g.bar()
            g.add("vector", lambda v: v.tensor_copy(sXT[:], pT[:]))
            g.bar()

            def stx(s, b=b, fc=fc):
                return s.dma_start(out=xnslc[b * LC:(b + 1) * LC, 128 * fc:128 * (fc + 1)]
                                   .rearrange("(q p) f -> p q f", p=128),
                                   in_=sXT[:].rearrange("p (q f) -> p q f", f=128))
            g.add("sync", stx, dma=True)
            g.bar()

        for q in range(4):
            def lmm(t, q=q):
                ii = None
                for ec in range(8):
                    ii = t.matmul(pL[:, q % 2, 0:8], sXn[:, ec, 128 * q:128 * (q + 1)],
                                  sWr[:, ec], start=(ec == 0), stop=(ec == 7))
                return ii
            g.add("tensor", lmm)
            g.bar()

            g.add("vector", lambda v, q=q: v.tensor_tensor(sLg[:], pL[:, q % 2, 0:8], sBr[:], ALU.add))
            g.bar()
            g.add("vector", lambda v: v.reduce_max(sTmp[:, 0:1], sLg[:], AX.X))
            g.bar()
            g.add("vector", lambda v: v.tensor_scalar(sLg[:], sLg[:], sTmp[:, 0:1], None, ALU.subtract))
            g.bar()
            g.add("scalar", lambda sc: sc.activation(out=sSm[:], in_=sLg[:], func=AF.Exp))
            g.bar()

            g.add("vector", lambda v: v.reduce_sum(sTmp[:, 0:1], sSm[:], AX.X))
            g.bar()
            g.add("vector", lambda v: v.reciprocal(sTmp[:, 1:2], sTmp[:, 0:1]))
            g.bar()
            g.add("vector", lambda v: v.tensor_scalar(sSm[:], sSm[:], sTmp[:, 1:2], None, ALU.mult))
            g.bar()
            g.add("tensor", lambda t: t.matmul(pT[0:8, 0:128], sSm[:], sIdent32[:], start=True, stop=True))
            g.bar()
            g.add("vector", lambda v: v.tensor_copy(sSmT[:], pT[0:8, 0:128]))
            g.bar()
            g.add("sync", lambda s, b=b, q=q: s.dma_start(
                out=smslc[:, b, 128 * q:128 * (q + 1)], in_=sSmT[:]), dma=True)
            g.bar()
    return g.build()


def build_ffn0_select():
    g = G()
    sme = g.io("sme", [1, NTOK])
    ident = g.io("ident", [128, 128])
    pmap = g.io("pmap", [NTOK], I32, out=True)
    pptT = g.io("pptT", [128, 128], I32, out=True)
    smvT = g.io("smvT", [128, 128], out=True)
    dbg = g.io("dbg", [128, 6, 128], out=True)
    dbgsc = g.io("dbgsc", [1, 16], out=True)

    sSmv = g.sb("sSmv", [128, 128])
    sIot = g.sb("sIotI", [128, 128], I32)
    sIotF = g.sb("sIotF", [128, 128])
    sMask = g.sb("sMask", [128, 128])
    sMask2 = g.sb("sMask2", [128, 128])
    sRows = g.sb("sRows", [128, 1])
    sSc = g.sb("sSc", [1, 16])
    sCol = g.sb("sCol", [128, 4])
    sPre = g.sb("sPre", [128, 128])
    sLTc = g.sb("sLTc", [128, 1], BF16)
    sLT = g.sb("sLT", [1, 128])
    sOffE = g.sb("sOffE", [1, 128])
    sOff = g.sb("sOff", [128, 1])
    sPp = g.sb("sPp", [128, 128])
    sPpI = g.sb("sPpI", [128, 128], I32)
    sPpT = g.sb("sPpT", [128, 128])
    sPpTI = g.sb("sPpTI", [128, 128], I32)
    sSmvT = g.sb("sSmvT", [128, 128])
    sIdentB = g.sb("sIdentB", [128, 128], BF16)
    sIdent32 = g.sb("sIdent32", [128, 128])
    sOnesRow = g.sb("sOnesRow", [1, 128])
    sOnes32 = g.sb("sOnes32", [128, 1])

    pC = g.ps("pC", [128, 512])
    pD = g.ps("pD", [128, 2, 512])

    g.add("sync", lambda s: s.dma_start(out=sSmv[:], in_=sme.rearrange("o (l p) -> (o l) p", p=128)), dma=True)
    g.add("gpsimd", lambda gp: gp.dma_start(out=sIdentB[:], in_=ident[:]), dma=True)
    g.add("sync", lambda s: s.dma_start(out=sIdent32[:], in_=ident[:]), dma=True)

    def io0(gp):
        gp.iota(sIot[:], pattern=[[1, 128]], base=0, channel_multiplier=128)
        return gp.tensor_copy(sIotF[:], sIot[:])
    g.add("gpsimd", io0)

    def cst(v):
        v.memset(sOnesRow[:], 1.0)
        v.memset(sOnes32[:], 1.0)
        v.memset(sSc[:], 0.0)
        v.memset(sSc[:, 1:2], 1.0)
        return v.memset(sSc[:, 8:9], -1.0)
    g.add("vector", cst)
    g.bar()
    for it in range(40):
        g.add("vector", lambda v: v.tensor_tensor(sSc[:, 2:3], sSc[:, 0:1], sSc[:, 1:2], ALU.add))
        g.bar()
        g.add("vector", lambda v: v.tensor_scalar(sSc[:, 2:3], sSc[:, 2:3], 0.5, None, ALU.mult))
        g.bar()

        g.add("tensor", lambda t: t.matmul(pC[:, 0:1], sOnesRow[:], sSc[:, 2:3], start=True, stop=True))
        g.bar()
        g.add("vector", lambda v: v.tensor_copy(sCol[:, 0:1], pC[:, 0:1]))
        g.bar()

        g.add("vector", lambda v: v.tensor_scalar(sMask[:], sSmv[:], sCol[:, 0:1], None, ALU.is_gt))
        g.bar()
        g.add("vector", lambda v: v.reduce_sum(sRows[:], sMask[:], AX.X))
        g.bar()

        g.add("tensor", lambda t: t.matmul(pC[0:1, 4:5], sOnes32[:], sRows[:], start=True, stop=True))
        g.bar()

        g.add("vector", lambda v: v.tensor_copy(sSc[:, 3:4], pC[0:1, 4:5]))
        g.bar()
        g.add("vector", lambda v: v.tensor_scalar(sSc[:, 4:5], sSc[:, 3:4], float(KSEL) - 0.5, None, ALU.is_gt))
        g.bar()
        g.add("vector", lambda v: v.tensor_tensor(sSc[:, 5:6], sSc[:, 2:3], sSc[:, 0:1], ALU.subtract))
        g.bar()
        g.add("vector", lambda v: v.tensor_tensor(sSc[:, 5:6], sSc[:, 5:6], sSc[:, 4:5], ALU.mult))
        g.bar()
        g.add("vector", lambda v: v.tensor_tensor(sSc[:, 0:1], sSc[:, 0:1], sSc[:, 5:6], ALU.add))
        g.bar()
        g.add("vector", lambda v: v.tensor_tensor(sSc[:, 6:7], sSc[:, 1:2], sSc[:, 2:3], ALU.subtract))
        g.bar()
        g.add("vector", lambda v: v.tensor_tensor(sSc[:, 6:7], sSc[:, 6:7], sSc[:, 4:5], ALU.mult))
        g.bar()
        g.add("vector", lambda v: v.tensor_tensor(sSc[:, 1:2], sSc[:, 2:3], sSc[:, 6:7], ALU.add))
        g.bar()


    g.add("tensor", lambda t: t.matmul(pC[:, 0:2], sOnesRow[:], sSc[:, 0:2], start=True, stop=True))
    g.bar()
    g.add("vector", lambda v: v.tensor_copy(sCol[:, 1:3], pC[:, 0:2]))
    g.bar()

    g.add("vector", lambda v: v.tensor_scalar(sMask[:], sSmv[:], sCol[:, 2:3], None, ALU.is_gt))
    g.bar()
    g.add("vector", lambda v: v.tensor_scalar(sMask2[:], sSmv[:], sCol[:, 1:2], None, ALU.is_gt))
    g.bar()
    g.add("vector", lambda v: v.tensor_tensor(sMask2[:], sMask2[:], sMask[:], ALU.subtract))
    g.bar()
    g.add("vector", lambda v: v.reduce_sum(sRows[:], sMask[:], AX.X))
    g.bar()

    g.add("tensor", lambda t: t.matmul(pC[0:1, 4:5], sOnes32[:], sRows[:], start=True, stop=True))
    g.bar()

    g.add("vector", lambda v: v.tensor_copy(sSc[:, 6:7], pC[0:1, 4:5]))
    g.bar()
    g.add("vector", lambda v: v.tensor_scalar(sSc[:, 7:8], sSc[:, 6:7], -1.0, float(KSEL), ALU.mult, ALU.add))
    g.bar()
    g.add("vector", lambda v: v.memset(sSc[:, 9:10], float(NTOK)))
    g.bar()


    for it in range(16):
        g.add("vector", lambda v: v.tensor_tensor(sSc[:, 10:11], sSc[:, 8:9], sSc[:, 9:10], ALU.add))
        g.bar()
        g.add("vector", lambda v: v.tensor_scalar(sSc[:, 10:11], sSc[:, 10:11], 0.5, None, ALU.mult))
        g.bar()

        g.add("tensor", lambda t: t.matmul(pC[:, 0:1], sOnesRow[:], sSc[:, 10:11], start=True, stop=True))
        g.bar()
        g.add("vector", lambda v: v.tensor_copy(sCol[:, 3:4], pC[:, 0:1]))
        g.bar()

        g.add("vector", lambda v: v.tensor_scalar(sPre[:], sIotF[:], sCol[:, 3:4], None, ALU.is_le))
        g.bar()
        g.add("vector", lambda v: v.tensor_tensor(sPre[:], sPre[:], sMask2[:], ALU.mult))
        g.bar()
        g.add("vector", lambda v: v.reduce_sum(sRows[:], sPre[:], AX.X))
        g.bar()

        g.add("tensor", lambda t: t.matmul(pC[0:1, 4:5], sOnes32[:], sRows[:], start=True, stop=True))
        g.bar()

        g.add("vector", lambda v: v.tensor_copy(sSc[:, 3:4], pC[0:1, 4:5]))
        g.bar()
        g.add("vector", lambda v: v.tensor_tensor(sSc[:, 4:5], sSc[:, 3:4], sSc[:, 7:8], ALU.is_lt))
        g.bar()
        g.add("vector", lambda v: v.tensor_tensor(sSc[:, 5:6], sSc[:, 10:11], sSc[:, 8:9], ALU.subtract))
        g.bar()
        g.add("vector", lambda v: v.tensor_tensor(sSc[:, 5:6], sSc[:, 5:6], sSc[:, 4:5], ALU.mult))
        g.bar()
        g.add("vector", lambda v: v.tensor_tensor(sSc[:, 8:9], sSc[:, 8:9], sSc[:, 5:6], ALU.add))
        g.bar()
        g.add("vector", lambda v: v.tensor_tensor(sSc[:, 6:7], sSc[:, 9:10], sSc[:, 10:11], ALU.subtract))
        g.bar()
        g.add("vector", lambda v: v.tensor_tensor(sSc[:, 6:7], sSc[:, 6:7], sSc[:, 4:5], ALU.mult))
        g.bar()
        g.add("vector", lambda v: v.tensor_tensor(sSc[:, 9:10], sSc[:, 10:11], sSc[:, 6:7], ALU.add))
        g.bar()


    g.add("tensor", lambda t: t.matmul(pC[:, 0:1], sOnesRow[:], sSc[:, 9:10], start=True, stop=True))
    g.bar()
    g.add("vector", lambda v: v.tensor_copy(sCol[:, 3:4], pC[:, 0:1]))
    g.bar()

    g.add("vector", lambda v: v.tensor_scalar(sPre[:], sIotF[:], sCol[:, 3:4], None, ALU.is_le))
    g.bar()
    g.add("vector", lambda v: v.tensor_tensor(sPre[:], sPre[:], sMask2[:], ALU.mult))
    g.bar()
    g.add("vector", lambda v: v.tensor_tensor(sMask[:], sMask[:], sPre[:], ALU.add))
    g.bar()


    g.add("vector", lambda v: v.tensor_tensor_scan(sPre[:], sMask[:], sMask[:], 0.0, ALU.add, ALU.bypass))
    g.bar()
    g.add("vector", lambda v: v.tensor_copy(sLTc[:], sPre[:, 127:128]))
    g.bar()

    g.add("tensor", lambda t: t.matmul(pC[0:1, 0:128], sLTc[:], sIdentB[:], start=True, stop=True))
    g.bar()
    g.add("vector", lambda v: v.tensor_copy(sLT[:], pC[0:1, 0:128]))
    g.bar()

    g.add("vector", lambda v: v.tensor_tensor_scan(sOffE[:], sLT[:], sLT[:], 0.0, ALU.add, ALU.bypass))
    g.bar()
    g.add("vector", lambda v: v.memset(sLT[:, 0:1], 0.0))
    g.bar()
    g.add("vector", lambda v: v.tensor_copy(sLT[:, 1:128], sOffE[:, 0:127]))
    g.bar()

    g.add("tensor", lambda t: t.matmul(pC[:, 0:1], sLT[:], sOnesRow[:, 0:1], start=True, stop=True))
    g.bar()
    g.add("vector", lambda v: v.tensor_copy(sOff[:], pC[:, 0:1]))
    g.bar()

    g.add("vector", lambda v: v.tensor_scalar(sPp[:], sPre[:], sOff[:], -1.0, ALU.add, ALU.add))
    g.bar()
    g.add("vector", lambda v: v.tensor_scalar(sPp[:], sPp[:], -60000.0, None, ALU.add))
    g.bar()
    g.add("vector", lambda v: v.tensor_tensor(sPp[:], sPp[:], sMask[:], ALU.mult))
    g.bar()
    g.add("vector", lambda v: v.tensor_scalar(sPp[:], sPp[:], 60000.0, None, ALU.add))
    g.bar()
    g.add("vector", lambda v: v.tensor_copy(sPpI[:], sPp[:]))
    g.bar()

    g.add("sync", lambda s: s.dma_start(out=pmap.rearrange("(l p) -> l p", p=128), in_=sPpI[:]), dma=True)
    g.add("tensor", lambda t: t.matmul(pD[:, 0, 0:128], sPp[:], sIdent32[:], start=True, stop=True))
    g.add("tensor", lambda t: t.matmul(pD[:, 1, 0:128], sSmv[:], sIdent32[:], start=True, stop=True))
    g.bar()

    g.add("vector", lambda v: v.tensor_copy(sPpT[:], pD[:, 0, 0:128]))
    g.bar()
    g.add("vector", lambda v: v.tensor_copy(sSmvT[:], pD[:, 1, 0:128]))
    g.bar()
    g.add("vector", lambda v: v.tensor_copy(sPpTI[:], sPpT[:]))
    g.bar()


    g.add("sync", lambda s: s.dma_start(out=pptT[:], in_=sPpTI[:]), dma=True)
    g.add("sync", lambda s: s.dma_start(out=smvT[:], in_=sSmvT[:]), dma=True)
    g.add("sync", lambda s: s.dma_start(out=dbg[:, 0], in_=sMask[:]), dma=True)
    g.add("sync", lambda s: s.dma_start(out=dbg[:, 1], in_=sMask2[:]), dma=True)
    g.add("sync", lambda s: s.dma_start(out=dbg[:, 2], in_=sPre[:]), dma=True)
    g.add("sync", lambda s: s.dma_start(out=dbg[:, 3].bitcast(F32) if False else dbg[:, 3], in_=sOff[:].to_broadcast([128, 128]) if False else sPp[:]), dma=True)
    def dbgcp(v):
        v.tensor_copy(sCol[:, 0:1], sOff[:])
        return v.tensor_copy(sCol[:, 3:4], sCol[:, 1:2])
    g.add("vector", dbgcp)
    g.bar()
    g.add("sync", lambda s: s.dma_start(out=dbg[:, 4, 0:4], in_=sCol[:]), dma=True)
    g.add("sync", lambda s: s.dma_start(out=dbgsc[:], in_=sSc[:]), dma=True)
    g.bar()
    return g.build()


def build_ffn0_route2(lo, hi, with_select_inputs=False):
    g = G()
    xntm = g.io("xntm", [NTOK, D], BF16)
    pptT = g.io("pptT", [128, 128], I32)
    smvT = g.io("smvT", [128, 128])
    xgin = g.io("xgin", [KSEL, XGW], BF16)
    xgout = g.io("xgout", [KSEL, XGW], BF16, out=True)

    sXr = g.sb("sXr", [128, 3, XGW], BF16)
    sPpTI = g.sb("sPpTI", [128, 128], I32)
    sSmvT = g.sb("sSmvT", [128, 128])

    g.add("sync", lambda s: s.dma_start(out=sPpTI[:], in_=pptT[:]), dma=True)
    g.add("sync", lambda s: s.dma_start(out=sSmvT[:], in_=smvT[:]), dma=True)
    g.add("sync", lambda s: s.dma_start(out=xgout[:], in_=xgin[:]), dma=True)
    g.add("vector", lambda v: v.memset(sXr[:, :, 1025:XGW], 0.0))
    g.bar()

    for k, l in enumerate(range(lo, hi)):
        def ldx(s, l=l):
            return s.dma_start(out=sXr[:, l % 3, 0:D], in_=xntm[128 * l:128 * (l + 1)])
        g.add("sync", ldx, dma=True)

        def gcp(v, l=l):
            return v.tensor_copy(sXr[:, l % 3, 1024:1025], sSmvT[:, l:l + 1])
        g.add("vector", gcp)
        g.bar()

        def sct(gp, l=l):
            return gp.indirect_dma_start(
                xgout[:], IndirectOffsetOnAxis(sPpTI[:, l:l + 1], 0),
                sXr[:, l % 3].opt(), None, bounds_check=KSEL - 1, oob_is_err=False)
        g.add("gpsimd", sct, dma=True)
        g.bar()
    return g.build()


def build_ffn0_ffn():
    g = G()
    xgio = g.io("xgio", [KSEL, XGW], BF16)
    WuP = g.io("WuP", [D, FF0])
    WdP = g.io("WdP", [FF0, D])
    bupc = g.io("bupc", [128, FF0 // 128])
    bdnc = g.io("bdnc", [128, 8])
    ident = g.io("ident", [128, 128])
    yscr = g.io("yscr", [KSEL, D], out=True)

    sXr = g.sb("sXr", [128, 3, XGW], BF16)
    sIdentB = g.sb("sIdentB", [128, 128], BF16)
    sIdent32 = g.sb("sIdent32", [128, 128])
    sWu = g.sb("sWu", [128, 8, 22, 128], BF16)
    sWdC = g.sb("sWdC", [128, 2, 22, 128], BF16)
    sXgf = g.sb("sXgf", [128, 8, KSEL], BF16)
    sGrow = g.sb("sGrow", [8, KSEL], BF16)
    sMid = g.sb("sMid", [128, 22, KSEL], BF16)
    sBup = g.sb("sBup", [128, 22])
    sBdn = g.sb("sBdn", [128, 8])
    sYe = g.sb("sYe", [128, 512])
    sYT = g.sb("sYT", [128, 512])
    sOnesRowB = g.sb("sOnesRowB", [1, 128], BF16)

    pD = g.ps("pD", [128, 2, 512])
    pT = g.ps("pT", [128, 512])
    pT2 = g.ps("pT2", [128, 512])

    g.add("gpsimd", lambda gp: gp.dma_start(out=sIdentB[:], in_=ident[:]), dma=True)
    g.add("sync", lambda s: s.dma_start(out=sIdent32[:], in_=ident[:]), dma=True)
    g.add("gpsimd", lambda gp: gp.dma_start(
        out=sWu[:], in_=WuP.rearrange("(ec e) (fc f) -> e ec fc f", e=128, f=128)), dma=True)
    g.add("sync", lambda s: s.dma_start(out=sBup[:], in_=bupc[:]), dma=True)
    g.add("sync", lambda s: s.dma_start(out=sBdn[:], in_=bdnc[:]), dma=True)
    g.add("vector", lambda v: v.memset(sOnesRowB[:], 1.0))
    g.bar()
    for j in range(16):
        def ldg(s, j=j):
            return s.dma_start(out=sXr[:, j % 3], in_=xgio[128 * j:128 * (j + 1)])
        g.add("sync", ldg, dma=True)
        g.bar()
        for dc in range(9):
            def gtr1(t, j=j, dc=dc):
                n = 128 if dc < 8 else 8
                src = sXr[:, j % 3, 128 * dc:128 * dc + n]
                return t.matmul(pD[0:n, dc % 2, 0:128], src, sIdentB[:], start=True, stop=True)
            g.add("tensor", gtr1)
            g.bar()

            def gev(v, j=j, dc=dc):
                if dc < 8:
                    return v.tensor_copy(sXgf[:, dc, 128 * j:128 * (j + 1)], pD[:, dc % 2, 0:128])
                return v.tensor_copy(sGrow[:, 128 * j:128 * (j + 1)], pD[0:8, dc % 2, 0:128])
            g.add("vector", gev)
            g.bar()

    for ts in range(4):
        for fc in range(22):
            def mmm(t, ts=ts, fc=fc):
                ii = None
                for ec in range(8):
                    ii = t.matmul(pD[:, fc % 2], sWu[:, ec, fc], sXgf[:, ec, 512 * ts:512 * (ts + 1)],
                                  start=(ec == 0), stop=(ec == 7))
                return ii
            g.add("tensor", mmm)
            g.bar()
            g.add("scalar", lambda sc, ts=ts, fc=fc: sc.activation(
                out=sMid[:, fc, 512 * ts:512 * (ts + 1)], in_=pD[:, fc % 2], func=AF.Silu,
                bias=sBup[:, fc:fc + 1]))
            g.bar()

    sGBall = g.sb("sGBall", [128, 4, 512])
    for ts in range(4):
        g.add("tensor", lambda t, ts=ts: t.matmul(pT[:], sOnesRowB[:],
                                                  sGrow[0:1, 512 * ts:512 * (ts + 1)],
                                                  start=True, stop=True))
        g.bar()
        g.add("vector", lambda v, ts=ts: v.tensor_copy(sGBall[:, ts], pT[:]))
        g.bar()
    for dc in range(8):
        g.add("gpsimd", lambda gp, dc=dc: gp.dma_start(
            out=sWdC[:, dc % 2], in_=WdP[:, 128 * dc:128 * (dc + 1)]
            .rearrange("(fc f) m -> f fc m", f=128)), dma=True)
        g.bar()
        for ts in range(4):
            def dmm(t, dc=dc, ts=ts):
                ii = None
                for fc in range(22):
                    ii = t.matmul(pD[:, ts % 2], sWdC[:, dc % 2, fc],
                                  sMid[:, fc, 512 * ts:512 * (ts + 1)],
                                  start=(fc == 0), stop=(fc == 21))
                return ii
            g.add("tensor", dmm)
            g.bar()

            def yev(v, dc=dc, ts=ts):
                v.tensor_scalar(sYe[:], pD[:, ts % 2], sBdn[:, dc:dc + 1], None, ALU.add)
                return v.tensor_tensor(sYe[:], sYe[:], sGBall[:, ts], ALU.mult)
            g.add("vector", yev)
            g.bar()

            def ytr(t, dc=dc):
                ii = None
                for q in range(4):
                    ii = t.matmul(pT2[:, 128 * q:128 * (q + 1)], sYe[:, 128 * q:128 * (q + 1)],
                                  sIdent32[:], start=(q == 0), stop=(q == 3))
                return ii
            g.add("tensor", ytr)
            g.bar()
            g.add("vector", lambda v: v.tensor_copy(sYT[:], pT2[:]))
            g.bar()

            def sty(s, ts=ts, dc=dc):
                return s.dma_start(out=yscr[512 * ts:512 * (ts + 1), 128 * dc:128 * (dc + 1)]
                                   .rearrange("(q p) f -> p q f", p=128),
                                   in_=sYT[:].rearrange("p (q f) -> p q f", f=128))
            g.add("sync", sty, dma=True)
            g.bar()
    return g.build()


def build_ffn0_gather(pairs):
    g = G()
    accin = g.io("accin", [B * LC, D])
    ys = [g.io(f"y{e}", [KSEL, D]) for e in range(8)]
    pmslc = g.io("pmslc", [B, 4, 128, 8], I32)
    accout = g.io("accout", [B * LC, D], out=True)

    sPm = g.sb("sPm", [128, B, 4, 8], I32)
    sAcc = g.sb("sAcc", [128, 2, D])

    g.add("sync", lambda s: s.dma_start(out=sPm[:], in_=pmslc.rearrange("b tt p e -> p b tt e")), dma=True)
    g.add("sync", lambda s: s.dma_start(out=accout[:], in_=accin[:]), dma=True)
    g.bar()

    for k, (b, tt) in enumerate(pairs):
        def lda(s, b=b, tt=tt, k=k):
            return s.dma_start(out=sAcc[:, k % 2],
                               in_=accin[b * LC + 128 * tt: b * LC + 128 * (tt + 1)])
        g.add("sync", lda, dma=True)
        g.bar()
        for e in range(8):
            def gth(gp, b=b, tt=tt, e=e, k=k):
                return gp.indirect_dma_start(
                    sAcc[:, k % 2].opt(), None, ys[e][:],
                    IndirectOffsetOnAxis(sPm[:, b, tt, e:e + 1].opt(), 0),
                    bounds_check=KSEL - 1, oob_is_err=False, compute_op=ALU.add)
            g.add("gpsimd", gth, dma=True)
            g.bar()

        def sta(s, b=b, tt=tt, k=k):
            return s.dma_start(out=accout[b * LC + 128 * tt: b * LC + 128 * (tt + 1)],
                               in_=sAcc[:, k % 2])
        g.add("sync", sta, dma=True)
        g.bar()
    return g.build()


def build_ffn0_final(pairs):
    g = G()
    xsl = g.io("xsl", [B, 8, 128, LC])
    accin = g.io("accin", [B * LC, D])
    ys = [g.io(f"y{e}", [KSEL, D]) for e in range(8)]
    pmslc = g.io("pmslc", [B, 4, 128, 8], I32)
    ident = g.io("ident", [128, 128])
    xout = g.io("xout", [B, 8, 128, LC], out=True)

    sX = g.sb("sX", [128, B, 8, LC])
    sPm = g.sb("sPm", [128, B, 4, 8], I32)
    sAccA = g.sb("sAccA", [128, B, 4, D])
    sIdent32 = g.sb("sIdent32", [128, 128])
    sY = g.sb("sY", [128, 2, LC])
    pY = g.ps("pY", [128, 2, 512])

    g.add("sync", lambda s: s.dma_start(out=sX[:], in_=xsl.rearrange("b c f l -> f b c l")), dma=True)
    g.add("sync", lambda s: s.dma_start(out=sPm[:], in_=pmslc.rearrange("b tt p e -> p b tt e")), dma=True)
    g.add("sync", lambda s: s.dma_start(
        out=sAccA[:], in_=accin.rearrange("(b tt p) d -> p b tt d", p=128, tt=4)), dma=True)
    g.add("gpsimd", lambda gp: gp.dma_start(out=sIdent32[:], in_=ident[:]), dma=True)
    g.bar()

    for (b, tt) in pairs:
        for e in range(8):
            def gth(gp, b=b, tt=tt, e=e):
                return gp.indirect_dma_start(
                    sAccA[:, b, tt].opt(), None, ys[e][:],
                    IndirectOffsetOnAxis(sPm[:, b, tt, e:e + 1].opt(), 0),
                    bounds_check=KSEL - 1, oob_is_err=False, compute_op=ALU.add)
            g.add("gpsimd", gth, dma=True)
        g.bar()

    for b in range(B):
        for fc in range(8):
            def ftr(t, b=b, fc=fc):
                ii = None
                for tt in range(4):
                    ii = t.matmul(pY[:, fc % 2, 128 * tt:128 * (tt + 1)],
                                  sAccA[:, b, tt, 128 * fc:128 * (fc + 1)], sIdent32[:],
                                  start=(tt == 0), stop=(tt == 3))
                return ii
            g.add("tensor", ftr)
            g.bar()

            def fev(v, b=b, fc=fc):
                return v.scalar_tensor_tensor(sY[:, fc % 2], pY[:, fc % 2], 1.0,
                                              sX[:, b, fc], ALU.mult, ALU.add)
            g.add("vector", fev)
            g.bar()
            g.add("sync", lambda s, b=b, fc=fc: s.dma_start(out=xout[b, fc], in_=sY[:, fc % 2]), dma=True)
            g.bar()
    return g.build()


def pad_w(w, rows=None, cols=None):
    r = rows or w.shape[0]
    c = cols or w.shape[1]
    out = np.zeros((r, c), np.float32)
    out[:w.shape[0], :w.shape[1]] = w
    return out


def run_ffn0(x_fm, Wr, br, W_up, b_up, W_down, b_down, builders={}):
    def get(name, fn):
        if name not in builders:
            builders[name] = fn()
        return builders[name]

    nc1 = get("f1", build_ffn0_route)
    ins = []
    for c in range(NCORES):
        ins.append({"xsl": f32c(x_fm[:, :, :, LC * c:LC * (c + 1)]),
                    "Wr": f32c(Wr), "br": f32c(br.reshape(1, 8)), "ident": IDENT})
    res = run_bass_kernel_spmd(nc1, ins, core_ids=list(range(NCORES)))
    SmT = np.zeros((8, NTOK), np.float32)
    xn_tm = np.zeros((NTOK, D), BF16NP)
    for c in range(NCORES):
        sm = res.results[c]["smslc"]
        xn = res.results[c]["xnslc"]
        for b in range(B):
            SmT[:, b * L + LC * c: b * L + LC * (c + 1)] = sm[:, b]
            xn_tm[b * L + LC * c: b * L + LC * (c + 1)] = xn[b * LC:(b + 1) * LC]

    nc2 = get("sel", build_ffn0_select)
    ins = [{"sme": f32c(SmT[e:e + 1]), "ident": IDENT} for e in range(NCORES)]
    res = run_bass_kernel_spmd(nc2, ins, core_ids=list(range(NCORES)))
    pmap_all = np.stack([res.results[e]["pmap"] for e in range(NCORES)], axis=0)
    pptT = [res.results[e]["pptT"] for e in range(NCORES)]
    smvT = [res.results[e]["smvT"] for e in range(NCORES)]

    xg = [np.zeros((KSEL, XGW), BF16NP) for _ in range(NCORES)]
    for (lo, hi) in [(0, 43), (43, 86), (86, 128)]:
        ncr = get(f"route_{lo}", lambda lo=lo, hi=hi: build_ffn0_route2(lo, hi))
        ins = [{"xntm": xn_tm, "pptT": pptT[e], "smvT": f32c(smvT[e]), "xgin": xg[e]}
               for e in range(NCORES)]
        res = run_bass_kernel_spmd(ncr, ins, core_ids=list(range(NCORES)))
        xg = [res.results[e]["xgout"] for e in range(NCORES)]

    nc3 = get("ffn", build_ffn0_ffn)
    ins = []
    for e in range(NCORES):
        gidx = e // 2
        ins.append({
            "xgio": xg[e],
            "WuP": pad_w(f32c(W_up[gidx]), D, FF0),
            "WdP": pad_w(f32c(W_down[e]), FF0, D),
            "bupc": pad_w(f32c(b_up[gidx]).reshape(1, -1), 1, FF0).reshape(22, 128).T.copy(),
            "bdnc": f32c(b_down[e]).reshape(8, 128).T.copy(),
            "ident": IDENT,
        })
    res = run_bass_kernel_spmd(nc3, ins, core_ids=list(range(NCORES)))
    yall = np.stack([res.results[e]["yscr"] for e in range(NCORES)], axis=0)

    pmslcs = []
    for c in range(NCORES):
        pmslc = np.zeros((B, 4, 128, 8), np.int32)
        for b in range(B):
            for tt in range(4):
                base = b * L + LC * c + 128 * tt
                pmslc[b, tt, :, :] = pmap_all[:, base:base + 128].T
        pmslcs.append(pmslc)

    pairs_all = [(b, tt) for b in range(B) for tt in range(4)]
    acc = [np.zeros((B * LC, D), np.float32) for _ in range(NCORES)]
    for gi, pr in enumerate([pairs_all[0:5], pairs_all[5:10], pairs_all[10:15]]):
        ncg = get(f"g{gi}", lambda pr=pr: build_ffn0_gather(pr))
        ins = [dict({"accin": acc[c], "pmslc": pmslcs[c]},
                    **{f"y{e}": yall[e] for e in range(8)}) for c in range(NCORES)]
        res = run_bass_kernel_spmd(ncg, ins, core_ids=list(range(NCORES)))
        acc = [res.results[c]["accout"] for c in range(NCORES)]

    ncf = get("fin", lambda: build_ffn0_final(pairs_all[15:16]))
    ins = [dict({"xsl": f32c(x_fm[:, :, :, LC * c:LC * (c + 1)]), "accin": acc[c],
                 "pmslc": pmslcs[c], "ident": IDENT},
                **{f"y{e}": yall[e] for e in range(8)}) for c in range(NCORES)]
    res = run_bass_kernel_spmd(ncf, ins, core_ids=list(range(NCORES)))
    out = np.zeros((B, 8, 128, L), np.float32)
    for c in range(NCORES):
        out[:, :, :, LC * c:LC * (c + 1)] = res.results[c]["xout"]
    return out




DI = 2048
HJ = 16
TCH = 128
NCH = L // TCH
BHC8 = 64 // NCORES
PI = float(np.pi)
TWO_PI = float(2 * np.pi)
MAGIC = 12582912.0


def seq(g, eng, fn, dma=False):
    g.add(eng, fn, dma=dma)
    g.bar()



def build_b1():
    g = G()
    xsl = g.io("xsl", [B, 8, 128, LC])
    WzT = g.io("WzT", [D, DI]); WgT = g.io("WgT", [D, DI]); WkT = g.io("WkT", [D, DI])
    WvT = g.io("WvT", [D, 32])
    bcolz = g.io("bcolz", [128, 16]); bcolg = g.io("bcolg", [128, 16]); bcolk = g.io("bcolk", [128, 16])
    bvr = g.io("bvr", [1, 32])
    rmsw = g.io("rmsw", [128, 8])
    ident = g.io("ident", [128, 128])
    zf = g.io("zf", [B, 16, 128, LC], BF16, out=True)
    xgf = g.io("xgf", [B, 16, 128, LC], BF16, out=True)
    kff = g.io("kff", [B, 16, 128, LC], BF16, out=True)
    vv = g.io("vv", [B, LC, 32], out=True)
    xlnf = g.io("xlnf", [B, 8, 128, LC], out=True)

    sX1 = g.sb("sX1", [128, 8, LC])
    sXl = g.sb("sXl", [128, 8, LC])
    sXn = g.sb("sXn", [128, 8, LC], BF16)
    sW = g.sb("sW", [128, 3, 8, 16, 128], BF16)
    sWv = g.sb("sWv", [128, 8, 32], BF16)
    sBia = g.sb("sBia", [128, 3, 16])
    sBv = g.sb("sBv", [1, 32])
    sBvB = g.sb("sBvB", [32, 1])
    sRms = g.sb("sRms", [128, 8])
    sOut = g.sb("sOut", [128, 2, LC], BF16)
    sVo = g.sb("sVo", [32, LC])
    sVT = g.sb("sVT", [128, 32])
    sIdent32 = g.sb("sIdent32", [128, 128])
    pP = g.ps("pP", [128, 2, 512])
    pV = g.ps("pV", [128, 512])

    C = common_bufs(g)

    for w, W in enumerate([WzT, WgT, WkT]):
        def ld(gp, w=w, W=W):
            return gp.dma_start(out=sW[:, w], in_=W.rearrange("(ec e) (fc f) -> e ec fc f", e=128, f=128))
        g.add("gpsimd", ld, dma=True, defer=10)
    g.add("gpsimd", lambda gp: gp.dma_start(out=sWv[:], in_=WvT.rearrange("(ec e) o -> e ec o", e=128)), dma=True)
    g.add("sync", lambda s: s.dma_start(out=sBia[:, 0], in_=bcolz[:]), dma=True)
    g.add("sync", lambda s: s.dma_start(out=sBia[:, 1], in_=bcolg[:]), dma=True)
    g.add("sync", lambda s: s.dma_start(out=sBia[:, 2], in_=bcolk[:]), dma=True)
    g.add("sync", lambda s: s.dma_start(out=sBv[:], in_=bvr[:]), dma=True)
    g.add("sync", lambda s: s.dma_start(out=sRms[:], in_=rmsw[:]), dma=True)
    g.add("sync", lambda s: s.dma_start(out=sIdent32[:], in_=ident[:]), dma=True)
    g.bar()
    seq(g, "tensor", lambda t: t.matmul(pP[0:32, 0, 0:1], sBv[:], C["sOnesRow"][:, 0:1], start=True, stop=True))
    seq(g, "vector", lambda v: v.tensor_copy(sBvB[:], pP[0:32, 0, 0:1]))

    sSq, sStat = C["sSq"], C["sStat"]
    seq(g, "vector", lambda v: v.memset(C["sEps"][:], 1e-5))
    for b in range(B):
        g.add("sync", lambda s, b=b: s.dma_start(out=sX1[:], in_=xsl[b].rearrange("c f l -> f c l")), dma=True)
        g.bar()
        emit_ln(g, sX1[:], sXl, C)
        g.add("sync", lambda s, b=b: s.dma_start(out=xlnf[b].rearrange("c f l -> f c l"), in_=sXl[:]), dma=True)
        for grp in range(4):
            def sqf(sc, grp=grp, b=b):
                sc.activation(out=sSq[:, 0], in_=sXl[:, 2 * grp], func=AF.Square)
                return sc.activation(out=sSq[:, 1], in_=sXl[:, 2 * grp + 1], func=AF.Square)
            g.add("scalar", sqf)
            g.bar()

            def mm2(t, grp=grp):
                t.matmul(C["pS2"][:], C["sOnes32"][:], sSq[:, 0], start=(grp == 0), stop=False)
                return t.matmul(C["pS2"][:], C["sOnes32"][:], sSq[:, 1], start=False, stop=(grp == 3))
            g.add("tensor", mm2)
            g.bar()
        seq(g, "vector", lambda v: v.tensor_scalar(sStat[:, 1], C["pS2"][:], 1.0 / D, 1e-6, ALU.mult, ALU.add))
        seq(g, "scalar", lambda sc: sc.activation(out=sStat[:, 3], in_=sStat[:, 1], func=AF.Ln))
        seq(g, "scalar", lambda sc: sc.activation(out=sStat[:, 4], in_=sStat[:, 3], func=AF.Exp, scale=-0.5))
        seq(g, "tensor", lambda t: t.matmul(C["pBC"][:], C["sOnesRow"][:], sStat[:, 4], start=True, stop=True))
        seq(g, "vector", lambda v: v.tensor_copy(C["sRB"][:], C["pBC"][:]))

        def norm(v, b=b):
            ii = None
            for fc in range(8):
                v.tensor_tensor(sSq[:, fc % 2], sXl[:, fc], C["sRB"][:], ALU.mult)
                ii = v.tensor_scalar(sXn[:, fc], sSq[:, fc % 2], sRms[:, fc:fc + 1], None, ALU.mult)
            return ii
        g.add("vector", norm)
        g.bar()

        pslots = [(w, fc) for w in range(3) for fc in range(16)]
        for i in range(len(pslots) + 2):
            if i < len(pslots):
                w, fc = pslots[i]

                def proj(t, w=w, fc=fc, i=i):
                    ii = None
                    for ec in range(8):
                        ii = t.matmul(pP[:, i % 2], sW[:, w, ec, fc], sXn[:, ec],
                                      start=(ec == 0), stop=(ec == 7))
                    return ii
                g.add("tensor", proj)
            if 1 <= i <= len(pslots):
                w1, fc1 = pslots[i - 1]
                g.add("vector", lambda v, w1=w1, fc1=fc1, i=i: v.tensor_scalar(
                    sOut[:, (i - 1) % 2], pP[:, (i - 1) % 2], sBia[:, w1, fc1:fc1 + 1], None, ALU.add))
            if 2 <= i <= len(pslots) + 1:
                w2, fc2 = pslots[i - 2]
                tgt2 = [zf, xgf, kff][w2]
                g.add("sync", lambda s, tgt2=tgt2, b=b, fc2=fc2, i=i: s.dma_start(
                    out=tgt2[b, fc2], in_=sOut[:, (i - 2) % 2]), dma=True)
            g.bar()

        def projv(t):
            ii = None
            for ec in range(8):
                ii = t.matmul(pV[0:32, 0:512], sWv[:, ec], sXn[:, ec], start=(ec == 0), stop=(ec == 7))
            return ii
        g.add("tensor", projv)
        g.bar()
        seq(g, "vector", lambda v: v.tensor_scalar(sVo[:], pV[0:32, 0:512], sBvB[:], None, ALU.add))
        for q in range(4):
            seq(g, "tensor", lambda t, q=q: t.matmul(pV[:, 0:32], sVo[:, 128 * q:128 * (q + 1)],
                                                     sIdent32[0:32, 0:32], start=True, stop=True))
            seq(g, "vector", lambda v: v.tensor_copy(sVT[:], pV[:, 0:32]))
            seq(g, "sync", lambda s, b=b, q=q: s.dma_start(out=vv[b, 128 * q:128 * (q + 1)], in_=sVT[:]), dma=True)
    return g.build()



def build_b2():
    g = G()
    xgh = g.io("xgh", [B, 16, 128, LC + 3])
    kin = g.io("kin", [B, 16, 128, LC], BF16)
    vv = g.io("vv", [B, LC, 32])
    QwT = g.io("QwT", [DI, DI])
    gwT = g.io("gwT", [DI, 112])
    gb = g.io("gb", [1, 112])
    lds = g.io("lds", [1, 16])
    frq = g.io("frq", [1, 16])
    convw = g.io("convw", [128, 16, 4])
    convb = g.io("convb", [128, 16])
    ident = g.io("ident", [128, 128])
    xcf = g.io("xcf", [B, 16, 128, LC], BF16, out=True)
    qnf = g.io("qnf", [B, 16, 128, LC], BF16, out=True)
    knf = g.io("knf", [B, 16, 128, LC], BF16, out=True)
    gsc = g.io("gsc", [B, LC, 16, 8], out=True)
    glt = g.io("glt", [B, 16, LC, 2], out=True)

    sXg = g.sb("sXg", [128, 16, LC + 3])
    sXc = g.sb("sXc", [128, 16, LC], BF16)
    sK = g.sb("sK", [128, 16, LC], BF16)
    sQw = g.sb("sQw", [128, 16, 16, 128], BF16)
    sGw = g.sb("sGw", [128, 16, 112], BF16)
    sGb = g.sb("sGb", [1, 112])
    sCw = g.sb("sCw", [128, 16, 4])
    sCb = g.sb("sCb", [128, 16])
    sT1 = g.sb("sT1", [128, LC])
    sT2 = g.sb("sT2", [128, LC])
    sGt = g.sb("sGt", [128, 4, 112])
    sS = g.sb("sS", [128, 21, 4, 16])
    sV4 = g.sb("sV4", [128, 4, 16, 2])
    sLds = g.sb("sLds", [128, 16])
    sFrq = g.sb("sFrq", [128, 16])
    sLds4 = g.sb("sLds4", [128, 4, 16])
    sFrq4 = g.sb("sFrq4", [128, 4, 16])
    sGbB = g.sb("sGbB", [128, 112])
    sOne = g.sb("sOne", [128, 1])
    sQs = g.sb("sQs", [128, 5, LC], BF16)
    sTB2 = g.sb("sTB2", [128, 2, LC], BF16)
    sIdent32 = g.sb("sIdent32", [128, 128])
    sGsc4 = g.sb("sGsc4", [128, 4, 16, 8])
    sGlt4 = g.sb("sGlt4", [128, 4, 16, 2])
    pP = g.ps("pP", [128, 3, 512])
    pG = g.ps("pG", [128, 512])
    pN = g.ps("pN", [1, 2, 512])
    pB = g.ps("pB", [128, 2, 512])

    C = common_bufs(g, psum=False)

    g.add("gpsimd", lambda gp: gp.dma_start(
        out=sQw[:], in_=QwT.rearrange("(ec e) (fc f) -> e ec fc f", e=128, f=128)), dma=True, defer=30)
    g.add("gpsimd", lambda gp: gp.dma_start(
        out=sGw[:], in_=gwT.rearrange("(ec e) o -> e ec o", e=128)), dma=True, defer=10)
    g.add("sync", lambda s: s.dma_start(out=sGb[:], in_=gb[:]), dma=True)
    g.add("sync", lambda s: s.dma_start(out=sCw[:], in_=convw[:]), dma=True)
    g.add("sync", lambda s: s.dma_start(out=sCb[:], in_=convb[:]), dma=True)
    g.add("sync", lambda s: s.dma_start(out=sIdent32[:], in_=ident[:]), dma=True)
    g.bar()
    seq(g, "tensor", lambda t: t.matmul(pG[:, 0:112], C["sOnesRow"][:], sGb[:], start=True, stop=True))
    seq(g, "vector", lambda v: v.tensor_copy(sGbB[:], pG[:, 0:112]))
    sLdsRow = g.sb("sLdsRow", [1, 16])
    sFrqRow = g.sb("sFrqRow", [1, 16])
    g.add("sync", lambda s: s.dma_start(out=sLdsRow[:], in_=lds[:]), dma=True)
    g.add("sync", lambda s: s.dma_start(out=sFrqRow[:], in_=frq[:]), dma=True)
    g.bar()
    seq(g, "scalar", lambda sc: sc.activation(out=sLdsRow[:], in_=sLdsRow[:], func=AF.Exp))
    seq(g, "tensor", lambda t: t.matmul(pG[:, 0:16], C["sOnesRow"][:], sLdsRow[:], start=True, stop=True))
    seq(g, "vector", lambda v: v.tensor_copy(sLds[:], pG[:, 0:16]))
    seq(g, "tensor", lambda t: t.matmul(pG[:, 0:16], C["sOnesRow"][:], sFrqRow[:], start=True, stop=True))
    seq(g, "vector", lambda v: v.tensor_copy(sFrq[:], pG[:, 0:16]))

    def l4(v):
        ii = None
        for q in range(4):
            v.tensor_copy(sLds4[:, q], sLds[:])
            ii = v.tensor_copy(sFrq4[:, q], sFrq[:])
        return ii
    g.add("vector", l4)
    g.bar()
    seq(g, "vector", lambda v: v.memset(sOne[:], 1.0))

    for b in range(B):
        g.add("sync", lambda s, b=b: s.dma_start(out=sXg[:], in_=xgh[b].rearrange("c f l -> f c l")), dma=True)
        g.add("sync", lambda s, b=b: s.dma_start(out=sK[:], in_=kin[b].rearrange("c f l -> f c l")), dma=True)
        g.bar()

        for hc in range(16):
            def cv0(v, hc=hc):
                return v.tensor_scalar(sT1[:], sXg[:, hc, 0:LC], sCw[:, hc, 0:1], None, ALU.mult)
            g.add("vector", cv0)
            g.bar()
            for kk in range(1, 4):
                def cvk(v, hc=hc, kk=kk):
                    return v.scalar_tensor_tensor(sT1[:], sXg[:, hc, kk:kk + LC],
                                                  sCw[:, hc, kk:kk + 1], sT1[:], ALU.mult, ALU.add)
                g.add("vector", cvk)
                g.bar()
            seq(g, "scalar", lambda sc, hc=hc: sc.activation(out=sT2[:], in_=sT1[:], func=AF.Sigmoid,
                                                             bias=sCb[:, hc:hc + 1]))
            seq(g, "vector", lambda v, hc=hc: v.tensor_scalar(sT1[:], sT1[:], sCb[:, hc:hc + 1], None, ALU.add))
            seq(g, "vector", lambda v, hc=hc: v.tensor_tensor(sXc[:, hc], sT1[:], sT2[:], ALU.mult))
            seq(g, "sync", lambda s, b=b, hc=hc: s.dma_start(out=xcf[b, hc], in_=sXc[:, hc]), dma=True)

        for q in range(4):
            def gmm(t, q=q):
                ii = None
                for ec in range(16):
                    ii = t.matmul(pG[:, 0:112], sXc[:, ec, 128 * q:128 * (q + 1)],
                                  sGw[:, ec], start=(ec == 0), stop=(ec == 15))
                return ii
            g.add("tensor", gmm)
            g.bar()
            seq(g, "vector", lambda v, q=q: v.tensor_tensor(sGt[:, q], pG[:, 0:112], sGbB[:], ALU.add))

        al, om, sB_, sC_, sdt, bt, rr = [sGt[:, :, 16 * i:16 * (i + 1)] for i in range(7)]

        def sp(i):
            return sS[:, i]

        g.add("sync", lambda s, b=b: s.dma_start(
            out=sV4[:].rearrange("p q h s -> p q (h s)"),
            in_=vv[b].rearrange("(q p) w -> p q w", p=128)), dma=True)
        seq(g, "scalar", lambda sc: sc.activation(out=sp(1), in_=al, func=AF.Abs))
        seq(g, "scalar", lambda sc: sc.activation(out=sp(2), in_=sp(1), func=AF.Exp, scale=-1.0))
        seq(g, "scalar", lambda sc: sc.activation(out=sp(3), in_=sp(2), func=AF.Ln, bias=sOne[:]))
        seq(g, "vector", lambda v: v.tensor_scalar(sp(4), al, 0.0, None, ALU.max))
        seq(g, "vector", lambda v: v.tensor_tensor(sp(0), sp(4), sp(3), ALU.add))
        seq(g, "scalar", lambda sc: sc.activation(out=sp(1), in_=sdt, func=AF.Abs))
        seq(g, "scalar", lambda sc: sc.activation(out=sp(2), in_=sp(1), func=AF.Exp, scale=-1.0))
        seq(g, "scalar", lambda sc: sc.activation(out=sp(3), in_=sp(2), func=AF.Ln, bias=sOne[:]))
        seq(g, "vector", lambda v: v.tensor_scalar(sp(4), sdt, 0.0, None, ALU.max))
        seq(g, "vector", lambda v: v.tensor_tensor(sp(5), sp(4), sp(3), ALU.add))
        seq(g, "vector", lambda v: v.tensor_tensor(sp(5), sp(5), sLds4[:], ALU.mult))
        seq(g, "vector", lambda v: v.tensor_scalar(sp(5), sp(5), 1e-4, None, ALU.add))
        seq(g, "scalar", lambda sc: sc.activation(out=sp(6), in_=om, func=AF.Tanh))
        seq(g, "vector", lambda v: v.tensor_scalar(sp(6), sp(6), 1.0, None, ALU.add))
        seq(g, "vector", lambda v: v.tensor_tensor(sp(6), sp(6), sFrq4[:], ALU.mult))
        seq(g, "vector", lambda v: v.tensor_tensor(sp(7), sp(5), sp(0), ALU.mult))
        seq(g, "vector", lambda v: v.tensor_scalar(sp(7), sp(7), 0.5, None, ALU.mult))
        seq(g, "vector", lambda v: v.tensor_tensor(sp(8), sp(5), sp(6), ALU.mult))
        seq(g, "vector", lambda v: v.tensor_scalar(sp(8), sp(8), 0.5, None, ALU.mult))
        seq(g, "vector", lambda v: v.tensor_scalar(sp(9), sp(7), 1.0, None, ALU.add))
        seq(g, "vector", lambda v: v.tensor_tensor(sp(9), sp(9), sp(9), ALU.mult))
        seq(g, "vector", lambda v: v.tensor_tensor(sp(10), sp(8), sp(8), ALU.mult))
        seq(g, "vector", lambda v: v.tensor_tensor(sp(9), sp(9), sp(10), ALU.add))
        seq(g, "vector", lambda v: v.reciprocal(sp(10), sp(9)))
        seq(g, "vector", lambda v: v.tensor_tensor(sp(11), sp(7), sp(7), ALU.mult))
        seq(g, "vector", lambda v: v.tensor_tensor(sp(13), sp(8), sp(8), ALU.mult))
        seq(g, "vector", lambda v: v.tensor_tensor(sp(11), sp(11), sp(13), ALU.add))
        seq(g, "vector", lambda v: v.tensor_scalar(sp(11), sp(11), -1.0, 1.0, ALU.mult, ALU.add))
        seq(g, "vector", lambda v: v.tensor_tensor(sp(11), sp(11), sp(10), ALU.mult))
        seq(g, "vector", lambda v: v.tensor_tensor(sp(12), sp(8), sp(10), ALU.mult))
        seq(g, "vector", lambda v: v.tensor_scalar(sp(12), sp(12), 2.0, None, ALU.mult))
        seq(g, "vector", lambda v: v.tensor_tensor(sp(13), sp(11), sp(11), ALU.mult))
        seq(g, "vector", lambda v: v.tensor_tensor(sp(14), sp(12), sp(12), ALU.mult))
        seq(g, "vector", lambda v: v.tensor_tensor(sp(13), sp(13), sp(14), ALU.add))
        seq(g, "scalar", lambda sc: sc.activation(out=sp(13), in_=sp(13), func=AF.Sqrt))
        seq(g, "vector", lambda v: v.tensor_scalar(sp(15), sp(11), 1e-30, None, ALU.add))
        seq(g, "vector", lambda v: v.reciprocal(sp(15), sp(15)))
        seq(g, "vector", lambda v: v.tensor_tensor(sp(15), sp(12), sp(15), ALU.mult))
        seq(g, "scalar", lambda sc: sc.activation(out=sp(14), in_=sp(15), func=AF.Arctan))
        seq(g, "vector", lambda v: v.tensor_scalar(sp(15), sp(11), 0.0, None, ALU.is_lt))
        seq(g, "vector", lambda v: v.tensor_scalar(sp(15), sp(15), PI, None, ALU.mult))
        seq(g, "vector", lambda v: v.tensor_tensor(sp(14), sp(14), sp(15), ALU.add))
        seq(g, "scalar", lambda sc: sc.activation(out=sp(15), in_=rr, func=AF.Sigmoid))
        seq(g, "vector", lambda v: v.tensor_scalar(sp(17), sp(13), 1e-8, None, ALU.max))
        seq(g, "vector", lambda v: v.tensor_scalar(sp(17), sp(17), 1.0, None, ALU.min))
        seq(g, "scalar", lambda sc: sc.activation(out=sp(17), in_=sp(17), func=AF.Ln))
        seq(g, "vector", lambda v: v.tensor_tensor(sp(16), sp(15), sp(17), ALU.mult))
        seq(g, "vector", lambda v: v.tensor_scalar(sp(16), sp(16), 8.0, None, ALU.mult))
        seq(g, "scalar", lambda sc: sc.activation(out=sp(17), in_=sp(16), func=AF.Exp))
        seq(g, "vector", lambda v: v.tensor_tensor(sp(18), sp(17), sp(17), ALU.mult))
        seq(g, "vector", lambda v: v.tensor_scalar(sp(18), sp(18), -1.0, 1.0, ALU.mult, ALU.add))
        seq(g, "vector", lambda v: v.tensor_scalar(sp(18), sp(18), 0.0, None, ALU.max))
        seq(g, "vector", lambda v: v.tensor_scalar(sp(18), sp(18), 1.0, None, ALU.min))
        seq(g, "vector", lambda v: v.tensor_scalar(sp(18), sp(18), 1e-6, None, ALU.add))
        seq(g, "scalar", lambda sc: sc.activation(out=sp(18), in_=sp(18), func=AF.Sqrt))
        seq(g, "scalar", lambda sc: sc.activation(out=sp(19), in_=sB_, func=AF.Sigmoid))
        seq(g, "vector", lambda v: v.tensor_tensor(sp(18), sp(18), sp(19), ALU.mult))
        seq(g, "scalar", lambda sc: sc.activation(out=sp(19), in_=bt, func=AF.Sigmoid))
        seq(g, "scalar", lambda sc: sc.activation(out=sp(20), in_=sC_, func=AF.Sigmoid))

        def vgm(v):
            v.tensor_tensor(sV4[:, :, :, 0], sV4[:, :, :, 0], sp(18), ALU.mult)
            return v.tensor_tensor(sV4[:, :, :, 1], sV4[:, :, :, 1], sp(18), ALU.mult)
        g.add("vector", vgm)
        g.bar()

        def gsca(v):
            v.memset(sGsc4[:].rearrange("p a b c -> p (a b c)"), 0.0)
            return None
        seq(g, "vector", lambda v: v.memset(sGsc4[:].rearrange("p a b c -> p (a b c)"), 0.0))

        def gscb(v):
            v.tensor_copy(sGsc4[:, :, :, 0], sp(16))
            v.tensor_copy(sGsc4[:, :, :, 1], sp(14))
            v.tensor_copy(sGsc4[:, :, :, 2], sp(19))
            v.tensor_copy(sGsc4[:, :, :, 3], sV4[:, :, :, 0])
            v.tensor_copy(sGsc4[:, :, :, 4], sV4[:, :, :, 1])
            v.tensor_copy(sGsc4[:, :, :, 5], sp(20))
            v.tensor_copy(sGlt4[:, :, :, 0], sp(16))
            return v.tensor_copy(sGlt4[:, :, :, 1], sp(14))
        g.add("vector", gscb)
        g.bar()
        g.add("sync", lambda s, b=b: s.dma_start(
            out=gsc[b].rearrange("(q p) h s -> p q h s", p=128), in_=sGsc4[:]), dma=True)
        for q in range(4):
            g.add("sync", lambda s, b=b, q=q: s.dma_start(
                out=glt[b, :, 128 * q:128 * (q + 1), :].rearrange("h l s -> l h s"),
                in_=sGlt4[:, q]), dma=True)
        g.bar()
        sqslot = [sT1, sT2]
        NQ = 16
        for i in range(NQ + 8):
            def qst(t, i=i):
                ii = None
                if i < NQ:
                    for ec in range(16):
                        ii = t.matmul(pP[:, i % 3], sQw[:, ec, i], sXc[:, ec],
                                      start=(ec == 0), stop=(ec == 15))
                if 0 <= i - 2 < NQ:
                    ii = t.matmul(pN[0:1, (i - 2) % 2, :], C["sOnes32"][:], sqslot[(i - 2) % 2][:],
                                  start=True, stop=True)
                if 0 <= i - 5 < NQ:
                    ii = t.matmul(pB[:, (i - 5) % 2, :], C["sOnesRow"][:],
                                  C["sStat"][0:1, (i - 5) % 4, :], start=True, stop=True)
                return ii
            if i < NQ or 0 <= i - 2 < NQ or 0 <= i - 5 < NQ:
                g.add("tensor", qst)

            def qsa(sc, i=i):
                ii = None
                if 0 <= i - 1 < NQ:
                    ii = sc.activation(out=sqslot[(i - 1) % 2][:], in_=pP[:, (i - 1) % 3], func=AF.Square)
                if 0 <= i - 3 < NQ:
                    ii = sc.activation(out=C["sStat"][0:1, (i - 3) % 4, :], in_=pN[0:1, (i - 3) % 2, :],
                                       func=AF.Sqrt)
                return ii
            if 0 <= i - 1 < NQ or 0 <= i - 3 < NQ:
                g.add("scalar", qsa)

            def qsv(v, i=i):
                ii = None
                if 0 <= i - 2 < NQ:
                    ii = v.tensor_copy(sQs[:, (i - 2) % 5, :], pP[:, (i - 2) % 3])
                if 0 <= i - 4 < NQ:
                    sl = C["sStat"][0:1, (i - 4) % 4, :]
                    v.tensor_scalar(sl, sl, 1e-6, None, ALU.add)
                    ii = v.reciprocal(sl, sl)
                if 0 <= i - 6 < NQ:
                    ii = v.tensor_tensor(sTB2[:, (i - 6) % 2], sQs[:, (i - 6) % 5, :],
                                         pB[:, (i - 6) % 2, :], ALU.mult)
                return ii
            if 0 <= i - 2 < NQ or 0 <= i - 4 < NQ or 0 <= i - 6 < NQ:
                g.add("vector", qsv)
            if 0 <= i - 7 < NQ:
                g.add("sync", lambda s, b=b, i=i: s.dma_start(out=qnf[b, i - 7],
                                                              in_=sTB2[:, (i - 7) % 2]), dma=True)
            g.bar()

        for i in range(NQ + 7):
            def kst(t, i=i):
                ii = None
                if 0 <= i - 1 < NQ:
                    ii = t.matmul(pN[0:1, (i - 1) % 2, :], C["sOnes32"][:], sqslot[(i - 1) % 2][:],
                                  start=True, stop=True)
                if 0 <= i - 4 < NQ:
                    ii = t.matmul(pB[:, (i - 4) % 2, :], C["sOnesRow"][:],
                                  C["sStat"][0:1, (i - 4) % 4, :], start=True, stop=True)
                return ii
            if 0 <= i - 1 < NQ or 0 <= i - 4 < NQ:
                g.add("tensor", kst)

            def ksa(sc, i=i):
                ii = None
                if i < NQ:
                    ii = sc.activation(out=sqslot[i % 2][:], in_=sK[:, i], func=AF.Square)
                if 0 <= i - 2 < NQ:
                    ii = sc.activation(out=C["sStat"][0:1, (i - 2) % 4, :], in_=pN[0:1, (i - 2) % 2, :],
                                       func=AF.Sqrt)
                return ii
            if i < NQ or 0 <= i - 2 < NQ:
                g.add("scalar", ksa)

            def ksv(v, i=i):
                ii = None
                if 0 <= i - 3 < NQ:
                    sl = C["sStat"][0:1, (i - 3) % 4, :]
                    v.tensor_scalar(sl, sl, 1e-6, None, ALU.add)
                    ii = v.reciprocal(sl, sl)
                if 0 <= i - 5 < NQ:
                    ii = v.tensor_tensor(sTB2[:, (i - 5) % 2], sK[:, i - 5],
                                         pB[:, (i - 5) % 2, :], ALU.mult)
                return ii
            if 0 <= i - 3 < NQ or 0 <= i - 5 < NQ:
                g.add("vector", ksv)
            if 0 <= i - 6 < NQ:
                g.add("sync", lambda s, b=b, i=i: s.dma_start(out=knf[b, i - 6],
                                                              in_=sTB2[:, (i - 6) % 2]), dma=True)
            g.bar()
    return g.build()




INV2PI = float(1.0 / (2 * np.pi))
HALF_PI = float(np.pi / 2)


def build_b3(clo, chi):
    NL = chi - clo
    g = G()
    kn8 = g.io("kn8", [BHC8, 128, L], BF16)
    qn8 = g.io("qn8", [BHC8, 128, L], BF16)
    gsc8 = g.io("gsc8", [BHC8, L, 8])
    glt8 = g.io("glt8", [BHC8, L, 2])
    brow8 = g.io("brow8", [BHC8, NCH, 128])
    sin0 = g.io("sin0", [128, BHC8, 2])
    ident = g.io("ident", [128, 128])
    identb = g.io("identb", [128, 128], BF16)
    rts = g.io("rts", [BHC8, NL * 128, 2], out=True)
    sout = g.io("sout", [128, BHC8, 2], out=True)

    sKc = g.sb("sKc", [128, 8, 128], BF16)
    sQc = g.sb("sQc", [128, 8, 128], BF16)
    sKT = g.sb("sKT", [128, 8, 128], BF16)
    sAp = g.sb("sAp", [128, 8, 7, 3, 128], BF16)
    sBp = g.sb("sBp", [128, 8, 2, 3, 128], BF16)
    sMr = g.sb("sMr", [128, 8, 3, 128], BF16)
    sDR = g.sb("sDR", [128, 8, 128])
    sDC = g.sb("sDC", [128, 8, 128])
    sC1 = g.sb("sC1", [128, 8, 128])
    sC2 = g.sb("sC2", [128, 8, 128])
    sYC = g.sb("sYC", [128, 8, 128])
    sYS = g.sb("sYS", [128, 8, 128])
    sEI = g.sb("sEI", [128, 8, 128])
    sET = g.sb("sET", [128, 8, 128])
    sBS = g.sb("sBS", [128, 8, 128])
    sBT = g.sb("sBT", [128, 8, 128])
    sKbS = g.sb("sKbS", [128, 8, 128])
    sKbT = g.sb("sKbT", [128, 8, 128])
    sMkS8 = g.sb("sMkS8", [128, 8, 128])
    sMkI8 = g.sb("sMkI8", [128, 8, 128])
    sMkST = g.sb("sMkST", [128, 128])
    sIotaI = g.sb("sIotaI", [128, 128], I32)
    sColI = g.sb("sColI", [128, 1], I32)
    sIotaF = g.sb("sIotaF", [128, 128])
    sColF = g.sb("sColF", [128, 1])
    sMk1 = g.sb("sMk1", [128, 128])
    sGv = g.sb("sGv", [128, 8, NL, 3])
    sGlc = g.sb("sGlc", [NL, 8, 128, 2])
    sBrow32 = g.sb("sBrow32", [NL, 8, 128])
    sRho32 = g.sb("sRho32", [NL, 8, 128])
    sPh32 = g.sb("sPh32", [NL, 8, 128])
    sRhoT = g.sb("sRhoT", [128, 8, NL])
    sPhT = g.sb("sPhT", [128, 8, NL])
    sRowR = g.sb("sRowR", [1, 8, 128])
    sRowP = g.sb("sRowP", [1, 8, 128])
    sRowB = g.sb("sRowB", [1, 8, 128])
    sRowSK = g.sb("sRowSK", [2, 8, 128])
    sRowSQ = g.sb("sRowSQ", [2, 8, 128])
    sIdent32 = g.sb("sIdent32", [128, 128])
    sIdentB = g.sb("sIdentB", [128, 128], BF16)
    sOnesRow = g.sb("sOnesRow", [1, 128])
    sS0 = g.sb("sS0", [128, 8, 2])
    sS0b = g.sb("sS0b", [128, 8, 2], BF16)
    sWv = g.sb("sWv", [128, 8, 2])
    sWb = g.sb("sWb", [128, 8, 2], BF16)
    sUv = g.sb("sUv", [128, 8, 2])
    sUb = g.sb("sUb", [128, 8, 2], BF16)
    sW2 = g.sb("sW2", [128, 8, 2])
    sW2b = g.sb("sW2b", [128, 8, 2], BF16)
    sSKt = g.sb("sSKt", [128, 8, 2])
    sSQt = g.sb("sSQt", [128, 8, 2])
    sEr = g.sb("sEr", [128, 8])
    sLr = g.sb("sLr", [128, 8])
    sLi = g.sb("sLi", [128, 8])
    sSc1 = g.sb("sSc1", [128, 8])
    sSc2 = g.sb("sSc2", [128, 8])
    sT1v = g.sb("sT1v", [128, 8])
    sT2v = g.sb("sT2v", [128, 8])
    sT3v = g.sb("sT3v", [128, 8])
    sT4v = g.sb("sT4v", [128, 8])
    sT5v = g.sb("sT5v", [128, 8])
    sT6v = g.sb("sT6v", [128, 8])
    sYc8 = g.sb("sYc8", [128, 8])
    sYs8 = g.sb("sYs8", [128, 8])
    sLrowS = g.sb("sLrowS", [1, 8, 4])
    sLrowT = g.sb("sLrowT", [1, 8, 2])
    sLamB = g.sb("sLamB", [128, 8, 4])
    sRall = g.sb("sRall", [128, 8, NL, 2])

    pA = g.ps("pA", [128, 8, 128])
    pB = g.ps("pB", [128, 8, 128])
    pC = g.ps("pC", [128, 8, 128])
    pD = g.ps("pD", [128, 8, 128])

    g.add("sync", lambda s: s.dma_start(out=sIdent32[:], in_=ident[:]), dma=True)
    g.add("sync", lambda s: s.dma_start(out=sIdentB[:], in_=identb[:]), dma=True)
    g.add("sync", lambda s: s.dma_start(out=sS0[:], in_=sin0[:]), dma=True)
    for bhi in range(8):
        g.add("sync", lambda s, bhi=bhi: s.dma_start(
            out=sGv[:, bhi], in_=gsc8[bhi].rearrange("(c t) s -> t c s", t=128)[:, clo:chi, 2:5]), dma=True)
        g.add("sync", lambda s, bhi=bhi: s.dma_start(
            out=sGlc[:, bhi], in_=glt8[bhi].rearrange("(c t) s -> c t s", t=128)[clo:chi]), dma=True)
        g.add("sync", lambda s, bhi=bhi: s.dma_start(
            out=sBrow32[:, bhi], in_=brow8[bhi, clo:chi]), dma=True)
    g.add("gpsimd", lambda gp: gp.iota(sIotaI[:], pattern=[[1, 128]], base=0, channel_multiplier=0))
    g.add("gpsimd", lambda gp: gp.iota(sColI[:], pattern=[[0, 1]], base=0, channel_multiplier=1))
    g.add("vector", lambda v: v.memset(sOnesRow[:], 1.0))
    g.bar()
    seq(g, "vector", lambda v: v.tensor_copy(sIotaF[:], sIotaI[:]))
    seq(g, "vector", lambda v: v.tensor_copy(sColF[:], sColI[:]))
    seq(g, "vector", lambda v: v.tensor_scalar(sMk1[:], sIotaF[:], sColF[:], None, ALU.is_gt))
    for bhi in range(8):
        g.add("vector", lambda v, bhi=bhi: v.tensor_copy(sMkS8[:, bhi], sMk1[:]))
    g.bar()
    seq(g, "vector", lambda v: v.tensor_scalar(sMk1[:], sIotaF[:], sColF[:], None, ALU.is_ge))
    for bhi in range(8):
        g.add("vector", lambda v, bhi=bhi: v.tensor_copy(sMkI8[:, bhi], sMk1[:]))
    g.bar()
    seq(g, "vector", lambda v: v.tensor_scalar(sMkST[:], sIotaF[:], sColF[:], None, ALU.is_lt))
    seq(g, "vector", lambda v: v.tensor_copy(sS0b[:], sS0[:]))
    for bhi in range(8):
        g.add("vector", lambda v, bhi=bhi: v.tensor_tensor_scan(
            sRho32[:, bhi], sGlc[:, bhi, :, 0], sGlc[:, bhi, :, 0], 0.0, ALU.add, ALU.bypass))
    g.bar()
    for bhi in range(8):
        g.add("vector", lambda v, bhi=bhi: v.tensor_tensor_scan(
            sPh32[:, bhi], sGlc[:, bhi, :, 1], sGlc[:, bhi, :, 1], 0.0, ALU.add, ALU.bypass))
    g.bar()
    def trs(t):
        ii = None
        for bhi in range(8):
            ii = t.matmul(pA[:, bhi, 0:NL], sRho32[0:NL, bhi, :], sIdent32[0:NL, 0:NL], start=True, stop=True)
            ii = t.matmul(pB[:, bhi, 0:NL], sPh32[0:NL, bhi, :], sIdent32[0:NL, 0:NL], start=True, stop=True)
        return ii
    g.add("tensor", trs)
    g.bar()
    seq(g, "vector", lambda v: v.tensor_copy(sRhoT[:], pA[:, :, 0:NL]))
    seq(g, "vector", lambda v: v.tensor_copy(sPhT[:], pB[:, :, 0:NL]))

    VW = [128, 8 * 128]

    def bview(tens):
        return tens[:]

    for cl in range(NL):
        ca = clo + cl
        g.add("sync", lambda s, ca=ca: s.dma_start(
            out=sKc[:], in_=kn8[:, :, 128 * ca:128 * (ca + 1)].rearrange("h d t -> d h t")), dma=True)
        g.add("sync", lambda s, ca=ca: s.dma_start(
            out=sQc[:], in_=qn8[:, :, 128 * ca:128 * (ca + 1)].rearrange("h d t -> d h t")), dma=True)
        g.add("sync", lambda s, cl=cl: s.dma_start(out=sRowR[:], in_=sRho32[cl:cl + 1]), dma=True)
        g.add("sync", lambda s, cl=cl: s.dma_start(out=sRowP[:], in_=sPh32[cl:cl + 1]), dma=True)
        g.add("sync", lambda s, cl=cl: s.dma_start(out=sRowB[:], in_=sBrow32[cl:cl + 1]), dma=True)
        g.bar()
        def stB(t):
            ii = None
            for bhi in range(8):
                ii = t.matmul(pA[:, bhi, :], sOnesRow[:], sRowR[0:1, bhi, :], start=True, stop=True)
                ii = t.matmul(pB[:, bhi, :], sOnesRow[:], sRowP[0:1, bhi, :], start=True, stop=True)
                ii = t.matmul(pC[:, bhi, :], sOnesRow[:], sRowB[0:1, bhi, :], start=True, stop=True)
                ii = t.matmul(pD[:, bhi, :], sKc[:, bhi, :], sKc[:, bhi, :], start=True, stop=True)
            return ii
        g.add("tensor", stB)
        g.bar()
        def stC(v, cl=cl):
            ii = None
            for bhi in range(8):
                v.tensor_scalar(sDR[:, bhi], pA[:, bhi, :], sRhoT[:, bhi, cl:cl + 1], None, ALU.subtract)
                ii = v.tensor_scalar(sDC[:, bhi], pB[:, bhi, :], sPhT[:, bhi, cl:cl + 1], None, ALU.subtract)
            return ii
        g.add("vector", stC)
        g.bar()
        def stD(v, cl=cl):
            v.tensor_tensor(bview(sBS), pC[:], bview(sMkS8), ALU.mult)
            for bhi in range(8):
                v.tensor_scalar(sBT[:, bhi], sMkST[:], sGv[:, bhi, cl, 0:1].opt(), None, ALU.mult)
            v.tensor_scalar(bview(sEI), bview(sDR), 0.0, None, ALU.min)
            return v.tensor_scalar(bview(sET), bview(sDR), 0.0, -1.0, ALU.max, ALU.mult)
        g.add("vector", stD)
        g.bar()
        g.add("scalar", lambda sc: sc.activation(out=bview(sEI), in_=bview(sEI), func=AF.Exp))
        g.add("scalar", lambda sc: sc.activation(out=bview(sET), in_=bview(sET), func=AF.Exp))

        def stE(v):
            v.tensor_tensor(bview(sKbS), pD[:], bview(sBS), ALU.mult)
            v.tensor_tensor(bview(sKbT), pD[:], bview(sBT), ALU.mult)
            v.tensor_scalar(bview(sC1), bview(sDC), INV2PI, MAGIC + 0.25, ALU.mult, ALU.add)
            return v.tensor_scalar(bview(sC2), bview(sDC), INV2PI, MAGIC, ALU.mult, ALU.add)
        g.add("vector", stE)
        g.bar()
        def stF(v):
            v.tensor_scalar(bview(sC1), bview(sC1), MAGIC, TWO_PI, ALU.subtract, ALU.mult)
            v.tensor_scalar(bview(sC2), bview(sC2), MAGIC, TWO_PI, ALU.subtract, ALU.mult)
            v.tensor_tensor(bview(sKbS), bview(sKbS), bview(sEI), ALU.mult)
            return v.tensor_tensor(bview(sKbT), bview(sKbT), bview(sET), ALU.mult)
        g.add("vector", stF)
        g.bar()
        def stG(v):
            v.scalar_tensor_tensor(bview(sYC), bview(sDC), HALF_PI, bview(sC1), ALU.add, ALU.subtract)
            return v.tensor_tensor(bview(sYS), bview(sDC), bview(sC2), ALU.subtract)
        g.add("vector", stG)
        g.bar()
        g.add("scalar", lambda sc: sc.activation(out=bview(sYC), in_=bview(sYC), func=AF.Sin))
        g.add("scalar", lambda sc: sc.activation(out=bview(sYS), in_=bview(sYS), func=AF.Sin, scale=-1.0))
        g.bar()
        def stI(v):
            v.tensor_tensor(sAp[:, :, 0, 0, :], bview(sKbS), bview(sYC), ALU.mult)
            v.tensor_tensor(sAp[:, :, 0, 1, :], bview(sKbS), bview(sYS), ALU.mult)
            v.tensor_tensor(sBp[:, :, 0, 0, :], bview(sKbT), bview(sYC), ALU.mult)
            return v.tensor_tensor(sBp[:, :, 0, 2, :], bview(sKbT), bview(sYS), ALU.mult)
        g.add("vector", stI)
        g.bar()
        def stJ(v):
            v.tensor_scalar(sAp[:, :, 0, 2, :],
                            sAp[:, :, 0, 1, :], -1.0, None, ALU.mult)
            return v.tensor_scalar(sBp[:, :, 0, 1, :],
                                   sBp[:, :, 0, 2, :], -1.0, None, ALU.mult)
        g.add("vector", stJ)

        def stJm(t):
            ii = None
            for bhi in range(8):
                ii = t.matmul(pB[:, bhi, :], sKc[:, bhi, :], sQc[:, bhi, :], start=True, stop=True)
                ii = t.matmul(pC[:, bhi, :], sKc[:, bhi, :], sIdentB[:], start=True, stop=True)
            return ii
        g.add("tensor", stJm)
        g.bar()
        g.add("vector", lambda v: v.tensor_tensor(bview(sC1), pB[:],
                                                  bview(sMkI8), ALU.mult))
        g.add("scalar", lambda sc: sc.activation(out=bview(sKT), in_=pC[:],
                                                 func=AF.Copy))
        g.bar()
        g.add("vector", lambda v: v.tensor_tensor(bview(sC1), bview(sC1), bview(sEI), ALU.mult))
        g.bar()
        def stM(v):
            v.tensor_tensor(sMr[:, :, 0, :], bview(sC1), bview(sYC), ALU.mult)
            return v.tensor_tensor(sMr[:, :, 1, :], bview(sC1), bview(sYS), ALU.mult)
        g.add("vector", stM)
        g.bar()
        g.add("vector", lambda v: v.tensor_scalar(sMr[:, :, 2, :],
                                                  sMr[:, :, 1, :], -1.0, None, ALU.mult))
        g.bar()

        def sqmm(t, lv, h):
            pb = (lv - 1) % 2
            last = (lv == 6)
            ii = None
            for bhi in range(4 * h, 4 * h + 4):
                Ar = sAp[:, bhi, lv - 1, 0, :]
                Ai = sAp[:, bhi, lv - 1, 1, :]
                An = sAp[:, bhi, lv - 1, 2, :]
                Br = sBp[:, bhi, pb, 0, :]
                Bi = sBp[:, bhi, pb, 1, :]
                t.matmul(pA[:, bhi, :], Br, Ar, start=True, stop=False)
                t.matmul(pA[:, bhi, :], Bi, An, start=False, stop=True)
                t.matmul(pB[:, bhi, :], Br, Ai, start=True, stop=False)
                ii = t.matmul(pB[:, bhi, :], Bi, Ar, start=False, stop=True)
                if not last:
                    t.matmul(pC[:, bhi, :], Ar, Br, start=True, stop=False)
                    t.matmul(pC[:, bhi, :], An, Bi, start=False, stop=True)
                    t.matmul(pD[:, bhi, :], Ar, Bi, start=True, stop=False)
                    ii = t.matmul(pD[:, bhi, :], Ai, Br, start=False, stop=True)
            return ii

        def sqev_a(sc, lv, h):
            nb = lv % 2
            last = (lv == 6)
            hs = slice(4 * h, 4 * h + 4)
            ii = sc.activation(out=sAp[:, hs, lv, 1, :], in_=pB[:, hs, :], func=AF.Copy)
            if not last:
                ii = sc.activation(out=sBp[:, hs, nb, 1, :], in_=pD[:, hs, :], func=AF.Copy)
            return ii

        def sqev_v(v, lv, h):
            nb = lv % 2
            last = (lv == 6)
            hs = slice(4 * h, 4 * h + 4)
            v.tensor_copy(sAp[:, hs, lv, 0, :], pA[:, hs, :])
            if not last:
                v.tensor_copy(sBp[:, hs, nb, 0, :], pC[:, hs, :])
            return v.tensor_scalar(sAp[:, hs, lv, 2, :], sAp[:, hs, lv, 1, :], -1.0, None, ALU.mult)

        slots = [(lv, h) for lv in range(1, 7) for h in range(2)]
        for k2 in range(2 * len(slots) + 2):
            k, ph = divmod(k2, 2)
            if ph == 0:
                if k < len(slots):
                    g.add("tensor", lambda t, s=slots[k]: sqmm(t, s[0], s[1]))
                if k >= 1:
                    g.add("scalar", lambda sc, s=slots[k - 1]: sqev_a(sc, s[0], s[1]))
            else:
                if k >= 1:
                    g.add("vector", lambda v, s=slots[k - 1]: sqev_v(v, s[0], s[1]))
            g.bar()

        def s1(t):
            ii = None
            for bhi in range(8):
                ii = t.matmul(pA[0:2, bhi, :], sS0b[:, bhi, :], sKc[:, bhi, :], start=True, stop=True)
                ii = t.matmul(pB[0:2, bhi, :], sS0b[:, bhi, :], sQc[:, bhi, :], start=True, stop=True)
            return ii
        g.add("tensor", s1)
        g.bar()

        def s2(v):
            v.tensor_copy(sRowSK[:], pA[0:2, :, :])
            return v.tensor_copy(sRowSQ[:], pB[0:2, :, :])
        g.add("vector", s2)
        g.bar()

        def s3(t):
            ii = None
            for bhi in range(8):
                ii = t.matmul(pC[:, bhi, 0:2], sRowSK[0:2, bhi, :], sIdent32[0:2, 0:2], start=True, stop=True)
                ii = t.matmul(pC[:, bhi, 2:4], sRowSQ[0:2, bhi, :], sIdent32[0:2, 0:2], start=True, stop=True)
            return ii
        g.add("tensor", s3)
        g.bar()

        def s4(v, cl=cl):
            v.tensor_copy(sSKt[:], pC[:, :, 0:2])
            v.tensor_copy(sSQt[:], pC[:, :, 2:4])
            v.tensor_scalar(sSc1[:], sPhT[:, :, cl], INV2PI, MAGIC + 0.25, ALU.mult, ALU.add)
            return v.tensor_scalar(sSc2[:], sPhT[:, :, cl], INV2PI, MAGIC, ALU.mult, ALU.add)
        g.add("vector", s4)
        g.add("scalar", lambda sc, cl=cl: sc.activation(out=sEr[:], in_=sRhoT[:, :, cl], func=AF.Exp))
        g.bar()

        def s5(v):
            v.tensor_scalar(sSc1[:], sSc1[:], MAGIC, TWO_PI, ALU.subtract, ALU.mult)
            return v.tensor_scalar(sSc2[:], sSc2[:], MAGIC, TWO_PI, ALU.subtract, ALU.mult)
        g.add("vector", s5)
        g.bar()

        def s6(v, cl=cl):
            v.scalar_tensor_tensor(sYc8[:], sPhT[:, :, cl], HALF_PI, sSc1[:], ALU.add, ALU.subtract)
            return v.tensor_tensor(sYs8[:], sPhT[:, :, cl], sSc2[:], ALU.subtract)
        g.add("vector", s6)
        g.bar()
        g.add("scalar", lambda sc: sc.activation(out=sYc8[:], in_=sYc8[:], func=AF.Sin))
        g.add("scalar", lambda sc: sc.activation(out=sYs8[:], in_=sYs8[:], func=AF.Sin, scale=-1.0))
        g.bar()

        def s8(v):
            v.tensor_tensor(sLr[:], sEr[:], sYc8[:], ALU.mult)
            return v.tensor_tensor(sLi[:], sEr[:], sYs8[:], ALU.mult)
        g.add("vector", s8)
        g.bar()

        def s9(v):
            v.tensor_tensor(sT1v[:], sLr[:], sSKt[:, :, 0], ALU.mult)
            return v.tensor_tensor(sT2v[:], sLi[:], sSKt[:, :, 1], ALU.mult)
        g.add("vector", s9)
        g.bar()

        def s10(v):
            v.tensor_tensor(sT3v[:], sT1v[:], sT2v[:], ALU.subtract)
            v.tensor_tensor(sT1v[:], sLr[:], sSKt[:, :, 1], ALU.mult)
            return v.tensor_tensor(sT2v[:], sLi[:], sSKt[:, :, 0], ALU.mult)
        g.add("vector", s10)
        g.bar()

        def s11(v, cl=cl):
            v.tensor_tensor(sT4v[:], sT1v[:], sT2v[:], ALU.add)
            return v.tensor_tensor(sT1v[:], sGv[:, :, cl, 1], sT3v[:], ALU.subtract)
        g.add("vector", s11)
        g.bar()
        g.add("vector", lambda v, cl=cl: v.tensor_tensor(sT2v[:], sGv[:, :, cl, 2], sT4v[:], ALU.subtract))
        g.bar()

        def s13(v, cl=cl):
            v.tensor_tensor(sWv[:, :, 0], sT1v[:], sGv[:, :, cl, 0], ALU.mult)
            return v.tensor_tensor(sWv[:, :, 1], sT2v[:], sGv[:, :, cl, 0], ALU.mult)
        g.add("vector", s13)
        g.bar()
        g.add("vector", lambda v: v.tensor_copy(sWb[:], sWv[:]))
        g.bar()

        for k in [6, 5, 4, 3, 2, 1, 0]:
            def fmm(t, k=k):
                ii = None
                for bhi in range(8):
                    t.matmul(pD[:, bhi, 0:1], sAp[:, bhi, k, 0, :], sWb[:, bhi, 0:1], start=True, stop=False)
                    t.matmul(pD[:, bhi, 0:1], sAp[:, bhi, k, 2, :], sWb[:, bhi, 1:2], start=False, stop=True)
                    t.matmul(pD[:, bhi, 1:2], sAp[:, bhi, k, 0, :], sWb[:, bhi, 1:2], start=True, stop=False)
                    ii = t.matmul(pD[:, bhi, 1:2], sAp[:, bhi, k, 1, :], sWb[:, bhi, 0:1], start=False, stop=True)
                return ii
            g.add("tensor", fmm)
            g.bar()
            if k > 0:
                g.add("vector", lambda v: v.tensor_tensor(
                    sWv[:], sWv[:], pD[:, :, 0:2], ALU.add))
                g.bar()
                g.add("vector", lambda v: v.tensor_copy(sWb[:], sWv[:]))
                g.bar()
            else:
                g.add("vector", lambda v: v.tensor_tensor(
                    sUv[:], sWv[:], pD[:, :, 0:2], ALU.subtract))
                g.bar()
                g.add("vector", lambda v: v.tensor_copy(sUb[:], sUv[:]))
                g.bar()

        def h1(t):
            ii = None
            for bhi in range(8):
                t.matmul(pC[:, bhi, 4:5], sMr[:, bhi, 0, :], sUb[:, bhi, 0:1], start=True, stop=False)
                t.matmul(pC[:, bhi, 4:5], sMr[:, bhi, 2, :], sUb[:, bhi, 1:2], start=False, stop=True)
                t.matmul(pC[:, bhi, 5:6], sMr[:, bhi, 0, :], sUb[:, bhi, 1:2], start=True, stop=False)
                ii = t.matmul(pC[:, bhi, 5:6], sMr[:, bhi, 1, :], sUb[:, bhi, 0:1], start=False, stop=True)
            return ii
        g.add("tensor", h1)

        def h1v(v):
            v.tensor_tensor(sT1v[:], sLr[:], sSQt[:, :, 0], ALU.mult)
            return v.tensor_tensor(sT2v[:], sLi[:], sSQt[:, :, 1], ALU.mult)
        g.add("vector", h1v)
        g.bar()

        def h2(v):
            v.tensor_tensor(sT3v[:], sT1v[:], sT2v[:], ALU.subtract)
            v.tensor_tensor(sT1v[:], sLr[:], sSQt[:, :, 1], ALU.mult)
            return v.tensor_tensor(sT2v[:], sLi[:], sSQt[:, :, 0], ALU.mult)
        g.add("vector", h2)
        g.bar()
        g.add("vector", lambda v: v.tensor_tensor(sT4v[:], sT1v[:], sT2v[:], ALU.add))
        g.bar()

        def h4(v, cl=cl):
            v.tensor_tensor(sRall[:, :, cl, 0], sT3v[:], pC[:, :, 4:5].opt(), ALU.add)
            return v.tensor_tensor(sRall[:, :, cl, 1], sT4v[:], pC[:, :, 5:6].opt(), ALU.add)
        g.add("vector", h4)
        g.bar()

        def i1(v):
            v.tensor_copy(sLrowS[0:1, :, 2], sRowR[0:1, :, 127])
            return v.tensor_copy(sLrowS[0:1, :, 3], sRowP[0:1, :, 127])
        g.add("vector", i1)
        g.bar()

        def i2(v):
            v.tensor_scalar(sLrowT[0:1, :, 0], sLrowS[0:1, :, 3], INV2PI, MAGIC + 0.25, ALU.mult, ALU.add)
            return v.tensor_scalar(sLrowT[0:1, :, 1], sLrowS[0:1, :, 3], INV2PI, MAGIC, ALU.mult, ALU.add)
        g.add("vector", i2)
        g.bar()

        def i3(v):
            v.tensor_scalar(sLrowT[0:1, :, 0], sLrowT[0:1, :, 0], MAGIC, TWO_PI, ALU.subtract, ALU.mult)
            return v.tensor_scalar(sLrowT[0:1, :, 1], sLrowT[0:1, :, 1], MAGIC, TWO_PI, ALU.subtract, ALU.mult)
        g.add("vector", i3)
        g.bar()

        def i4(v):
            v.scalar_tensor_tensor(sLrowT[0:1, :, 0], sLrowS[0:1, :, 3], HALF_PI, sLrowT[0:1, :, 0],
                                   ALU.add, ALU.subtract)
            return v.tensor_tensor(sLrowT[0:1, :, 1], sLrowS[0:1, :, 3], sLrowT[0:1, :, 1], ALU.subtract)
        g.add("vector", i4)
        g.bar()

        def i5(sc):
            sc.activation(out=sLrowT[0:1, :, 0], in_=sLrowT[0:1, :, 0], func=AF.Sin)
            sc.activation(out=sLrowT[0:1, :, 1], in_=sLrowT[0:1, :, 1], func=AF.Sin, scale=-1.0)
            return sc.activation(out=sLrowS[0:1, :, 0], in_=sLrowS[0:1, :, 2], func=AF.Exp)
        g.add("scalar", i5)
        g.bar()

        def i6(v):
            v.tensor_tensor(sLrowS[0:1, :, 1], sLrowS[0:1, :, 0], sLrowT[0:1, :, 1], ALU.mult)
            return v.tensor_tensor(sLrowS[0:1, :, 0], sLrowS[0:1, :, 0], sLrowT[0:1, :, 0], ALU.mult)
        g.add("vector", i6)
        g.bar()
        g.add("tensor", lambda t: t.matmul(pD[:, 2, 0:32], sOnesRow[:],
                                           sLrowS[0:1, :, :].rearrange("p a b -> p (a b)"),
                                           start=True, stop=True))
        g.bar()
        g.add("vector", lambda v: v.tensor_copy(sLamB[:].rearrange("p a b -> p (a b)"), pD[:, 2, 0:32]))
        g.bar()

        def j1(v, cl=cl):
            v.tensor_tensor(sT1v[:], sLamB[:, :, 2], sRhoT[:, :, cl], ALU.subtract)
            return v.tensor_tensor(sT2v[:], sLamB[:, :, 3], sPhT[:, :, cl], ALU.subtract)
        g.add("vector", j1)
        g.bar()
        g.add("scalar", lambda sc: sc.activation(out=sT1v[:], in_=sT1v[:], func=AF.Exp))

        def j2(v):
            v.tensor_scalar(sT5v[:], sT2v[:], INV2PI, MAGIC + 0.25, ALU.mult, ALU.add)
            return v.tensor_scalar(sT6v[:], sT2v[:], INV2PI, MAGIC, ALU.mult, ALU.add)
        g.add("vector", j2)
        g.bar()

        def j3(v):
            v.tensor_scalar(sT5v[:], sT5v[:], MAGIC, TWO_PI, ALU.subtract, ALU.mult)
            return v.tensor_scalar(sT6v[:], sT6v[:], MAGIC, TWO_PI, ALU.subtract, ALU.mult)
        g.add("vector", j3)
        g.bar()

        def j4(v):
            v.scalar_tensor_tensor(sT5v[:], sT2v[:], HALF_PI, sT5v[:], ALU.add, ALU.subtract)
            return v.tensor_tensor(sT6v[:], sT2v[:], sT6v[:], ALU.subtract)
        g.add("vector", j4)
        g.bar()

        def j5(sc):
            sc.activation(out=sT5v[:], in_=sT5v[:], func=AF.Sin)
            return sc.activation(out=sT6v[:], in_=sT6v[:], func=AF.Sin, scale=-1.0)
        g.add("scalar", j5)
        g.bar()

        def j6(v):
            v.tensor_tensor(sT3v[:], sT1v[:], sT5v[:], ALU.mult)
            return v.tensor_tensor(sT4v[:], sT1v[:], sT6v[:], ALU.mult)
        g.add("vector", j6)
        g.bar()

        def j7(v):
            v.tensor_tensor(sT1v[:], sT3v[:], sUv[:, :, 0], ALU.mult)
            return v.tensor_tensor(sT2v[:], sT4v[:], sUv[:, :, 1], ALU.mult)
        g.add("vector", j7)
        g.bar()

        def j8(v):
            v.tensor_tensor(sW2[:, :, 0], sT1v[:], sT2v[:], ALU.subtract)
            v.tensor_tensor(sT1v[:], sT3v[:], sUv[:, :, 1], ALU.mult)
            return v.tensor_tensor(sT2v[:], sT4v[:], sUv[:, :, 0], ALU.mult)
        g.add("vector", j8)
        g.bar()
        g.add("vector", lambda v: v.tensor_tensor(sW2[:, :, 1], sT1v[:], sT2v[:], ALU.add))
        g.bar()
        g.add("vector", lambda v: v.tensor_copy(sW2b[:], sW2[:]))
        g.bar()

        def k1(t):
            ii = None
            for bhi in range(8):
                ii = t.matmul(pD[:, bhi, 4:6], sKT[:, bhi, :], sW2b[:, bhi, :], start=True, stop=True)
            return ii
        g.add("tensor", k1)
        g.bar()

        def k2(v):
            v.tensor_tensor(sT1v[:], sLamB[:, :, 0], sS0[:, :, 0], ALU.mult)
            return v.tensor_tensor(sT2v[:], sLamB[:, :, 1], sS0[:, :, 1], ALU.mult)
        g.add("vector", k2)
        g.bar()

        def k3(v):
            v.tensor_tensor(sT3v[:], sT1v[:], sT2v[:], ALU.subtract)
            v.tensor_tensor(sT1v[:], sLamB[:, :, 0], sS0[:, :, 1], ALU.mult)
            return v.tensor_tensor(sT2v[:], sLamB[:, :, 1], sS0[:, :, 0], ALU.mult)
        g.add("vector", k3)
        g.bar()
        g.add("vector", lambda v: v.tensor_tensor(sT4v[:], sT1v[:], sT2v[:], ALU.add))
        g.bar()

        def k5(v):
            v.tensor_tensor(sS0[:, :, 0], sT3v[:], pD[:, :, 4:5].opt(), ALU.add)
            return v.tensor_tensor(sS0[:, :, 1], sT4v[:], pD[:, :, 5:6].opt(), ALU.add)
        g.add("vector", k5)
        g.bar()
        g.add("vector", lambda v: v.tensor_copy(sS0b[:], sS0[:]))
        g.bar()

    for bhi in range(8):
        g.add("sync", lambda s, bhi=bhi: s.dma_start(
            out=rts[bhi].rearrange("(c t) s -> t c s", t=128), in_=sRall[:, bhi]), dma=True)
    g.add("sync", lambda s: s.dma_start(out=sout[:], in_=sS0[:]), dma=True)
    g.bar()
    return g.build()



def build_b4a():
    g = G()
    rslT = g.io("rslT", [B, 32, LC])
    selc32 = g.io("selc32", [B, 32, LC])
    WroT = g.io("WroT", [32, DI])
    stat = g.io("stat", [1, B, 16, 2], out=True)

    sRg = g.sb("sRg", [32, B, LC])
    sSel = g.sb("sSel", [32, B, LC])
    sRgb = g.sb("sRgb", [32, B, LC], BF16)
    sWro = g.sb("sWro", [32, 16, 128], BF16)
    sSq = g.sb("sSq", [128, 4, LC])
    sCols = g.sb("sCols", [128, B, 16, 2])
    sStatO = g.sb("sStatO", [1, B, 16, 2])
    pY = g.ps("pY", [128, 4, 512])
    pS = g.ps("pS", [1, 512])

    C = common_bufs(g)
    g.add("sync", lambda s: s.dma_start(out=sRg[:], in_=rslT.rearrange("b k l -> k b l")), dma=True)
    g.add("sync", lambda s: s.dma_start(out=sSel[:], in_=selc32.rearrange("b k l -> k b l")), dma=True)
    g.add("gpsimd", lambda gp: gp.dma_start(out=sWro[:], in_=WroT.rearrange("k (fc f) -> k fc f", f=128)), dma=True)
    g.bar()
    seq(g, "vector", lambda v: v.tensor_tensor(sRg[:].rearrange("p a b -> p (a b)"),
                                               sRg[:].rearrange("p a b -> p (a b)"),
                                               sSel[:].rearrange("p a b -> p (a b)"), ALU.mult))
    seq(g, "vector", lambda v: v.tensor_copy(sRgb[:].rearrange("p a b -> p (a b)"),
                                             sRg[:].rearrange("p a b -> p (a b)")))
    for b in range(B):
        for qt in range(4):
            def ymm(t, b=b, qt=qt):
                ii = None
                for j in range(4):
                    ii = t.matmul(pY[:, j], sWro[0:32, 4 * qt + j], sRgb[0:32, b], start=True, stop=True)
                return ii
            g.add("tensor", ymm)
            g.bar()

            def yst(v, b=b, qt=qt):
                ii = None
                for j in range(4):
                    ii = v.reduce_sum(sCols[:, b, 4 * qt + j, 0:1], pY[:, j], AX.X)
                return ii
            g.add("vector", yst)
            g.bar()
            def ysq(sc):
                ii = None
                for j in range(4):
                    ii = sc.activation(out=sSq[:, j], in_=pY[:, j], func=AF.Square)
                return ii
            g.add("scalar", ysq)
            g.bar()

            def yst2(v, b=b, qt=qt):
                ii = None
                for j in range(4):
                    ii = v.reduce_sum(sCols[:, b, 4 * qt + j, 1:2], sSq[:, j], AX.X)
                return ii
            g.add("vector", yst2)
            g.bar()

    def smm(t):
        ii = None
        for b in range(B):
            ii = t.matmul(pS[0:1, 32 * b:32 * (b + 1)], C["sOnes32"][:],
                          sCols[:, b].rearrange("p a b -> p (a b)"), start=True, stop=True)
        return ii
    g.add("tensor", smm)
    g.bar()
    seq(g, "vector", lambda v: v.tensor_copy(sStatO[:].rearrange("p a b c -> p (a b c)"), pS[0:1, 0:128]))
    seq(g, "sync", lambda s: s.dma_start(out=stat[:], in_=sStatO[0:1]), dma=True)
    return g.build()



def build_b4b():
    g = G()
    rslT = g.io("rslT", [B, 32, LC])
    selc32 = g.io("selc32", [B, 32, LC])
    WroT = g.io("WroT", [32, DI])
    st8 = g.io("st8", [8, B, 16, 2])
    zfs = g.io("zfs", [B, 16, 128, LC], BF16)
    xcs = g.io("xcs", [B, 16, 128, LC], BF16)
    gnw = g.io("gnw", [128, 16])
    gnb = g.io("gnb", [128, 16])
    dcol = g.io("dcol", [128, 16])
    WoutT = g.io("WoutT", [DI, D])
    xres = g.io("xres", [B, 8, 128, LC])
    xlnf = g.io("xlnf", [B, 8, 128, LC])
    xout = g.io("xout", [B, 8, 128, LC], out=True)

    sRg = g.sb("sRg", [32, B, LC])
    sSel = g.sb("sSel", [32, B, LC])
    sRgb = g.sb("sRgb", [32, B, LC], BF16)
    sWro = g.sb("sWro", [32, 16, 128], BF16)
    sWout = g.sb("sWout", [128, 16, 8, 128], BF16)
    sSt8 = g.sb("sSt8", [8, B, 16, 2])
    sM = g.sb("sM", [1, B, 16, 2])
    sVar = g.sb("sVar", [1, B, 16])
    sBr2 = g.sb("sBr2", [1, B, 16, 2])
    sStB = g.sb("sStB", [128, B, 16, 2])
    sGnw = g.sb("sGnw", [128, 16])
    sGnb = g.sb("sGnb", [128, 16])
    sDc = g.sb("sDc", [128, 16])
    sOnes8 = g.sb("sOnes8", [8, 1])
    sZ = g.sb("sZ", [128, 3, LC], BF16)
    sXc = g.sb("sXc", [128, 3, LC], BF16)
    sSig = g.sb("sSig", [128, 3, LC])
    sY1 = g.sb("sY1", [128, LC])
    sYb = g.sb("sYb", [128, 16, LC], BF16)
    sXr = g.sb("sXr", [128, 8, LC])
    sXl8 = g.sb("sXl8", [128, 8, LC])
    sXo = g.sb("sXo", [128, 2, LC])
    pY = g.ps("pY", [128, 3, 512])
    pO = g.ps("pO", [128, 2, 512])

    C = common_bufs(g)
    pS = C["pS1"]
    pB2 = C["pBC"]
    g.add("sync", lambda s: s.dma_start(out=sRg[:], in_=rslT.rearrange("b k l -> k b l")), dma=True)
    g.add("sync", lambda s: s.dma_start(out=sSel[:], in_=selc32.rearrange("b k l -> k b l")), dma=True)
    g.add("gpsimd", lambda gp: gp.dma_start(out=sWro[:], in_=WroT.rearrange("k (fc f) -> k fc f", f=128)), dma=True)
    g.add("gpsimd", lambda gp: gp.dma_start(
        out=sWout[:], in_=WoutT.rearrange("(ec e) (fc f) -> e ec fc f", e=128, f=128)), dma=True,
          defer=12)
    g.add("sync", lambda s: s.dma_start(out=sSt8[:], in_=st8[:]), dma=True)
    g.add("sync", lambda s: s.dma_start(out=sGnw[:], in_=gnw[:]), dma=True)
    g.add("sync", lambda s: s.dma_start(out=sGnb[:], in_=gnb[:]), dma=True)
    g.add("sync", lambda s: s.dma_start(out=sDc[:], in_=dcol[:]), dma=True)
    g.add("vector", lambda v: v.memset(sOnes8[:], 1.0))
    g.bar()
    seq(g, "tensor", lambda t: t.matmul(pS[0:1, 0:128], sOnes8[:],
                                        sSt8[:].rearrange("p a b c -> p (a b c)"), start=True, stop=True))
    seq(g, "vector", lambda v: v.tensor_scalar(sM[:].rearrange("p a b c -> p (a b c)"), pS[0:1, 0:128],
                                               1.0 / float(L * 128), None, ALU.mult))
    seq(g, "vector", lambda v: v.tensor_tensor(sVar[:],
                                               sM[:, :, :, 0],
                                               sM[:, :, :, 0], ALU.mult))
    seq(g, "vector", lambda v: v.tensor_tensor(sVar[:],
                                               sM[:, :, :, 1],
                                               sVar[:], ALU.subtract))
    seq(g, "vector", lambda v: v.tensor_scalar(sVar[:],
                                               sVar[:], 1e-5, None, ALU.add))
    seq(g, "scalar", lambda sc: sc.activation(out=sVar[:],
                                              in_=sVar[:], func=AF.Ln))
    seq(g, "scalar", lambda sc: sc.activation(out=sVar[:],
                                              in_=sVar[:], func=AF.Exp, scale=-0.5))
    seq(g, "vector", lambda v: v.tensor_copy(sBr2[:, :, :, 0],
                                             sM[:, :, :, 0]))
    seq(g, "vector", lambda v: v.tensor_copy(sBr2[:, :, :, 1],
                                             sVar[:]))
    seq(g, "tensor", lambda t: t.matmul(pB2[:, 0:128], C["sOnesRow"][:],
                                        sBr2[:].rearrange("p a b c -> p (a b c)"), start=True, stop=True))
    seq(g, "vector", lambda v: v.tensor_copy(sStB[:].rearrange("p a b c -> p (a b c)"), pB2[:, 0:128]))
    seq(g, "vector", lambda v: v.tensor_tensor(sRg[:].rearrange("p a b -> p (a b)"),
                                               sRg[:].rearrange("p a b -> p (a b)"),
                                               sSel[:].rearrange("p a b -> p (a b)"), ALU.mult))
    seq(g, "vector", lambda v: v.tensor_copy(sRgb[:].rearrange("p a b -> p (a b)"),
                                             sRg[:].rearrange("p a b -> p (a b)")))

    for b in range(B):
        g.add("sync", lambda s, b=b: s.dma_start(out=sXr[:], in_=xres[b].rearrange("c f l -> f c l")), dma=True)
        g.add("sync", lambda s, b=b: s.dma_start(out=sXl8[:], in_=xlnf[b].rearrange("c f l -> f c l")), dma=True)
        g.bar()
        g.add("vector", lambda v: v.tensor_tensor(sXr[:], sXr[:], sXl8[:], ALU.add))
        g.bar()
        for i in range(18):
            if i < 16:
                g.add("tensor", lambda t, b=b, i=i: t.matmul(
                    pY[:, i % 3], sWro[0:32, i], sRgb[0:32, b], start=True, stop=True))
                g.add("sync", lambda s, b=b, i=i: s.dma_start(out=sZ[:, i % 3], in_=zfs[b, i]), dma=True)
                g.add("sync", lambda s, b=b, i=i: s.dma_start(out=sXc[:, i % 3], in_=xcs[b, i]), dma=True)
            if 0 <= i - 1 < 16:
                g.add("scalar", lambda sc, i=i: sc.activation(out=sSig[:, (i - 1) % 3],
                                                              in_=sZ[:, (i - 1) % 3], func=AF.Sigmoid))
            if 0 <= i - 2 < 16:
                def ych(v, b=b, i=i):
                    fc = i - 2
                    bk = fc % 3
                    v.tensor_scalar(sY1[:], pY[:, bk], sStB[:, b, fc, 0:1], sStB[:, b, fc, 1:2],
                                    ALU.subtract, ALU.mult)
                    v.tensor_scalar(sY1[:], sY1[:], sGnw[:, fc:fc + 1], sGnb[:, fc:fc + 1],
                                    ALU.mult, ALU.add)
                    v.tensor_tensor(sSig[:, bk], sSig[:, bk], sZ[:, bk], ALU.mult)
                    v.tensor_tensor(sY1[:], sY1[:], sSig[:, bk], ALU.mult)
                    v.scalar_tensor_tensor(sY1[:], sXc[:, bk], sDc[:, fc:fc + 1], sY1[:],
                                           ALU.mult, ALU.add)
                    return v.tensor_copy(sYb[:, fc], sY1[:])
                g.add("vector", ych)
            g.bar()
        for i in range(10):
            if i < 8:
                def omm(t, i=i):
                    ii = None
                    for ec in range(16):
                        ii = t.matmul(pO[:, i % 2], sWout[:, ec, i], sYb[:, ec],
                                      start=(ec == 0), stop=(ec == 15))
                    return ii
                g.add("tensor", omm)
            if 0 <= i - 1 < 8:
                g.add("vector", lambda v, i=i: v.tensor_tensor(
                    sXo[:, (i - 1) % 2], pO[:, (i - 1) % 2], sXr[:, i - 1], ALU.add))
            if 0 <= i - 2 < 8:
                g.add("sync", lambda s, b=b, i=i: s.dma_start(
                    out=xout[b, i - 2], in_=sXo[:, (i - 2) % 2]), dma=True)
            g.bar()
    return g.build()



def run_attnb(x_fm, p, builders):
    pre = "L0_attn_b_"
    Wip = p[pre + "in_proj_w"]
    bip = p[pre + "in_proj_b"]
    if "b1" not in builders:
        builders["b1"] = build_b1()
    ins = []
    for c in range(NCORES):
        ins.append({
            "xsl": f32c(x_fm[:, :, :, LC * c:LC * (c + 1)]),
            "WzT": f32c(Wip[:, 0:DI]), "WgT": f32c(Wip[:, DI:2 * DI]), "WkT": f32c(Wip[:, 2 * DI:3 * DI]),
            "WvT": f32c(Wip[:, 3 * DI:3 * DI + 32]),
            "bcolz": f32c(bip[0:DI].reshape(16, 128).T), "bcolg": f32c(bip[DI:2 * DI].reshape(16, 128).T),
            "bcolk": f32c(bip[2 * DI:3 * DI].reshape(16, 128).T),
            "bvr": f32c(bip[3 * DI:3 * DI + 32].reshape(1, 32)),
            "rmsw": f32c(p[pre + "rms_w"].reshape(8, 128).T),
            "ident": IDENT,
        })
    r1 = run_bass_kernel_spmd(builders["b1"], ins, core_ids=list(range(NCORES)))
    zf = [r1.results[c]["zf"] for c in range(NCORES)]
    xlnf = [r1.results[c]["xlnf"] for c in range(NCORES)]
    xgf = [r1.results[c]["xgf"] for c in range(NCORES)]
    kff = [r1.results[c]["kff"] for c in range(NCORES)]
    vv = [r1.results[c]["vv"] for c in range(NCORES)]

    xgf_full = np.concatenate(xgf, axis=3)
    xgp = np.concatenate([np.zeros((B, 16, 128, 3), BF16NP), xgf_full], axis=3)
    frq = (1.0 / (10000.0 ** (np.arange(16, dtype=np.float64) / 16.0))).astype(np.float32)

    if "b2" not in builders:
        builders["b2"] = build_b2()
    ins = []
    for c in range(NCORES):
        ins.append({
            "xgh": f32c(xgp[:, :, :, LC * c:LC * (c + 1) + 3].astype(np.float32)),
            "kin": kff[c],
            "vv": vv[c],
            "QwT": f32c(p[pre + "Q_w"]),
            "gwT": f32c(p[pre + "gate_w"]),
            "gb": f32c(p[pre + "gate_b"].reshape(1, 112)),
            "lds": f32c(p[pre + "log_dt_scale"].reshape(1, 16)),
            "frq": frq.reshape(1, 16),
            "convw": f32c(p[pre + "conv_w"].reshape(16, 128, 4).transpose(1, 0, 2)),
            "convb": f32c(p[pre + "conv_b"].reshape(16, 128).T),
            "ident": IDENT,
        })
    r2 = run_bass_kernel_spmd(builders["b2"], ins, core_ids=list(range(NCORES)))
    xcf = [r2.results[c]["xcf"] for c in range(NCORES)]
    qnf_full = np.concatenate([r2.results[c]["qnf"] for c in range(NCORES)], axis=3)
    knf_full = np.concatenate([r2.results[c]["knf"] for c in range(NCORES)], axis=3)
    gsc_full = np.concatenate([r2.results[c]["gsc"] for c in range(NCORES)], axis=1)
    glt_full = np.concatenate([r2.results[c]["glt"] for c in range(NCORES)], axis=2)

    IDENTB = IDENT.astype(BF16NP)
    sin = [np.zeros((128, BHC8, 2), np.float32) for _ in range(NCORES)]
    rparts = []
    for (clo, chi) in [(0, 16), (16, 32)]:
        key = f"b3_{clo}"
        if key not in builders:
            builders[key] = build_b3(clo, chi)
        ins = []
        for c in range(NCORES):
            bhs = list(range(8 * c, 8 * c + 8))
            ins.append({
                "kn8": np.stack([knf_full[bh // 16, bh % 16] for bh in bhs]),
                "qn8": np.stack([qnf_full[bh // 16, bh % 16] for bh in bhs]),
                "gsc8": np.stack([gsc_full[bh // 16, :, bh % 16, :] for bh in bhs]),
                "glt8": np.stack([glt_full[bh // 16, bh % 16] for bh in bhs]),
                "brow8": np.ascontiguousarray(
                    np.stack([gsc_full[bh // 16, :, bh % 16, 2] for bh in bhs]).reshape(BHC8, NCH, 128)),
                "sin0": sin[c],
                "ident": IDENT, "identb": IDENTB,
            })
        r3 = run_bass_kernel_spmd(builders[key], ins, core_ids=list(range(NCORES)))
        rparts.append([r3.results[c]["rts"] for c in range(NCORES)])
        sin = [r3.results[c]["sout"] for c in range(NCORES)]

    rfull = np.zeros((B, 16, L, 2), np.float32)
    for c in range(NCORES):
        for i, bh in enumerate(range(8 * c, 8 * c + 8)):
            rfull[bh // 16, bh % 16, 0:2048] = rparts[0][c][i]
            rfull[bh // 16, bh % 16, 2048:4096] = rparts[1][c][i]

    rT = np.ascontiguousarray(rfull.transpose(0, 1, 3, 2)).reshape(B, 32, L)
    selc = np.ascontiguousarray(gsc_full[:, :, :, 5].transpose(0, 2, 1))
    selc32 = np.repeat(selc, 2, axis=1)

    if "b4a" not in builders:
        builders["b4a"] = build_b4a()
    ins = []
    for c in range(NCORES):
        sl = slice(LC * c, LC * (c + 1))
        ins.append({"rslT": f32c(rT[:, :, sl]), "selc32": f32c(selc32[:, :, sl]),
                    "WroT": f32c(p[pre + "readout_w"])})
    r4a = run_bass_kernel_spmd(builders["b4a"], ins, core_ids=list(range(NCORES)))
    st8 = np.stack([r4a.results[c]["stat"][0] for c in range(NCORES)], axis=0)

    if "b4b" not in builders:
        builders["b4b"] = build_b4b()
    ins = []
    for c in range(NCORES):
        sl = slice(LC * c, LC * (c + 1))
        ins.append({
            "rslT": f32c(rT[:, :, sl]), "selc32": f32c(selc32[:, :, sl]),
            "WroT": f32c(p[pre + "readout_w"]), "st8": st8,
            "zfs": zf[c], "xcs": xcf[c],
            "gnw": f32c(p[pre + "gn_w"].reshape(16, 128).T),
            "gnb": f32c(p[pre + "gn_b"].reshape(16, 128).T),
            "dcol": f32c(p[pre + "D_param"].reshape(16, 128).T),
            "WoutT": f32c(p[pre + "out_w"]),
            "xres": f32c(x_fm[:, :, :, sl]),
            "xlnf": xlnf[c],
        })
    r4b = run_bass_kernel_spmd(builders["b4b"], ins, core_ids=list(range(NCORES)))
    return np.concatenate([r4b.results[c]["xout"] for c in range(NCORES)], axis=3)




FF1 = 1408
NFC1 = 11

NE = 10
BIGV = 60000.0



def build_ffn1():
    g = G()
    xsl = g.io("xsl", [B, 8, 128, LC])
    gw8 = g.io("gw8", [D, 8])
    Wg = g.io("Wg", [NE, D, FF1])
    Wu = g.io("Wu", [NE, D, FF1])
    Wd = g.io("Wd", [NE, FF1, D])
    ident = g.io("ident", [128, 128])
    xout = g.io("xout", [B, 8, 128, LC], out=True)

    sX1 = g.sb("sX1", [128, 8, LC])
    sXnb = g.sb("sXnb", [128, 8, B, LC], BF16)
    sGw8 = g.sb("sGw8", [128, 8, 8], BF16)
    sWgE = g.sb("sWgE", [128, 8, NFC1, 128], BF16)
    sWuE = g.sb("sWuE", [128, 8, NFC1, 128], BF16)
    sWdE = g.sb("sWdE", [128, NFC1, 8, 128], BF16)
    sP = g.sb("sP", [128, 16, 8])
    sB1 = g.sb("sB1", [128, 16, 8])
    sB2 = g.sb("sB2", [128, 16, 8])
    sB3 = g.sb("sB3", [128, 16, 8])
    sIE = g.sb("sIE", [128, 16, 8])
    sIotaE = g.sb("sIotaE", [128, 8], I32)
    sIEf = g.sb("sIEf", [128, 8])
    sCol = g.sb("sCol", [128, 16, 1])
    sCol2 = g.sb("sCol2", [128, 16, 1])
    sWte = g.sb("sWte", [128, 16, 8])
    sWteT = g.sb("sWteT", [8, 16, 128])
    sWrow1 = g.sb("sWrow1", [1, 4, 128])
    sIdent32 = g.sb("sIdent32", [128, 128])
    sMid = g.sb("sMid", [128, NFC1, LC], BF16)
    sAcc = g.sb("sAcc", [128, 8, B, LC], BF16)
    pM = g.ps("pM", [128, 3, 512])
    pD2 = g.ps("pD2", [128, 2, 512])

    sTg = g.sb("sTg", [128, LC])
    sSg = g.sb("sSg", [128, 2, LC])
    sWF = g.sb("sWF", [128, LC])
    C = common_bufs(g)
    g.add("sync", lambda s: s.dma_start(out=sIdent32[:], in_=ident[:]), dma=True)
    g.add("gpsimd", lambda gp: gp.dma_start(out=sGw8[:], in_=gw8.rearrange("(ec e) o -> e ec o", e=128)), dma=True)
    g.add("gpsimd", lambda gp: gp.iota(sIotaE[:], pattern=[[1, 8]], base=0, channel_multiplier=0))
    g.bar()
    seq(g, "vector", lambda v: v.tensor_copy(sIEf[:], sIotaE[:]))
    for j in range(16):
        g.add("vector", lambda v, j=j: v.tensor_copy(sIE[:, j], sIEf[:]))
    g.bar()
    for b in range(B):
        g.add("sync", lambda s, b=b: s.dma_start(out=sX1[:], in_=xsl[b].rearrange("c f l -> f c l")), dma=True)
        g.bar()
        emit_ln(g, sX1[:], sXnb[:, :, b, :], C)
    for b in range(B):
        for q in range(4):
            def lmm(t, b=b, q=q):
                ii = None
                j = 4 * b + q
                for ec in range(8):
                    ii = t.matmul(C["pBC"][:, 8 * j:8 * j + 8], sXnb[:, ec, b, 128 * q:128 * (q + 1)],
                                  sGw8[:, ec], start=(ec == 0), stop=(ec == 7))
                return ii
            g.add("tensor", lmm)
            g.bar()
    seq(g, "vector", lambda v: v.tensor_copy(sP[:], C["pBC"][:, 0:128]))

    def bcast(src_col, dst):
        def fn(v):
            ii = None
            for j in range(8):
                ii = v.tensor_copy(dst[:, :, j:j + 1], src_col[:])
            return ii
        return fn

    seq(g, "vector", lambda v: v.reduce_max(sCol[:], sP[:], AX.X))
    seq(g, "vector", bcast(sCol, sB1))
    seq(g, "vector", lambda v: v.tensor_tensor(sP[:], sP[:], sB1[:], ALU.subtract))
    seq(g, "scalar", lambda sc: sc.activation(out=sP[:], in_=sP[:], func=AF.Exp))
    seq(g, "vector", lambda v: v.reduce_sum(sCol[:], sP[:], AX.X))
    seq(g, "vector", lambda v: v.reciprocal(sCol2[:], sCol[:]))
    seq(g, "vector", bcast(sCol2, sB1))
    seq(g, "vector", lambda v: v.tensor_tensor(sP[:], sP[:], sB1[:], ALU.mult))
    seq(g, "vector", lambda v: v.reduce_max(sCol[:], sP[:], AX.X))
    seq(g, "vector", bcast(sCol, sB1))
    seq(g, "vector", lambda v: v.tensor_tensor(sB2[:], sP[:], sB1[:], ALU.is_equal))
    seq(g, "vector", lambda v: v.tensor_scalar(sB2[:], sB2[:], -BIGV, BIGV, ALU.mult, ALU.add))
    seq(g, "vector", lambda v: v.tensor_tensor(sB2[:], sB2[:], sIE[:], ALU.add))
    seq(g, "vector", lambda v: v.tensor_reduce(sCol2[:], sB2[:], AX.X, ALU.min))
    seq(g, "vector", bcast(sCol2, sB2))
    seq(g, "vector", lambda v: v.tensor_tensor(sB2[:], sIE[:], sB2[:], ALU.is_equal))
    seq(g, "vector", lambda v: v.tensor_tensor(sWte[:], sB2[:], sB1[:], ALU.mult))
    seq(g, "vector", lambda v: v.tensor_scalar(sB3[:], sB2[:], BIGV, None, ALU.mult))
    seq(g, "vector", lambda v: v.tensor_tensor(sB3[:], sP[:], sB3[:], ALU.subtract))
    seq(g, "vector", lambda v: v.reduce_max(sCol[:], sB3[:], AX.X))
    seq(g, "vector", bcast(sCol, sB1))
    seq(g, "vector", lambda v: v.tensor_tensor(sB2[:], sB3[:], sB1[:], ALU.is_equal))
    seq(g, "vector", lambda v: v.tensor_scalar(sB2[:], sB2[:], -BIGV, BIGV, ALU.mult, ALU.add))
    seq(g, "vector", lambda v: v.tensor_tensor(sB2[:], sB2[:], sIE[:], ALU.add))
    seq(g, "vector", lambda v: v.tensor_reduce(sCol2[:], sB2[:], AX.X, ALU.min))
    seq(g, "vector", bcast(sCol2, sB2))
    seq(g, "vector", lambda v: v.tensor_tensor(sB2[:], sIE[:], sB2[:], ALU.is_equal))
    seq(g, "vector", lambda v: v.tensor_tensor(sB1[:], sB2[:], sB1[:], ALU.mult))
    seq(g, "vector", lambda v: v.tensor_tensor(sWte[:], sWte[:], sB1[:], ALU.add))
    seq(g, "vector", lambda v: v.reduce_sum(sCol[:], sWte[:], AX.X))
    seq(g, "vector", lambda v: v.reciprocal(sCol2[:], sCol[:]))
    seq(g, "vector", bcast(sCol2, sB1))
    seq(g, "vector", lambda v: v.tensor_tensor(sWte[:], sWte[:], sB1[:], ALU.mult))
    for j in range(16):
        g.add("tensor", lambda t, j=j: t.matmul(pM[0:8, j % 2, 0:128], sWte[:, j, :], sIdent32[:],
                                                start=True, stop=True))
        g.bar()
        g.add("vector", lambda v, j=j: v.tensor_copy(sWteT[:, j, :], pM[0:8, j % 2, 0:128]))
        g.bar()

    for e in range(NE):
        g.add("gpsimd", lambda gp, e=e: gp.dma_start(
            out=sWgE[:], in_=Wg[e].rearrange("(ec ee) (fc f) -> ee ec fc f", ee=128, f=128)), dma=True)
        g.add("gpsimd", lambda gp, e=e: gp.dma_start(
            out=sWuE[:], in_=Wu[e].rearrange("(ec ee) (fc f) -> ee ec fc f", ee=128, f=128)), dma=True)
        g.add("gpsimd", lambda gp, e=e: gp.dma_start(
            out=sWdE[:], in_=Wd[e].rearrange("(kc kk) (fc f) -> kk kc fc f", kk=128, f=128)), dma=True,
              defer=8)
        g.bar()
        for b in range(B):
            if e < 8:
                seq(g, "sync", lambda s, e=e, b=b: s.dma_start(
                    out=sWrow1[:], in_=sWteT[e:e + 1, 4 * b:4 * b + 4, :]), dma=True)
                seq(g, "tensor", lambda t: t.matmul(
                    C["pBC"][:], C["sOnesRow"][:], sWrow1[0:1].rearrange("p a b -> p (a b)"),
                    start=True, stop=True))
                seq(g, "vector", lambda v: v.tensor_copy(sWF[:], C["pBC"][:]))
            for step in range(NFC1 + 2):
                def mstage(t, step=step, b=b):
                    ii = None
                    if step < NFC1:
                        for ec in range(8):
                            ii = t.matmul(pM[:, step % 3], sWgE[:, ec, step], sXnb[:, ec, b, :],
                                          start=(ec == 0), stop=(ec == 7))
                    if 1 <= step <= NFC1:
                        fu = step - 1
                        for ec in range(8):
                            ii = t.matmul(pD2[:, fu % 2], sWuE[:, ec, fu], sXnb[:, ec, b, :],
                                          start=(ec == 0), stop=(ec == 7))
                    return ii
                if step <= NFC1:
                    g.add("tensor", mstage)
                if 1 <= step <= NFC1:
                    g.add("scalar", lambda sc, step=step: sc.activation(
                        out=sSg[:, (step - 1) % 2], in_=pM[:, (step - 1) % 3], func=AF.Sigmoid))
                if step >= 2:
                    def mdve(v, step=step):
                        f = step - 2
                        v.tensor_tensor(sTg[:], pM[:, f % 3], sSg[:, f % 2], ALU.mult)
                        return v.tensor_tensor(sMid[:, f], sTg[:], pD2[:, f % 2], ALU.mult)
                    g.add("vector", mdve)
                g.bar()
            if e < 8:
                def gate(v):
                    ii = None
                    for fc in range(NFC1):
                        ii = v.tensor_tensor(sMid[:, fc], sMid[:, fc], sWF[:], ALU.mult)
                    return ii
                g.add("vector", gate)
                g.bar()
            for step in range(9):
                if step < 8:
                    def dstage(t, step=step):
                        ii = None
                        for kc in range(NFC1):
                            ii = t.matmul(pM[:, step % 2], sWdE[:, kc, step], sMid[:, kc],
                                          start=(kc == 0), stop=(kc == NFC1 - 1))
                        return ii
                    g.add("tensor", dstage)
                if step >= 1:
                    fo = step - 1
                    if e == 0:
                        g.add("vector", lambda v, b=b, fo=fo: v.tensor_copy(
                            sAcc[:, fo, b, :], pM[:, fo % 2]))
                    else:
                        g.add("vector", lambda v, b=b, fo=fo: v.tensor_tensor(
                            sAcc[:, fo, b, :], sAcc[:, fo, b, :], pM[:, fo % 2], ALU.add))
                g.bar()
    for b in range(B):
        g.add("sync", lambda s, b=b: s.dma_start(out=sX1[:], in_=xsl[b].rearrange("c f l -> f c l")), dma=True)
        g.bar()

        def res(v, b=b):
            ii = None
            for fc in range(8):
                ii = v.tensor_tensor(sX1[:, fc], sX1[:, fc], sAcc[:, fc, b, :], ALU.add)
            return ii
        g.add("vector", res)
        g.bar()
        for fc in range(8):
            g.add("sync", lambda s, b=b, fc=fc: s.dma_start(out=xout[b, fc], in_=sX1[:, fc]), dma=True)
        g.bar()
    return g.build()


def run_ffn1(x_fm, p, builders):
    pre = "L1_ffn_"
    if "ffn1" not in builders:
        builders["ffn1"] = build_ffn1()
    Wg = np.concatenate([p[pre + "exp_gate"], p[pre + "sh_gate"]], axis=0)
    Wu = np.concatenate([p[pre + "exp_up"], p[pre + "sh_up"]], axis=0)
    Wd = np.concatenate([p[pre + "exp_down"], p[pre + "sh_down"]], axis=0)
    ins = []
    for c in range(NCORES):
        ins.append({
            "xsl": f32c(x_fm[:, :, :, LC * c:LC * (c + 1)]),
            "gw8": f32c(p[pre + "gate_w"]),
            "Wg": f32c(Wg), "Wu": f32c(Wu), "Wd": f32c(Wd),
            "ident": IDENT,
        })
    r = run_bass_kernel_spmd(builders["ffn1"], ins, core_ids=list(range(NCORES)))
    return np.concatenate([r.results[c]["xout"] for c in range(NCORES)], axis=3)



def build_l1b1():
    g = G()
    xsl = g.io("xsl", [B, 8, 128, LC])
    WkvT = g.io("WkvT", [D, 2048])
    xnf = g.io("xnf", [B, 8, 128, LC], BF16, out=True)
    kvp = g.io("kvp", [B, 64, 16, 64], out=True)
    ksm = g.io("ksm", [B, 64, 2, 8], out=True)

    sX1 = g.sb("sX1", [128, 8, LC])
    sXn = g.sb("sXn", [128, 8, LC], BF16)
    sWkv = g.sb("sWkv", [128, 8, 16, 128], BF16)
    sQKV = g.sb("sQKV", [128, 16, 128], BF16)
    sKVo = g.sb("sKVo", [64, 16, 64])
    sKSo = g.sb("sKSo", [64, 2, 8])
    pQ = g.ps("pQ", [128, 2, 512])
    pKV = g.ps("pKV", [64, 16, 64])
    pKS = g.ps("pKS", [64, 16])

    C = common_bufs(g)
    g.add("gpsimd", lambda gp: gp.dma_start(
        out=sWkv[:], in_=WkvT.rearrange("(ec e) (fc f) -> e ec fc f", e=128, f=128)), dma=True, defer=6)
    g.bar()
    for b in range(B):
        g.add("sync", lambda s, b=b: s.dma_start(out=sX1[:], in_=xsl[b].rearrange("c f l -> f c l")), dma=True)
        g.bar()
        emit_ln(g, sX1[:], sXn[:], C)
        for fc in range(8):
            g.add("sync", lambda s, b=b, fc=fc: s.dma_start(out=xnf[b, fc], in_=sXn[:, fc]), dma=True)
        for q in range(4):
            for pair in range(2):
                def kvmm(t, q=q, pair=pair):
                    ii = None
                    for half in range(2):
                        for ec in range(8):
                            hh = 2 * pair + half
                            ii = t.matmul(pQ[:, half], sXn[:, ec, 128 * q:128 * (q + 1)],
                                          sWkv[:, ec, 4 * hh:4 * hh + 4, :].rearrange("e a b -> e (a b)"),
                                          start=(ec == 0), stop=(ec == 7))
                    return ii
                g.add("tensor", kvmm)
                g.bar()
                if pair == 0:
                    seq(g, "scalar", lambda sc: sc.activation(
                        out=sQKV[:, 0:8, :].rearrange("p a b -> p (a b)"),
                        in_=pQ[:].rearrange("p a b -> p (a b)"), func=AF.Square))
                    seq(g, "scalar", lambda sc: sc.activation(
                        out=sQKV[:, 0:8, :].rearrange("p a b -> p (a b)"),
                        in_=sQKV[:, 0:8, :].rearrange("p a b -> p (a b)"), func=AF.Exp, scale=-0.5))
                else:
                    seq(g, "vector", lambda v: v.tensor_copy(
                        sQKV[:, 8:16, :].rearrange("p a b -> p (a b)"), pQ[:].rearrange("p a b -> p (a b)")))

            def kva(t, b=b, q=q):
                ii = None
                st = (q == 0)
                sp = (q == 3)
                for h in range(16):
                    ko = sQKV[:, h // 2, 64 * (h % 2):64 * (h % 2) + 64]
                    vo = sQKV[:, 8 + h // 2, 64 * (h % 2):64 * (h % 2) + 64]
                    t.matmul(pKV[:, h, :], ko, vo, start=st, stop=sp)
                    ii = t.matmul(pKS[:, h:h + 1], ko, C["sOnesB"][:], start=st, stop=sp)
                return ii
            g.add("tensor", kva)
            g.bar()
        seq(g, "vector", lambda v: v.tensor_copy(sKVo[:], pKV[:]))

        def kse(v):
            v.tensor_copy(sKSo[:, 0, :], pKS[:].rearrange("p (hp par) -> p hp par", par=2)[:, :, 0])
            return v.tensor_copy(sKSo[:, 1, :], pKS[:].rearrange("p (hp par) -> p hp par", par=2)[:, :, 1])
        g.add("vector", kse)
        g.bar()
        g.add("sync", lambda s, b=b: s.dma_start(out=kvp[b], in_=sKVo[:]), dma=True)
        g.add("sync", lambda s, b=b: s.dma_start(out=ksm[b], in_=sKSo[:]), dma=True)
        g.bar()
    return g.build()


def build_l1b2():
    g = G()
    xnf = g.io("xnf", [B, 8, 128, LC], BF16)
    kvp8 = g.io("kvp8", [8, B, 64, 16, 64])
    ksm8 = g.io("ksm8", [8, B, 64, 2, 8])
    WqT = g.io("WqT", [D, D])
    WpT = g.io("WpT", [D, D])
    bpcol = g.io("bpcol", [128, 8])
    sel2in = g.io("sel2in", [2, 128])
    xres = g.io("xres", [B, 8, 128, LC])
    xout = g.io("xout", [B, 8, 128, LC], out=True)

    sXn = g.sb("sXn", [128, 8, LC], BF16)
    sWq = g.sb("sWq", [128, 8, 8, 128], BF16)
    sWp = g.sb("sWp", [128, 8, 8, 128], BF16)
    sBpc = g.sb("sBpc", [128, 8])
    sKVc = g.sb("sKVc", [64, 2, B, 16, 64])
    sKV = g.sb("sKV", [64, B, 16, 64])
    sKVb = g.sb("sKVb", [64, B, 16, 64], BF16)
    sKSc = g.sb("sKSc", [64, 8, B, 2, 8])
    sKSum = g.sb("sKSum", [64, B, 2, 8])
    sKSb = g.sb("sKSb", [64, B, 2, 8], BF16)
    sKVz2 = g.sb("sKVz2", [128, B, 8, 128], BF16)
    sKS3 = g.sb("sKS3", [128, B, 2, 8], BF16)
    sSel2 = g.sb("sSel2", [2, 128])
    sQg = g.sb("sQg", [128, 8, LC], BF16)
    sZrP = g.sb("sZrP", [1, 2, LC])
    sZ2P = g.sb("sZ2P", [2, LC])
    sE1 = g.sb("sE1", [128, LC])
    sO = g.sb("sO", [128, 8, LC], BF16)
    sXr = g.sb("sXr", [128, 8, LC])
    sXo = g.sb("sXo", [128, 2, LC])
    pQ = g.ps("pQ", [128, 2, 512])
    pO = g.ps("pO", [128, 2, 512])

    C = common_bufs(g)
    pZ = C["pS1"]
    pZB = C["pBC"]
    g.add("gpsimd", lambda gp: gp.dma_start(
        out=sWq[:], in_=WqT.rearrange("(ec e) (fc f) -> e ec fc f", e=128, f=128)), dma=True, defer=8)
    g.add("gpsimd", lambda gp: gp.dma_start(
        out=sWp[:], in_=WpT.rearrange("(ec e) (fc f) -> e ec fc f", e=128, f=128)), dma=True, defer=12)
    g.add("sync", lambda s: s.dma_start(out=sBpc[:], in_=bpcol[:]), dma=True)
    g.add("sync", lambda s: s.dma_start(out=sKVc[:, 0], in_=kvp8[0].rearrange("b d h e -> d b h e")), dma=True)
    g.add("sync", lambda s: s.dma_start(out=sKSc[:], in_=ksm8.rearrange("c b d p h -> d c b p h")), dma=True)

    g.add("sync", lambda s: s.dma_start(out=sSel2[:], in_=sel2in[:]), dma=True)

    def sel(v):
        v.memset(sKVz2[:].rearrange("p a b c -> p (a b c)"), 0.0)
        return v.memset(sKS3[:].rearrange("p a b c -> p (a b c)"), 0.0)
    g.add("vector", sel)
    g.add("vector", lambda v: v.tensor_copy(sKV[:].rearrange("p a b c -> p (a b c)"),
                                            sKVc[:, 0].rearrange("p a b c -> p (a b c)")))
    g.add("sync", lambda s: s.dma_start(out=sKVc[:, 1], in_=kvp8[1].rearrange("b d h e -> d b h e")), dma=True)
    g.bar()
    for c in range(1, 8):
        if c < 7:
            g.add("sync", lambda s, c=c: s.dma_start(
                out=sKVc[:, c % 2], in_=kvp8[c + 1].rearrange("b d h e -> d b h e")), dma=True)
        g.add("vector", lambda v, c=c: v.tensor_tensor(
            sKV[:].rearrange("p a b c -> p (a b c)"), sKV[:].rearrange("p a b c -> p (a b c)"),
            sKVc[:, c % 2].rearrange("p a b c -> p (a b c)"), ALU.add))
        g.bar()
    seq(g, "vector", lambda v: v.tensor_copy(sKVb[:].rearrange("p a b c -> p (a b c)"),
                                             sKV[:].rearrange("p a b c -> p (a b c)")))
    seq(g, "vector", lambda v: v.tensor_copy(sKSum[:].rearrange("p a b c -> p (a b c)"),
                                             sKSc[:, 0].rearrange("p a b c -> p (a b c)")))
    for c in range(1, 8):
        seq(g, "vector", lambda v, c=c: v.tensor_tensor(
            sKSum[:].rearrange("p a b c -> p (a b c)"), sKSum[:].rearrange("p a b c -> p (a b c)"),
            sKSc[:, c].rearrange("p a b c -> p (a b c)"), ALU.add))
    seq(g, "vector", lambda v: v.tensor_copy(sKSb[:].rearrange("p a b c -> p (a b c)"),
                                             sKSum[:].rearrange("p a b c -> p (a b c)")))
    g.add("sync", lambda s: s.dma_start(
        out=sKVz2[0:64, :, :, 0:64],
        in_=sKVb[:].rearrange("p b (hp par) e -> p par b hp e", par=2)[:, 0]), dma=True)
    g.add("sync", lambda s: s.dma_start(
        out=sKVz2[64:128, :, :, 64:128],
        in_=sKVb[:].rearrange("p b (hp par) e -> p par b hp e", par=2)[:, 1]), dma=True)
    g.add("sync", lambda s: s.dma_start(out=sKS3[0:64, :, 0, :], in_=sKSb[:, :, 0, :]), dma=True)
    g.add("sync", lambda s: s.dma_start(out=sKS3[64:128, :, 1, :], in_=sKSb[:, :, 1, :]), dma=True)
    g.bar()

    for b in range(B):
        g.add("sync", lambda s, b=b: s.dma_start(out=sXn[:], in_=xnf[b].rearrange("c f l -> f c l")), dma=True)
        g.add("sync", lambda s, b=b: s.dma_start(out=sXr[:], in_=xres[b].rearrange("c f l -> f c l")), dma=True)
        g.bar()
        for fc in range(8):
            bank = fc % 2

            def qmm(t, fc=fc, bank=bank):
                ii = None
                for ec in range(8):
                    ii = t.matmul(pQ[:, bank], sWq[:, ec, fc], sXn[:, ec], start=(ec == 0), stop=(ec == 7))
                return ii
            g.add("tensor", qmm)
            g.bar()
            seq(g, "scalar", lambda sc, bank=bank: sc.activation(out=sXo[:, bank], in_=pQ[:, bank],
                                                                 func=AF.Square))
            seq(g, "scalar", lambda sc, fc=fc, bank=bank: sc.activation(out=sQg[:, fc], in_=sXo[:, bank],
                                                                        func=AF.Exp, scale=-0.5))
        for hp in range(8):
            bank = hp % 2
            for par in range(2):
                h = 2 * hp + par
                g.add("tensor", lambda t, b=b, h=h: t.matmul(
                    pZ[0:1, 0:LC], sKS3[:, b, h % 2, h // 2:h // 2 + 1],
                    sQg[:, h // 2, :], start=True, stop=True))
                g.bar()
                g.add("vector", lambda v, par=par: v.tensor_copy(sZrP[0:1, par, :], pZ[0:1, 0:LC]))
                g.bar()
            g.add("vector", lambda v: v.tensor_scalar(sZrP[:].rearrange("p a b -> p (a b)"),
                                                      sZrP[:].rearrange("p a b -> p (a b)"),
                                                      1e-6, None, ALU.add))
            g.bar()
            g.add("vector", lambda v: v.reciprocal(sZrP[:].rearrange("p a b -> p (a b)"),
                                                   sZrP[:].rearrange("p a b -> p (a b)")))
            g.bar()
            g.add("sync", lambda s: s.dma_start(out=sZ2P[:], in_=sZrP[0:1]), dma=True)
            g.bar()

            def hmm(t, b=b, hp=hp, bank=bank):
                t.matmul(pQ[:, bank], sKVz2[:, b, hp, :], sQg[:, hp, :], start=True, stop=True)
                return t.matmul(pZB[:, 0:LC], sSel2[:], sZ2P[:], start=True, stop=True)
            g.add("tensor", hmm)
            g.bar()
            seq(g, "vector", lambda v, bank=bank: v.tensor_copy(sE1[:], pQ[:, bank]))
            seq(g, "vector", lambda v, hp=hp: v.tensor_tensor(sO[:, hp], sE1[:], pZB[:, 0:LC], ALU.mult))
        for fo in range(8):
            bank = fo % 2

            def omm(t, fo=fo, bank=bank):
                ii = None
                for ec in range(8):
                    ii = t.matmul(pO[:, bank], sWp[:, ec, fo], sO[:, ec], start=(ec == 0), stop=(ec == 7))
                return ii
            g.add("tensor", omm)
            g.bar()
            seq(g, "vector", lambda v, fo=fo, bank=bank: v.scalar_tensor_tensor(
                sXo[:, bank], sXr[:, fo], sBpc[:, fo:fo + 1], pO[:, bank], ALU.add, ALU.add))
            seq(g, "sync", lambda s, b=b, fo=fo, bank=bank: s.dma_start(
                out=xout[b, fo], in_=sXo[:, bank]), dma=True)
    return g.build()


def run_attnb1(x_fm, p, builders):
    pre = "L1_attn_b_"
    WqkvT = np.ascontiguousarray(p[pre + "W_qkv"].T)
    if "l1b1" not in builders:
        builders["l1b1"] = build_l1b1()
    ins = []
    for c in range(NCORES):
        ins.append({
            "xsl": f32c(x_fm[:, :, :, LC * c:LC * (c + 1)]),
            "WkvT": f32c(WqkvT[:, D:3 * D]),
        })
    r1 = run_bass_kernel_spmd(builders["l1b1"], ins, core_ids=list(range(NCORES)))
    kvp8 = np.stack([r1.results[c]["kvp"] for c in range(NCORES)], axis=0)
    ksm8 = np.stack([r1.results[c]["ksm"] for c in range(NCORES)], axis=0)
    xnf = [r1.results[c]["xnf"] for c in range(NCORES)]

    if "l1b2" not in builders:
        builders["l1b2"] = build_l1b2()
    ins = []
    for c in range(NCORES):
        sl = slice(LC * c, LC * (c + 1))
        ins.append({
            "xnf": xnf[c],
            "kvp8": kvp8,
            "ksm8": ksm8,
            "WqT": f32c(WqkvT[:, 0:D]),
            "WpT": f32c(np.ascontiguousarray(p[pre + "W_proj"].T)),
            "bpcol": f32c(p[pre + "b_proj"].reshape(8, 128).T),
            "sel2in": np.kron(np.eye(2), np.ones((1, 64))).astype(np.float32),
            "xres": f32c(x_fm[:, :, :, sl]),
        })
    r2 = run_bass_kernel_spmd(builders["l1b2"], ins, core_ids=list(range(NCORES)))
    return np.concatenate([r2.results[c]["xout"] for c in range(NCORES)], axis=3)




WREL1 = 1536

PADL = 640
CC = PADL + 20
VPAST1 = 16
VFUT1 = 1320


def _slab_window_l1(s):
    return max(0, 4 * s - 1), min(31, 4 * s + 4)


def make_relT_l1(rel_k):
    pu = np.arange(WREL1)
    idx = np.clip(pu - PADL, 0, 40)
    rel_ext = rel_k[idx]
    return np.ascontiguousarray(rel_ext.T)


def build_attn_core_l1():
    g = G()
    q8 = g.io("q8", [BHC, HD, L], BF16)
    kf = g.io("kf", [BHC, HD, L], BF16)
    vt = g.io("vt", [BHC, L, HD], BF16)
    relT = g.io("relT", [HD, WREL1])
    relv = g.io("relv", [41, HD])
    relv40 = g.io("relv40", [1, HD])
    nrm8 = g.io("nrm8", [BHC, 2, NCORES])
    ident = g.io("ident", [128, 128])
    ao = g.io("ao", [BHC, HD, L], BF16, out=True)

    sQ8 = g.sb("sQ8", [64, L], BF16)
    sKf = g.sb("sKf", [64, L], BF16)
    sVa = g.sb("sVa", [128, 32, 65], BF16)
    sRel = g.sb("sRel", [64, WREL1], BF16)
    sRelV = g.sb("sRelV", [41, 64], BF16)
    sRelV40 = g.sb("sRelV40", [1, 64], BF16)
    sRelSq = g.sb("sRelSq", [64, 512], BF16)
    sP = g.sb("sP", [128, 2, WREL1], BF16)
    sSkew = g.sb("sSkew", [128, 4, 768], BF16)
    sAT = g.sb("sAT", [128, 4, 512], BF16)
    sATB = g.sb("sATB", [128, 6, 512], BF16)
    sTR4 = g.sb("sTR4", [128, 4, 384], BF16)
    sMsk4 = g.sb("sMsk4", [128, 4, 384])
    sMk3P4 = g.sb("sMk3P4", [128, 4, 384])
    sMk3F4 = g.sb("sMk3F4", [128, 4, 384])
    sMk3P = g.sb("sMk3P", [128, 384])
    sMk3F = g.sb("sMk3F", [128, 384])
    sIo3 = g.sb("sIo3", [128, 384], I32)
    sIo3F = g.sb("sIo3F", [128, 384])
    sColI = g.sb("sColI", [128, 1], I32)
    sColF = g.sb("sColF", [128, 1])
    sThr = g.sb("sThr", [128, 2])
    sExtrP = g.sb("sExtrP", [128, 4])
    sExtrF = g.sb("sExtrF", [128, 4])
    sAband4 = g.sb("sAband4", [128, 4, 41], BF16)
    sA41T = g.sb("sA41T", [41, 4, 128], BF16)
    sCS = g.sb("sCS", [1, 6, 512])
    sPFf = g.sb("sPFf", [1, 2, 512])
    sPFb = g.sb("sPFb", [1, 2, 512], BF16)
    sEdge = g.sb("sEdge", [128, 4, 2], BF16)
    sERow = g.sb("sERow", [1, 2, 512], BF16)
    sIdentB = g.sb("sIdentB", [128, 128], BF16)
    sNrm8 = g.sb("sNrm8", [BHC, 2, NCORES])
    sMQK = g.sb("sMQK", [BHC, 6])
    sMcolB = g.sb("sMcolB", [BHC, 1], BF16)
    sMrowB = g.sb("sMrowB", [1, BHC], BF16)
    sMB128 = g.sb("sMB128", [128, BHC])
    sRn = g.sb("sRn", [1, WREL1])
    sRnMax = g.sb("sRnMax", [1, 2])
    sComb = g.sb("sComb", [65, 512])
    sFar2 = g.sb("sFar2", [65, 2, 512])
    sFarRb = g.sb("sFarRb", [65, 2, 512], BF16)
    sFarR0 = g.sb("sFarR0", [1, 2, 512], BF16)
    sBCf = g.sb("sBCf", [65, 512])
    sDrowB = g.sb("sDrowB", [1, 512], BF16)
    sDrow = g.sb("sDrow", [1, 512])
    sDB = g.sb("sDB", [64, 512])
    sAO = g.sb("sAO", [64, 512], BF16)
    sOnesRow = g.sb("sOnesRow", [1, 128])
    sOnesRowB = g.sb("sOnesRowB", [1, 128], BF16)
    sOnes64B = g.sb("sOnes64B", [64, 1], BF16)
    sOnes128B = g.sb("sOnes128B", [128, 1], BF16)

    pSc4 = g.ps("pSc4", [128, 4, 512])
    pPV = g.ps("pPV", [65, 3, 512])
    pBC = g.ps("pBC", [128, 512])

    g.add("gpsimd", lambda gp: gp.dma_start(out=sRel[:], in_=relT[:]), dma=True)
    g.add("gpsimd", lambda gp: gp.dma_start(out=sRelV[:], in_=relv[:]), dma=True)
    g.add("gpsimd", lambda gp: gp.dma_start(out=sRelV40[:], in_=relv40[:]), dma=True)
    g.add("gpsimd", lambda gp: gp.dma_start(out=sIdentB[:], in_=ident[:]), dma=True)
    g.add("sync", lambda s: s.dma_start(out=sNrm8[:], in_=nrm8[:]), dma=True)
    g.add("gpsimd", lambda gp: gp.iota(sIo3[:], pattern=[[1, 384]], base=0, channel_multiplier=0))
    g.add("gpsimd", lambda gp: gp.iota(sColI[:], pattern=[[0, 1]], base=0, channel_multiplier=1))

    def cst(v):
        v.memset(sOnesRow[:], 1.0)
        v.memset(sOnesRowB[:], 1.0)
        v.memset(sOnes64B[:], 1.0)
        v.memset(sOnes128B[:], 1.0)
        return v.memset(sVa[:, :, 64:65], 1.0)
    g.add("vector", cst)
    g.bar()
    seq(g, "vector", lambda v: v.tensor_copy(sIo3F[:], sIo3[:]))
    seq(g, "vector", lambda v: v.tensor_copy(sColF[:], sColI[:]))
    seq(g, "vector", lambda v: v.tensor_scalar(sThr[:, 0:1], sColF[:], 107.5, None, ALU.add))
    seq(g, "vector", lambda v: v.tensor_scalar(sThr[:, 1:2], sColF[:], 148.5, None, ALU.add))
    seq(g, "vector", lambda v: v.tensor_scalar(sMk3P[:], sIo3F[:], sThr[:, 0:1], None, ALU.is_lt))
    seq(g, "vector", lambda v: v.tensor_scalar(sMk3F[:], sIo3F[:], sThr[:, 1:2], None, ALU.is_gt))
    def mk4(v):
        ii = None
        for qt in range(4):
            v.tensor_copy(sMk3P4[:, qt], sMk3P[:])
            ii = v.tensor_copy(sMk3F4[:, qt], sMk3F[:])
        return ii
    g.add("vector", mk4)
    g.bar()

    for k in range(WREL1 // 512):
        seq(g, "scalar", lambda sc, k=k: sc.activation(out=sRelSq[:], in_=sRel[:, 512 * k:512 * (k + 1)],
                                                       func=AF.Square))
        seq(g, "tensor", lambda t: t.matmul(pSc4[0:1, 0, :], sOnes64B[:], sRelSq[:], start=True, stop=True))
        seq(g, "vector", lambda v, k=k: v.tensor_copy(sRn[:, 512 * k:512 * (k + 1)], pSc4[0:1, 0, :]))
    g.add("vector", lambda v: v.reduce_max(sRnMax[:, 0:1], sRn[:], AX.X))
    g.bar()
    g.add("scalar", lambda sc: sc.activation(out=sRnMax[:, 1:2], in_=sRnMax[:, 0:1], func=AF.Sqrt))
    g.bar()

    def mred(v):
        v.reduce_max(sMQK[:, 0:1], sNrm8[:, 0], AX.X)
        return v.reduce_max(sMQK[:, 1:2], sNrm8[:, 1], AX.X)
    g.add("vector", mred)
    g.bar()
    g.add("scalar", lambda sc: sc.activation(out=sMQK[:, 2:4], in_=sMQK[:, 0:2], func=AF.Sqrt))
    g.bar()
    g.add("tensor", lambda t: t.matmul(pBC[0:BHC, 0:1], sOnesRow[:, 0:BHC], sRnMax[:, 1:2],
                                       start=True, stop=True))
    g.bar()
    g.add("vector", lambda v: v.tensor_copy(sMQK[:, 4:5], pBC[0:BHC, 0:1]))
    g.bar()
    g.add("vector", lambda v: v.tensor_tensor(sMQK[:, 4:5], sMQK[:, 3:4], sMQK[:, 4:5], ALU.add))
    g.bar()
    g.add("vector", lambda v: v.tensor_tensor(sMQK[:, 5:6], sMQK[:, 2:3], sMQK[:, 4:5], ALU.mult))
    g.bar()
    g.add("vector", lambda v: v.tensor_scalar(sMcolB[:], sMQK[:, 5:6], -1.0, None, ALU.mult))
    g.bar()
    g.add("tensor", lambda t: t.matmul(pBC[0:1, 0:BHC], sMcolB[:], sIdentB[0:BHC, 0:BHC],
                                       start=True, stop=True))
    g.bar()
    g.add("vector", lambda v: v.tensor_copy(sMrowB[:], pBC[0:1, 0:BHC]))
    g.bar()
    g.add("tensor", lambda t: t.matmul(pBC[:, 0:BHC], sOnesRowB[:], sMrowB[:], start=True, stop=True))
    g.bar()
    g.add("vector", lambda v: v.tensor_copy(sMB128[:], pBC[:, 0:BHC]))
    g.bar()

    for bhi in range(BHC):
        g.add("sync", lambda s, bhi=bhi: s.dma_start(out=sQ8[:], in_=q8[bhi]), dma=True)
        g.add("sync", lambda s, bhi=bhi: s.dma_start(out=sKf[:], in_=kf[bhi]), dma=True)
        g.add("sync", lambda s, bhi=bhi: s.dma_start(
            out=sVa[:, :, 0:64], in_=vt[bhi].rearrange("(jc p) d -> p jc d", p=128)), dma=True)
        g.bar()

        for s_ in range(8):
            jclo, jchi = _slab_window_l1(s_)
            nch = jchi - jclo + 1
            width = nch * 128
            has_past = jclo > 0
            has_fut = jchi < 31

            def eext(v, qt):
                v.tensor_copy(sEdge[:, qt, 0:1], sP[:, qt % 2, VPAST1:VPAST1 + 1])
                return v.tensor_copy(sEdge[:, qt, 1:2], sP[:, qt % 2, VFUT1:VFUT1 + 1])

            def skdma(s, qt, s_, jclo, width):
                S0 = (qt % 2) * WREL1 + CC + 128 * jclo - 512 * s_ - 128 * qt
                src = AP(sP, S0, [[2 * WREL1 - 1, 128], [1, width]])
                return s.dma_start(out=sSkew[:, qt, 0:width], in_=src)

            NKR = WREL1 // 512
            for qt in range(5):
                for k in range((NKR + 1) if qt < 4 else 1):
                    if k < NKR and qt < 4:
                        def pmm(t, qt=qt, k=k, s_=s_):
                            return t.matmul(pSc4[:, k % 2],
                                            sQ8[:, 512 * s_ + 128 * qt: 512 * s_ + 128 * (qt + 1)],
                                            sRel[:, 512 * k:512 * (k + 1)], start=True, stop=True)
                        g.add("tensor", pmm)
                    if k >= 1 and qt < 4:
                        g.add("scalar", lambda sc, qt=qt, k=k: sc.activation(
                            out=sP[:, qt % 2, 512 * (k - 1):512 * k], in_=pSc4[:, (k - 1) % 2],
                            func=AF.Copy))
                    if k == 0 and qt >= 1:
                        g.add("vector", lambda v, qt=qt: eext(v, qt - 1))
                        g.add("sync",
                              lambda s, qt=qt, s_=s_, jclo=jclo, width=width: skdma(
                                  s, qt - 1, s_, jclo, width), dma=True,
                              defer=4 if qt < 4 else 0)
                    g.bar()

            cls_of = {}
            for jc in range(32):
                cls_of[jc] = 0 if jclo <= jc <= jchi else (1 if jc < jclo else 2)
            firsts = {c: min((j for j in range(32) if cls_of[j] == c), default=None) for c in range(3)}
            lasts = {c: max((j for j in range(32) if cls_of[j] == c), default=None) for c in range(3)}

            def content2(t, p0, s_=s_, jclo=jclo, cls_of=cls_of):
                ii = None
                for jc in (2 * p0, 2 * p0 + 1):
                    band = cls_of[jc] == 0
                    ii = t.matmul(pSc4[:, jc % 4], sKf[:, 128 * jc:128 * (jc + 1)],
                                  sQ8[:, 512 * s_:512 * (s_ + 1)], start=True, stop=not band)
                    if band:
                        for qt in range(4):
                            ii = t.matmul(pSc4[:, jc % 4, 128 * qt:128 * (qt + 1)],
                                          sSkew[:, qt, (jc - jclo) * 128:(jc - jclo + 1) * 128],
                                          sIdentB[:], start=False, stop=(qt == 3))
                return ii

            def exp2(sc, p0, bhi=bhi, jclo=jclo, cls_of=cls_of):
                jc0, jc1 = 2 * p0, 2 * p0 + 1
                base = 2 * (p0 % 2)
                tg0 = sATB[:, jc0 - jclo, :] if cls_of[jc0] == 0 else sAT[:, base, :]
                tg1 = sATB[:, jc1 - jclo, :] if cls_of[jc1] == 0 else sAT[:, base + 1, :]
                if cls_of[jc0] == 0 and cls_of[jc1] == 0:
                    return sc.activation(out=sATB[:, jc0 - jclo:jc0 - jclo + 2, :],
                                         in_=pSc4[:, (2 * p0) % 4:(2 * p0) % 4 + 2, :],
                                         func=AF.Exp, bias=sMB128[:, bhi:bhi + 1])
                if cls_of[jc0] != 0 and cls_of[jc1] != 0:
                    return sc.activation(out=sAT[:, base:base + 2, :],
                                         in_=pSc4[:, (2 * p0) % 4:(2 * p0) % 4 + 2, :],
                                         func=AF.Exp, bias=sMB128[:, bhi:bhi + 1])
                sc.activation(out=tg0, in_=pSc4[:, (2 * p0) % 4, :],
                              func=AF.Exp, bias=sMB128[:, bhi:bhi + 1])
                return sc.activation(out=tg1, in_=pSc4[:, (2 * p0) % 4 + 1, :],
                                     func=AF.Exp, bias=sMB128[:, bhi:bhi + 1])

            def pv2(t, p0, jclo=jclo, cls_of=cls_of, firsts=firsts, lasts=lasts):
                ii = None
                base = 2 * (p0 % 2)
                for k, jc in enumerate((2 * p0, 2 * p0 + 1)):
                    c = cls_of[jc]
                    srcb = sATB[:, jc - jclo, :] if c == 0 else sAT[:, base + k, :]
                    ii = t.matmul(pPV[:, c], sVa[:, jc], srcb,
                                  start=(jc == firsts[c]), stop=(jc == lasts[c]))
                return ii

            for st in range(18):
                if st < 16:
                    g.add("tensor", lambda t, st=st, f=content2: f(t, st))
                if 1 <= st <= 16:
                    g.add("scalar", lambda sc, st=st, f=exp2: f(sc, st - 1))
                if st >= 2:
                    g.add("tensor", lambda t, st=st, f=pv2: f(t, st - 2))
                g.bar()

            psl = [pSc4[:, i] for i in range(4)]
            psl1 = [pSc4[0:1, i, :] for i in range(4)]
            psl41 = [pSc4[0:41, i, 0:128] for i in range(4)]
            def csmm1(t, nch=nch):
                ii = None
                for jcw in range(min(4, nch)):
                    ii = t.matmul(psl1[jcw], sOnes128B[:], sATB[:, jcw], start=True, stop=True)
                return ii
            g.add("tensor", csmm1)
            g.bar()

            def csev1(v, nch=nch):
                ii = None
                for jcw in range(min(4, nch)):
                    ii = v.tensor_copy(sCS[0:1, jcw], psl1[jcw])
                return ii
            g.add("vector", csev1)
            g.bar()
            if nch > 4:
                def csmm2(t, nch=nch):
                    ii = None
                    for jcw in range(4, nch):
                        ii = t.matmul(psl1[jcw - 4], sOnes128B[:], sATB[:, jcw], start=True, stop=True)
                    return ii
                g.add("tensor", csmm2)
                g.bar()
            if nch > 4:
                def csev2(v, nch=nch):
                    ii = None
                    for jcw in range(4, nch):
                        ii = v.tensor_copy(sCS[0:1, jcw], psl1[jcw - 4])
                    return ii
                g.add("vector", csev2)
            g.add("vector", lambda v: v.memset(sPFf[:].rearrange("p a b -> p (a b)"), 0.0))
            g.bar()
            layers = {0: [], 1: [], 2: []}
            for qt in range(4):
                c = 4 * s_ + qt
                pl = [(0, jc) for jc in range(jclo, c - 1)]
                fl = [(1, jc) for jc in range(c + 2, jchi + 1)]
                for r, (col, jc) in enumerate(pl):
                    layers[r].append((col, jc, qt))
                for r, (col, jc) in enumerate(fl):
                    layers[r].append((col, jc, qt))
            for r in range(3):
                if not layers[r]:
                    continue

                def lfn(v, items=tuple(layers[r]), jclo=jclo):
                    ii = None
                    for col, jc, qt in items:
                        ii = v.tensor_tensor(
                            sPFf[0:1, col, 128 * qt:128 * (qt + 1)],
                            sPFf[0:1, col, 128 * qt:128 * (qt + 1)],
                            sCS[0:1, jc - jclo, 128 * qt:128 * (qt + 1)], ALU.add)
                    return ii
                g.add("vector", lfn)
                g.bar()
            seq(g, "vector", lambda v: v.tensor_copy(sPFb[:].rearrange("p a b -> p (a b)"),
                                                     sPFf[:].rearrange("p a b -> p (a b)")))
            if s_ in (0, 7):
                seq(g, "vector", lambda v: v.memset(sTR4[:].rearrange("p a b -> p (a b)"), 0.0))

            def trmm(t, s_=s_, jclo=jclo):
                ii = None
                for qt in range(4):
                    c = 4 * s_ + qt
                    for dc, jc in enumerate([c - 1, c, c + 1]):
                        if jc < 0 or jc > 31:
                            continue
                        ii = t.matmul(psl[qt][:, 128 * dc:128 * (dc + 1)],
                                      sATB[:, jc - jclo, 128 * qt:128 * (qt + 1)],
                                      sIdentB[:], start=True, stop=True)
                return ii
            g.add("tensor", trmm)
            g.bar()

            def trev(v, s_=s_):
                ii = None
                for qt in range(4):
                    c = 4 * s_ + qt
                    for dc, jc in enumerate([c - 1, c, c + 1]):
                        if jc < 0 or jc > 31:
                            continue
                        ii = v.tensor_copy(sTR4[:, qt, 128 * dc:128 * (dc + 1)],
                                           psl[qt][:, 128 * dc:128 * (dc + 1)])
                return ii
            g.add("vector", trev)
            g.bar()

            def msks(v):
                v.tensor_tensor(sMsk4[:], sTR4[:], sMk3P4[:], ALU.mult)
                return v.reduce_sum(sExtrP[:], sMsk4[:], AX.X)
            g.add("vector", msks)
            g.bar()

            def msks2(v):
                v.tensor_tensor(sMsk4[:], sTR4[:], sMk3F4[:], ALU.mult)
                return v.reduce_sum(sExtrF[:], sMsk4[:], AX.X)
            g.add("vector", msks2)
            g.bar()
            for qt in range(4):
                g.add("sync", lambda s, qt=qt: s.dma_start(
                    out=sAband4[:, qt, :],
                    in_=AP(sTR4, 384 * qt + 108, [[4 * 384 + 1, 128], [1, 41]])), dma=True)
            g.bar()

            def merge(v):
                v.tensor_tensor(sAband4[:, :, 0], sAband4[:, :, 0], sExtrP[:].opt(), ALU.add)
                return v.tensor_tensor(sAband4[:, :, 40], sAband4[:, :, 40], sExtrF[:].opt(), ALU.add)
            g.add("vector", merge)
            g.bar()

            def a41mm(t):
                ii = None
                for qt in range(4):
                    ii = t.matmul(psl41[qt], sAband4[:, qt, :], sIdentB[:], start=True, stop=True)
                return ii
            g.add("tensor", a41mm)
            g.bar()

            def a41ev(v):
                ii = None
                for qt in range(4):
                    ii = v.tensor_copy(sA41T[:, qt, :], psl41[qt])
                return ii
            g.add("vector", a41ev)
            g.bar()

            def w2mm(t):
                ii = None
                for qt in range(4):
                    ii = t.matmul(pPV[0:64, 0, 128 * qt:128 * (qt + 1)], sRelV[:], sA41T[:, qt, :],
                                  start=False, stop=False)
                ii = t.matmul(pPV[0:64, 0, :], sRelV[0:1, :], sPFb[0:1, 0, :], start=False, stop=False)
                ii = t.matmul(pPV[0:64, 0, :], sRelV40[:], sPFb[0:1, 1, :], start=False, stop=True)
                return ii
            g.add("tensor", w2mm)
            g.bar()

            def etr(t, col, tgt):
                ii = None
                for qt in range(4):
                    ii = t.matmul(tgt[0:1, 128 * qt:128 * (qt + 1)], sEdge[:, qt, col:col + 1],
                                  sIdentB[:], start=(qt == 0), stop=(qt == 3))
                return ii

            if has_past:
                g.add("tensor", lambda t: etr(t, 0, pBC))
            else:
                g.add("vector", lambda v: v.memset(sFar2[:, 0, :], 0.0))
            if not has_fut:
                g.add("vector", lambda v: v.memset(sFar2[:, 1, :], 0.0))
            g.bar()
            if has_fut:
                g.add("tensor", lambda t: etr(t, 1, pSc4[:, 0]))
            if has_past:
                g.add("scalar", lambda sc: sc.activation(out=sERow[:, 0], in_=pBC[0:1, :], func=AF.Exp))
            g.bar()
            if has_past:
                g.add("tensor", lambda t: t.matmul(pBC[0:65, :], sOnesRowB[:, 0:65], sERow[:, 0],
                                                   start=True, stop=True))
            if has_fut:
                g.add("scalar", lambda sc: sc.activation(out=sERow[:, 1], in_=pSc4[0:1, 0, :], func=AF.Exp))
            g.bar()
            if has_fut:
                g.add("tensor", lambda t: t.matmul(pSc4[0:65, 0, :], sOnesRowB[:, 0:65], sERow[:, 1],
                                                   start=True, stop=True))
            if has_past:
                def fm0(v):
                    v.tensor_copy(sBCf[:], pBC[0:65, :])
                    return v.tensor_tensor(sFar2[:, 0, :], pPV[:, 1], sBCf[:], ALU.mult)
                g.add("vector", fm0)
            g.bar()
            def fm1(v, has_fut=has_fut):
                if has_fut:
                    v.tensor_copy(sBCf[:], pSc4[0:65, 0, :])
                    v.tensor_tensor(sFar2[:, 1, :], pPV[:, 2], sBCf[:], ALU.mult)
                return v.tensor_copy(sFarRb[64:65, :, :].rearrange("p a b -> p (a b)"),
                                     sFar2[64:65, :, :].rearrange("p a b -> p (a b)"))
            g.add("vector", fm1)
            g.bar()
            g.add("sync", lambda s: s.dma_start(out=sFarR0[:], in_=sFarRb[64:65, :, :]), dma=True)
            g.bar()

            def w2far(t):
                t.matmul(pPV[0:64, 0, :], sRelV[0:1, :], sFarR0[0:1, 0, :], start=False, stop=False)
                return t.matmul(pPV[0:64, 0, :], sRelV40[:], sFarR0[0:1, 1, :], start=False, stop=True)
            g.add("tensor", w2far)
            g.bar()

            def comb(v):
                v.tensor_copy(sComb[:], pPV[:, 0])
                v.tensor_tensor(sComb[:], sComb[:], sFar2[:, 0, :], ALU.add)
                v.tensor_tensor(sComb[:], sComb[:], sFar2[:, 1, :], ALU.add)
                v.reciprocal(sDrow[:], sComb[64:65, :])
                return v.tensor_copy(sDrowB[:], sDrow[:])
            g.add("vector", comb)
            g.bar()
            g.add("tensor", lambda t: t.matmul(pBC[0:64, :], sOnesRowB[:, 0:64], sDrowB[:],
                                               start=True, stop=True))
            g.bar()

            def fin(v):
                v.tensor_copy(sDB[:], pBC[0:64, :])
                return v.tensor_tensor(sAO[:], sComb[0:64, :], sDB[:], ALU.mult)
            g.add("vector", fin)
            g.bar()
            g.add("sync", lambda s, bhi=bhi, s_=s_: s.dma_start(
                out=ao[bhi, :, 512 * s_:512 * (s_ + 1)], in_=sAO[:]), dma=True)
            g.bar()
    return g.build()


def run_attn_core_l1(q8f, kff, vtf, nrm, rel_k, rel_v, builders):
    if "a2l1" not in builders:
        builders["a2l1"] = build_attn_core_l1()
    relT = f32c(make_relT_l1(rel_k))
    ins = []
    for c in range(NCORES):
        sl = slice(BHC * c, BHC * (c + 1))
        ins.append({
            "q8": q8f[sl], "kf": kff[sl], "vt": vtf[sl],
            "relT": relT,
            "relv": f32c(rel_v),
            "relv40": f32c(rel_v[40:41]),
            "nrm8": f32c(np.transpose(nrm[:, :, sl], (2, 1, 0))),
            "ident": IDENT,
        })
    res = run_bass_kernel_spmd(builders["a2l1"], ins, core_ids=list(range(NCORES)))
    return np.concatenate([r["ao"] for r in res.results], axis=0)




def kernel(**inputs):
    p = {k: np.asarray(v) for k, v in inputs.items()}
    b = _BUILDERS
    x_fm = to_fm(f32c(p["L0_attn_a_x"]))
    q8, kf, vt, nrm = run_attn_prep(x_fm, p["L0_attn_a_Wq"], p["L0_attn_a_bq"],
                                    p["L0_attn_a_Wk"], p["L0_attn_a_bk"],
                                    p["L0_attn_a_Wv"], p["L0_attn_a_bv"], 0.125)
    ao = run_attn_core(q8, kf, vt, nrm, p["L0_attn_a_rel_table"])
    x_fm = run_attn_out(x_fm, ao, p["L0_attn_a_Wo"], p["L0_attn_a_bo"])
    x_fm = run_ffn0(x_fm, p["L0_ffn_Wr"], p["L0_ffn_br"], p["L0_ffn_W_up"],
                    p["L0_ffn_b_up"], p["L0_ffn_W_down"], p["L0_ffn_b_down"], b)
    x_fm = run_attnb(x_fm, p, b)
    x_fm = run_ffn0(x_fm, p["L0_ffn_Wr"], p["L0_ffn_br"], p["L0_ffn_W_up"],
                    p["L0_ffn_b_up"], p["L0_ffn_W_down"], p["L0_ffn_b_down"], b)
    q8, kf, vt, nrm = run_attn_prep(x_fm, p["L1_attn_a_Wq"], p["L1_attn_a_bq"],
                                    p["L1_attn_a_Wk"], p["L1_attn_a_bk"],
                                    p["L1_attn_a_Wv"], p["L1_attn_a_bv"], 0.125)
    ao = run_attn_core_l1(q8, kf, vt, nrm, p["L1_attn_a_rel_k"], p["L1_attn_a_rel_v"], b)
    x_fm = run_attn_out(x_fm, ao, p["L1_attn_a_Wo"], p["L1_attn_a_bo"])
    x_fm = run_ffn1(x_fm, p, b)
    x_fm = run_attnb1(x_fm, p, b)
    x_fm = run_ffn1(x_fm, p, b)
    return from_fm(x_fm)


# version 09
# speedup vs baseline: 1.2040x; 1.2040x over previous
import numpy as np
import ml_dtypes
from contextlib import ExitStack

import concourse.bass as bass
import concourse.mybir as mybir
from concourse.ap import AP
from concourse.bass import IndirectOffsetOnAxis
from concourse.bass_utils import run_bass_kernel_spmd

F32 = mybir.dt.float32
BF16 = mybir.dt.bfloat16
I32 = mybir.dt.int32
AF = mybir.ActivationFunctionType
ALU = mybir.AluOpType
AX = mybir.AxisListType
BF16NP = ml_dtypes.bfloat16

B, L, D, NCORES = 4, 4096, 1024, 8
LC = L // NCORES
NH, HD = 16, 64
NBH = B * NH

ENGINES = ["tensor", "vector", "scalar", "gpsimd", "sync"]


class G:

    def __init__(self, num_devices=None):
        self.nc = bass.Bass("TRN2", target_bir_lowering=False, debug=False,
                            num_devices=num_devices)
        self.stack = ExitStack()
        self.ctx = self.stack.__enter__()
        self._stage = {e: [] for e in ENGINES}
        self._stages = []

    def io(self, name, shape, dtype=F32, out=False):
        return self.nc.dram_tensor(name, list(shape), dtype,
                                   kind="ExternalOutput" if out else "ExternalInput").ap()

    def dram(self, name, shape, dtype=F32):
        return self.nc.dram_tensor(name, list(shape), dtype, kind="Internal").ap()

    def sb(self, name, shape, dtype=F32):
        return self.ctx.enter_context(self.nc.sbuf_tensor(name, list(shape), dtype))

    def ps(self, name, shape):
        return self.ctx.enter_context(self.nc.psum_tensor(name, list(shape), F32))

    def add(self, eng, fn, dma=False, defer=0):
        self._stage[eng].append((fn, dma, defer))

    def bar(self):
        if any(self._stage[e] for e in ENGINES):
            self._stages.append(self._stage)
            self._stage = {e: [] for e in ENGINES}

    def build(self):
        self.bar()
        sems = {e: self.ctx.enter_context(self.nc.semaphore("S_" + e)) for e in ENGINES}
        nst = len(self._stages)
        adds = {e: [0] * (nst + 1) for e in ENGINES}
        for s, st in enumerate(self._stages):
            for e in ENGINES:
                for fn, dma, defer in st[e]:
                    acct = min(s + defer, nst - 1)
                    adds[e][acct] += 16 if dma else 1
        totals = {e: [0] * (nst + 1) for e in ENGINES}
        for e in ENGINES:
            for s in range(nst):
                totals[e][s + 1] = totals[e][s] + adds[e][s]
        block = self.ctx.enter_context(self.nc.Block())
        for e in ENGINES:
            def stream(E, e=e):
                last_wait = {oe: 0 for oe in ENGINES}
                for s in range(nst):
                    ops = self._stages[s][e]
                    if not ops:
                        continue
                    for oe in ENGINES:
                        need = totals[oe][s]
                        if need > last_wait[oe]:
                            E.wait_ge(sems[oe], need)
                            last_wait[oe] = need
                    for fn, dma, defer in ops:
                        instr = fn(E)
                        assert instr is not None, f"op on {e} returned None at stage {s}"
                        instr.then_inc(sems[e], 16 if dma else 1)
                if e == "sync":
                    for oe in ENGINES:
                        E.wait_ge(sems[oe], totals[oe][nst])
            getattr(block, e)(stream)
        self.stack.__exit__(None, None, None)
        return self.nc


def bfc(x):
    return np.ascontiguousarray(np.asarray(x).astype(np.float32))


def f32c(x):
    return np.ascontiguousarray(np.asarray(x), dtype=np.float32)


IDENT = np.eye(128, dtype=np.float32)



_BUILDERS = {}


def emit_ln(g, sXb, sXn, C):
    sOnes32, sSq, sStat, sEps = C["sOnes32"], C["sSq"], C["sStat"], C["sEps"]
    sMB, sRB = C["sMB"], C["sRB"]
    pS1, pS2, pBC = C["pS1"], C["pS2"], C["pBC"]

    def s1(t):
        ii = None
        for fc in range(8):
            ii = t.matmul(pS1[:], sOnes32[:], sXb[:, fc], start=(fc == 0), stop=(fc == 7))
        return ii
    g.add("tensor", s1)
    g.bar()
    for grp in range(4):
        def sq(sc, grp=grp):
            sc.activation(out=sSq[:, 0], in_=sXb[:, 2 * grp], func=AF.Square)
            return sc.activation(out=sSq[:, 1], in_=sXb[:, 2 * grp + 1], func=AF.Square)
        g.add("scalar", sq)
        g.bar()

        def mm2(t, grp=grp):
            t.matmul(pS2[:], sOnes32[:], sSq[:, 0], start=(grp == 0), stop=False)
            return t.matmul(pS2[:], sOnes32[:], sSq[:, 1], start=False, stop=(grp == 3))
        g.add("tensor", mm2)
        g.bar()

    def mv(v):
        v.tensor_scalar(sStat[:, 0], pS1[:], 1.0 / D, None, ALU.mult)
        v.tensor_scalar(sStat[:, 1], pS2[:], 1.0 / D, None, ALU.mult)
        v.tensor_tensor(sStat[:, 2], sStat[:, 0], sStat[:, 0], ALU.mult)
        return v.tensor_tensor(sStat[:, 2], sStat[:, 1], sStat[:, 2], ALU.subtract)
    g.add("vector", mv)
    g.bar()

    def rstd(sc):
        sc.activation(out=sStat[:, 3], in_=sStat[:, 2], func=AF.Ln, bias=sEps[:])
        return sc.activation(out=sStat[:, 4], in_=sStat[:, 3], func=AF.Exp, scale=-0.5)
    g.add("scalar", rstd)
    g.bar()

    g.add("tensor", lambda t: t.matmul(pBC[:], C["sOnesRow"][:], sStat[:, 0], start=True, stop=True))
    g.bar()
    g.add("vector", lambda v: v.tensor_copy(sMB[:], pBC[:]))
    g.bar()
    g.add("tensor", lambda t: t.matmul(pBC[:], C["sOnesRow"][:], sStat[:, 4], start=True, stop=True))
    g.bar()
    g.add("vector", lambda v: v.tensor_copy(sRB[:], pBC[:]))
    g.bar()

    def norm(v):
        ii = None
        for fc in range(8):
            v.tensor_tensor(sSq[:, fc % 2], sXb[:, fc], sMB[:], ALU.subtract)
            ii = v.tensor_tensor(sXn[:, fc], sSq[:, fc % 2], sRB[:], ALU.mult)
        return ii
    g.add("vector", norm)
    g.bar()


def common_bufs(g, psum=True):
    C = {}
    C["sOnes32"] = g.sb("c_ones32", [128, 1])
    C["sOnesRow"] = g.sb("c_onesrow", [1, 128])
    C["sOnesB"] = g.sb("c_onesb", [128, 1], BF16)
    C["sSq"] = g.sb("c_sq", [128, 2, LC])
    C["sStat"] = g.sb("c_stat", [1, 8, LC])
    C["sEps"] = g.sb("c_eps", [1, 1])
    C["sMB"] = g.sb("c_mb", [128, LC])
    C["sRB"] = g.sb("c_rb", [128, LC])
    if psum:
        C["pS1"] = g.ps("c_ps1", [1, 512])
        C["pS2"] = g.ps("c_ps2", [1, 512])
        C["pBC"] = g.ps("c_pbc", [128, 512])

    def cst(v):
        v.memset(C["sOnes32"][:], 1.0)
        v.memset(C["sOnesRow"][:], 1.0)
        v.memset(C["sOnesB"][:], 1.0)
        return v.memset(C["sEps"][:], 1e-5)
    g.add("vector", cst)
    return C



def build_attn_prep(qscale):
    g = G()
    xsl = g.io("xsl", [B, 8, 128, LC])
    WqT = g.io("WqT", [D, D]); WkT = g.io("WkT", [D, D]); WvT = g.io("WvT", [D, D])
    bcol = g.io("bcol", [128, 3, 8])
    ident = g.io("ident", [128, 128])
    q8 = g.io("q8", [NBH, HD, LC], BF16, out=True)
    kf = g.io("kf", [NBH, HD, LC], BF16, out=True)
    vt = g.io("vt", [NBH, LC, HD], BF16, out=True)
    nrm = g.io("nrm", [1, 2, NBH], out=True)

    sX = g.sb("sX", [128, B, 8, LC])
    sXn = g.sb("sXn", [128, 8, LC], BF16)
    sW = g.sb("sW", [128, 3, 8, 8, 128], BF16)
    sBia = g.sb("sBia", [128, 3, 8])
    sOut = g.sb("sOut", [128, 2, LC], BF16)
    sVT = g.sb("sVT", [128, 512], BF16)
    sQsq = g.sb("sQsq", [128, 2, LC], BF16)
    sNrm = g.sb("sNrm", [1, 2, NBH])
    sIdent = g.sb("sIdent", [128, 128], BF16)

    pP = g.ps("pP", [128, 2, 512])
    pT = g.ps("pT", [128, 512])
    pN = g.ps("pN", [1, 1024])

    C = common_bufs(g)
    sOnes32, sOnesB = C["sOnes32"], C["sOnesB"]

    for w, W in enumerate([WqT, WkT, WvT]):
        def ld(gp, w=w, W=W):
            return gp.dma_start(out=sW[:, w], in_=W.rearrange("(ec e) (fc f) -> e ec fc f", e=128, f=128))
        g.add("gpsimd", ld, dma=True, defer=10)
    g.add("gpsimd", lambda gp: gp.dma_start(out=sBia[:], in_=bcol[:]), dma=True)
    g.add("gpsimd", lambda gp: gp.dma_start(out=sIdent[:], in_=ident[:]), dma=True)
    g.add("sync", lambda s: s.dma_start(out=sX[:], in_=xsl.rearrange("b c f l -> f b c l")), dma=True)
    g.add("vector", lambda v: v.memset(sNrm[:], 0.0))
    g.bar()

    for b in range(B):
        emit_ln(g, sX[:, b], sXn, C)
        slots = [(w, fc) for w in range(3) for fc in range(8)]
        NS = len(slots)

        def proj(t, i):
            w, fc = slots[i]
            ii = None
            for ec in range(8):
                ii = t.matmul(pP[:, i % 2], sW[:, w, ec, fc], sXn[:, ec],
                              start=(ec == 0), stop=(ec == 7))
            return ii

        def ev(v, i):
            w, fc = slots[i]
            if w == 0:
                return v.tensor_scalar(sOut[:, i % 2], pP[:, i % 2],
                                       sBia[:, 0, fc:fc + 1], qscale, ALU.add, ALU.mult)
            return v.tensor_scalar(sOut[:, i % 2], pP[:, i % 2],
                                   sBia[:, w, fc:fc + 1], None, ALU.add)

        def nmm(t, i):
            return (t.matmul(pN[:, 0:512], sOnesB[0:64], sQsq[0:64, i % 2, :], start=True, stop=True,
                             tile_position=(0, 0)),
                    t.matmul(pN[:, 512:1024], sOnesB[64:128], sQsq[64:128, i % 2, :], start=True,
                             stop=True, tile_position=(64, 0)))[1]

        def vtr(t, i):
            ii = None
            for q in range(4):
                ii = t.matmul(pT[:, q * 128:(q + 1) * 128],
                              sOut[:, i % 2, q * 128:(q + 1) * 128],
                              sIdent[:], start=(q == 0), stop=(q == 3))
            return ii

        for i in range(NS + 2):
            def stA(t, i=i, b=b):
                ii = None
                if i < NS:
                    ii = proj(t, i)
                if 0 <= i - 2 < NS and slots[i - 2][0] < 2:
                    ii = nmm(t, i - 2)
                return ii
            if i < NS or (0 <= i - 2 < NS and slots[i - 2][0] < 2):
                g.add("tensor", stA)

            def stAv(v, i=i):
                ii = None
                if 0 <= i - 1 < NS:
                    ii = ev(v, i - 1)
                if 0 <= i - 2 < NS and slots[i - 2][0] == 2:
                    ii = v.tensor_copy(sVT[:], pT[:])
                return ii
            if 0 <= i - 1 < NS or (0 <= i - 2 < NS and slots[i - 2][0] == 2):
                g.add("vector", stAv)
            g.bar()
            j = i - 1
            if 0 <= j < NS:
                w, fc = slots[j]
                h0 = fc * 2
                bh0 = b * NH + h0
                if w < 2:
                    g.add("scalar", lambda sc, j=j: sc.activation(
                        out=sQsq[:, j % 2, :], in_=sOut[:, j % 2], func=AF.Square))
                    tgt = q8 if w == 0 else kf
                    g.add("sync", lambda s, j=j, tgt=tgt, bh0=bh0: s.dma_start(
                        out=tgt[bh0:bh0 + 2].rearrange("t d l -> (t d) l"), in_=sOut[:, j % 2]), dma=True)
                else:
                    g.add("tensor", lambda t, j=j: vtr(t, j))
            j2 = i - 2
            if 0 <= j2 < NS:
                w2, fc2 = slots[j2]
                bh0b = b * NH + fc2 * 2
                if w2 < 2:
                    g.add("vector", lambda v, j2=j2, w2=w2, bh0b=bh0b: (
                        v.reduce_max(sNrm[:, w2, bh0b:bh0b + 1], pN[:, 0:512], AX.X),
                        v.reduce_max(sNrm[:, w2, bh0b + 1:bh0b + 2], pN[:, 512:1024], AX.X))[1])
                else:
                    g.add("sync", lambda s, bh0b=bh0b: s.dma_start(
                        out=vt[bh0b].rearrange("(q p) d -> p q d", p=128),
                        in_=sVT[:].rearrange("p (q d) -> p q d", d=128)[:, :, 0:64]), dma=True)
                    g.add("sync", lambda s, bh0b=bh0b: s.dma_start(
                        out=vt[bh0b + 1].rearrange("(q p) d -> p q d", p=128),
                        in_=sVT[:].rearrange("p (q d) -> p q d", d=128)[:, :, 64:128]), dma=True)
            g.bar()

    g.add("sync", lambda s: s.dma_start(out=nrm[:], in_=sNrm[:]), dma=True)
    g.bar()
    return g.build()



def to_fm(x):
    return np.ascontiguousarray(x.transpose(0, 2, 1).reshape(B, 8, 128, L))


def from_fm(x_fm):
    return np.ascontiguousarray(x_fm.reshape(B, D, L).transpose(0, 2, 1))


def run_attn_prep(x_fm, Wq, bq, Wk, bk, Wv, bv, qscale):
    key = f"a1_{qscale}"
    if key not in _BUILDERS:
        _BUILDERS[key] = build_attn_prep(qscale)
    nc = _BUILDERS[key]
    bcol = np.stack([b_.reshape(8, 128).T for b_ in (bq, bk, bv)], axis=1)
    ins = []
    for c in range(NCORES):
        ins.append({
            "xsl": f32c(x_fm[:, :, :, LC * c:LC * (c + 1)]),
            "WqT": f32c(Wq.T), "WkT": f32c(Wk.T), "WvT": f32c(Wv.T),
            "bcol": f32c(bcol), "ident": IDENT,
        })
    res = run_bass_kernel_spmd(nc, ins, core_ids=list(range(NCORES)))
    q8 = np.concatenate([r["q8"][None] for r in res.results], axis=0)
    kf = np.concatenate([r["kf"][None] for r in res.results], axis=0)
    vtt = np.concatenate([r["vt"][None] for r in res.results], axis=0)
    nrm = np.stack([r["nrm"][0] for r in res.results], axis=0)
    q8f = np.transpose(q8, (1, 2, 0, 3)).reshape(NBH, HD, L)
    kff = np.transpose(kf, (1, 2, 0, 3)).reshape(NBH, HD, L)
    vtf = np.transpose(vtt, (1, 0, 2, 3)).reshape(NBH, L, HD)
    return q8f, kff, vtf, nrm





PADA = 512
WRELA = 3072
VPAST = 513
VFUT = 2559
BHC = NBH // NCORES


def _slab_window(s):
    return max(0, 4 * s - 8), min(31, 4 * s + 11)


def build_attn_core():
    g = G()
    q8 = g.io("q8", [BHC, HD, L], BF16)
    kf = g.io("kf", [BHC, HD, L], BF16)
    vt = g.io("vt", [BHC, L, HD], BF16)
    relT = g.io("relT", [HD, WRELA])
    nrm8 = g.io("nrm8", [BHC, 2, NCORES])
    ident = g.io("ident", [128, 128])
    ao = g.io("ao", [BHC, HD, L], BF16, out=True)

    sQ8 = g.sb("sQ8", [64, L], BF16)
    sKf = g.sb("sKf", [64, L], BF16)
    sVa = g.sb("sVa", [128, 32, 65], BF16)
    sRel = g.sb("sRel", [64, WRELA], BF16)
    sRelSq = g.sb("sRelSq", [64, 512], BF16)
    sP = g.sb("sP", [128, 2, WRELA], BF16)
    sSkew = g.sb("sSkew", [128, 4, 2560], BF16)
    sAT = g.sb("sAT", [128, 4, 512], BF16)
    sEdge = g.sb("sEdge", [128, 4, 2], BF16)
    sERow = g.sb("sERow", [1, 2, 512], BF16)
    sIdentB = g.sb("sIdentB", [128, 128], BF16)
    sNrm8 = g.sb("sNrm8", [BHC, 2, NCORES])
    sMQK = g.sb("sMQK", [BHC, 6])
    sMcolB = g.sb("sMcolB", [BHC, 1], BF16)
    sMrowB = g.sb("sMrowB", [1, BHC], BF16)
    sMB128 = g.sb("sMB128", [128, BHC])
    sRn = g.sb("sRn", [1, WRELA])
    sRnMax = g.sb("sRnMax", [1, 2])
    sComb = g.sb("sComb", [65, 512])
    sFar = g.sb("sFar", [65, 512])
    sBCf = g.sb("sBCf", [65, 512])
    sDrowB = g.sb("sDrowB", [1, 512], BF16)
    sDrow = g.sb("sDrow", [1, 512])
    sDB = g.sb("sDB", [64, 512])
    sAO = g.sb("sAO", [64, 512], BF16)
    sOnesRow = g.sb("sOnesRow", [1, 128])
    sOnesRowB = g.sb("sOnesRowB", [1, 128], BF16)
    sOnes64B = g.sb("sOnes64B", [64, 1], BF16)

    pSc4 = g.ps("pSc4", [128, 4, 512])
    pPV = g.ps("pPV", [65, 3, 512])
    pBC = g.ps("pBC", [128, 512])

    g.add("gpsimd", lambda gp: gp.dma_start(out=sRel[:], in_=relT[:]), dma=True)
    g.add("gpsimd", lambda gp: gp.dma_start(out=sIdentB[:], in_=ident[:]), dma=True)
    g.add("sync", lambda s: s.dma_start(out=sNrm8[:], in_=nrm8[:]), dma=True)

    def cst(v):
        v.memset(sOnesRow[:], 1.0)
        v.memset(sOnesRowB[:], 1.0)
        v.memset(sOnes64B[:], 1.0)
        return v.memset(sVa[:, :, 64:65], 1.0)
    g.add("vector", cst)
    g.bar()

    for k in range(WRELA // 512):
        def rsq(sc, k=k):
            return sc.activation(out=sRelSq[:], in_=sRel[:, 512 * k:512 * (k + 1)], func=AF.Square)
        g.add("scalar", rsq)
        g.bar()

        def rnmm(t, k=k):
            return t.matmul(pSc4[0:1, 0, :], sOnes64B[:], sRelSq[:], start=True, stop=True)
        g.add("tensor", rnmm)
        g.bar()

        def rncp(v, k=k):
            return v.tensor_copy(sRn[:, 512 * k:512 * (k + 1)], pSc4[0:1, 0, :])
        g.add("vector", rncp)
        g.bar()
    g.add("vector", lambda v: v.reduce_max(sRnMax[:, 0:1], sRn[:], AX.X))
    g.bar()
    g.add("scalar", lambda sc: sc.activation(out=sRnMax[:, 1:2], in_=sRnMax[:, 0:1], func=AF.Sqrt))
    g.bar()

    def mred(v):
        v.reduce_max(sMQK[:, 0:1], sNrm8[:, 0], AX.X)
        return v.reduce_max(sMQK[:, 1:2], sNrm8[:, 1], AX.X)
    g.add("vector", mred)
    g.bar()
    g.add("scalar", lambda sc: sc.activation(out=sMQK[:, 2:4], in_=sMQK[:, 0:2], func=AF.Sqrt))
    g.bar()
    g.add("tensor", lambda t: t.matmul(pBC[0:BHC, 0:1], sOnesRow[:, 0:BHC], sRnMax[:, 1:2],
                                       start=True, stop=True))
    g.bar()

    g.add("vector", lambda v: v.tensor_copy(sMQK[:, 4:5], pBC[0:BHC, 0:1]))
    g.bar()
    g.add("vector", lambda v: v.tensor_tensor(sMQK[:, 4:5], sMQK[:, 3:4], sMQK[:, 4:5], ALU.add))
    g.bar()
    g.add("vector", lambda v: v.tensor_tensor(sMQK[:, 5:6], sMQK[:, 2:3], sMQK[:, 4:5], ALU.mult))
    g.bar()
    g.add("vector", lambda v: v.tensor_scalar(sMcolB[:], sMQK[:, 5:6], -1.0, None, ALU.mult))
    g.bar()
    g.add("tensor", lambda t: t.matmul(pBC[0:1, 0:BHC], sMcolB[:], sIdentB[0:BHC, 0:BHC],
                                       start=True, stop=True))
    g.bar()
    g.add("vector", lambda v: v.tensor_copy(sMrowB[:], pBC[0:1, 0:BHC]))
    g.bar()
    g.add("tensor", lambda t: t.matmul(pBC[:, 0:BHC], sOnesRowB[:], sMrowB[:], start=True, stop=True))
    g.bar()
    g.add("vector", lambda v: v.tensor_copy(sMB128[:], pBC[:, 0:BHC]))
    g.bar()

    for bhi in range(BHC):
        g.add("sync", lambda s, bhi=bhi: s.dma_start(out=sQ8[:], in_=q8[bhi]), dma=True)
        g.add("sync", lambda s, bhi=bhi: s.dma_start(out=sKf[:], in_=kf[bhi]), dma=True)
        g.add("sync", lambda s, bhi=bhi: s.dma_start(
            out=sVa[:, :, 0:64], in_=vt[bhi].rearrange("(jc p) d -> p jc d", p=128)), dma=True)
        g.bar()

        for s_ in range(8):
            jclo, jchi = _slab_window(s_)
            nch = jchi - jclo + 1
            width = nch * 128
            has_past = jclo > 0
            has_fut = jchi < 31

            def eext(v, qt):
                v.tensor_copy(sEdge[:, qt, 0:1], sP[:, qt % 2, VPAST:VPAST + 1])
                return v.tensor_copy(sEdge[:, qt, 1:2], sP[:, qt % 2, VFUT:VFUT + 1])

            def skdma(s, qt, s_, jclo, width):
                S0 = (qt % 2) * WRELA + 1536 + jclo * 128 - 512 * s_ - 128 * qt
                src = AP(sP, S0, [[2 * WRELA - 1, 128], [1, width]])
                return s.dma_start(out=sSkew[:, qt, 0:width], in_=src)

            for qt in range(5):
                for k in range(7 if qt < 4 else 1):
                    if k < 6 and qt < 4:
                        def pmm(t, qt=qt, k=k, s_=s_):
                            return t.matmul(pSc4[:, k % 2],
                                            sQ8[:, 512 * s_ + 128 * qt: 512 * s_ + 128 * (qt + 1)],
                                            sRel[:, 512 * k:512 * (k + 1)], start=True, stop=True)
                        g.add("tensor", pmm)
                    if k >= 1 and qt < 4:
                        g.add("scalar", lambda sc, qt=qt, k=k: sc.activation(
                            out=sP[:, qt % 2, 512 * (k - 1):512 * k], in_=pSc4[:, (k - 1) % 2],
                            func=AF.Copy))
                    if k == 0 and qt >= 1:
                        g.add("vector", lambda v, qt=qt: eext(v, qt - 1))
                        g.add("sync",
                              lambda s, qt=qt, s_=s_, jclo=jclo, width=width: skdma(
                                  s, qt - 1, s_, jclo, width), dma=True,
                              defer=6 if qt < 4 else 0)
                    g.bar()

            cls_of = {}
            for jc in range(32):
                cls_of[jc] = 0 if jclo <= jc <= jchi else (1 if jc < jclo else 2)
            firsts = {c: min((j for j in range(32) if cls_of[j] == c), default=None) for c in range(3)}
            lasts = {c: max((j for j in range(32) if cls_of[j] == c), default=None) for c in range(3)}

            farp = [p for p in range(16) if cls_of[2 * p] != 0 and cls_of[2 * p + 1] != 0]
            rest = [p for p in range(16) if p not in farp]
            order = list(range(16))
            exjc = [jc for p in order for jc in (2 * p, 2 * p + 1)]
            efirst = {c: next((jc for jc in exjc if cls_of[jc] == c), None) for c in range(3)}
            elast = {c: next((jc for jc in reversed(exjc) if cls_of[jc] == c), None) for c in range(3)}

            def content2(t, k, s_=s_, jclo=jclo, cls_of=cls_of, order=order):
                ii = None
                for i, jc in enumerate((2 * order[k], 2 * order[k] + 1)):
                    band = cls_of[jc] == 0
                    bk = 2 * (k % 2) + i
                    ii = t.matmul(pSc4[:, bk], sKf[:, 128 * jc:128 * (jc + 1)],
                                  sQ8[:, 512 * s_:512 * (s_ + 1)], start=True, stop=not band)
                    if band:
                        for qt in range(4):
                            ii = t.matmul(pSc4[:, bk, 128 * qt:128 * (qt + 1)],
                                          sSkew[:, qt, (jc - jclo) * 128:(jc - jclo + 1) * 128],
                                          sIdentB[:], start=False, stop=(qt == 3))
                return ii

            def exp2(sc, k, bhi=bhi):
                base = 2 * (k % 2)
                return sc.activation(out=sAT[:, base:base + 2, :],
                                     in_=pSc4[:, base:base + 2, :],
                                     func=AF.Exp, bias=sMB128[:, bhi:bhi + 1])

            def pv2(t, k, cls_of=cls_of, order=order, efirst=efirst, elast=elast):
                ii = None
                base = 2 * (k % 2)
                for i, jc in enumerate((2 * order[k], 2 * order[k] + 1)):
                    c = cls_of[jc]
                    ii = t.matmul(pPV[:, c], sVa[:, jc], sAT[:, base + i, :],
                                  start=(jc == efirst[c]), stop=(jc == elast[c]))
                return ii

            for st in range(18):
                if st < 16:
                    g.add("tensor", lambda t, st=st, f=content2: f(t, st))
                if 1 <= st <= 16:
                    g.add("scalar", lambda sc, st=st, f=exp2: f(sc, st - 1))
                if st >= 2:
                    g.add("tensor", lambda t, st=st, f=pv2: f(t, st - 2))
                g.bar()

            def etr(t, col, tgt):
                ii = None
                for qt in range(4):
                    ii = t.matmul(tgt[0:1, 128 * qt:128 * (qt + 1)], sEdge[:, qt, col:col + 1],
                                  sIdentB[:], start=(qt == 0), stop=(qt == 3))
                return ii

            if has_past:
                g.add("tensor", lambda t: etr(t, 0, pBC))
            g.add("vector", lambda v: v.tensor_copy(sComb[:], pPV[:, 0]))
            g.bar()
            if has_fut:
                g.add("tensor", lambda t: etr(t, 1, pSc4[:, 0, :]))
            if has_past:
                g.add("scalar", lambda sc: sc.activation(out=sERow[:, 0], in_=pBC[0:1, :], func=AF.Exp))
            g.bar()
            if has_past:
                g.add("tensor", lambda t: t.matmul(pBC[0:65, :], sOnesRowB[:, 0:65], sERow[:, 0],
                                                   start=True, stop=True))
            if has_fut:
                g.add("scalar", lambda sc: sc.activation(out=sERow[:, 1], in_=pSc4[0:1, 0, :], func=AF.Exp))
            g.bar()
            if has_fut:
                g.add("tensor", lambda t: t.matmul(pSc4[0:65, 0, :], sOnesRowB[:, 0:65], sERow[:, 1],
                                                   start=True, stop=True))
            if has_past:
                def fm0(v):
                    v.tensor_copy(sBCf[:], pBC[0:65, :])
                    v.tensor_tensor(sFar[:], pPV[:, 1], sBCf[:], ALU.mult)
                    return v.tensor_tensor(sComb[:], sComb[:], sFar[:], ALU.add)
                g.add("vector", fm0)
            g.bar()
            def fm1(v, has_fut=has_fut):
                if has_fut:
                    v.tensor_copy(sBCf[:], pSc4[0:65, 0, :])
                    v.tensor_tensor(sFar[:], pPV[:, 2], sBCf[:], ALU.mult)
                    v.tensor_tensor(sComb[:], sComb[:], sFar[:], ALU.add)
                v.reciprocal(sDrow[:], sComb[64:65, :])
                return v.tensor_copy(sDrowB[:], sDrow[:])
            g.add("vector", fm1)
            g.bar()
            g.add("tensor", lambda t: t.matmul(pBC[0:64, :], sOnesRowB[:, 0:64], sDrowB[:],
                                               start=True, stop=True))
            g.bar()

            def fin(v):
                v.tensor_copy(sDB[:], pBC[0:64, :])
                return v.tensor_tensor(sAO[:], sComb[0:64, :], sDB[:], ALU.mult)
            g.add("vector", fin)
            g.bar()
            g.add("sync", lambda s, bhi=bhi, s_=s_: s.dma_start(
                out=ao[bhi, :, 512 * s_:512 * (s_ + 1)], in_=sAO[:]), dma=True)
            g.bar()

    return g.build()


def build_attn_out():
    g = G()
    xsl = g.io("xsl", [B, 8, 128, LC])
    aosl = g.io("aosl", [NBH, HD, LC], BF16)
    WoT = g.io("WoT", [D, D])
    bocol = g.io("bocol", [128, 8])
    xout = g.io("xout", [B, 8, 128, LC], out=True)

    sX = g.sb("sX", [128, B, 8, LC])
    sAo = g.sb("sAo", [64, NBH, LC], BF16)
    sWo = g.sb("sWo", [64, 16, 8, 128], BF16)
    sBo = g.sb("sBo", [128, 8])
    sY = g.sb("sY", [128, 2, LC])
    pY = g.ps("pY", [128, 2, 512])

    g.add("sync", lambda s: s.dma_start(out=sX[:], in_=xsl.rearrange("b c f l -> f b c l")), dma=True)
    g.add("sync", lambda s: s.dma_start(out=sAo[:], in_=aosl.rearrange("t d l -> d t l")), dma=True)
    g.add("gpsimd", lambda gp: gp.dma_start(
        out=sWo[:], in_=WoT.rearrange("(h e) (fc f) -> e h fc f", e=64, f=128)), dma=True)
    g.add("gpsimd", lambda gp: gp.dma_start(out=sBo[:], in_=bocol[:]), dma=True)
    g.bar()

    for b in range(B):
        for fc in range(8):
            bank = fc % 2

            def ymm(t, b=b, fc=fc, bank=bank):
                ii = None
                for h in range(16):
                    ii = t.matmul(pY[:, bank], sWo[:, h, fc], sAo[:, b * 16 + h],
                                  start=(h == 0), stop=(h == 15))
                return ii
            g.add("tensor", ymm)
            g.bar()

            def yev(v, b=b, fc=fc, bank=bank):
                return v.scalar_tensor_tensor(sY[:, bank], pY[:, bank], sBo[:, fc:fc + 1],
                                              sX[:, b, fc], ALU.add, ALU.add)
            g.add("vector", yev)
            g.bar()
            g.add("sync", lambda s, b=b, fc=fc, bank=bank: s.dma_start(
                out=xout[b, fc], in_=sY[:, bank]), dma=True)
    g.bar()
    return g.build()


def make_relT_l0(rel_table):
    pu = np.arange(WRELA)
    idx = np.clip(pu - PADA, 0, 2046)
    rel_ext = rel_table[idx]
    return np.ascontiguousarray(rel_ext[::-1].T)


def run_attn_core(q8f, kff, vtf, nrm, rel_table):
    if "a2l0" not in _BUILDERS:
        _BUILDERS["a2l0"] = build_attn_core()
    nc = _BUILDERS["a2l0"]
    relT = f32c(make_relT_l0(rel_table))
    ins = []
    for c in range(NCORES):
        sl = slice(BHC * c, BHC * (c + 1))
        ins.append({
            "q8": q8f[sl], "kf": kff[sl], "vt": vtf[sl],
            "relT": relT,
            "nrm8": f32c(np.transpose(nrm[:, :, sl], (2, 1, 0))),
            "ident": IDENT,
        })
    res = run_bass_kernel_spmd(nc, ins, core_ids=list(range(NCORES)))
    return np.concatenate([r["ao"] for r in res.results], axis=0)


def run_attn_out(x_fm, ao, Wo, bo):
    if "a3" not in _BUILDERS:
        _BUILDERS["a3"] = build_attn_out()
    nc = _BUILDERS["a3"]
    bocol = bo.reshape(8, 128).T
    ins = []
    for c in range(NCORES):
        ins.append({
            "xsl": f32c(x_fm[:, :, :, LC * c:LC * (c + 1)]),
            "aosl": np.ascontiguousarray(ao[:, :, LC * c:LC * (c + 1)]),
            "WoT": f32c(Wo.T), "bocol": f32c(bocol),
        })
    res = run_bass_kernel_spmd(nc, ins, core_ids=list(range(NCORES)))
    out = np.zeros((B, 8, 128, L), np.float32)
    for c in range(NCORES):
        out[:, :, :, LC * c:LC * (c + 1)] = res.results[c]["xout"]
    return out








NTOK = B * L
KSEL = 2048
FF0 = 2816
XGW = 1032


def build_ffn0_route():
    g = G()
    xsl = g.io("xsl", [B, 8, 128, LC])
    Wr = g.io("Wr", [D, 8])
    br = g.io("br", [1, 8])
    ident = g.io("ident", [128, 128])
    smslc = g.io("smslc", [8, B, LC], out=True)
    xnslc = g.io("xnslc", [B * LC, D], BF16, out=True)

    sX = g.sb("sX", [128, B, 8, LC])
    sXn = g.sb("sXn", [128, 8, LC], BF16)
    sWr = g.sb("sWr", [128, 8, 8], BF16)
    sBrRow = g.sb("sBrRow", [1, 8])
    sBr = g.sb("sBr", [128, 8])
    sLg = g.sb("sLg", [128, 8])
    sSm = g.sb("sSm", [128, 8])
    sSmT = g.sb("sSmT", [8, 128])
    sTmp = g.sb("sTmp", [128, 2])
    sXT = g.sb("sXT", [128, 512], BF16)
    sIdentB = g.sb("sIdentB", [128, 128], BF16)
    sIdent32 = g.sb("sIdent32", [128, 128])
    pL = g.ps("pL", [128, 2, 512])
    pT = g.ps("pT", [128, 512])

    C = common_bufs(g)
    g.add("gpsimd", lambda gp: gp.dma_start(
        out=sWr[:], in_=Wr.rearrange("(ec e) o -> e ec o", e=128)), dma=True)
    g.add("gpsimd", lambda gp: gp.dma_start(out=sIdentB[:], in_=ident[:]), dma=True)
    g.add("sync", lambda s: s.dma_start(out=sBrRow[:], in_=br[:]), dma=True)
    g.add("sync", lambda s: s.dma_start(out=sIdent32[:], in_=ident[:]), dma=True)
    g.add("sync", lambda s: s.dma_start(out=sX[:], in_=xsl.rearrange("b c f l -> f b c l")), dma=True)
    g.bar()
    g.add("tensor", lambda t: t.matmul(pT[:, 0:8], C["sOnesRow"][:], sBrRow[:], start=True, stop=True))
    g.bar()
    g.add("vector", lambda v: v.tensor_copy(sBr[:], pT[:, 0:8]))
    g.bar()

    for b in range(B):
        emit_ln(g, sX[:, b], sXn, C)
        for fc in range(8):
            def xtr(t, fc=fc):
                ii = None
                for q in range(4):
                    ii = t.matmul(pT[:, q * 128:(q + 1) * 128], sXn[:, fc, q * 128:(q + 1) * 128],
                                  sIdentB[:], start=(q == 0), stop=(q == 3))
                return ii
            g.add("tensor", xtr)
            g.bar()
            g.add("vector", lambda v: v.tensor_copy(sXT[:], pT[:]))
            g.bar()

            def stx(s, b=b, fc=fc):
                return s.dma_start(out=xnslc[b * LC:(b + 1) * LC, 128 * fc:128 * (fc + 1)]
                                   .rearrange("(q p) f -> p q f", p=128),
                                   in_=sXT[:].rearrange("p (q f) -> p q f", f=128))
            g.add("sync", stx, dma=True)
            g.bar()

        for q in range(4):
            def lmm(t, q=q):
                ii = None
                for ec in range(8):
                    ii = t.matmul(pL[:, q % 2, 0:8], sXn[:, ec, 128 * q:128 * (q + 1)],
                                  sWr[:, ec], start=(ec == 0), stop=(ec == 7))
                return ii
            g.add("tensor", lmm)
            g.bar()

            g.add("vector", lambda v, q=q: v.tensor_tensor(sLg[:], pL[:, q % 2, 0:8], sBr[:], ALU.add))
            g.bar()
            g.add("vector", lambda v: v.reduce_max(sTmp[:, 0:1], sLg[:], AX.X))
            g.bar()
            g.add("vector", lambda v: v.tensor_scalar(sLg[:], sLg[:], sTmp[:, 0:1], None, ALU.subtract))
            g.bar()
            g.add("scalar", lambda sc: sc.activation(out=sSm[:], in_=sLg[:], func=AF.Exp))
            g.bar()

            g.add("vector", lambda v: v.reduce_sum(sTmp[:, 0:1], sSm[:], AX.X))
            g.bar()
            g.add("vector", lambda v: v.reciprocal(sTmp[:, 1:2], sTmp[:, 0:1]))
            g.bar()
            g.add("vector", lambda v: v.tensor_scalar(sSm[:], sSm[:], sTmp[:, 1:2], None, ALU.mult))
            g.bar()
            g.add("tensor", lambda t: t.matmul(pT[0:8, 0:128], sSm[:], sIdent32[:], start=True, stop=True))
            g.bar()
            g.add("vector", lambda v: v.tensor_copy(sSmT[:], pT[0:8, 0:128]))
            g.bar()
            g.add("sync", lambda s, b=b, q=q: s.dma_start(
                out=smslc[:, b, 128 * q:128 * (q + 1)], in_=sSmT[:]), dma=True)
            g.bar()
    return g.build()


def build_ffn0_select():
    g = G()
    sme = g.io("sme", [1, NTOK])
    ident = g.io("ident", [128, 128])
    pmap = g.io("pmap", [NTOK], I32, out=True)
    pptT = g.io("pptT", [128, 128], I32, out=True)
    smvT = g.io("smvT", [128, 128], out=True)
    dbg = g.io("dbg", [128, 6, 128], out=True)
    dbgsc = g.io("dbgsc", [1, 16], out=True)

    sSmv = g.sb("sSmv", [128, 128])
    sIot = g.sb("sIotI", [128, 128], I32)
    sIotF = g.sb("sIotF", [128, 128])
    sMask = g.sb("sMask", [128, 128])
    sMask2 = g.sb("sMask2", [128, 128])
    sRows = g.sb("sRows", [128, 1])
    sSc = g.sb("sSc", [1, 16])
    sCol = g.sb("sCol", [128, 4])
    sPre = g.sb("sPre", [128, 128])
    sLTc = g.sb("sLTc", [128, 1], BF16)
    sLT = g.sb("sLT", [1, 128])
    sOffE = g.sb("sOffE", [1, 128])
    sOff = g.sb("sOff", [128, 1])
    sPp = g.sb("sPp", [128, 128])
    sPpI = g.sb("sPpI", [128, 128], I32)
    sPpT = g.sb("sPpT", [128, 128])
    sPpTI = g.sb("sPpTI", [128, 128], I32)
    sSmvT = g.sb("sSmvT", [128, 128])
    sIdentB = g.sb("sIdentB", [128, 128], BF16)
    sIdent32 = g.sb("sIdent32", [128, 128])
    sOnesRow = g.sb("sOnesRow", [1, 128])
    sOnes32 = g.sb("sOnes32", [128, 1])

    pC = g.ps("pC", [128, 512])
    pD = g.ps("pD", [128, 2, 512])

    g.add("sync", lambda s: s.dma_start(out=sSmv[:], in_=sme.rearrange("o (l p) -> (o l) p", p=128)), dma=True)
    g.add("gpsimd", lambda gp: gp.dma_start(out=sIdentB[:], in_=ident[:]), dma=True)
    g.add("sync", lambda s: s.dma_start(out=sIdent32[:], in_=ident[:]), dma=True)

    def io0(gp):
        gp.iota(sIot[:], pattern=[[1, 128]], base=0, channel_multiplier=128)
        return gp.tensor_copy(sIotF[:], sIot[:])
    g.add("gpsimd", io0)

    def cst(v):
        v.memset(sOnesRow[:], 1.0)
        v.memset(sOnes32[:], 1.0)
        v.memset(sSc[:], 0.0)
        v.memset(sSc[:, 1:2], 1.0)
        return v.memset(sSc[:, 8:9], -1.0)
    g.add("vector", cst)
    g.bar()
    for it in range(40):
        g.add("vector", lambda v: v.tensor_tensor(sSc[:, 2:3], sSc[:, 0:1], sSc[:, 1:2], ALU.add))
        g.bar()
        g.add("vector", lambda v: v.tensor_scalar(sSc[:, 2:3], sSc[:, 2:3], 0.5, None, ALU.mult))
        g.bar()

        g.add("tensor", lambda t: t.matmul(pC[:, 0:1], sOnesRow[:], sSc[:, 2:3], start=True, stop=True))
        g.bar()
        g.add("vector", lambda v: v.tensor_copy(sCol[:, 0:1], pC[:, 0:1]))
        g.bar()

        g.add("vector", lambda v: v.tensor_scalar(sMask[:], sSmv[:], sCol[:, 0:1], None, ALU.is_gt))
        g.bar()
        g.add("vector", lambda v: v.reduce_sum(sRows[:], sMask[:], AX.X))
        g.bar()

        g.add("tensor", lambda t: t.matmul(pC[0:1, 4:5], sOnes32[:], sRows[:], start=True, stop=True))
        g.bar()

        g.add("vector", lambda v: v.tensor_copy(sSc[:, 3:4], pC[0:1, 4:5]))
        g.bar()
        g.add("vector", lambda v: v.tensor_scalar(sSc[:, 4:5], sSc[:, 3:4], float(KSEL) - 0.5, None, ALU.is_gt))
        g.bar()
        g.add("vector", lambda v: v.tensor_tensor(sSc[:, 5:6], sSc[:, 2:3], sSc[:, 0:1], ALU.subtract))
        g.bar()
        g.add("vector", lambda v: v.tensor_tensor(sSc[:, 5:6], sSc[:, 5:6], sSc[:, 4:5], ALU.mult))
        g.bar()
        g.add("vector", lambda v: v.tensor_tensor(sSc[:, 0:1], sSc[:, 0:1], sSc[:, 5:6], ALU.add))
        g.bar()
        g.add("vector", lambda v: v.tensor_tensor(sSc[:, 6:7], sSc[:, 1:2], sSc[:, 2:3], ALU.subtract))
        g.bar()
        g.add("vector", lambda v: v.tensor_tensor(sSc[:, 6:7], sSc[:, 6:7], sSc[:, 4:5], ALU.mult))
        g.bar()
        g.add("vector", lambda v: v.tensor_tensor(sSc[:, 1:2], sSc[:, 2:3], sSc[:, 6:7], ALU.add))
        g.bar()


    g.add("tensor", lambda t: t.matmul(pC[:, 0:2], sOnesRow[:], sSc[:, 0:2], start=True, stop=True))
    g.bar()
    g.add("vector", lambda v: v.tensor_copy(sCol[:, 1:3], pC[:, 0:2]))
    g.bar()

    g.add("vector", lambda v: v.tensor_scalar(sMask[:], sSmv[:], sCol[:, 2:3], None, ALU.is_gt))
    g.bar()
    g.add("vector", lambda v: v.tensor_scalar(sMask2[:], sSmv[:], sCol[:, 1:2], None, ALU.is_gt))
    g.bar()
    g.add("vector", lambda v: v.tensor_tensor(sMask2[:], sMask2[:], sMask[:], ALU.subtract))
    g.bar()
    g.add("vector", lambda v: v.reduce_sum(sRows[:], sMask[:], AX.X))
    g.bar()

    g.add("tensor", lambda t: t.matmul(pC[0:1, 4:5], sOnes32[:], sRows[:], start=True, stop=True))
    g.bar()

    g.add("vector", lambda v: v.tensor_copy(sSc[:, 6:7], pC[0:1, 4:5]))
    g.bar()
    g.add("vector", lambda v: v.tensor_scalar(sSc[:, 7:8], sSc[:, 6:7], -1.0, float(KSEL), ALU.mult, ALU.add))
    g.bar()
    g.add("vector", lambda v: v.memset(sSc[:, 9:10], float(NTOK)))
    g.bar()


    for it in range(16):
        g.add("vector", lambda v: v.tensor_tensor(sSc[:, 10:11], sSc[:, 8:9], sSc[:, 9:10], ALU.add))
        g.bar()
        g.add("vector", lambda v: v.tensor_scalar(sSc[:, 10:11], sSc[:, 10:11], 0.5, None, ALU.mult))
        g.bar()

        g.add("tensor", lambda t: t.matmul(pC[:, 0:1], sOnesRow[:], sSc[:, 10:11], start=True, stop=True))
        g.bar()
        g.add("vector", lambda v: v.tensor_copy(sCol[:, 3:4], pC[:, 0:1]))
        g.bar()

        g.add("vector", lambda v: v.tensor_scalar(sPre[:], sIotF[:], sCol[:, 3:4], None, ALU.is_le))
        g.bar()
        g.add("vector", lambda v: v.tensor_tensor(sPre[:], sPre[:], sMask2[:], ALU.mult))
        g.bar()
        g.add("vector", lambda v: v.reduce_sum(sRows[:], sPre[:], AX.X))
        g.bar()

        g.add("tensor", lambda t: t.matmul(pC[0:1, 4:5], sOnes32[:], sRows[:], start=True, stop=True))
        g.bar()

        g.add("vector", lambda v: v.tensor_copy(sSc[:, 3:4], pC[0:1, 4:5]))
        g.bar()
        g.add("vector", lambda v: v.tensor_tensor(sSc[:, 4:5], sSc[:, 3:4], sSc[:, 7:8], ALU.is_lt))
        g.bar()
        g.add("vector", lambda v: v.tensor_tensor(sSc[:, 5:6], sSc[:, 10:11], sSc[:, 8:9], ALU.subtract))
        g.bar()
        g.add("vector", lambda v: v.tensor_tensor(sSc[:, 5:6], sSc[:, 5:6], sSc[:, 4:5], ALU.mult))
        g.bar()
        g.add("vector", lambda v: v.tensor_tensor(sSc[:, 8:9], sSc[:, 8:9], sSc[:, 5:6], ALU.add))
        g.bar()
        g.add("vector", lambda v: v.tensor_tensor(sSc[:, 6:7], sSc[:, 9:10], sSc[:, 10:11], ALU.subtract))
        g.bar()
        g.add("vector", lambda v: v.tensor_tensor(sSc[:, 6:7], sSc[:, 6:7], sSc[:, 4:5], ALU.mult))
        g.bar()
        g.add("vector", lambda v: v.tensor_tensor(sSc[:, 9:10], sSc[:, 10:11], sSc[:, 6:7], ALU.add))
        g.bar()


    g.add("tensor", lambda t: t.matmul(pC[:, 0:1], sOnesRow[:], sSc[:, 9:10], start=True, stop=True))
    g.bar()
    g.add("vector", lambda v: v.tensor_copy(sCol[:, 3:4], pC[:, 0:1]))
    g.bar()

    g.add("vector", lambda v: v.tensor_scalar(sPre[:], sIotF[:], sCol[:, 3:4], None, ALU.is_le))
    g.bar()
    g.add("vector", lambda v: v.tensor_tensor(sPre[:], sPre[:], sMask2[:], ALU.mult))
    g.bar()
    g.add("vector", lambda v: v.tensor_tensor(sMask[:], sMask[:], sPre[:], ALU.add))
    g.bar()


    g.add("vector", lambda v: v.tensor_tensor_scan(sPre[:], sMask[:], sMask[:], 0.0, ALU.add, ALU.bypass))
    g.bar()
    g.add("vector", lambda v: v.tensor_copy(sLTc[:], sPre[:, 127:128]))
    g.bar()

    g.add("tensor", lambda t: t.matmul(pC[0:1, 0:128], sLTc[:], sIdentB[:], start=True, stop=True))
    g.bar()
    g.add("vector", lambda v: v.tensor_copy(sLT[:], pC[0:1, 0:128]))
    g.bar()

    g.add("vector", lambda v: v.tensor_tensor_scan(sOffE[:], sLT[:], sLT[:], 0.0, ALU.add, ALU.bypass))
    g.bar()
    g.add("vector", lambda v: v.memset(sLT[:, 0:1], 0.0))
    g.bar()
    g.add("vector", lambda v: v.tensor_copy(sLT[:, 1:128], sOffE[:, 0:127]))
    g.bar()

    g.add("tensor", lambda t: t.matmul(pC[:, 0:1], sLT[:], sOnesRow[:, 0:1], start=True, stop=True))
    g.bar()
    g.add("vector", lambda v: v.tensor_copy(sOff[:], pC[:, 0:1]))
    g.bar()

    g.add("vector", lambda v: v.tensor_scalar(sPp[:], sPre[:], sOff[:], -1.0, ALU.add, ALU.add))
    g.bar()
    g.add("vector", lambda v: v.tensor_scalar(sPp[:], sPp[:], -60000.0, None, ALU.add))
    g.bar()
    g.add("vector", lambda v: v.tensor_tensor(sPp[:], sPp[:], sMask[:], ALU.mult))
    g.bar()
    g.add("vector", lambda v: v.tensor_scalar(sPp[:], sPp[:], 60000.0, None, ALU.add))
    g.bar()
    g.add("vector", lambda v: v.tensor_copy(sPpI[:], sPp[:]))
    g.bar()

    g.add("sync", lambda s: s.dma_start(out=pmap.rearrange("(l p) -> l p", p=128), in_=sPpI[:]), dma=True)
    g.add("tensor", lambda t: t.matmul(pD[:, 0, 0:128], sPp[:], sIdent32[:], start=True, stop=True))
    g.add("tensor", lambda t: t.matmul(pD[:, 1, 0:128], sSmv[:], sIdent32[:], start=True, stop=True))
    g.bar()

    g.add("vector", lambda v: v.tensor_copy(sPpT[:], pD[:, 0, 0:128]))
    g.bar()
    g.add("vector", lambda v: v.tensor_copy(sSmvT[:], pD[:, 1, 0:128]))
    g.bar()
    g.add("vector", lambda v: v.tensor_copy(sPpTI[:], sPpT[:]))
    g.bar()


    g.add("sync", lambda s: s.dma_start(out=pptT[:], in_=sPpTI[:]), dma=True)
    g.add("sync", lambda s: s.dma_start(out=smvT[:], in_=sSmvT[:]), dma=True)
    g.add("sync", lambda s: s.dma_start(out=dbg[:, 0], in_=sMask[:]), dma=True)
    g.add("sync", lambda s: s.dma_start(out=dbg[:, 1], in_=sMask2[:]), dma=True)
    g.add("sync", lambda s: s.dma_start(out=dbg[:, 2], in_=sPre[:]), dma=True)
    g.add("sync", lambda s: s.dma_start(out=dbg[:, 3].bitcast(F32) if False else dbg[:, 3], in_=sOff[:].to_broadcast([128, 128]) if False else sPp[:]), dma=True)
    def dbgcp(v):
        v.tensor_copy(sCol[:, 0:1], sOff[:])
        return v.tensor_copy(sCol[:, 3:4], sCol[:, 1:2])
    g.add("vector", dbgcp)
    g.bar()
    g.add("sync", lambda s: s.dma_start(out=dbg[:, 4, 0:4], in_=sCol[:]), dma=True)
    g.add("sync", lambda s: s.dma_start(out=dbgsc[:], in_=sSc[:]), dma=True)
    g.bar()
    return g.build()


def build_ffn0_route2(lo, hi, with_select_inputs=False):
    g = G()
    xntm = g.io("xntm", [NTOK, D], BF16)
    pptT = g.io("pptT", [128, 128], I32)
    smvT = g.io("smvT", [128, 128])
    xgin = g.io("xgin", [KSEL, XGW], BF16)
    xgout = g.io("xgout", [KSEL, XGW], BF16, out=True)

    sXr = g.sb("sXr", [128, 3, XGW], BF16)
    sPpTI = g.sb("sPpTI", [128, 128], I32)
    sSmvT = g.sb("sSmvT", [128, 128])

    g.add("sync", lambda s: s.dma_start(out=sPpTI[:], in_=pptT[:]), dma=True)
    g.add("sync", lambda s: s.dma_start(out=sSmvT[:], in_=smvT[:]), dma=True)
    g.add("sync", lambda s: s.dma_start(out=xgout[:], in_=xgin[:]), dma=True)
    g.add("vector", lambda v: v.memset(sXr[:, :, 1025:XGW], 0.0))
    g.bar()

    for k, l in enumerate(range(lo, hi)):
        def ldx(s, l=l):
            return s.dma_start(out=sXr[:, l % 3, 0:D], in_=xntm[128 * l:128 * (l + 1)])
        g.add("sync", ldx, dma=True)

        def gcp(v, l=l):
            return v.tensor_copy(sXr[:, l % 3, 1024:1025], sSmvT[:, l:l + 1])
        g.add("vector", gcp)
        g.bar()

        def sct(gp, l=l):
            return gp.indirect_dma_start(
                xgout[:], IndirectOffsetOnAxis(sPpTI[:, l:l + 1], 0),
                sXr[:, l % 3].opt(), None, bounds_check=KSEL - 1, oob_is_err=False)
        g.add("gpsimd", sct, dma=True)
        g.bar()
    return g.build()


def build_ffn0_ffn():
    g = G()
    xgio = g.io("xgio", [KSEL, XGW], BF16)
    WuP = g.io("WuP", [D, FF0])
    WdP = g.io("WdP", [FF0, D])
    bupc = g.io("bupc", [128, FF0 // 128])
    bdnc = g.io("bdnc", [128, 8])
    ident = g.io("ident", [128, 128])
    yscr = g.io("yscr", [KSEL, D], out=True)

    sXr = g.sb("sXr", [128, 3, XGW], BF16)
    sIdentB = g.sb("sIdentB", [128, 128], BF16)
    sIdent32 = g.sb("sIdent32", [128, 128])
    sWu = g.sb("sWu", [128, 8, 22, 128], BF16)
    sWdC = g.sb("sWdC", [128, 2, 22, 128], BF16)
    sXgf = g.sb("sXgf", [128, 8, KSEL], BF16)
    sGrow = g.sb("sGrow", [8, KSEL], BF16)
    sMid = g.sb("sMid", [128, 22, KSEL], BF16)
    sBup = g.sb("sBup", [128, 22])
    sBdn = g.sb("sBdn", [128, 8])
    sYe = g.sb("sYe", [128, 2, 512])
    sYT = g.sb("sYT", [128, 2, 512])
    sGBall = g.sb("sGBall", [128, 4, 512])
    sOnesRowB = g.sb("sOnesRowB", [1, 128], BF16)

    pD = g.ps("pD", [128, 2, 512])
    pT2 = g.ps("pT2", [128, 2, 512])

    g.add("gpsimd", lambda gp: gp.dma_start(out=sIdentB[:], in_=ident[:]), dma=True)
    g.add("sync", lambda s: s.dma_start(out=sIdent32[:], in_=ident[:]), dma=True)
    g.add("gpsimd", lambda gp: gp.dma_start(
        out=sWu[:], in_=WuP.rearrange("(ec e) (fc f) -> e ec fc f", e=128, f=128)), dma=True, defer=8)
    g.add("sync", lambda s: s.dma_start(out=sBup[:], in_=bupc[:]), dma=True)
    g.add("sync", lambda s: s.dma_start(out=sBdn[:], in_=bdnc[:]), dma=True)
    g.add("sync", lambda s: s.dma_start(out=sXr[:, 0], in_=xgio[0:128]), dma=True)
    g.add("vector", lambda v: v.memset(sOnesRowB[:], 1.0))
    g.bar()
    for j in range(16):
        for dc in range(10):
            if dc == 0 and j + 1 < 16:
                g.add("sync", lambda s, j=j: s.dma_start(
                    out=sXr[:, (j + 1) % 3], in_=xgio[128 * (j + 1):128 * (j + 2)]), dma=True, defer=7)
            if dc < 9:
                def gtr1(t, j=j, dc=dc):
                    n = 128 if dc < 8 else 8
                    srcb = sXr[:, j % 3, 128 * dc:128 * dc + n]
                    return t.matmul(pD[0:n, dc % 2, 0:128], srcb, sIdentB[:], start=True, stop=True)
                g.add("tensor", gtr1)
            if dc >= 1:
                def gev(v, j=j, dc=dc):
                    d = dc - 1
                    if d < 8:
                        return v.tensor_copy(sXgf[:, d, 128 * j:128 * (j + 1)], pD[:, d % 2, 0:128])
                    return v.tensor_copy(sGrow[:, 128 * j:128 * (j + 1)], pD[0:8, d % 2, 0:128])
                g.add("vector", gev)
            g.bar()

    slots = [(ts, fc) for ts in range(4) for fc in range(22)]
    for i in range(len(slots) + 1):
        if i < len(slots):
            def mmm(t, i=i):
                ts, fc = slots[i]
                ii = None
                for ec in range(8):
                    ii = t.matmul(pD[:, i % 2], sWu[:, ec, fc], sXgf[:, ec, 512 * ts:512 * (ts + 1)],
                                  start=(ec == 0), stop=(ec == 7))
                return ii
            g.add("tensor", mmm)
        if i >= 1:
            def msl(sc, i=i):
                ts, fc = slots[i - 1]
                return sc.activation(out=sMid[:, fc, 512 * ts:512 * (ts + 1)], in_=pD[:, (i - 1) % 2],
                                     func=AF.Silu, bias=sBup[:, fc:fc + 1])
            g.add("scalar", msl)
        g.bar()

    for ts in range(4):
        g.add("tensor", lambda t, ts=ts: t.matmul(pT2[:, 0], sOnesRowB[:],
                                                  sGrow[0:1, 512 * ts:512 * (ts + 1)],
                                                  start=True, stop=True))
        g.bar()
        g.add("vector", lambda v, ts=ts: v.tensor_copy(sGBall[:, ts], pT2[:, 0]))
        g.bar()

    g.add("gpsimd", lambda gp: gp.dma_start(
        out=sWdC[:, 0], in_=WdP[:, 0:128].rearrange("(fc f) m -> f fc m", f=128)), dma=True)
    g.bar()
    dslots = [(dc, ts) for dc in range(8) for ts in range(4)]
    ND = len(dslots)
    for i in range(ND + 4):
        if i < ND:
            dc, ts = dslots[i]
            if ts == 0 and dc + 1 < 8:
                g.add("gpsimd", lambda gp, dc=dc: gp.dma_start(
                    out=sWdC[:, (dc + 1) % 2], in_=WdP[:, 128 * (dc + 1):128 * (dc + 2)]
                    .rearrange("(fc f) m -> f fc m", f=128)), dma=True, defer=3)

            def dmm(t, i=i):
                dc, ts = dslots[i]
                ii = None
                for fc in range(22):
                    ii = t.matmul(pD[:, i % 2], sWdC[:, dc % 2, fc],
                                  sMid[:, fc, 512 * ts:512 * (ts + 1)],
                                  start=(fc == 0), stop=(fc == 21))
                return ii
            g.add("tensor", dmm)
        if 0 <= i - 1 < ND:
            def yev(v, i=i):
                dc, ts = dslots[i - 1]
                v.tensor_scalar(sYe[:, (i - 1) % 2], pD[:, (i - 1) % 2], sBdn[:, dc:dc + 1],
                                None, ALU.add)
                return v.tensor_tensor(sYe[:, (i - 1) % 2], sYe[:, (i - 1) % 2], sGBall[:, ts],
                                       ALU.mult)
            g.add("vector", yev)
        if 0 <= i - 2 < ND:
            def ytr(t, i=i):
                ii = None
                for q in range(4):
                    ii = t.matmul(pT2[:, (i - 2) % 2, 128 * q:128 * (q + 1)],
                                  sYe[:, (i - 2) % 2, 128 * q:128 * (q + 1)],
                                  sIdent32[:], start=(q == 0), stop=(q == 3))
                return ii
            g.add("tensor", ytr)
        if 0 <= i - 3 < ND:
            g.add("vector", lambda v, i=i: v.tensor_copy(sYT[:, (i - 3) % 2], pT2[:, (i - 3) % 2]))
        if 0 <= i - 4 < ND:
            def sty(s, i=i):
                dc, ts = dslots[i - 4]
                return s.dma_start(out=yscr[512 * ts:512 * (ts + 1), 128 * dc:128 * (dc + 1)]
                                   .rearrange("(q p) f -> p q f", p=128),
                                   in_=sYT[:, (i - 4) % 2].rearrange("p (q f) -> p q f", f=128))
            g.add("sync", sty, dma=True)
        g.bar()
    return g.build()


def build_ffn0_gather(pairs):
    g = G()
    accin = g.io("accin", [B * LC, D])
    ys = [g.io(f"y{e}", [KSEL, D]) for e in range(8)]
    pmslc = g.io("pmslc", [B, 4, 128, 8], I32)
    accout = g.io("accout", [B * LC, D], out=True)

    sPm = g.sb("sPm", [128, B, 4, 8], I32)
    sAcc = g.sb("sAcc", [128, 2, D])

    g.add("sync", lambda s: s.dma_start(out=sPm[:], in_=pmslc.rearrange("b tt p e -> p b tt e")), dma=True)
    g.add("sync", lambda s: s.dma_start(out=accout[:], in_=accin[:]), dma=True)
    g.bar()

    for k, (b, tt) in enumerate(pairs):
        def lda(s, b=b, tt=tt, k=k):
            return s.dma_start(out=sAcc[:, k % 2],
                               in_=accin[b * LC + 128 * tt: b * LC + 128 * (tt + 1)])
        g.add("sync", lda, dma=True)
        g.bar()
        for e in range(8):
            def gth(gp, b=b, tt=tt, e=e, k=k):
                return gp.indirect_dma_start(
                    sAcc[:, k % 2].opt(), None, ys[e][:],
                    IndirectOffsetOnAxis(sPm[:, b, tt, e:e + 1].opt(), 0),
                    bounds_check=KSEL - 1, oob_is_err=False, compute_op=ALU.add)
            g.add("gpsimd", gth, dma=True)
            g.bar()

        def sta(s, b=b, tt=tt, k=k):
            return s.dma_start(out=accout[b * LC + 128 * tt: b * LC + 128 * (tt + 1)],
                               in_=sAcc[:, k % 2])
        g.add("sync", sta, dma=True)
        g.bar()
    return g.build()


def build_ffn0_final(pairs):
    g = G()
    xsl = g.io("xsl", [B, 8, 128, LC])
    accin = g.io("accin", [B * LC, D])
    ys = [g.io(f"y{e}", [KSEL, D]) for e in range(8)]
    pmslc = g.io("pmslc", [B, 4, 128, 8], I32)
    ident = g.io("ident", [128, 128])
    xout = g.io("xout", [B, 8, 128, LC], out=True)

    sX = g.sb("sX", [128, B, 8, LC])
    sPm = g.sb("sPm", [128, B, 4, 8], I32)
    sAccA = g.sb("sAccA", [128, B, 4, D])
    sIdent32 = g.sb("sIdent32", [128, 128])
    sY = g.sb("sY", [128, 2, LC])
    pY = g.ps("pY", [128, 2, 512])

    g.add("sync", lambda s: s.dma_start(out=sX[:], in_=xsl.rearrange("b c f l -> f b c l")), dma=True)
    g.add("sync", lambda s: s.dma_start(out=sPm[:], in_=pmslc.rearrange("b tt p e -> p b tt e")), dma=True)
    g.add("sync", lambda s: s.dma_start(
        out=sAccA[:], in_=accin.rearrange("(b tt p) d -> p b tt d", p=128, tt=4)), dma=True)
    g.add("gpsimd", lambda gp: gp.dma_start(out=sIdent32[:], in_=ident[:]), dma=True)
    g.bar()

    for (b, tt) in pairs:
        for e in range(8):
            def gth(gp, b=b, tt=tt, e=e):
                return gp.indirect_dma_start(
                    sAccA[:, b, tt].opt(), None, ys[e][:],
                    IndirectOffsetOnAxis(sPm[:, b, tt, e:e + 1].opt(), 0),
                    bounds_check=KSEL - 1, oob_is_err=False, compute_op=ALU.add)
            g.add("gpsimd", gth, dma=True)
        g.bar()

    for b in range(B):
        for fc in range(8):
            def ftr(t, b=b, fc=fc):
                ii = None
                for tt in range(4):
                    ii = t.matmul(pY[:, fc % 2, 128 * tt:128 * (tt + 1)],
                                  sAccA[:, b, tt, 128 * fc:128 * (fc + 1)], sIdent32[:],
                                  start=(tt == 0), stop=(tt == 3))
                return ii
            g.add("tensor", ftr)
            g.bar()

            def fev(v, b=b, fc=fc):
                return v.scalar_tensor_tensor(sY[:, fc % 2], pY[:, fc % 2], 1.0,
                                              sX[:, b, fc], ALU.mult, ALU.add)
            g.add("vector", fev)
            g.bar()
            g.add("sync", lambda s, b=b, fc=fc: s.dma_start(out=xout[b, fc], in_=sY[:, fc % 2]), dma=True)
            g.bar()
    return g.build()


def pad_w(w, rows=None, cols=None):
    r = rows or w.shape[0]
    c = cols or w.shape[1]
    out = np.zeros((r, c), np.float32)
    out[:w.shape[0], :w.shape[1]] = w
    return out


def run_ffn0(x_fm, Wr, br, W_up, b_up, W_down, b_down, builders={}):
    def get(name, fn):
        if name not in builders:
            builders[name] = fn()
        return builders[name]

    nc1 = get("f1", build_ffn0_route)
    ins = []
    for c in range(NCORES):
        ins.append({"xsl": f32c(x_fm[:, :, :, LC * c:LC * (c + 1)]),
                    "Wr": f32c(Wr), "br": f32c(br.reshape(1, 8)), "ident": IDENT})
    res = run_bass_kernel_spmd(nc1, ins, core_ids=list(range(NCORES)))
    SmT = np.zeros((8, NTOK), np.float32)
    xn_tm = np.zeros((NTOK, D), BF16NP)
    for c in range(NCORES):
        sm = res.results[c]["smslc"]
        xn = res.results[c]["xnslc"]
        for b in range(B):
            SmT[:, b * L + LC * c: b * L + LC * (c + 1)] = sm[:, b]
            xn_tm[b * L + LC * c: b * L + LC * (c + 1)] = xn[b * LC:(b + 1) * LC]

    nc2 = get("sel", build_ffn0_select)
    ins = [{"sme": f32c(SmT[e:e + 1]), "ident": IDENT} for e in range(NCORES)]
    res = run_bass_kernel_spmd(nc2, ins, core_ids=list(range(NCORES)))
    pmap_all = np.stack([res.results[e]["pmap"] for e in range(NCORES)], axis=0)
    pptT = [res.results[e]["pptT"] for e in range(NCORES)]
    smvT = [res.results[e]["smvT"] for e in range(NCORES)]

    xg = [np.zeros((KSEL, XGW), BF16NP) for _ in range(NCORES)]
    for (lo, hi) in [(0, 43), (43, 86), (86, 128)]:
        ncr = get(f"route_{lo}", lambda lo=lo, hi=hi: build_ffn0_route2(lo, hi))
        ins = [{"xntm": xn_tm, "pptT": pptT[e], "smvT": f32c(smvT[e]), "xgin": xg[e]}
               for e in range(NCORES)]
        res = run_bass_kernel_spmd(ncr, ins, core_ids=list(range(NCORES)))
        xg = [res.results[e]["xgout"] for e in range(NCORES)]

    nc3 = get("ffn", build_ffn0_ffn)
    ins = []
    for e in range(NCORES):
        gidx = e // 2
        ins.append({
            "xgio": xg[e],
            "WuP": pad_w(f32c(W_up[gidx]), D, FF0),
            "WdP": pad_w(f32c(W_down[e]), FF0, D),
            "bupc": pad_w(f32c(b_up[gidx]).reshape(1, -1), 1, FF0).reshape(22, 128).T.copy(),
            "bdnc": f32c(b_down[e]).reshape(8, 128).T.copy(),
            "ident": IDENT,
        })
    res = run_bass_kernel_spmd(nc3, ins, core_ids=list(range(NCORES)))
    yall = np.stack([res.results[e]["yscr"] for e in range(NCORES)], axis=0)

    pmslcs = []
    for c in range(NCORES):
        pmslc = np.zeros((B, 4, 128, 8), np.int32)
        for b in range(B):
            for tt in range(4):
                base = b * L + LC * c + 128 * tt
                pmslc[b, tt, :, :] = pmap_all[:, base:base + 128].T
        pmslcs.append(pmslc)

    pairs_all = [(b, tt) for b in range(B) for tt in range(4)]
    acc = [np.zeros((B * LC, D), np.float32) for _ in range(NCORES)]
    for gi, pr in enumerate([pairs_all[0:5], pairs_all[5:10], pairs_all[10:15]]):
        ncg = get(f"g{gi}", lambda pr=pr: build_ffn0_gather(pr))
        ins = [dict({"accin": acc[c], "pmslc": pmslcs[c]},
                    **{f"y{e}": yall[e] for e in range(8)}) for c in range(NCORES)]
        res = run_bass_kernel_spmd(ncg, ins, core_ids=list(range(NCORES)))
        acc = [res.results[c]["accout"] for c in range(NCORES)]

    ncf = get("fin", lambda: build_ffn0_final(pairs_all[15:16]))
    ins = [dict({"xsl": f32c(x_fm[:, :, :, LC * c:LC * (c + 1)]), "accin": acc[c],
                 "pmslc": pmslcs[c], "ident": IDENT},
                **{f"y{e}": yall[e] for e in range(8)}) for c in range(NCORES)]
    res = run_bass_kernel_spmd(ncf, ins, core_ids=list(range(NCORES)))
    out = np.zeros((B, 8, 128, L), np.float32)
    for c in range(NCORES):
        out[:, :, :, LC * c:LC * (c + 1)] = res.results[c]["xout"]
    return out




DI = 2048
HJ = 16
TCH = 128
NCH = L // TCH
BHC8 = 64 // NCORES
PI = float(np.pi)
TWO_PI = float(2 * np.pi)
MAGIC = 12582912.0


def seq(g, eng, fn, dma=False):
    g.add(eng, fn, dma=dma)
    g.bar()



def build_b1():
    g = G()
    xsl = g.io("xsl", [B, 8, 128, LC])
    WzT = g.io("WzT", [D, DI]); WgT = g.io("WgT", [D, DI]); WkT = g.io("WkT", [D, DI])
    WvT = g.io("WvT", [D, 32])
    bcolz = g.io("bcolz", [128, 16]); bcolg = g.io("bcolg", [128, 16]); bcolk = g.io("bcolk", [128, 16])
    bvr = g.io("bvr", [1, 32])
    rmsw = g.io("rmsw", [128, 8])
    ident = g.io("ident", [128, 128])
    zf = g.io("zf", [B, 16, 128, LC], BF16, out=True)
    xgf = g.io("xgf", [B, 16, 128, LC], BF16, out=True)
    kff = g.io("kff", [B, 16, 128, LC], BF16, out=True)
    vv = g.io("vv", [B, LC, 32], out=True)
    xlnf = g.io("xlnf", [B, 8, 128, LC], out=True)

    sX1 = g.sb("sX1", [128, 8, LC])
    sXl = g.sb("sXl", [128, 8, LC])
    sXn = g.sb("sXn", [128, 8, LC], BF16)
    sW = g.sb("sW", [128, 3, 8, 16, 128], BF16)
    sWv = g.sb("sWv", [128, 8, 32], BF16)
    sBia = g.sb("sBia", [128, 3, 16])
    sBv = g.sb("sBv", [1, 32])
    sBvB = g.sb("sBvB", [32, 1])
    sRms = g.sb("sRms", [128, 8])
    sOut = g.sb("sOut", [128, 2, LC], BF16)
    sVo = g.sb("sVo", [32, LC])
    sVT = g.sb("sVT", [128, 32])
    sIdent32 = g.sb("sIdent32", [128, 128])
    pP = g.ps("pP", [128, 2, 512])
    pV = g.ps("pV", [128, 512])

    C = common_bufs(g)

    for w, W in enumerate([WzT, WgT, WkT]):
        def ld(gp, w=w, W=W):
            return gp.dma_start(out=sW[:, w], in_=W.rearrange("(ec e) (fc f) -> e ec fc f", e=128, f=128))
        g.add("gpsimd", ld, dma=True, defer=10)
    g.add("gpsimd", lambda gp: gp.dma_start(out=sWv[:], in_=WvT.rearrange("(ec e) o -> e ec o", e=128)), dma=True)
    g.add("sync", lambda s: s.dma_start(out=sBia[:, 0], in_=bcolz[:]), dma=True)
    g.add("sync", lambda s: s.dma_start(out=sBia[:, 1], in_=bcolg[:]), dma=True)
    g.add("sync", lambda s: s.dma_start(out=sBia[:, 2], in_=bcolk[:]), dma=True)
    g.add("sync", lambda s: s.dma_start(out=sBv[:], in_=bvr[:]), dma=True)
    g.add("sync", lambda s: s.dma_start(out=sRms[:], in_=rmsw[:]), dma=True)
    g.add("sync", lambda s: s.dma_start(out=sIdent32[:], in_=ident[:]), dma=True)
    g.bar()
    seq(g, "tensor", lambda t: t.matmul(pP[0:32, 0, 0:1], sBv[:], C["sOnesRow"][:, 0:1], start=True, stop=True))
    seq(g, "vector", lambda v: v.tensor_copy(sBvB[:], pP[0:32, 0, 0:1]))

    sSq, sStat = C["sSq"], C["sStat"]
    seq(g, "vector", lambda v: v.memset(C["sEps"][:], 1e-5))
    for b in range(B):
        g.add("sync", lambda s, b=b: s.dma_start(out=sX1[:], in_=xsl[b].rearrange("c f l -> f c l")), dma=True)
        g.bar()
        emit_ln(g, sX1[:], sXl, C)
        g.add("sync", lambda s, b=b: s.dma_start(out=xlnf[b].rearrange("c f l -> f c l"), in_=sXl[:]), dma=True)
        for grp in range(4):
            def sqf(sc, grp=grp, b=b):
                sc.activation(out=sSq[:, 0], in_=sXl[:, 2 * grp], func=AF.Square)
                return sc.activation(out=sSq[:, 1], in_=sXl[:, 2 * grp + 1], func=AF.Square)
            g.add("scalar", sqf)
            g.bar()

            def mm2(t, grp=grp):
                t.matmul(C["pS2"][:], C["sOnes32"][:], sSq[:, 0], start=(grp == 0), stop=False)
                return t.matmul(C["pS2"][:], C["sOnes32"][:], sSq[:, 1], start=False, stop=(grp == 3))
            g.add("tensor", mm2)
            g.bar()
        seq(g, "vector", lambda v: v.tensor_scalar(sStat[:, 1], C["pS2"][:], 1.0 / D, 1e-6, ALU.mult, ALU.add))
        seq(g, "scalar", lambda sc: sc.activation(out=sStat[:, 3], in_=sStat[:, 1], func=AF.Ln))
        seq(g, "scalar", lambda sc: sc.activation(out=sStat[:, 4], in_=sStat[:, 3], func=AF.Exp, scale=-0.5))
        seq(g, "tensor", lambda t: t.matmul(C["pBC"][:], C["sOnesRow"][:], sStat[:, 4], start=True, stop=True))
        seq(g, "vector", lambda v: v.tensor_copy(C["sRB"][:], C["pBC"][:]))

        def norm(v, b=b):
            ii = None
            for fc in range(8):
                v.tensor_tensor(sSq[:, fc % 2], sXl[:, fc], C["sRB"][:], ALU.mult)
                ii = v.tensor_scalar(sXn[:, fc], sSq[:, fc % 2], sRms[:, fc:fc + 1], None, ALU.mult)
            return ii
        g.add("vector", norm)
        g.bar()

        pslots = [(w, fc) for w in range(3) for fc in range(16)]
        for i in range(len(pslots) + 2):
            if i < len(pslots):
                w, fc = pslots[i]

                def proj(t, w=w, fc=fc, i=i):
                    ii = None
                    for ec in range(8):
                        ii = t.matmul(pP[:, i % 2], sW[:, w, ec, fc], sXn[:, ec],
                                      start=(ec == 0), stop=(ec == 7))
                    return ii
                g.add("tensor", proj)
            if 1 <= i <= len(pslots):
                w1, fc1 = pslots[i - 1]
                g.add("vector", lambda v, w1=w1, fc1=fc1, i=i: v.tensor_scalar(
                    sOut[:, (i - 1) % 2], pP[:, (i - 1) % 2], sBia[:, w1, fc1:fc1 + 1], None, ALU.add))
            if 2 <= i <= len(pslots) + 1:
                w2, fc2 = pslots[i - 2]
                tgt2 = [zf, xgf, kff][w2]
                g.add("sync", lambda s, tgt2=tgt2, b=b, fc2=fc2, i=i: s.dma_start(
                    out=tgt2[b, fc2], in_=sOut[:, (i - 2) % 2]), dma=True)
            g.bar()

        def projv(t):
            ii = None
            for ec in range(8):
                ii = t.matmul(pV[0:32, 0:512], sWv[:, ec], sXn[:, ec], start=(ec == 0), stop=(ec == 7))
            return ii
        g.add("tensor", projv)
        g.bar()
        seq(g, "vector", lambda v: v.tensor_scalar(sVo[:], pV[0:32, 0:512], sBvB[:], None, ALU.add))
        for q in range(4):
            seq(g, "tensor", lambda t, q=q: t.matmul(pV[:, 0:32], sVo[:, 128 * q:128 * (q + 1)],
                                                     sIdent32[0:32, 0:32], start=True, stop=True))
            seq(g, "vector", lambda v: v.tensor_copy(sVT[:], pV[:, 0:32]))
            seq(g, "sync", lambda s, b=b, q=q: s.dma_start(out=vv[b, 128 * q:128 * (q + 1)], in_=sVT[:]), dma=True)
    return g.build()



def build_b2():
    g = G()
    xgh = g.io("xgh", [B, 16, 128, LC + 3])
    kin = g.io("kin", [B, 16, 128, LC], BF16)
    vv = g.io("vv", [B, LC, 32])
    QwT = g.io("QwT", [DI, DI])
    gwT = g.io("gwT", [DI, 112])
    gb = g.io("gb", [1, 112])
    lds = g.io("lds", [1, 16])
    frq = g.io("frq", [1, 16])
    convw = g.io("convw", [128, 16, 4])
    convb = g.io("convb", [128, 16])
    ident = g.io("ident", [128, 128])
    xcf = g.io("xcf", [B, 16, 128, LC], BF16, out=True)
    qnf = g.io("qnf", [B, 16, 128, LC], BF16, out=True)
    knf = g.io("knf", [B, 16, 128, LC], BF16, out=True)
    gsc = g.io("gsc", [B, LC, 16, 8], out=True)
    glt = g.io("glt", [B, 16, LC, 2], out=True)

    sXg = g.sb("sXg", [128, 16, LC + 3])
    sXc = g.sb("sXc", [128, 16, LC], BF16)
    sK = g.sb("sK", [128, 16, LC], BF16)
    sQw = g.sb("sQw", [128, 16, 16, 128], BF16)
    sGw = g.sb("sGw", [128, 16, 112], BF16)
    sGb = g.sb("sGb", [1, 112])
    sCw = g.sb("sCw", [128, 16, 4])
    sCb = g.sb("sCb", [128, 16])
    sT1 = g.sb("sT1", [128, LC])
    sT2 = g.sb("sT2", [128, LC])
    sCT1 = g.sb("sCT1", [128, 3, LC])
    sCT2 = g.sb("sCT2", [128, 2, LC])
    sGt = g.sb("sGt", [128, 4, 112])
    sS = g.sb("sS", [128, 21, 4, 16])
    sV4 = g.sb("sV4", [128, 4, 16, 2])
    sLds = g.sb("sLds", [128, 16])
    sFrq = g.sb("sFrq", [128, 16])
    sLds4 = g.sb("sLds4", [128, 4, 16])
    sFrq4 = g.sb("sFrq4", [128, 4, 16])
    sGbB = g.sb("sGbB", [128, 112])
    sOne = g.sb("sOne", [128, 1])
    sQs = g.sb("sQs", [128, 5, LC], BF16)
    sTB2 = g.sb("sTB2", [128, 2, LC], BF16)
    sIdent32 = g.sb("sIdent32", [128, 128])
    sGsc4 = g.sb("sGsc4", [128, 4, 16, 8])
    sGlt4 = g.sb("sGlt4", [128, 4, 16, 2])
    pP = g.ps("pP", [128, 3, 512])
    pG = g.ps("pG", [128, 512])
    pN = g.ps("pN", [1, 2, 512])
    pB = g.ps("pB", [128, 2, 512])

    C = common_bufs(g, psum=False)

    g.add("gpsimd", lambda gp: gp.dma_start(
        out=sQw[:], in_=QwT.rearrange("(ec e) (fc f) -> e ec fc f", e=128, f=128)), dma=True, defer=30)
    g.add("gpsimd", lambda gp: gp.dma_start(
        out=sGw[:], in_=gwT.rearrange("(ec e) o -> e ec o", e=128)), dma=True, defer=10)
    g.add("sync", lambda s: s.dma_start(out=sGb[:], in_=gb[:]), dma=True)
    g.add("sync", lambda s: s.dma_start(out=sCw[:], in_=convw[:]), dma=True)
    g.add("sync", lambda s: s.dma_start(out=sCb[:], in_=convb[:]), dma=True)
    g.add("sync", lambda s: s.dma_start(out=sIdent32[:], in_=ident[:]), dma=True)
    g.bar()
    seq(g, "tensor", lambda t: t.matmul(pG[:, 0:112], C["sOnesRow"][:], sGb[:], start=True, stop=True))
    seq(g, "vector", lambda v: v.tensor_copy(sGbB[:], pG[:, 0:112]))
    sLdsRow = g.sb("sLdsRow", [1, 16])
    sFrqRow = g.sb("sFrqRow", [1, 16])
    g.add("sync", lambda s: s.dma_start(out=sLdsRow[:], in_=lds[:]), dma=True)
    g.add("sync", lambda s: s.dma_start(out=sFrqRow[:], in_=frq[:]), dma=True)
    g.bar()
    seq(g, "scalar", lambda sc: sc.activation(out=sLdsRow[:], in_=sLdsRow[:], func=AF.Exp))
    seq(g, "tensor", lambda t: t.matmul(pG[:, 0:16], C["sOnesRow"][:], sLdsRow[:], start=True, stop=True))
    seq(g, "vector", lambda v: v.tensor_copy(sLds[:], pG[:, 0:16]))
    seq(g, "tensor", lambda t: t.matmul(pG[:, 0:16], C["sOnesRow"][:], sFrqRow[:], start=True, stop=True))
    seq(g, "vector", lambda v: v.tensor_copy(sFrq[:], pG[:, 0:16]))

    def l4(v):
        ii = None
        for q in range(4):
            v.tensor_copy(sLds4[:, q], sLds[:])
            ii = v.tensor_copy(sFrq4[:, q], sFrq[:])
        return ii
    g.add("vector", l4)
    g.bar()
    seq(g, "vector", lambda v: v.memset(sOne[:], 1.0))

    for b in range(B):
        g.add("sync", lambda s, b=b: s.dma_start(out=sXg[:], in_=xgh[b].rearrange("c f l -> f c l")), dma=True)
        g.add("sync", lambda s, b=b: s.dma_start(out=sK[:], in_=kin[b].rearrange("c f l -> f c l")), dma=True)
        g.bar()

        for i in range(19):
            def cstg(v, i=i):
                ii = None
                if i < 16:
                    v.tensor_scalar(sCT1[:, i % 3], sXg[:, i, 0:LC], sCw[:, i, 0:1], None, ALU.mult)
                    for kk in range(1, 4):
                        ii = v.scalar_tensor_tensor(sCT1[:, i % 3], sXg[:, i, kk:kk + LC],
                                                    sCw[:, i, kk:kk + 1], sCT1[:, i % 3],
                                                    ALU.mult, ALU.add)
                if 0 <= i - 2 < 16:
                    hc = i - 2
                    v.tensor_scalar(sCT1[:, hc % 3], sCT1[:, hc % 3], sCb[:, hc:hc + 1], None, ALU.add)
                    ii = v.tensor_tensor(sXc[:, hc], sCT1[:, hc % 3], sCT2[:, hc % 2], ALU.mult)
                return ii
            if i < 16 or 0 <= i - 2 < 16:
                g.add("vector", cstg)
            if 0 <= i - 1 < 16:
                g.add("scalar", lambda sc, i=i: sc.activation(
                    out=sCT2[:, (i - 1) % 2], in_=sCT1[:, (i - 1) % 3], func=AF.Sigmoid,
                    bias=sCb[:, i - 1:i]))
            if 0 <= i - 3 < 16:
                g.add("sync", lambda s, b=b, i=i: s.dma_start(out=xcf[b, i - 3],
                                                              in_=sXc[:, i - 3]), dma=True)
            g.bar()

        for q in range(4):
            def gmm(t, q=q):
                ii = None
                for ec in range(16):
                    ii = t.matmul(pG[:, 0:112], sXc[:, ec, 128 * q:128 * (q + 1)],
                                  sGw[:, ec], start=(ec == 0), stop=(ec == 15))
                return ii
            g.add("tensor", gmm)
            g.bar()
            seq(g, "vector", lambda v, q=q: v.tensor_tensor(sGt[:, q], pG[:, 0:112], sGbB[:], ALU.add))

        al, om, sB_, sC_, sdt, bt, rr = [sGt[:, :, 16 * i:16 * (i + 1)] for i in range(7)]

        def sp(i):
            return sS[:, i]

        g.add("sync", lambda s, b=b: s.dma_start(
            out=sV4[:].rearrange("p q h s -> p q (h s)"),
            in_=vv[b].rearrange("(q p) w -> p q w", p=128)), dma=True)
        seq(g, "scalar", lambda sc: sc.activation(out=sp(1), in_=al, func=AF.Abs))
        seq(g, "scalar", lambda sc: sc.activation(out=sp(2), in_=sp(1), func=AF.Exp, scale=-1.0))
        seq(g, "scalar", lambda sc: sc.activation(out=sp(3), in_=sp(2), func=AF.Ln, bias=sOne[:]))
        seq(g, "vector", lambda v: v.tensor_scalar(sp(4), al, 0.0, None, ALU.max))
        seq(g, "vector", lambda v: v.tensor_tensor(sp(0), sp(4), sp(3), ALU.add))
        seq(g, "scalar", lambda sc: sc.activation(out=sp(1), in_=sdt, func=AF.Abs))
        seq(g, "scalar", lambda sc: sc.activation(out=sp(2), in_=sp(1), func=AF.Exp, scale=-1.0))
        seq(g, "scalar", lambda sc: sc.activation(out=sp(3), in_=sp(2), func=AF.Ln, bias=sOne[:]))
        seq(g, "vector", lambda v: v.tensor_scalar(sp(4), sdt, 0.0, None, ALU.max))
        seq(g, "vector", lambda v: v.tensor_tensor(sp(5), sp(4), sp(3), ALU.add))
        seq(g, "vector", lambda v: v.tensor_tensor(sp(5), sp(5), sLds4[:], ALU.mult))
        seq(g, "vector", lambda v: v.tensor_scalar(sp(5), sp(5), 1e-4, None, ALU.add))
        seq(g, "scalar", lambda sc: sc.activation(out=sp(6), in_=om, func=AF.Tanh))
        seq(g, "vector", lambda v: v.tensor_scalar(sp(6), sp(6), 1.0, None, ALU.add))
        seq(g, "vector", lambda v: v.tensor_tensor(sp(6), sp(6), sFrq4[:], ALU.mult))
        seq(g, "vector", lambda v: v.tensor_tensor(sp(7), sp(5), sp(0), ALU.mult))
        seq(g, "vector", lambda v: v.tensor_scalar(sp(7), sp(7), 0.5, None, ALU.mult))
        seq(g, "vector", lambda v: v.tensor_tensor(sp(8), sp(5), sp(6), ALU.mult))
        seq(g, "vector", lambda v: v.tensor_scalar(sp(8), sp(8), 0.5, None, ALU.mult))
        seq(g, "vector", lambda v: v.tensor_scalar(sp(9), sp(7), 1.0, None, ALU.add))
        seq(g, "vector", lambda v: v.tensor_tensor(sp(9), sp(9), sp(9), ALU.mult))
        seq(g, "vector", lambda v: v.tensor_tensor(sp(10), sp(8), sp(8), ALU.mult))
        seq(g, "vector", lambda v: v.tensor_tensor(sp(9), sp(9), sp(10), ALU.add))
        seq(g, "vector", lambda v: v.reciprocal(sp(10), sp(9)))
        seq(g, "vector", lambda v: v.tensor_tensor(sp(11), sp(7), sp(7), ALU.mult))
        seq(g, "vector", lambda v: v.tensor_tensor(sp(13), sp(8), sp(8), ALU.mult))
        seq(g, "vector", lambda v: v.tensor_tensor(sp(11), sp(11), sp(13), ALU.add))
        seq(g, "vector", lambda v: v.tensor_scalar(sp(11), sp(11), -1.0, 1.0, ALU.mult, ALU.add))
        seq(g, "vector", lambda v: v.tensor_tensor(sp(11), sp(11), sp(10), ALU.mult))
        seq(g, "vector", lambda v: v.tensor_tensor(sp(12), sp(8), sp(10), ALU.mult))
        seq(g, "vector", lambda v: v.tensor_scalar(sp(12), sp(12), 2.0, None, ALU.mult))
        seq(g, "vector", lambda v: v.tensor_tensor(sp(13), sp(11), sp(11), ALU.mult))
        seq(g, "vector", lambda v: v.tensor_tensor(sp(14), sp(12), sp(12), ALU.mult))
        seq(g, "vector", lambda v: v.tensor_tensor(sp(13), sp(13), sp(14), ALU.add))
        seq(g, "scalar", lambda sc: sc.activation(out=sp(13), in_=sp(13), func=AF.Sqrt))
        seq(g, "vector", lambda v: v.tensor_scalar(sp(15), sp(11), 1e-30, None, ALU.add))
        seq(g, "vector", lambda v: v.reciprocal(sp(15), sp(15)))
        seq(g, "vector", lambda v: v.tensor_tensor(sp(15), sp(12), sp(15), ALU.mult))
        seq(g, "scalar", lambda sc: sc.activation(out=sp(14), in_=sp(15), func=AF.Arctan))
        seq(g, "vector", lambda v: v.tensor_scalar(sp(15), sp(11), 0.0, None, ALU.is_lt))
        seq(g, "vector", lambda v: v.tensor_scalar(sp(15), sp(15), PI, None, ALU.mult))
        seq(g, "vector", lambda v: v.tensor_tensor(sp(14), sp(14), sp(15), ALU.add))
        seq(g, "scalar", lambda sc: sc.activation(out=sp(15), in_=rr, func=AF.Sigmoid))
        seq(g, "vector", lambda v: v.tensor_scalar(sp(17), sp(13), 1e-8, None, ALU.max))
        seq(g, "vector", lambda v: v.tensor_scalar(sp(17), sp(17), 1.0, None, ALU.min))
        seq(g, "scalar", lambda sc: sc.activation(out=sp(17), in_=sp(17), func=AF.Ln))
        seq(g, "vector", lambda v: v.tensor_tensor(sp(16), sp(15), sp(17), ALU.mult))
        seq(g, "vector", lambda v: v.tensor_scalar(sp(16), sp(16), 8.0, None, ALU.mult))
        seq(g, "scalar", lambda sc: sc.activation(out=sp(17), in_=sp(16), func=AF.Exp))
        seq(g, "vector", lambda v: v.tensor_tensor(sp(18), sp(17), sp(17), ALU.mult))
        seq(g, "vector", lambda v: v.tensor_scalar(sp(18), sp(18), -1.0, 1.0, ALU.mult, ALU.add))
        seq(g, "vector", lambda v: v.tensor_scalar(sp(18), sp(18), 0.0, None, ALU.max))
        seq(g, "vector", lambda v: v.tensor_scalar(sp(18), sp(18), 1.0, None, ALU.min))
        seq(g, "vector", lambda v: v.tensor_scalar(sp(18), sp(18), 1e-6, None, ALU.add))
        seq(g, "scalar", lambda sc: sc.activation(out=sp(18), in_=sp(18), func=AF.Sqrt))
        seq(g, "scalar", lambda sc: sc.activation(out=sp(19), in_=sB_, func=AF.Sigmoid))
        seq(g, "vector", lambda v: v.tensor_tensor(sp(18), sp(18), sp(19), ALU.mult))
        seq(g, "scalar", lambda sc: sc.activation(out=sp(19), in_=bt, func=AF.Sigmoid))
        seq(g, "scalar", lambda sc: sc.activation(out=sp(20), in_=sC_, func=AF.Sigmoid))

        def vgm(v):
            v.tensor_tensor(sV4[:, :, :, 0], sV4[:, :, :, 0], sp(18), ALU.mult)
            return v.tensor_tensor(sV4[:, :, :, 1], sV4[:, :, :, 1], sp(18), ALU.mult)
        g.add("vector", vgm)
        g.bar()

        def gsca(v):
            v.memset(sGsc4[:].rearrange("p a b c -> p (a b c)"), 0.0)
            return None
        seq(g, "vector", lambda v: v.memset(sGsc4[:].rearrange("p a b c -> p (a b c)"), 0.0))

        def gscb(v):
            v.tensor_copy(sGsc4[:, :, :, 0], sp(16))
            v.tensor_copy(sGsc4[:, :, :, 1], sp(14))
            v.tensor_copy(sGsc4[:, :, :, 2], sp(19))
            v.tensor_copy(sGsc4[:, :, :, 3], sV4[:, :, :, 0])
            v.tensor_copy(sGsc4[:, :, :, 4], sV4[:, :, :, 1])
            v.tensor_copy(sGsc4[:, :, :, 5], sp(20))
            v.tensor_copy(sGlt4[:, :, :, 0], sp(16))
            return v.tensor_copy(sGlt4[:, :, :, 1], sp(14))
        g.add("vector", gscb)
        g.bar()
        g.add("sync", lambda s, b=b: s.dma_start(
            out=gsc[b].rearrange("(q p) h s -> p q h s", p=128), in_=sGsc4[:]), dma=True)
        for q in range(4):
            g.add("sync", lambda s, b=b, q=q: s.dma_start(
                out=glt[b, :, 128 * q:128 * (q + 1), :].rearrange("h l s -> l h s"),
                in_=sGlt4[:, q]), dma=True)
        g.bar()
        sqslot = [sT1, sT2]
        NQ = 16
        for i in range(NQ + 8):
            def qst(t, i=i):
                ii = None
                if i < NQ:
                    for ec in range(16):
                        ii = t.matmul(pP[:, i % 3], sQw[:, ec, i], sXc[:, ec],
                                      start=(ec == 0), stop=(ec == 15))
                if 0 <= i - 2 < NQ:
                    ii = t.matmul(pN[0:1, (i - 2) % 2, :], C["sOnes32"][:], sqslot[(i - 2) % 2][:],
                                  start=True, stop=True)
                if 0 <= i - 5 < NQ:
                    ii = t.matmul(pB[:, (i - 5) % 2, :], C["sOnesRow"][:],
                                  C["sStat"][0:1, (i - 5) % 4, :], start=True, stop=True)
                return ii
            if i < NQ or 0 <= i - 2 < NQ or 0 <= i - 5 < NQ:
                g.add("tensor", qst)

            def qsa(sc, i=i):
                ii = None
                if 0 <= i - 1 < NQ:
                    ii = sc.activation(out=sqslot[(i - 1) % 2][:], in_=pP[:, (i - 1) % 3], func=AF.Square)
                if 0 <= i - 3 < NQ:
                    ii = sc.activation(out=C["sStat"][0:1, (i - 3) % 4, :], in_=pN[0:1, (i - 3) % 2, :],
                                       func=AF.Sqrt)
                return ii
            if 0 <= i - 1 < NQ or 0 <= i - 3 < NQ:
                g.add("scalar", qsa)

            def qsv(v, i=i):
                ii = None
                if 0 <= i - 2 < NQ:
                    ii = v.tensor_copy(sQs[:, (i - 2) % 5, :], pP[:, (i - 2) % 3])
                if 0 <= i - 4 < NQ:
                    sl = C["sStat"][0:1, (i - 4) % 4, :]
                    v.tensor_scalar(sl, sl, 1e-6, None, ALU.add)
                    ii = v.reciprocal(sl, sl)
                if 0 <= i - 6 < NQ:
                    ii = v.tensor_tensor(sTB2[:, (i - 6) % 2], sQs[:, (i - 6) % 5, :],
                                         pB[:, (i - 6) % 2, :], ALU.mult)
                return ii
            if 0 <= i - 2 < NQ or 0 <= i - 4 < NQ or 0 <= i - 6 < NQ:
                g.add("vector", qsv)
            if 0 <= i - 7 < NQ:
                g.add("sync", lambda s, b=b, i=i: s.dma_start(out=qnf[b, i - 7],
                                                              in_=sTB2[:, (i - 7) % 2]), dma=True)
            g.bar()

        for i in range(NQ + 7):
            def kst(t, i=i):
                ii = None
                if 0 <= i - 1 < NQ:
                    ii = t.matmul(pN[0:1, (i - 1) % 2, :], C["sOnes32"][:], sqslot[(i - 1) % 2][:],
                                  start=True, stop=True)
                if 0 <= i - 4 < NQ:
                    ii = t.matmul(pB[:, (i - 4) % 2, :], C["sOnesRow"][:],
                                  C["sStat"][0:1, (i - 4) % 4, :], start=True, stop=True)
                return ii
            if 0 <= i - 1 < NQ or 0 <= i - 4 < NQ:
                g.add("tensor", kst)

            def ksa(sc, i=i):
                ii = None
                if i < NQ:
                    ii = sc.activation(out=sqslot[i % 2][:], in_=sK[:, i], func=AF.Square)
                if 0 <= i - 2 < NQ:
                    ii = sc.activation(out=C["sStat"][0:1, (i - 2) % 4, :], in_=pN[0:1, (i - 2) % 2, :],
                                       func=AF.Sqrt)
                return ii
            if i < NQ or 0 <= i - 2 < NQ:
                g.add("scalar", ksa)

            def ksv(v, i=i):
                ii = None
                if 0 <= i - 3 < NQ:
                    sl = C["sStat"][0:1, (i - 3) % 4, :]
                    v.tensor_scalar(sl, sl, 1e-6, None, ALU.add)
                    ii = v.reciprocal(sl, sl)
                if 0 <= i - 5 < NQ:
                    ii = v.tensor_tensor(sTB2[:, (i - 5) % 2], sK[:, i - 5],
                                         pB[:, (i - 5) % 2, :], ALU.mult)
                return ii
            if 0 <= i - 3 < NQ or 0 <= i - 5 < NQ:
                g.add("vector", ksv)
            if 0 <= i - 6 < NQ:
                g.add("sync", lambda s, b=b, i=i: s.dma_start(out=knf[b, i - 6],
                                                              in_=sTB2[:, (i - 6) % 2]), dma=True)
            g.bar()
    return g.build()




INV2PI = float(1.0 / (2 * np.pi))
HALF_PI = float(np.pi / 2)


def build_b3(clo, chi):
    NL = chi - clo
    g = G()
    kn8 = g.io("kn8", [BHC8, 128, L], BF16)
    qn8 = g.io("qn8", [BHC8, 128, L], BF16)
    gsc8 = g.io("gsc8", [BHC8, L, 8])
    glt8 = g.io("glt8", [BHC8, L, 2])
    brow8 = g.io("brow8", [BHC8, NCH, 128])
    sin0 = g.io("sin0", [128, BHC8, 2])
    ident = g.io("ident", [128, 128])
    identb = g.io("identb", [128, 128], BF16)
    rts = g.io("rts", [BHC8, NL * 128, 2], out=True)
    sout = g.io("sout", [128, BHC8, 2], out=True)

    sKc = g.sb("sKc", [128, 8, 128], BF16)
    sQc = g.sb("sQc", [128, 8, 128], BF16)
    sKT = g.sb("sKT", [128, 8, 128], BF16)
    sAp = g.sb("sAp", [128, 8, 7, 3, 128], BF16)
    sBp = g.sb("sBp", [128, 8, 2, 3, 128], BF16)
    sMr = g.sb("sMr", [128, 8, 3, 128], BF16)
    sDR = g.sb("sDR", [128, 8, 128])
    sDC = g.sb("sDC", [128, 8, 128])
    sC1 = g.sb("sC1", [128, 8, 128])
    sC2 = g.sb("sC2", [128, 8, 128])
    sYC = g.sb("sYC", [128, 8, 128])
    sYS = g.sb("sYS", [128, 8, 128])
    sEI = g.sb("sEI", [128, 8, 128])
    sET = g.sb("sET", [128, 8, 128])
    sBS = g.sb("sBS", [128, 8, 128])
    sBT = g.sb("sBT", [128, 8, 128])
    sKbS = g.sb("sKbS", [128, 8, 128])
    sKbT = g.sb("sKbT", [128, 8, 128])
    sMkS8 = g.sb("sMkS8", [128, 8, 128])
    sMkI8 = g.sb("sMkI8", [128, 8, 128])
    sMkST = g.sb("sMkST", [128, 128])
    sIotaI = g.sb("sIotaI", [128, 128], I32)
    sColI = g.sb("sColI", [128, 1], I32)
    sIotaF = g.sb("sIotaF", [128, 128])
    sColF = g.sb("sColF", [128, 1])
    sMk1 = g.sb("sMk1", [128, 128])
    sGv = g.sb("sGv", [128, 8, NL, 3])
    sGlc = g.sb("sGlc", [NL, 8, 128, 2])
    sBrow32 = g.sb("sBrow32", [NL, 8, 128])
    sRho32 = g.sb("sRho32", [NL, 8, 128])
    sPh32 = g.sb("sPh32", [NL, 8, 128])
    sRhoT = g.sb("sRhoT", [128, 8, NL])
    sPhT = g.sb("sPhT", [128, 8, NL])
    sRowR = g.sb("sRowR", [1, 8, 128])
    sRowP = g.sb("sRowP", [1, 8, 128])
    sRowB = g.sb("sRowB", [1, 8, 128])
    sRowSK = g.sb("sRowSK", [2, 8, 128])
    sRowSQ = g.sb("sRowSQ", [2, 8, 128])
    sIdent32 = g.sb("sIdent32", [128, 128])
    sIdentB = g.sb("sIdentB", [128, 128], BF16)
    sOnesRow = g.sb("sOnesRow", [1, 128])
    sS0 = g.sb("sS0", [128, 8, 2])
    sS0b = g.sb("sS0b", [128, 8, 2], BF16)
    sWv = g.sb("sWv", [128, 8, 2])
    sWb = g.sb("sWb", [128, 8, 2], BF16)
    sUv = g.sb("sUv", [128, 8, 2])
    sUb = g.sb("sUb", [128, 8, 2], BF16)
    sW2 = g.sb("sW2", [128, 8, 2])
    sW2b = g.sb("sW2b", [128, 8, 2], BF16)
    sSKt = g.sb("sSKt", [128, 8, 2])
    sSQt = g.sb("sSQt", [128, 8, 2])
    sEr = g.sb("sEr", [128, 8])
    sLr = g.sb("sLr", [128, 8])
    sLi = g.sb("sLi", [128, 8])
    sSc1 = g.sb("sSc1", [128, 8])
    sSc2 = g.sb("sSc2", [128, 8])
    sT1v = g.sb("sT1v", [128, 8])
    sT2v = g.sb("sT2v", [128, 8])
    sT3v = g.sb("sT3v", [128, 8])
    sT4v = g.sb("sT4v", [128, 8])
    sT5v = g.sb("sT5v", [128, 8])
    sT6v = g.sb("sT6v", [128, 8])
    sYc8 = g.sb("sYc8", [128, 8])
    sYs8 = g.sb("sYs8", [128, 8])
    sLrowS = g.sb("sLrowS", [1, 8, 4])
    sLrowT = g.sb("sLrowT", [1, 8, 2])
    sLamB = g.sb("sLamB", [128, 8, 4])
    sRall = g.sb("sRall", [128, 8, NL, 2])

    pA = g.ps("pA", [128, 8, 128])
    pB = g.ps("pB", [128, 8, 128])
    pC = g.ps("pC", [128, 8, 128])
    pD = g.ps("pD", [128, 8, 128])

    g.add("sync", lambda s: s.dma_start(out=sIdent32[:], in_=ident[:]), dma=True)
    g.add("sync", lambda s: s.dma_start(out=sIdentB[:], in_=identb[:]), dma=True)
    g.add("sync", lambda s: s.dma_start(out=sS0[:], in_=sin0[:]), dma=True)
    for bhi in range(8):
        g.add("sync", lambda s, bhi=bhi: s.dma_start(
            out=sGv[:, bhi], in_=gsc8[bhi].rearrange("(c t) s -> t c s", t=128)[:, clo:chi, 2:5]), dma=True)
        g.add("sync", lambda s, bhi=bhi: s.dma_start(
            out=sGlc[:, bhi], in_=glt8[bhi].rearrange("(c t) s -> c t s", t=128)[clo:chi]), dma=True)
        g.add("sync", lambda s, bhi=bhi: s.dma_start(
            out=sBrow32[:, bhi], in_=brow8[bhi, clo:chi]), dma=True)
    g.add("gpsimd", lambda gp: gp.iota(sIotaI[:], pattern=[[1, 128]], base=0, channel_multiplier=0))
    g.add("gpsimd", lambda gp: gp.iota(sColI[:], pattern=[[0, 1]], base=0, channel_multiplier=1))
    g.add("vector", lambda v: v.memset(sOnesRow[:], 1.0))
    g.bar()
    seq(g, "vector", lambda v: v.tensor_copy(sIotaF[:], sIotaI[:]))
    seq(g, "vector", lambda v: v.tensor_copy(sColF[:], sColI[:]))
    seq(g, "vector", lambda v: v.tensor_scalar(sMk1[:], sIotaF[:], sColF[:], None, ALU.is_gt))
    for bhi in range(8):
        g.add("vector", lambda v, bhi=bhi: v.tensor_copy(sMkS8[:, bhi], sMk1[:]))
    g.bar()
    seq(g, "vector", lambda v: v.tensor_scalar(sMk1[:], sIotaF[:], sColF[:], None, ALU.is_ge))
    for bhi in range(8):
        g.add("vector", lambda v, bhi=bhi: v.tensor_copy(sMkI8[:, bhi], sMk1[:]))
    g.bar()
    seq(g, "vector", lambda v: v.tensor_scalar(sMkST[:], sIotaF[:], sColF[:], None, ALU.is_lt))
    seq(g, "vector", lambda v: v.tensor_copy(sS0b[:], sS0[:]))
    for bhi in range(8):
        g.add("vector", lambda v, bhi=bhi: v.tensor_tensor_scan(
            sRho32[:, bhi], sGlc[:, bhi, :, 0], sGlc[:, bhi, :, 0], 0.0, ALU.add, ALU.bypass))
    g.bar()
    for bhi in range(8):
        g.add("vector", lambda v, bhi=bhi: v.tensor_tensor_scan(
            sPh32[:, bhi], sGlc[:, bhi, :, 1], sGlc[:, bhi, :, 1], 0.0, ALU.add, ALU.bypass))
    g.bar()
    def trs(t):
        ii = None
        for bhi in range(8):
            ii = t.matmul(pA[:, bhi, 0:NL], sRho32[0:NL, bhi, :], sIdent32[0:NL, 0:NL], start=True, stop=True)
            ii = t.matmul(pB[:, bhi, 0:NL], sPh32[0:NL, bhi, :], sIdent32[0:NL, 0:NL], start=True, stop=True)
        return ii
    g.add("tensor", trs)
    g.bar()
    seq(g, "vector", lambda v: v.tensor_copy(sRhoT[:], pA[:, :, 0:NL]))
    seq(g, "vector", lambda v: v.tensor_copy(sPhT[:], pB[:, :, 0:NL]))

    VW = [128, 8 * 128]

    def bview(tens):
        return tens[:]

    for cl in range(NL):
        ca = clo + cl
        g.add("sync", lambda s, ca=ca: s.dma_start(
            out=sKc[:], in_=kn8[:, :, 128 * ca:128 * (ca + 1)].rearrange("h d t -> d h t")), dma=True)
        g.add("sync", lambda s, ca=ca: s.dma_start(
            out=sQc[:], in_=qn8[:, :, 128 * ca:128 * (ca + 1)].rearrange("h d t -> d h t")), dma=True)
        g.add("sync", lambda s, cl=cl: s.dma_start(out=sRowR[:], in_=sRho32[cl:cl + 1]), dma=True)
        g.add("sync", lambda s, cl=cl: s.dma_start(out=sRowP[:], in_=sPh32[cl:cl + 1]), dma=True)
        g.add("sync", lambda s, cl=cl: s.dma_start(out=sRowB[:], in_=sBrow32[cl:cl + 1]), dma=True)
        g.bar()
        def stB(t):
            ii = None
            for bhi in range(8):
                ii = t.matmul(pA[:, bhi, :], sOnesRow[:], sRowR[0:1, bhi, :], start=True, stop=True)
                ii = t.matmul(pB[:, bhi, :], sOnesRow[:], sRowP[0:1, bhi, :], start=True, stop=True)
                ii = t.matmul(pC[:, bhi, :], sOnesRow[:], sRowB[0:1, bhi, :], start=True, stop=True)
                ii = t.matmul(pD[:, bhi, :], sKc[:, bhi, :], sKc[:, bhi, :], start=True, stop=True)
            return ii
        g.add("tensor", stB)
        g.bar()
        def stC(v, cl=cl):
            ii = None
            for bhi in range(8):
                v.tensor_scalar(sDR[:, bhi], pA[:, bhi, :], sRhoT[:, bhi, cl:cl + 1], None, ALU.subtract)
                ii = v.tensor_scalar(sDC[:, bhi], pB[:, bhi, :], sPhT[:, bhi, cl:cl + 1], None, ALU.subtract)
            return ii
        g.add("vector", stC)
        g.bar()
        def stD(v, cl=cl):
            v.tensor_tensor(bview(sBS), pC[:], bview(sMkS8), ALU.mult)
            for bhi in range(8):
                v.tensor_scalar(sBT[:, bhi], sMkST[:], sGv[:, bhi, cl, 0:1].opt(), None, ALU.mult)
            v.tensor_scalar(bview(sEI), bview(sDR), 0.0, None, ALU.min)
            return v.tensor_scalar(bview(sET), bview(sDR), 0.0, -1.0, ALU.max, ALU.mult)
        g.add("vector", stD)
        g.bar()
        g.add("scalar", lambda sc: sc.activation(out=bview(sEI), in_=bview(sEI), func=AF.Exp))
        g.add("scalar", lambda sc: sc.activation(out=bview(sET), in_=bview(sET), func=AF.Exp))

        def stE(v):
            v.tensor_tensor(bview(sKbS), pD[:], bview(sBS), ALU.mult)
            v.tensor_tensor(bview(sKbT), pD[:], bview(sBT), ALU.mult)
            v.tensor_scalar(bview(sC1), bview(sDC), INV2PI, MAGIC + 0.25, ALU.mult, ALU.add)
            return v.tensor_scalar(bview(sC2), bview(sDC), INV2PI, MAGIC, ALU.mult, ALU.add)
        g.add("vector", stE)
        g.bar()
        def stF(v):
            v.tensor_scalar(bview(sC1), bview(sC1), MAGIC, TWO_PI, ALU.subtract, ALU.mult)
            v.tensor_scalar(bview(sC2), bview(sC2), MAGIC, TWO_PI, ALU.subtract, ALU.mult)
            v.tensor_tensor(bview(sKbS), bview(sKbS), bview(sEI), ALU.mult)
            return v.tensor_tensor(bview(sKbT), bview(sKbT), bview(sET), ALU.mult)
        g.add("vector", stF)
        g.bar()
        def stG(v):
            v.scalar_tensor_tensor(bview(sYC), bview(sDC), HALF_PI, bview(sC1), ALU.add, ALU.subtract)
            return v.tensor_tensor(bview(sYS), bview(sDC), bview(sC2), ALU.subtract)
        g.add("vector", stG)
        g.bar()
        g.add("scalar", lambda sc: sc.activation(out=bview(sYC), in_=bview(sYC), func=AF.Sin))
        g.add("scalar", lambda sc: sc.activation(out=bview(sYS), in_=bview(sYS), func=AF.Sin, scale=-1.0))
        g.bar()
        def stI(v):
            v.tensor_tensor(sAp[:, :, 0, 0, :], bview(sKbS), bview(sYC), ALU.mult)
            v.tensor_tensor(sAp[:, :, 0, 1, :], bview(sKbS), bview(sYS), ALU.mult)
            v.tensor_tensor(sBp[:, :, 0, 0, :], bview(sKbT), bview(sYC), ALU.mult)
            return v.tensor_tensor(sBp[:, :, 0, 2, :], bview(sKbT), bview(sYS), ALU.mult)
        g.add("vector", stI)
        g.bar()
        def stJ(v):
            v.tensor_scalar(sAp[:, :, 0, 2, :],
                            sAp[:, :, 0, 1, :], -1.0, None, ALU.mult)
            return v.tensor_scalar(sBp[:, :, 0, 1, :],
                                   sBp[:, :, 0, 2, :], -1.0, None, ALU.mult)
        g.add("vector", stJ)

        def stJm(t):
            ii = None
            for bhi in range(8):
                ii = t.matmul(pB[:, bhi, :], sKc[:, bhi, :], sQc[:, bhi, :], start=True, stop=True)
                ii = t.matmul(pC[:, bhi, :], sKc[:, bhi, :], sIdentB[:], start=True, stop=True)
            return ii
        g.add("tensor", stJm)
        g.bar()
        g.add("vector", lambda v: v.tensor_tensor(bview(sC1), pB[:],
                                                  bview(sMkI8), ALU.mult))
        g.add("scalar", lambda sc: sc.activation(out=bview(sKT), in_=pC[:],
                                                 func=AF.Copy))
        g.bar()
        g.add("vector", lambda v: v.tensor_tensor(bview(sC1), bview(sC1), bview(sEI), ALU.mult))
        g.bar()
        def stM(v):
            v.tensor_tensor(sMr[:, :, 0, :], bview(sC1), bview(sYC), ALU.mult)
            return v.tensor_tensor(sMr[:, :, 1, :], bview(sC1), bview(sYS), ALU.mult)
        g.add("vector", stM)
        g.bar()
        g.add("vector", lambda v: v.tensor_scalar(sMr[:, :, 2, :],
                                                  sMr[:, :, 1, :], -1.0, None, ALU.mult))
        g.bar()

        def sqmm(t, lv, h):
            pb = (lv - 1) % 2
            last = (lv == 6)
            ii = None
            for bhi in range(4 * h, 4 * h + 4):
                Ar = sAp[:, bhi, lv - 1, 0, :]
                Ai = sAp[:, bhi, lv - 1, 1, :]
                An = sAp[:, bhi, lv - 1, 2, :]
                Br = sBp[:, bhi, pb, 0, :]
                Bi = sBp[:, bhi, pb, 1, :]
                t.matmul(pA[:, bhi, :], Br, Ar, start=True, stop=False)
                t.matmul(pA[:, bhi, :], Bi, An, start=False, stop=True)
                t.matmul(pB[:, bhi, :], Br, Ai, start=True, stop=False)
                ii = t.matmul(pB[:, bhi, :], Bi, Ar, start=False, stop=True)
                if not last:
                    t.matmul(pC[:, bhi, :], Ar, Br, start=True, stop=False)
                    t.matmul(pC[:, bhi, :], An, Bi, start=False, stop=True)
                    t.matmul(pD[:, bhi, :], Ar, Bi, start=True, stop=False)
                    ii = t.matmul(pD[:, bhi, :], Ai, Br, start=False, stop=True)
            return ii

        def sqev_a(sc, lv, h):
            nb = lv % 2
            last = (lv == 6)
            hs = slice(4 * h, 4 * h + 4)
            ii = sc.activation(out=sAp[:, hs, lv, 1, :], in_=pB[:, hs, :], func=AF.Copy)
            if not last:
                ii = sc.activation(out=sBp[:, hs, nb, 1, :], in_=pD[:, hs, :], func=AF.Copy)
            return ii

        def sqev_v(v, lv, h):
            nb = lv % 2
            last = (lv == 6)
            hs = slice(4 * h, 4 * h + 4)
            v.tensor_copy(sAp[:, hs, lv, 0, :], pA[:, hs, :])
            if not last:
                v.tensor_copy(sBp[:, hs, nb, 0, :], pC[:, hs, :])
            return v.tensor_scalar(sAp[:, hs, lv, 2, :], sAp[:, hs, lv, 1, :], -1.0, None, ALU.mult)

        slots = [(lv, h) for lv in range(1, 7) for h in range(2)]
        for k2 in range(2 * len(slots) + 2):
            k, ph = divmod(k2, 2)
            if ph == 0:
                if k < len(slots):
                    g.add("tensor", lambda t, s=slots[k]: sqmm(t, s[0], s[1]))
                if k >= 1:
                    g.add("scalar", lambda sc, s=slots[k - 1]: sqev_a(sc, s[0], s[1]))
            else:
                if k >= 1:
                    g.add("vector", lambda v, s=slots[k - 1]: sqev_v(v, s[0], s[1]))
            g.bar()

        def s1(t):
            ii = None
            for bhi in range(8):
                ii = t.matmul(pA[0:2, bhi, :], sS0b[:, bhi, :], sKc[:, bhi, :], start=True, stop=True)
                ii = t.matmul(pB[0:2, bhi, :], sS0b[:, bhi, :], sQc[:, bhi, :], start=True, stop=True)
            return ii
        g.add("tensor", s1)
        g.bar()

        def s2(v):
            v.tensor_copy(sRowSK[:], pA[0:2, :, :])
            return v.tensor_copy(sRowSQ[:], pB[0:2, :, :])
        g.add("vector", s2)
        g.bar()

        def s3(t):
            ii = None
            for bhi in range(8):
                ii = t.matmul(pC[:, bhi, 0:2], sRowSK[0:2, bhi, :], sIdent32[0:2, 0:2], start=True, stop=True)
                ii = t.matmul(pC[:, bhi, 2:4], sRowSQ[0:2, bhi, :], sIdent32[0:2, 0:2], start=True, stop=True)
            return ii
        g.add("tensor", s3)
        g.bar()

        def s4(v, cl=cl):
            v.tensor_copy(sSKt[:], pC[:, :, 0:2])
            v.tensor_copy(sSQt[:], pC[:, :, 2:4])
            v.tensor_scalar(sSc1[:], sPhT[:, :, cl], INV2PI, MAGIC + 0.25, ALU.mult, ALU.add)
            return v.tensor_scalar(sSc2[:], sPhT[:, :, cl], INV2PI, MAGIC, ALU.mult, ALU.add)
        g.add("vector", s4)
        g.add("scalar", lambda sc, cl=cl: sc.activation(out=sEr[:], in_=sRhoT[:, :, cl], func=AF.Exp))
        g.bar()

        def s5(v):
            v.tensor_scalar(sSc1[:], sSc1[:], MAGIC, TWO_PI, ALU.subtract, ALU.mult)
            return v.tensor_scalar(sSc2[:], sSc2[:], MAGIC, TWO_PI, ALU.subtract, ALU.mult)
        g.add("vector", s5)
        g.bar()

        def s6(v, cl=cl):
            v.scalar_tensor_tensor(sYc8[:], sPhT[:, :, cl], HALF_PI, sSc1[:], ALU.add, ALU.subtract)
            return v.tensor_tensor(sYs8[:], sPhT[:, :, cl], sSc2[:], ALU.subtract)
        g.add("vector", s6)
        g.bar()
        g.add("scalar", lambda sc: sc.activation(out=sYc8[:], in_=sYc8[:], func=AF.Sin))
        g.add("scalar", lambda sc: sc.activation(out=sYs8[:], in_=sYs8[:], func=AF.Sin, scale=-1.0))
        g.bar()

        def s8(v):
            v.tensor_tensor(sLr[:], sEr[:], sYc8[:], ALU.mult)
            return v.tensor_tensor(sLi[:], sEr[:], sYs8[:], ALU.mult)
        g.add("vector", s8)
        g.bar()

        def s9(v):
            v.tensor_tensor(sT1v[:], sLr[:], sSKt[:, :, 0], ALU.mult)
            return v.tensor_tensor(sT2v[:], sLi[:], sSKt[:, :, 1], ALU.mult)
        g.add("vector", s9)
        g.bar()

        def s10(v):
            v.tensor_tensor(sT3v[:], sT1v[:], sT2v[:], ALU.subtract)
            v.tensor_tensor(sT1v[:], sLr[:], sSKt[:, :, 1], ALU.mult)
            return v.tensor_tensor(sT2v[:], sLi[:], sSKt[:, :, 0], ALU.mult)
        g.add("vector", s10)
        g.bar()

        def s11(v, cl=cl):
            v.tensor_tensor(sT4v[:], sT1v[:], sT2v[:], ALU.add)
            return v.tensor_tensor(sT1v[:], sGv[:, :, cl, 1], sT3v[:], ALU.subtract)
        g.add("vector", s11)
        g.bar()
        g.add("vector", lambda v, cl=cl: v.tensor_tensor(sT2v[:], sGv[:, :, cl, 2], sT4v[:], ALU.subtract))
        g.bar()

        def s13(v, cl=cl):
            v.tensor_tensor(sWv[:, :, 0], sT1v[:], sGv[:, :, cl, 0], ALU.mult)
            return v.tensor_tensor(sWv[:, :, 1], sT2v[:], sGv[:, :, cl, 0], ALU.mult)
        g.add("vector", s13)
        g.bar()
        g.add("vector", lambda v: v.tensor_copy(sWb[:], sWv[:]))
        g.bar()

        for k in [6, 5, 4, 3, 2, 1, 0]:
            def fmm(t, k=k):
                ii = None
                for bhi in range(8):
                    t.matmul(pD[:, bhi, 0:1], sAp[:, bhi, k, 0, :], sWb[:, bhi, 0:1], start=True, stop=False)
                    t.matmul(pD[:, bhi, 0:1], sAp[:, bhi, k, 2, :], sWb[:, bhi, 1:2], start=False, stop=True)
                    t.matmul(pD[:, bhi, 1:2], sAp[:, bhi, k, 0, :], sWb[:, bhi, 1:2], start=True, stop=False)
                    ii = t.matmul(pD[:, bhi, 1:2], sAp[:, bhi, k, 1, :], sWb[:, bhi, 0:1], start=False, stop=True)
                return ii
            g.add("tensor", fmm)
            g.bar()
            if k > 0:
                g.add("vector", lambda v: v.tensor_tensor(
                    sWv[:], sWv[:], pD[:, :, 0:2], ALU.add))
                g.bar()
                g.add("vector", lambda v: v.tensor_copy(sWb[:], sWv[:]))
                g.bar()
            else:
                g.add("vector", lambda v: v.tensor_tensor(
                    sUv[:], sWv[:], pD[:, :, 0:2], ALU.subtract))
                g.bar()
                g.add("vector", lambda v: v.tensor_copy(sUb[:], sUv[:]))
                g.bar()

        def h1(t):
            ii = None
            for bhi in range(8):
                t.matmul(pC[:, bhi, 4:5], sMr[:, bhi, 0, :], sUb[:, bhi, 0:1], start=True, stop=False)
                t.matmul(pC[:, bhi, 4:5], sMr[:, bhi, 2, :], sUb[:, bhi, 1:2], start=False, stop=True)
                t.matmul(pC[:, bhi, 5:6], sMr[:, bhi, 0, :], sUb[:, bhi, 1:2], start=True, stop=False)
                ii = t.matmul(pC[:, bhi, 5:6], sMr[:, bhi, 1, :], sUb[:, bhi, 0:1], start=False, stop=True)
            return ii
        g.add("tensor", h1)

        def h1v(v):
            v.tensor_tensor(sT1v[:], sLr[:], sSQt[:, :, 0], ALU.mult)
            return v.tensor_tensor(sT2v[:], sLi[:], sSQt[:, :, 1], ALU.mult)
        g.add("vector", h1v)
        g.bar()

        def h2(v):
            v.tensor_tensor(sT3v[:], sT1v[:], sT2v[:], ALU.subtract)
            v.tensor_tensor(sT1v[:], sLr[:], sSQt[:, :, 1], ALU.mult)
            return v.tensor_tensor(sT2v[:], sLi[:], sSQt[:, :, 0], ALU.mult)
        g.add("vector", h2)
        g.bar()
        g.add("vector", lambda v: v.tensor_tensor(sT4v[:], sT1v[:], sT2v[:], ALU.add))
        g.bar()

        def h4(v, cl=cl):
            v.tensor_tensor(sRall[:, :, cl, 0], sT3v[:], pC[:, :, 4:5].opt(), ALU.add)
            return v.tensor_tensor(sRall[:, :, cl, 1], sT4v[:], pC[:, :, 5:6].opt(), ALU.add)
        g.add("vector", h4)
        g.bar()

        def i1(v):
            v.tensor_copy(sLrowS[0:1, :, 2], sRowR[0:1, :, 127])
            return v.tensor_copy(sLrowS[0:1, :, 3], sRowP[0:1, :, 127])
        g.add("vector", i1)
        g.bar()

        def i2(v):
            v.tensor_scalar(sLrowT[0:1, :, 0], sLrowS[0:1, :, 3], INV2PI, MAGIC + 0.25, ALU.mult, ALU.add)
            return v.tensor_scalar(sLrowT[0:1, :, 1], sLrowS[0:1, :, 3], INV2PI, MAGIC, ALU.mult, ALU.add)
        g.add("vector", i2)
        g.bar()

        def i3(v):
            v.tensor_scalar(sLrowT[0:1, :, 0], sLrowT[0:1, :, 0], MAGIC, TWO_PI, ALU.subtract, ALU.mult)
            return v.tensor_scalar(sLrowT[0:1, :, 1], sLrowT[0:1, :, 1], MAGIC, TWO_PI, ALU.subtract, ALU.mult)
        g.add("vector", i3)
        g.bar()

        def i4(v):
            v.scalar_tensor_tensor(sLrowT[0:1, :, 0], sLrowS[0:1, :, 3], HALF_PI, sLrowT[0:1, :, 0],
                                   ALU.add, ALU.subtract)
            return v.tensor_tensor(sLrowT[0:1, :, 1], sLrowS[0:1, :, 3], sLrowT[0:1, :, 1], ALU.subtract)
        g.add("vector", i4)
        g.bar()

        def i5(sc):
            sc.activation(out=sLrowT[0:1, :, 0], in_=sLrowT[0:1, :, 0], func=AF.Sin)
            sc.activation(out=sLrowT[0:1, :, 1], in_=sLrowT[0:1, :, 1], func=AF.Sin, scale=-1.0)
            return sc.activation(out=sLrowS[0:1, :, 0], in_=sLrowS[0:1, :, 2], func=AF.Exp)
        g.add("scalar", i5)
        g.bar()

        def i6(v):
            v.tensor_tensor(sLrowS[0:1, :, 1], sLrowS[0:1, :, 0], sLrowT[0:1, :, 1], ALU.mult)
            return v.tensor_tensor(sLrowS[0:1, :, 0], sLrowS[0:1, :, 0], sLrowT[0:1, :, 0], ALU.mult)
        g.add("vector", i6)
        g.bar()
        g.add("tensor", lambda t: t.matmul(pD[:, 2, 0:32], sOnesRow[:],
                                           sLrowS[0:1, :, :].rearrange("p a b -> p (a b)"),
                                           start=True, stop=True))
        g.bar()
        g.add("vector", lambda v: v.tensor_copy(sLamB[:].rearrange("p a b -> p (a b)"), pD[:, 2, 0:32]))
        g.bar()

        def j1(v, cl=cl):
            v.tensor_tensor(sT1v[:], sLamB[:, :, 2], sRhoT[:, :, cl], ALU.subtract)
            return v.tensor_tensor(sT2v[:], sLamB[:, :, 3], sPhT[:, :, cl], ALU.subtract)
        g.add("vector", j1)
        g.bar()
        g.add("scalar", lambda sc: sc.activation(out=sT1v[:], in_=sT1v[:], func=AF.Exp))

        def j2(v):
            v.tensor_scalar(sT5v[:], sT2v[:], INV2PI, MAGIC + 0.25, ALU.mult, ALU.add)
            return v.tensor_scalar(sT6v[:], sT2v[:], INV2PI, MAGIC, ALU.mult, ALU.add)
        g.add("vector", j2)
        g.bar()

        def j3(v):
            v.tensor_scalar(sT5v[:], sT5v[:], MAGIC, TWO_PI, ALU.subtract, ALU.mult)
            return v.tensor_scalar(sT6v[:], sT6v[:], MAGIC, TWO_PI, ALU.subtract, ALU.mult)
        g.add("vector", j3)
        g.bar()

        def j4(v):
            v.scalar_tensor_tensor(sT5v[:], sT2v[:], HALF_PI, sT5v[:], ALU.add, ALU.subtract)
            return v.tensor_tensor(sT6v[:], sT2v[:], sT6v[:], ALU.subtract)
        g.add("vector", j4)
        g.bar()

        def j5(sc):
            sc.activation(out=sT5v[:], in_=sT5v[:], func=AF.Sin)
            return sc.activation(out=sT6v[:], in_=sT6v[:], func=AF.Sin, scale=-1.0)
        g.add("scalar", j5)
        g.bar()

        def j6(v):
            v.tensor_tensor(sT3v[:], sT1v[:], sT5v[:], ALU.mult)
            return v.tensor_tensor(sT4v[:], sT1v[:], sT6v[:], ALU.mult)
        g.add("vector", j6)
        g.bar()

        def j7(v):
            v.tensor_tensor(sT1v[:], sT3v[:], sUv[:, :, 0], ALU.mult)
            return v.tensor_tensor(sT2v[:], sT4v[:], sUv[:, :, 1], ALU.mult)
        g.add("vector", j7)
        g.bar()

        def j8(v):
            v.tensor_tensor(sW2[:, :, 0], sT1v[:], sT2v[:], ALU.subtract)
            v.tensor_tensor(sT1v[:], sT3v[:], sUv[:, :, 1], ALU.mult)
            return v.tensor_tensor(sT2v[:], sT4v[:], sUv[:, :, 0], ALU.mult)
        g.add("vector", j8)
        g.bar()
        g.add("vector", lambda v: v.tensor_tensor(sW2[:, :, 1], sT1v[:], sT2v[:], ALU.add))
        g.bar()
        g.add("vector", lambda v: v.tensor_copy(sW2b[:], sW2[:]))
        g.bar()

        def k1(t):
            ii = None
            for bhi in range(8):
                ii = t.matmul(pD[:, bhi, 4:6], sKT[:, bhi, :], sW2b[:, bhi, :], start=True, stop=True)
            return ii
        g.add("tensor", k1)
        g.bar()

        def k2(v):
            v.tensor_tensor(sT1v[:], sLamB[:, :, 0], sS0[:, :, 0], ALU.mult)
            return v.tensor_tensor(sT2v[:], sLamB[:, :, 1], sS0[:, :, 1], ALU.mult)
        g.add("vector", k2)
        g.bar()

        def k3(v):
            v.tensor_tensor(sT3v[:], sT1v[:], sT2v[:], ALU.subtract)
            v.tensor_tensor(sT1v[:], sLamB[:, :, 0], sS0[:, :, 1], ALU.mult)
            return v.tensor_tensor(sT2v[:], sLamB[:, :, 1], sS0[:, :, 0], ALU.mult)
        g.add("vector", k3)
        g.bar()
        g.add("vector", lambda v: v.tensor_tensor(sT4v[:], sT1v[:], sT2v[:], ALU.add))
        g.bar()

        def k5(v):
            v.tensor_tensor(sS0[:, :, 0], sT3v[:], pD[:, :, 4:5].opt(), ALU.add)
            return v.tensor_tensor(sS0[:, :, 1], sT4v[:], pD[:, :, 5:6].opt(), ALU.add)
        g.add("vector", k5)
        g.bar()
        g.add("vector", lambda v: v.tensor_copy(sS0b[:], sS0[:]))
        g.bar()

    for bhi in range(8):
        g.add("sync", lambda s, bhi=bhi: s.dma_start(
            out=rts[bhi].rearrange("(c t) s -> t c s", t=128), in_=sRall[:, bhi]), dma=True)
    g.add("sync", lambda s: s.dma_start(out=sout[:], in_=sS0[:]), dma=True)
    g.bar()
    return g.build()



def build_b4a():
    g = G()
    rslT = g.io("rslT", [B, 32, LC])
    selc32 = g.io("selc32", [B, 32, LC])
    WroT = g.io("WroT", [32, DI])
    stat = g.io("stat", [1, B, 16, 2], out=True)

    sRg = g.sb("sRg", [32, B, LC])
    sSel = g.sb("sSel", [32, B, LC])
    sRgb = g.sb("sRgb", [32, B, LC], BF16)
    sWro = g.sb("sWro", [32, 16, 128], BF16)
    sSq = g.sb("sSq", [128, 4, LC])
    sCols = g.sb("sCols", [128, B, 16, 2])
    sStatO = g.sb("sStatO", [1, B, 16, 2])
    pY = g.ps("pY", [128, 4, 512])
    pS = g.ps("pS", [1, 512])

    C = common_bufs(g)
    g.add("sync", lambda s: s.dma_start(out=sRg[:], in_=rslT.rearrange("b k l -> k b l")), dma=True)
    g.add("sync", lambda s: s.dma_start(out=sSel[:], in_=selc32.rearrange("b k l -> k b l")), dma=True)
    g.add("gpsimd", lambda gp: gp.dma_start(out=sWro[:], in_=WroT.rearrange("k (fc f) -> k fc f", f=128)), dma=True)
    g.bar()
    seq(g, "vector", lambda v: v.tensor_tensor(sRg[:].rearrange("p a b -> p (a b)"),
                                               sRg[:].rearrange("p a b -> p (a b)"),
                                               sSel[:].rearrange("p a b -> p (a b)"), ALU.mult))
    seq(g, "vector", lambda v: v.tensor_copy(sRgb[:].rearrange("p a b -> p (a b)"),
                                             sRg[:].rearrange("p a b -> p (a b)")))
    for b in range(B):
        for qt in range(4):
            def ymm(t, b=b, qt=qt):
                ii = None
                for j in range(4):
                    ii = t.matmul(pY[:, j], sWro[0:32, 4 * qt + j], sRgb[0:32, b], start=True, stop=True)
                return ii
            g.add("tensor", ymm)
            g.bar()

            def yst(v, b=b, qt=qt):
                ii = None
                for j in range(4):
                    ii = v.reduce_sum(sCols[:, b, 4 * qt + j, 0:1], pY[:, j], AX.X)
                return ii
            g.add("vector", yst)
            g.bar()
            def ysq(sc):
                ii = None
                for j in range(4):
                    ii = sc.activation(out=sSq[:, j], in_=pY[:, j], func=AF.Square)
                return ii
            g.add("scalar", ysq)
            g.bar()

            def yst2(v, b=b, qt=qt):
                ii = None
                for j in range(4):
                    ii = v.reduce_sum(sCols[:, b, 4 * qt + j, 1:2], sSq[:, j], AX.X)
                return ii
            g.add("vector", yst2)
            g.bar()

    def smm(t):
        ii = None
        for b in range(B):
            ii = t.matmul(pS[0:1, 32 * b:32 * (b + 1)], C["sOnes32"][:],
                          sCols[:, b].rearrange("p a b -> p (a b)"), start=True, stop=True)
        return ii
    g.add("tensor", smm)
    g.bar()
    seq(g, "vector", lambda v: v.tensor_copy(sStatO[:].rearrange("p a b c -> p (a b c)"), pS[0:1, 0:128]))
    seq(g, "sync", lambda s: s.dma_start(out=stat[:], in_=sStatO[0:1]), dma=True)
    return g.build()



def build_b4b():
    g = G()
    rslT = g.io("rslT", [B, 32, LC])
    selc32 = g.io("selc32", [B, 32, LC])
    WroT = g.io("WroT", [32, DI])
    st8 = g.io("st8", [8, B, 16, 2])
    zfs = g.io("zfs", [B, 16, 128, LC], BF16)
    xcs = g.io("xcs", [B, 16, 128, LC], BF16)
    gnw = g.io("gnw", [128, 16])
    gnb = g.io("gnb", [128, 16])
    dcol = g.io("dcol", [128, 16])
    WoutT = g.io("WoutT", [DI, D])
    xres = g.io("xres", [B, 8, 128, LC])
    xlnf = g.io("xlnf", [B, 8, 128, LC])
    xout = g.io("xout", [B, 8, 128, LC], out=True)

    sRg = g.sb("sRg", [32, B, LC])
    sSel = g.sb("sSel", [32, B, LC])
    sRgb = g.sb("sRgb", [32, B, LC], BF16)
    sWro = g.sb("sWro", [32, 16, 128], BF16)
    sWout = g.sb("sWout", [128, 16, 8, 128], BF16)
    sSt8 = g.sb("sSt8", [8, B, 16, 2])
    sM = g.sb("sM", [1, B, 16, 2])
    sVar = g.sb("sVar", [1, B, 16])
    sBr2 = g.sb("sBr2", [1, B, 16, 2])
    sStB = g.sb("sStB", [128, B, 16, 2])
    sGnw = g.sb("sGnw", [128, 16])
    sGnb = g.sb("sGnb", [128, 16])
    sDc = g.sb("sDc", [128, 16])
    sOnes8 = g.sb("sOnes8", [8, 1])
    sZ = g.sb("sZ", [128, 3, LC], BF16)
    sXc = g.sb("sXc", [128, 3, LC], BF16)
    sSig = g.sb("sSig", [128, 3, LC])
    sY1 = g.sb("sY1", [128, LC])
    sYb = g.sb("sYb", [128, 16, LC], BF16)
    sXr = g.sb("sXr", [128, 8, LC])
    sXl8 = g.sb("sXl8", [128, 8, LC])
    sXo = g.sb("sXo", [128, 2, LC])
    pY = g.ps("pY", [128, 3, 512])
    pO = g.ps("pO", [128, 2, 512])

    C = common_bufs(g)
    pS = C["pS1"]
    pB2 = C["pBC"]
    g.add("sync", lambda s: s.dma_start(out=sRg[:], in_=rslT.rearrange("b k l -> k b l")), dma=True)
    g.add("sync", lambda s: s.dma_start(out=sSel[:], in_=selc32.rearrange("b k l -> k b l")), dma=True)
    g.add("gpsimd", lambda gp: gp.dma_start(out=sWro[:], in_=WroT.rearrange("k (fc f) -> k fc f", f=128)), dma=True)
    g.add("gpsimd", lambda gp: gp.dma_start(
        out=sWout[:], in_=WoutT.rearrange("(ec e) (fc f) -> e ec fc f", e=128, f=128)), dma=True,
          defer=12)
    g.add("sync", lambda s: s.dma_start(out=sSt8[:], in_=st8[:]), dma=True)
    g.add("sync", lambda s: s.dma_start(out=sGnw[:], in_=gnw[:]), dma=True)
    g.add("sync", lambda s: s.dma_start(out=sGnb[:], in_=gnb[:]), dma=True)
    g.add("sync", lambda s: s.dma_start(out=sDc[:], in_=dcol[:]), dma=True)
    g.add("vector", lambda v: v.memset(sOnes8[:], 1.0))
    g.bar()
    seq(g, "tensor", lambda t: t.matmul(pS[0:1, 0:128], sOnes8[:],
                                        sSt8[:].rearrange("p a b c -> p (a b c)"), start=True, stop=True))
    seq(g, "vector", lambda v: v.tensor_scalar(sM[:].rearrange("p a b c -> p (a b c)"), pS[0:1, 0:128],
                                               1.0 / float(L * 128), None, ALU.mult))
    seq(g, "vector", lambda v: v.tensor_tensor(sVar[:],
                                               sM[:, :, :, 0],
                                               sM[:, :, :, 0], ALU.mult))
    seq(g, "vector", lambda v: v.tensor_tensor(sVar[:],
                                               sM[:, :, :, 1],
                                               sVar[:], ALU.subtract))
    seq(g, "vector", lambda v: v.tensor_scalar(sVar[:],
                                               sVar[:], 1e-5, None, ALU.add))
    seq(g, "scalar", lambda sc: sc.activation(out=sVar[:],
                                              in_=sVar[:], func=AF.Ln))
    seq(g, "scalar", lambda sc: sc.activation(out=sVar[:],
                                              in_=sVar[:], func=AF.Exp, scale=-0.5))
    seq(g, "vector", lambda v: v.tensor_copy(sBr2[:, :, :, 0],
                                             sM[:, :, :, 0]))
    seq(g, "vector", lambda v: v.tensor_copy(sBr2[:, :, :, 1],
                                             sVar[:]))
    seq(g, "tensor", lambda t: t.matmul(pB2[:, 0:128], C["sOnesRow"][:],
                                        sBr2[:].rearrange("p a b c -> p (a b c)"), start=True, stop=True))
    seq(g, "vector", lambda v: v.tensor_copy(sStB[:].rearrange("p a b c -> p (a b c)"), pB2[:, 0:128]))
    seq(g, "vector", lambda v: v.tensor_tensor(sRg[:].rearrange("p a b -> p (a b)"),
                                               sRg[:].rearrange("p a b -> p (a b)"),
                                               sSel[:].rearrange("p a b -> p (a b)"), ALU.mult))
    seq(g, "vector", lambda v: v.tensor_copy(sRgb[:].rearrange("p a b -> p (a b)"),
                                             sRg[:].rearrange("p a b -> p (a b)")))

    for b in range(B):
        g.add("sync", lambda s, b=b: s.dma_start(out=sXr[:], in_=xres[b].rearrange("c f l -> f c l")), dma=True)
        g.add("sync", lambda s, b=b: s.dma_start(out=sXl8[:], in_=xlnf[b].rearrange("c f l -> f c l")), dma=True)
        g.bar()
        g.add("vector", lambda v: v.tensor_tensor(sXr[:], sXr[:], sXl8[:], ALU.add))
        g.bar()
        for i in range(18):
            if i < 16:
                g.add("tensor", lambda t, b=b, i=i: t.matmul(
                    pY[:, i % 3], sWro[0:32, i], sRgb[0:32, b], start=True, stop=True))
                g.add("sync", lambda s, b=b, i=i: s.dma_start(out=sZ[:, i % 3], in_=zfs[b, i]), dma=True)
                g.add("sync", lambda s, b=b, i=i: s.dma_start(out=sXc[:, i % 3], in_=xcs[b, i]), dma=True)
            if 0 <= i - 1 < 16:
                g.add("scalar", lambda sc, i=i: sc.activation(out=sSig[:, (i - 1) % 3],
                                                              in_=sZ[:, (i - 1) % 3], func=AF.Sigmoid))
            if 0 <= i - 2 < 16:
                def ych(v, b=b, i=i):
                    fc = i - 2
                    bk = fc % 3
                    v.tensor_scalar(sY1[:], pY[:, bk], sStB[:, b, fc, 0:1], sStB[:, b, fc, 1:2],
                                    ALU.subtract, ALU.mult)
                    v.tensor_scalar(sY1[:], sY1[:], sGnw[:, fc:fc + 1], sGnb[:, fc:fc + 1],
                                    ALU.mult, ALU.add)
                    v.tensor_tensor(sSig[:, bk], sSig[:, bk], sZ[:, bk], ALU.mult)
                    v.tensor_tensor(sY1[:], sY1[:], sSig[:, bk], ALU.mult)
                    v.scalar_tensor_tensor(sY1[:], sXc[:, bk], sDc[:, fc:fc + 1], sY1[:],
                                           ALU.mult, ALU.add)
                    return v.tensor_copy(sYb[:, fc], sY1[:])
                g.add("vector", ych)
            g.bar()
        for i in range(10):
            if i < 8:
                def omm(t, i=i):
                    ii = None
                    for ec in range(16):
                        ii = t.matmul(pO[:, i % 2], sWout[:, ec, i], sYb[:, ec],
                                      start=(ec == 0), stop=(ec == 15))
                    return ii
                g.add("tensor", omm)
            if 0 <= i - 1 < 8:
                g.add("vector", lambda v, i=i: v.tensor_tensor(
                    sXo[:, (i - 1) % 2], pO[:, (i - 1) % 2], sXr[:, i - 1], ALU.add))
            if 0 <= i - 2 < 8:
                g.add("sync", lambda s, b=b, i=i: s.dma_start(
                    out=xout[b, i - 2], in_=sXo[:, (i - 2) % 2]), dma=True)
            g.bar()
    return g.build()



def run_attnb(x_fm, p, builders):
    pre = "L0_attn_b_"
    Wip = p[pre + "in_proj_w"]
    bip = p[pre + "in_proj_b"]
    if "b1" not in builders:
        builders["b1"] = build_b1()
    ins = []
    for c in range(NCORES):
        ins.append({
            "xsl": f32c(x_fm[:, :, :, LC * c:LC * (c + 1)]),
            "WzT": f32c(Wip[:, 0:DI]), "WgT": f32c(Wip[:, DI:2 * DI]), "WkT": f32c(Wip[:, 2 * DI:3 * DI]),
            "WvT": f32c(Wip[:, 3 * DI:3 * DI + 32]),
            "bcolz": f32c(bip[0:DI].reshape(16, 128).T), "bcolg": f32c(bip[DI:2 * DI].reshape(16, 128).T),
            "bcolk": f32c(bip[2 * DI:3 * DI].reshape(16, 128).T),
            "bvr": f32c(bip[3 * DI:3 * DI + 32].reshape(1, 32)),
            "rmsw": f32c(p[pre + "rms_w"].reshape(8, 128).T),
            "ident": IDENT,
        })
    r1 = run_bass_kernel_spmd(builders["b1"], ins, core_ids=list(range(NCORES)))
    zf = [r1.results[c]["zf"] for c in range(NCORES)]
    xlnf = [r1.results[c]["xlnf"] for c in range(NCORES)]
    xgf = [r1.results[c]["xgf"] for c in range(NCORES)]
    kff = [r1.results[c]["kff"] for c in range(NCORES)]
    vv = [r1.results[c]["vv"] for c in range(NCORES)]

    xgf_full = np.concatenate(xgf, axis=3)
    xgp = np.concatenate([np.zeros((B, 16, 128, 3), BF16NP), xgf_full], axis=3)
    frq = (1.0 / (10000.0 ** (np.arange(16, dtype=np.float64) / 16.0))).astype(np.float32)

    if "b2" not in builders:
        builders["b2"] = build_b2()
    ins = []
    for c in range(NCORES):
        ins.append({
            "xgh": f32c(xgp[:, :, :, LC * c:LC * (c + 1) + 3].astype(np.float32)),
            "kin": kff[c],
            "vv": vv[c],
            "QwT": f32c(p[pre + "Q_w"]),
            "gwT": f32c(p[pre + "gate_w"]),
            "gb": f32c(p[pre + "gate_b"].reshape(1, 112)),
            "lds": f32c(p[pre + "log_dt_scale"].reshape(1, 16)),
            "frq": frq.reshape(1, 16),
            "convw": f32c(p[pre + "conv_w"].reshape(16, 128, 4).transpose(1, 0, 2)),
            "convb": f32c(p[pre + "conv_b"].reshape(16, 128).T),
            "ident": IDENT,
        })
    r2 = run_bass_kernel_spmd(builders["b2"], ins, core_ids=list(range(NCORES)))
    xcf = [r2.results[c]["xcf"] for c in range(NCORES)]
    qnf_full = np.concatenate([r2.results[c]["qnf"] for c in range(NCORES)], axis=3)
    knf_full = np.concatenate([r2.results[c]["knf"] for c in range(NCORES)], axis=3)
    gsc_full = np.concatenate([r2.results[c]["gsc"] for c in range(NCORES)], axis=1)
    glt_full = np.concatenate([r2.results[c]["glt"] for c in range(NCORES)], axis=2)

    IDENTB = IDENT.astype(BF16NP)
    sin = [np.zeros((128, BHC8, 2), np.float32) for _ in range(NCORES)]
    rparts = []
    for (clo, chi) in [(0, 16), (16, 32)]:
        key = f"b3_{clo}"
        if key not in builders:
            builders[key] = build_b3(clo, chi)
        ins = []
        for c in range(NCORES):
            bhs = list(range(8 * c, 8 * c + 8))
            ins.append({
                "kn8": np.stack([knf_full[bh // 16, bh % 16] for bh in bhs]),
                "qn8": np.stack([qnf_full[bh // 16, bh % 16] for bh in bhs]),
                "gsc8": np.stack([gsc_full[bh // 16, :, bh % 16, :] for bh in bhs]),
                "glt8": np.stack([glt_full[bh // 16, bh % 16] for bh in bhs]),
                "brow8": np.ascontiguousarray(
                    np.stack([gsc_full[bh // 16, :, bh % 16, 2] for bh in bhs]).reshape(BHC8, NCH, 128)),
                "sin0": sin[c],
                "ident": IDENT, "identb": IDENTB,
            })
        r3 = run_bass_kernel_spmd(builders[key], ins, core_ids=list(range(NCORES)))
        rparts.append([r3.results[c]["rts"] for c in range(NCORES)])
        sin = [r3.results[c]["sout"] for c in range(NCORES)]

    rfull = np.zeros((B, 16, L, 2), np.float32)
    for c in range(NCORES):
        for i, bh in enumerate(range(8 * c, 8 * c + 8)):
            rfull[bh // 16, bh % 16, 0:2048] = rparts[0][c][i]
            rfull[bh // 16, bh % 16, 2048:4096] = rparts[1][c][i]

    rT = np.ascontiguousarray(rfull.transpose(0, 1, 3, 2)).reshape(B, 32, L)
    selc = np.ascontiguousarray(gsc_full[:, :, :, 5].transpose(0, 2, 1))
    selc32 = np.repeat(selc, 2, axis=1)

    if "b4a" not in builders:
        builders["b4a"] = build_b4a()
    ins = []
    for c in range(NCORES):
        sl = slice(LC * c, LC * (c + 1))
        ins.append({"rslT": f32c(rT[:, :, sl]), "selc32": f32c(selc32[:, :, sl]),
                    "WroT": f32c(p[pre + "readout_w"])})
    r4a = run_bass_kernel_spmd(builders["b4a"], ins, core_ids=list(range(NCORES)))
    st8 = np.stack([r4a.results[c]["stat"][0] for c in range(NCORES)], axis=0)

    if "b4b" not in builders:
        builders["b4b"] = build_b4b()
    ins = []
    for c in range(NCORES):
        sl = slice(LC * c, LC * (c + 1))
        ins.append({
            "rslT": f32c(rT[:, :, sl]), "selc32": f32c(selc32[:, :, sl]),
            "WroT": f32c(p[pre + "readout_w"]), "st8": st8,
            "zfs": zf[c], "xcs": xcf[c],
            "gnw": f32c(p[pre + "gn_w"].reshape(16, 128).T),
            "gnb": f32c(p[pre + "gn_b"].reshape(16, 128).T),
            "dcol": f32c(p[pre + "D_param"].reshape(16, 128).T),
            "WoutT": f32c(p[pre + "out_w"]),
            "xres": f32c(x_fm[:, :, :, sl]),
            "xlnf": xlnf[c],
        })
    r4b = run_bass_kernel_spmd(builders["b4b"], ins, core_ids=list(range(NCORES)))
    return np.concatenate([r4b.results[c]["xout"] for c in range(NCORES)], axis=3)




FF1 = 1408
NFC1 = 11

NE = 10
BIGV = 60000.0



def build_ffn1():
    g = G()
    xsl = g.io("xsl", [B, 8, 128, LC])
    gw8 = g.io("gw8", [D, 8])
    Wg = g.io("Wg", [NE, D, FF1])
    Wu = g.io("Wu", [NE, D, FF1])
    Wd = g.io("Wd", [NE, FF1, D])
    ident = g.io("ident", [128, 128])
    xout = g.io("xout", [B, 8, 128, LC], out=True)

    sX1 = g.sb("sX1", [128, 8, LC])
    sXnb = g.sb("sXnb", [128, 8, B, LC], BF16)
    sGw8 = g.sb("sGw8", [128, 8, 8], BF16)
    sWgE = g.sb("sWgE", [128, 8, NFC1, 128], BF16)
    sWuE = g.sb("sWuE", [128, 8, NFC1, 128], BF16)
    sWdE = g.sb("sWdE", [128, NFC1, 8, 128], BF16)
    sP = g.sb("sP", [128, 16, 8])
    sB1 = g.sb("sB1", [128, 16, 8])
    sB2 = g.sb("sB2", [128, 16, 8])
    sB3 = g.sb("sB3", [128, 16, 8])
    sIE = g.sb("sIE", [128, 16, 8])
    sIotaE = g.sb("sIotaE", [128, 8], I32)
    sIEf = g.sb("sIEf", [128, 8])
    sCol = g.sb("sCol", [128, 16, 1])
    sCol2 = g.sb("sCol2", [128, 16, 1])
    sWte = g.sb("sWte", [128, 16, 8])
    sWteT = g.sb("sWteT", [8, 16, 128])
    sWrow1 = g.sb("sWrow1", [1, 4, 128])
    sIdent32 = g.sb("sIdent32", [128, 128])
    sMid = g.sb("sMid", [128, NFC1, LC], BF16)
    sAcc = g.sb("sAcc", [128, 8, B, LC], BF16)
    pM = g.ps("pM", [128, 3, 512])
    pD2 = g.ps("pD2", [128, 2, 512])

    sTg = g.sb("sTg", [128, LC])
    sSg = g.sb("sSg", [128, 2, LC])
    sWF = g.sb("sWF", [128, LC])
    C = common_bufs(g)
    g.add("sync", lambda s: s.dma_start(out=sIdent32[:], in_=ident[:]), dma=True)
    g.add("gpsimd", lambda gp: gp.dma_start(out=sGw8[:], in_=gw8.rearrange("(ec e) o -> e ec o", e=128)), dma=True)
    g.add("gpsimd", lambda gp: gp.iota(sIotaE[:], pattern=[[1, 8]], base=0, channel_multiplier=0))
    g.bar()
    seq(g, "vector", lambda v: v.tensor_copy(sIEf[:], sIotaE[:]))
    for j in range(16):
        g.add("vector", lambda v, j=j: v.tensor_copy(sIE[:, j], sIEf[:]))
    g.bar()
    for b in range(B):
        g.add("sync", lambda s, b=b: s.dma_start(out=sX1[:], in_=xsl[b].rearrange("c f l -> f c l")), dma=True)
        g.bar()
        emit_ln(g, sX1[:], sXnb[:, :, b, :], C)
    for b in range(B):
        for q in range(4):
            def lmm(t, b=b, q=q):
                ii = None
                j = 4 * b + q
                for ec in range(8):
                    ii = t.matmul(C["pBC"][:, 8 * j:8 * j + 8], sXnb[:, ec, b, 128 * q:128 * (q + 1)],
                                  sGw8[:, ec], start=(ec == 0), stop=(ec == 7))
                return ii
            g.add("tensor", lmm)
            g.bar()
    seq(g, "vector", lambda v: v.tensor_copy(sP[:], C["pBC"][:, 0:128]))

    def bcast(src_col, dst):
        def fn(v):
            ii = None
            for j in range(8):
                ii = v.tensor_copy(dst[:, :, j:j + 1], src_col[:])
            return ii
        return fn

    seq(g, "vector", lambda v: v.reduce_max(sCol[:], sP[:], AX.X))
    seq(g, "vector", bcast(sCol, sB1))
    seq(g, "vector", lambda v: v.tensor_tensor(sP[:], sP[:], sB1[:], ALU.subtract))
    seq(g, "scalar", lambda sc: sc.activation(out=sP[:], in_=sP[:], func=AF.Exp))
    seq(g, "vector", lambda v: v.reduce_sum(sCol[:], sP[:], AX.X))
    seq(g, "vector", lambda v: v.reciprocal(sCol2[:], sCol[:]))
    seq(g, "vector", bcast(sCol2, sB1))
    seq(g, "vector", lambda v: v.tensor_tensor(sP[:], sP[:], sB1[:], ALU.mult))
    seq(g, "vector", lambda v: v.reduce_max(sCol[:], sP[:], AX.X))
    seq(g, "vector", bcast(sCol, sB1))
    seq(g, "vector", lambda v: v.tensor_tensor(sB2[:], sP[:], sB1[:], ALU.is_equal))
    seq(g, "vector", lambda v: v.tensor_scalar(sB2[:], sB2[:], -BIGV, BIGV, ALU.mult, ALU.add))
    seq(g, "vector", lambda v: v.tensor_tensor(sB2[:], sB2[:], sIE[:], ALU.add))
    seq(g, "vector", lambda v: v.tensor_reduce(sCol2[:], sB2[:], AX.X, ALU.min))
    seq(g, "vector", bcast(sCol2, sB2))
    seq(g, "vector", lambda v: v.tensor_tensor(sB2[:], sIE[:], sB2[:], ALU.is_equal))
    seq(g, "vector", lambda v: v.tensor_tensor(sWte[:], sB2[:], sB1[:], ALU.mult))
    seq(g, "vector", lambda v: v.tensor_scalar(sB3[:], sB2[:], BIGV, None, ALU.mult))
    seq(g, "vector", lambda v: v.tensor_tensor(sB3[:], sP[:], sB3[:], ALU.subtract))
    seq(g, "vector", lambda v: v.reduce_max(sCol[:], sB3[:], AX.X))
    seq(g, "vector", bcast(sCol, sB1))
    seq(g, "vector", lambda v: v.tensor_tensor(sB2[:], sB3[:], sB1[:], ALU.is_equal))
    seq(g, "vector", lambda v: v.tensor_scalar(sB2[:], sB2[:], -BIGV, BIGV, ALU.mult, ALU.add))
    seq(g, "vector", lambda v: v.tensor_tensor(sB2[:], sB2[:], sIE[:], ALU.add))
    seq(g, "vector", lambda v: v.tensor_reduce(sCol2[:], sB2[:], AX.X, ALU.min))
    seq(g, "vector", bcast(sCol2, sB2))
    seq(g, "vector", lambda v: v.tensor_tensor(sB2[:], sIE[:], sB2[:], ALU.is_equal))
    seq(g, "vector", lambda v: v.tensor_tensor(sB1[:], sB2[:], sB1[:], ALU.mult))
    seq(g, "vector", lambda v: v.tensor_tensor(sWte[:], sWte[:], sB1[:], ALU.add))
    seq(g, "vector", lambda v: v.reduce_sum(sCol[:], sWte[:], AX.X))
    seq(g, "vector", lambda v: v.reciprocal(sCol2[:], sCol[:]))
    seq(g, "vector", bcast(sCol2, sB1))
    seq(g, "vector", lambda v: v.tensor_tensor(sWte[:], sWte[:], sB1[:], ALU.mult))
    for j in range(16):
        g.add("tensor", lambda t, j=j: t.matmul(pM[0:8, j % 2, 0:128], sWte[:, j, :], sIdent32[:],
                                                start=True, stop=True))
        g.bar()
        g.add("vector", lambda v, j=j: v.tensor_copy(sWteT[:, j, :], pM[0:8, j % 2, 0:128]))
        g.bar()

    for e in range(NE):
        g.add("gpsimd", lambda gp, e=e: gp.dma_start(
            out=sWgE[:], in_=Wg[e].rearrange("(ec ee) (fc f) -> ee ec fc f", ee=128, f=128)), dma=True)
        g.add("gpsimd", lambda gp, e=e: gp.dma_start(
            out=sWuE[:], in_=Wu[e].rearrange("(ec ee) (fc f) -> ee ec fc f", ee=128, f=128)), dma=True)
        g.add("gpsimd", lambda gp, e=e: gp.dma_start(
            out=sWdE[:], in_=Wd[e].rearrange("(kc kk) (fc f) -> kk kc fc f", kk=128, f=128)), dma=True,
              defer=8)
        g.bar()
        for b in range(B):
            if e < 8:
                seq(g, "sync", lambda s, e=e, b=b: s.dma_start(
                    out=sWrow1[:], in_=sWteT[e:e + 1, 4 * b:4 * b + 4, :]), dma=True)
                seq(g, "tensor", lambda t: t.matmul(
                    C["pBC"][:], C["sOnesRow"][:], sWrow1[0:1].rearrange("p a b -> p (a b)"),
                    start=True, stop=True))
                seq(g, "vector", lambda v: v.tensor_copy(sWF[:], C["pBC"][:]))
            for step in range(NFC1 + 2):
                def mstage(t, step=step, b=b):
                    ii = None
                    if step < NFC1:
                        for ec in range(8):
                            ii = t.matmul(pM[:, step % 3], sWgE[:, ec, step], sXnb[:, ec, b, :],
                                          start=(ec == 0), stop=(ec == 7))
                    if 1 <= step <= NFC1:
                        fu = step - 1
                        for ec in range(8):
                            ii = t.matmul(pD2[:, fu % 2], sWuE[:, ec, fu], sXnb[:, ec, b, :],
                                          start=(ec == 0), stop=(ec == 7))
                    return ii
                if step <= NFC1:
                    g.add("tensor", mstage)
                if 1 <= step <= NFC1:
                    g.add("scalar", lambda sc, step=step: sc.activation(
                        out=sSg[:, (step - 1) % 2], in_=pM[:, (step - 1) % 3], func=AF.Sigmoid))
                if step >= 2:
                    def mdve(v, step=step):
                        f = step - 2
                        v.tensor_tensor(sTg[:], pM[:, f % 3], sSg[:, f % 2], ALU.mult)
                        return v.tensor_tensor(sMid[:, f], sTg[:], pD2[:, f % 2], ALU.mult)
                    g.add("vector", mdve)
                g.bar()
            if e < 8:
                def gate(v):
                    ii = None
                    for fc in range(NFC1):
                        ii = v.tensor_tensor(sMid[:, fc], sMid[:, fc], sWF[:], ALU.mult)
                    return ii
                g.add("vector", gate)
                g.bar()
            for step in range(9):
                if step < 8:
                    def dstage(t, step=step):
                        ii = None
                        for kc in range(NFC1):
                            ii = t.matmul(pM[:, step % 2], sWdE[:, kc, step], sMid[:, kc],
                                          start=(kc == 0), stop=(kc == NFC1 - 1))
                        return ii
                    g.add("tensor", dstage)
                if step >= 1:
                    fo = step - 1
                    if e == 0:
                        g.add("vector", lambda v, b=b, fo=fo: v.tensor_copy(
                            sAcc[:, fo, b, :], pM[:, fo % 2]))
                    else:
                        g.add("vector", lambda v, b=b, fo=fo: v.tensor_tensor(
                            sAcc[:, fo, b, :], sAcc[:, fo, b, :], pM[:, fo % 2], ALU.add))
                g.bar()
    for b in range(B):
        g.add("sync", lambda s, b=b: s.dma_start(out=sX1[:], in_=xsl[b].rearrange("c f l -> f c l")), dma=True)
        g.bar()

        def res(v, b=b):
            ii = None
            for fc in range(8):
                ii = v.tensor_tensor(sX1[:, fc], sX1[:, fc], sAcc[:, fc, b, :], ALU.add)
            return ii
        g.add("vector", res)
        g.bar()
        for fc in range(8):
            g.add("sync", lambda s, b=b, fc=fc: s.dma_start(out=xout[b, fc], in_=sX1[:, fc]), dma=True)
        g.bar()
    return g.build()


def run_ffn1(x_fm, p, builders):
    pre = "L1_ffn_"
    if "ffn1" not in builders:
        builders["ffn1"] = build_ffn1()
    Wg = np.concatenate([p[pre + "exp_gate"], p[pre + "sh_gate"]], axis=0)
    Wu = np.concatenate([p[pre + "exp_up"], p[pre + "sh_up"]], axis=0)
    Wd = np.concatenate([p[pre + "exp_down"], p[pre + "sh_down"]], axis=0)
    ins = []
    for c in range(NCORES):
        ins.append({
            "xsl": f32c(x_fm[:, :, :, LC * c:LC * (c + 1)]),
            "gw8": f32c(p[pre + "gate_w"]),
            "Wg": f32c(Wg), "Wu": f32c(Wu), "Wd": f32c(Wd),
            "ident": IDENT,
        })
    r = run_bass_kernel_spmd(builders["ffn1"], ins, core_ids=list(range(NCORES)))
    return np.concatenate([r.results[c]["xout"] for c in range(NCORES)], axis=3)



def build_l1b1():
    g = G()
    xsl = g.io("xsl", [B, 8, 128, LC])
    WkvT = g.io("WkvT", [D, 2048])
    xnf = g.io("xnf", [B, 8, 128, LC], BF16, out=True)
    kvp = g.io("kvp", [B, 64, 16, 64], out=True)
    ksm = g.io("ksm", [B, 64, 2, 8], out=True)

    sX1 = g.sb("sX1", [128, 8, LC])
    sXn = g.sb("sXn", [128, 8, LC], BF16)
    sWkv = g.sb("sWkv", [128, 8, 16, 128], BF16)
    sQKV = g.sb("sQKV", [128, 16, 128], BF16)
    sKVo = g.sb("sKVo", [64, 16, 64])
    sKSo = g.sb("sKSo", [64, 2, 8])
    pQ = g.ps("pQ", [128, 2, 512])
    pKV = g.ps("pKV", [64, 16, 64])
    pKS = g.ps("pKS", [64, 16])

    C = common_bufs(g)
    g.add("gpsimd", lambda gp: gp.dma_start(
        out=sWkv[:], in_=WkvT.rearrange("(ec e) (fc f) -> e ec fc f", e=128, f=128)), dma=True, defer=6)
    g.bar()
    for b in range(B):
        g.add("sync", lambda s, b=b: s.dma_start(out=sX1[:], in_=xsl[b].rearrange("c f l -> f c l")), dma=True)
        g.bar()
        emit_ln(g, sX1[:], sXn[:], C)
        for fc in range(8):
            g.add("sync", lambda s, b=b, fc=fc: s.dma_start(out=xnf[b, fc], in_=sXn[:, fc]), dma=True)
        for q in range(4):
            for pair in range(2):
                def kvmm(t, q=q, pair=pair):
                    ii = None
                    for half in range(2):
                        for ec in range(8):
                            hh = 2 * pair + half
                            ii = t.matmul(pQ[:, half], sXn[:, ec, 128 * q:128 * (q + 1)],
                                          sWkv[:, ec, 4 * hh:4 * hh + 4, :].rearrange("e a b -> e (a b)"),
                                          start=(ec == 0), stop=(ec == 7))
                    return ii
                g.add("tensor", kvmm)
                g.bar()
                if pair == 0:
                    seq(g, "scalar", lambda sc: sc.activation(
                        out=sQKV[:, 0:8, :].rearrange("p a b -> p (a b)"),
                        in_=pQ[:].rearrange("p a b -> p (a b)"), func=AF.Square))
                    seq(g, "scalar", lambda sc: sc.activation(
                        out=sQKV[:, 0:8, :].rearrange("p a b -> p (a b)"),
                        in_=sQKV[:, 0:8, :].rearrange("p a b -> p (a b)"), func=AF.Exp, scale=-0.5))
                else:
                    seq(g, "vector", lambda v: v.tensor_copy(
                        sQKV[:, 8:16, :].rearrange("p a b -> p (a b)"), pQ[:].rearrange("p a b -> p (a b)")))

            def kva(t, b=b, q=q):
                ii = None
                st = (q == 0)
                sp = (q == 3)
                for h in range(16):
                    ko = sQKV[:, h // 2, 64 * (h % 2):64 * (h % 2) + 64]
                    vo = sQKV[:, 8 + h // 2, 64 * (h % 2):64 * (h % 2) + 64]
                    t.matmul(pKV[:, h, :], ko, vo, start=st, stop=sp)
                    ii = t.matmul(pKS[:, h:h + 1], ko, C["sOnesB"][:], start=st, stop=sp)
                return ii
            g.add("tensor", kva)
            g.bar()
        seq(g, "vector", lambda v: v.tensor_copy(sKVo[:], pKV[:]))

        def kse(v):
            v.tensor_copy(sKSo[:, 0, :], pKS[:].rearrange("p (hp par) -> p hp par", par=2)[:, :, 0])
            return v.tensor_copy(sKSo[:, 1, :], pKS[:].rearrange("p (hp par) -> p hp par", par=2)[:, :, 1])
        g.add("vector", kse)
        g.bar()
        g.add("sync", lambda s, b=b: s.dma_start(out=kvp[b], in_=sKVo[:]), dma=True)
        g.add("sync", lambda s, b=b: s.dma_start(out=ksm[b], in_=sKSo[:]), dma=True)
        g.bar()
    return g.build()


def build_l1b2():
    g = G()
    xnf = g.io("xnf", [B, 8, 128, LC], BF16)
    kvp8 = g.io("kvp8", [8, B, 64, 16, 64])
    ksm8 = g.io("ksm8", [8, B, 64, 2, 8])
    WqT = g.io("WqT", [D, D])
    WpT = g.io("WpT", [D, D])
    bpcol = g.io("bpcol", [128, 8])
    sel2in = g.io("sel2in", [2, 128])
    xres = g.io("xres", [B, 8, 128, LC])
    xout = g.io("xout", [B, 8, 128, LC], out=True)

    sXn = g.sb("sXn", [128, 8, LC], BF16)
    sWq = g.sb("sWq", [128, 8, 8, 128], BF16)
    sWp = g.sb("sWp", [128, 8, 8, 128], BF16)
    sBpc = g.sb("sBpc", [128, 8])
    sKVc = g.sb("sKVc", [64, 2, B, 16, 64])
    sKV = g.sb("sKV", [64, B, 16, 64])
    sKVb = g.sb("sKVb", [64, B, 16, 64], BF16)
    sKSc = g.sb("sKSc", [64, 8, B, 2, 8])
    sKSum = g.sb("sKSum", [64, B, 2, 8])
    sKSb = g.sb("sKSb", [64, B, 2, 8], BF16)
    sKVz2 = g.sb("sKVz2", [128, B, 8, 128], BF16)
    sKS3 = g.sb("sKS3", [128, B, 2, 8], BF16)
    sSel2 = g.sb("sSel2", [2, 128])
    sQg = g.sb("sQg", [128, 8, LC], BF16)
    sZrP = g.sb("sZrP", [1, 2, LC])
    sZ2P = g.sb("sZ2P", [2, LC])
    sE1 = g.sb("sE1", [128, LC])
    sO = g.sb("sO", [128, 8, LC], BF16)
    sXr = g.sb("sXr", [128, 8, LC])
    sXo = g.sb("sXo", [128, 2, LC])
    pQ = g.ps("pQ", [128, 2, 512])
    pO = g.ps("pO", [128, 2, 512])

    C = common_bufs(g)
    pZ = C["pS1"]
    pZB = C["pBC"]
    g.add("gpsimd", lambda gp: gp.dma_start(
        out=sWq[:], in_=WqT.rearrange("(ec e) (fc f) -> e ec fc f", e=128, f=128)), dma=True, defer=8)
    g.add("gpsimd", lambda gp: gp.dma_start(
        out=sWp[:], in_=WpT.rearrange("(ec e) (fc f) -> e ec fc f", e=128, f=128)), dma=True, defer=12)
    g.add("sync", lambda s: s.dma_start(out=sBpc[:], in_=bpcol[:]), dma=True)
    g.add("sync", lambda s: s.dma_start(out=sKVc[:, 0], in_=kvp8[0].rearrange("b d h e -> d b h e")), dma=True)
    g.add("sync", lambda s: s.dma_start(out=sKSc[:], in_=ksm8.rearrange("c b d p h -> d c b p h")), dma=True)

    g.add("sync", lambda s: s.dma_start(out=sSel2[:], in_=sel2in[:]), dma=True)

    def sel(v):
        v.memset(sKVz2[:].rearrange("p a b c -> p (a b c)"), 0.0)
        return v.memset(sKS3[:].rearrange("p a b c -> p (a b c)"), 0.0)
    g.add("vector", sel)
    g.add("vector", lambda v: v.tensor_copy(sKV[:].rearrange("p a b c -> p (a b c)"),
                                            sKVc[:, 0].rearrange("p a b c -> p (a b c)")))
    g.add("sync", lambda s: s.dma_start(out=sKVc[:, 1], in_=kvp8[1].rearrange("b d h e -> d b h e")), dma=True)
    g.bar()
    for c in range(1, 8):
        if c < 7:
            g.add("sync", lambda s, c=c: s.dma_start(
                out=sKVc[:, c % 2], in_=kvp8[c + 1].rearrange("b d h e -> d b h e")), dma=True)
        g.add("vector", lambda v, c=c: v.tensor_tensor(
            sKV[:].rearrange("p a b c -> p (a b c)"), sKV[:].rearrange("p a b c -> p (a b c)"),
            sKVc[:, c % 2].rearrange("p a b c -> p (a b c)"), ALU.add))
        g.bar()
    seq(g, "vector", lambda v: v.tensor_copy(sKVb[:].rearrange("p a b c -> p (a b c)"),
                                             sKV[:].rearrange("p a b c -> p (a b c)")))
    seq(g, "vector", lambda v: v.tensor_copy(sKSum[:].rearrange("p a b c -> p (a b c)"),
                                             sKSc[:, 0].rearrange("p a b c -> p (a b c)")))
    for c in range(1, 8):
        seq(g, "vector", lambda v, c=c: v.tensor_tensor(
            sKSum[:].rearrange("p a b c -> p (a b c)"), sKSum[:].rearrange("p a b c -> p (a b c)"),
            sKSc[:, c].rearrange("p a b c -> p (a b c)"), ALU.add))
    seq(g, "vector", lambda v: v.tensor_copy(sKSb[:].rearrange("p a b c -> p (a b c)"),
                                             sKSum[:].rearrange("p a b c -> p (a b c)")))
    g.add("sync", lambda s: s.dma_start(
        out=sKVz2[0:64, :, :, 0:64],
        in_=sKVb[:].rearrange("p b (hp par) e -> p par b hp e", par=2)[:, 0]), dma=True)
    g.add("sync", lambda s: s.dma_start(
        out=sKVz2[64:128, :, :, 64:128],
        in_=sKVb[:].rearrange("p b (hp par) e -> p par b hp e", par=2)[:, 1]), dma=True)
    g.add("sync", lambda s: s.dma_start(out=sKS3[0:64, :, 0, :], in_=sKSb[:, :, 0, :]), dma=True)
    g.add("sync", lambda s: s.dma_start(out=sKS3[64:128, :, 1, :], in_=sKSb[:, :, 1, :]), dma=True)
    g.bar()

    for b in range(B):
        g.add("sync", lambda s, b=b: s.dma_start(out=sXn[:], in_=xnf[b].rearrange("c f l -> f c l")), dma=True)
        g.add("sync", lambda s, b=b: s.dma_start(out=sXr[:], in_=xres[b].rearrange("c f l -> f c l")), dma=True)
        g.bar()
        for fc in range(8):
            bank = fc % 2

            def qmm(t, fc=fc, bank=bank):
                ii = None
                for ec in range(8):
                    ii = t.matmul(pQ[:, bank], sWq[:, ec, fc], sXn[:, ec], start=(ec == 0), stop=(ec == 7))
                return ii
            g.add("tensor", qmm)
            g.bar()
            seq(g, "scalar", lambda sc, bank=bank: sc.activation(out=sXo[:, bank], in_=pQ[:, bank],
                                                                 func=AF.Square))
            seq(g, "scalar", lambda sc, fc=fc, bank=bank: sc.activation(out=sQg[:, fc], in_=sXo[:, bank],
                                                                        func=AF.Exp, scale=-0.5))
        for hp in range(8):
            bank = hp % 2
            for par in range(2):
                h = 2 * hp + par
                g.add("tensor", lambda t, b=b, h=h: t.matmul(
                    pZ[0:1, 0:LC], sKS3[:, b, h % 2, h // 2:h // 2 + 1],
                    sQg[:, h // 2, :], start=True, stop=True))
                g.bar()
                g.add("vector", lambda v, par=par: v.tensor_copy(sZrP[0:1, par, :], pZ[0:1, 0:LC]))
                g.bar()
            g.add("vector", lambda v: v.tensor_scalar(sZrP[:].rearrange("p a b -> p (a b)"),
                                                      sZrP[:].rearrange("p a b -> p (a b)"),
                                                      1e-6, None, ALU.add))
            g.bar()
            g.add("vector", lambda v: v.reciprocal(sZrP[:].rearrange("p a b -> p (a b)"),
                                                   sZrP[:].rearrange("p a b -> p (a b)")))
            g.bar()
            g.add("sync", lambda s: s.dma_start(out=sZ2P[:], in_=sZrP[0:1]), dma=True)
            g.bar()

            def hmm(t, b=b, hp=hp, bank=bank):
                t.matmul(pQ[:, bank], sKVz2[:, b, hp, :], sQg[:, hp, :], start=True, stop=True)
                return t.matmul(pZB[:, 0:LC], sSel2[:], sZ2P[:], start=True, stop=True)
            g.add("tensor", hmm)
            g.bar()
            seq(g, "vector", lambda v, bank=bank: v.tensor_copy(sE1[:], pQ[:, bank]))
            seq(g, "vector", lambda v, hp=hp: v.tensor_tensor(sO[:, hp], sE1[:], pZB[:, 0:LC], ALU.mult))
        for fo in range(8):
            bank = fo % 2

            def omm(t, fo=fo, bank=bank):
                ii = None
                for ec in range(8):
                    ii = t.matmul(pO[:, bank], sWp[:, ec, fo], sO[:, ec], start=(ec == 0), stop=(ec == 7))
                return ii
            g.add("tensor", omm)
            g.bar()
            seq(g, "vector", lambda v, fo=fo, bank=bank: v.scalar_tensor_tensor(
                sXo[:, bank], sXr[:, fo], sBpc[:, fo:fo + 1], pO[:, bank], ALU.add, ALU.add))
            seq(g, "sync", lambda s, b=b, fo=fo, bank=bank: s.dma_start(
                out=xout[b, fo], in_=sXo[:, bank]), dma=True)
    return g.build()


def run_attnb1(x_fm, p, builders):
    pre = "L1_attn_b_"
    WqkvT = np.ascontiguousarray(p[pre + "W_qkv"].T)
    if "l1b1" not in builders:
        builders["l1b1"] = build_l1b1()
    ins = []
    for c in range(NCORES):
        ins.append({
            "xsl": f32c(x_fm[:, :, :, LC * c:LC * (c + 1)]),
            "WkvT": f32c(WqkvT[:, D:3 * D]),
        })
    r1 = run_bass_kernel_spmd(builders["l1b1"], ins, core_ids=list(range(NCORES)))
    kvp8 = np.stack([r1.results[c]["kvp"] for c in range(NCORES)], axis=0)
    ksm8 = np.stack([r1.results[c]["ksm"] for c in range(NCORES)], axis=0)
    xnf = [r1.results[c]["xnf"] for c in range(NCORES)]

    if "l1b2" not in builders:
        builders["l1b2"] = build_l1b2()
    ins = []
    for c in range(NCORES):
        sl = slice(LC * c, LC * (c + 1))
        ins.append({
            "xnf": xnf[c],
            "kvp8": kvp8,
            "ksm8": ksm8,
            "WqT": f32c(WqkvT[:, 0:D]),
            "WpT": f32c(np.ascontiguousarray(p[pre + "W_proj"].T)),
            "bpcol": f32c(p[pre + "b_proj"].reshape(8, 128).T),
            "sel2in": np.kron(np.eye(2), np.ones((1, 64))).astype(np.float32),
            "xres": f32c(x_fm[:, :, :, sl]),
        })
    r2 = run_bass_kernel_spmd(builders["l1b2"], ins, core_ids=list(range(NCORES)))
    return np.concatenate([r2.results[c]["xout"] for c in range(NCORES)], axis=3)




WREL1 = 1536

PADL = 640
CC = PADL + 20
VPAST1 = 16
VFUT1 = 1320


def _slab_window_l1(s):
    return max(0, 4 * s - 1), min(31, 4 * s + 4)


def make_relT_l1(rel_k):
    pu = np.arange(WREL1)
    idx = np.clip(pu - PADL, 0, 40)
    rel_ext = rel_k[idx]
    return np.ascontiguousarray(rel_ext.T)


def build_attn_core_l1():
    g = G()
    q8 = g.io("q8", [BHC, HD, L], BF16)
    kf = g.io("kf", [BHC, HD, L], BF16)
    vt = g.io("vt", [BHC, L, HD], BF16)
    relT = g.io("relT", [HD, WREL1])
    relv = g.io("relv", [41, HD])
    relv40 = g.io("relv40", [1, HD])
    nrm8 = g.io("nrm8", [BHC, 2, NCORES])
    ident = g.io("ident", [128, 128])
    ao = g.io("ao", [BHC, HD, L], BF16, out=True)

    sQ8 = g.sb("sQ8", [64, L], BF16)
    sKf = g.sb("sKf", [64, L], BF16)
    sVa = g.sb("sVa", [128, 32, 65], BF16)
    sRel = g.sb("sRel", [64, WREL1], BF16)
    sRelV = g.sb("sRelV", [41, 64], BF16)
    sRelV40 = g.sb("sRelV40", [1, 64], BF16)
    sRelSq = g.sb("sRelSq", [64, 512], BF16)
    sP = g.sb("sP", [128, 2, WREL1], BF16)
    sSkew = g.sb("sSkew", [128, 4, 768], BF16)
    sAT = g.sb("sAT", [128, 4, 512], BF16)
    sATB = g.sb("sATB", [128, 6, 512], BF16)
    sTR4 = g.sb("sTR4", [128, 4, 384], BF16)
    sMsk4 = g.sb("sMsk4", [128, 4, 384])
    sMk3P4 = g.sb("sMk3P4", [128, 4, 384])
    sMk3F4 = g.sb("sMk3F4", [128, 4, 384])
    sMk3P = g.sb("sMk3P", [128, 384])
    sMk3F = g.sb("sMk3F", [128, 384])
    sIo3 = g.sb("sIo3", [128, 384], I32)
    sIo3F = g.sb("sIo3F", [128, 384])
    sColI = g.sb("sColI", [128, 1], I32)
    sColF = g.sb("sColF", [128, 1])
    sThr = g.sb("sThr", [128, 2])
    sExtrP = g.sb("sExtrP", [128, 4])
    sExtrF = g.sb("sExtrF", [128, 4])
    sAband4 = g.sb("sAband4", [128, 4, 41], BF16)
    sA41T = g.sb("sA41T", [41, 4, 128], BF16)
    sCS = g.sb("sCS", [1, 6, 512])
    sPFf = g.sb("sPFf", [1, 2, 512])
    sPFb = g.sb("sPFb", [1, 2, 512], BF16)
    sEdge = g.sb("sEdge", [128, 4, 2], BF16)
    sERow = g.sb("sERow", [1, 2, 512], BF16)
    sIdentB = g.sb("sIdentB", [128, 128], BF16)
    sNrm8 = g.sb("sNrm8", [BHC, 2, NCORES])
    sMQK = g.sb("sMQK", [BHC, 6])
    sMcolB = g.sb("sMcolB", [BHC, 1], BF16)
    sMrowB = g.sb("sMrowB", [1, BHC], BF16)
    sMB128 = g.sb("sMB128", [128, BHC])
    sRn = g.sb("sRn", [1, WREL1])
    sRnMax = g.sb("sRnMax", [1, 2])
    sComb = g.sb("sComb", [65, 512])
    sFar2 = g.sb("sFar2", [65, 2, 512])
    sFarRb = g.sb("sFarRb", [65, 2, 512], BF16)
    sFarR0 = g.sb("sFarR0", [1, 2, 512], BF16)
    sBCf = g.sb("sBCf", [65, 512])
    sDrowB = g.sb("sDrowB", [1, 512], BF16)
    sDrow = g.sb("sDrow", [1, 512])
    sDB = g.sb("sDB", [64, 512])
    sAO = g.sb("sAO", [64, 512], BF16)
    sOnesRow = g.sb("sOnesRow", [1, 128])
    sOnesRowB = g.sb("sOnesRowB", [1, 128], BF16)
    sOnes64B = g.sb("sOnes64B", [64, 1], BF16)
    sOnes128B = g.sb("sOnes128B", [128, 1], BF16)

    pSc4 = g.ps("pSc4", [128, 4, 512])
    pPV = g.ps("pPV", [65, 3, 512])
    pBC = g.ps("pBC", [128, 512])

    g.add("gpsimd", lambda gp: gp.dma_start(out=sRel[:], in_=relT[:]), dma=True)
    g.add("gpsimd", lambda gp: gp.dma_start(out=sRelV[:], in_=relv[:]), dma=True)
    g.add("gpsimd", lambda gp: gp.dma_start(out=sRelV40[:], in_=relv40[:]), dma=True)
    g.add("gpsimd", lambda gp: gp.dma_start(out=sIdentB[:], in_=ident[:]), dma=True)
    g.add("sync", lambda s: s.dma_start(out=sNrm8[:], in_=nrm8[:]), dma=True)
    g.add("gpsimd", lambda gp: gp.iota(sIo3[:], pattern=[[1, 384]], base=0, channel_multiplier=0))
    g.add("gpsimd", lambda gp: gp.iota(sColI[:], pattern=[[0, 1]], base=0, channel_multiplier=1))

    def cst(v):
        v.memset(sOnesRow[:], 1.0)
        v.memset(sOnesRowB[:], 1.0)
        v.memset(sOnes64B[:], 1.0)
        v.memset(sOnes128B[:], 1.0)
        return v.memset(sVa[:, :, 64:65], 1.0)
    g.add("vector", cst)
    g.bar()
    seq(g, "vector", lambda v: v.tensor_copy(sIo3F[:], sIo3[:]))
    seq(g, "vector", lambda v: v.tensor_copy(sColF[:], sColI[:]))
    seq(g, "vector", lambda v: v.tensor_scalar(sThr[:, 0:1], sColF[:], 107.5, None, ALU.add))
    seq(g, "vector", lambda v: v.tensor_scalar(sThr[:, 1:2], sColF[:], 148.5, None, ALU.add))
    seq(g, "vector", lambda v: v.tensor_scalar(sMk3P[:], sIo3F[:], sThr[:, 0:1], None, ALU.is_lt))
    seq(g, "vector", lambda v: v.tensor_scalar(sMk3F[:], sIo3F[:], sThr[:, 1:2], None, ALU.is_gt))
    def mk4(v):
        ii = None
        for qt in range(4):
            v.tensor_copy(sMk3P4[:, qt], sMk3P[:])
            ii = v.tensor_copy(sMk3F4[:, qt], sMk3F[:])
        return ii
    g.add("vector", mk4)
    g.bar()

    for k in range(WREL1 // 512):
        seq(g, "scalar", lambda sc, k=k: sc.activation(out=sRelSq[:], in_=sRel[:, 512 * k:512 * (k + 1)],
                                                       func=AF.Square))
        seq(g, "tensor", lambda t: t.matmul(pSc4[0:1, 0, :], sOnes64B[:], sRelSq[:], start=True, stop=True))
        seq(g, "vector", lambda v, k=k: v.tensor_copy(sRn[:, 512 * k:512 * (k + 1)], pSc4[0:1, 0, :]))
    g.add("vector", lambda v: v.reduce_max(sRnMax[:, 0:1], sRn[:], AX.X))
    g.bar()
    g.add("scalar", lambda sc: sc.activation(out=sRnMax[:, 1:2], in_=sRnMax[:, 0:1], func=AF.Sqrt))
    g.bar()

    def mred(v):
        v.reduce_max(sMQK[:, 0:1], sNrm8[:, 0], AX.X)
        return v.reduce_max(sMQK[:, 1:2], sNrm8[:, 1], AX.X)
    g.add("vector", mred)
    g.bar()
    g.add("scalar", lambda sc: sc.activation(out=sMQK[:, 2:4], in_=sMQK[:, 0:2], func=AF.Sqrt))
    g.bar()
    g.add("tensor", lambda t: t.matmul(pBC[0:BHC, 0:1], sOnesRow[:, 0:BHC], sRnMax[:, 1:2],
                                       start=True, stop=True))
    g.bar()
    g.add("vector", lambda v: v.tensor_copy(sMQK[:, 4:5], pBC[0:BHC, 0:1]))
    g.bar()
    g.add("vector", lambda v: v.tensor_tensor(sMQK[:, 4:5], sMQK[:, 3:4], sMQK[:, 4:5], ALU.add))
    g.bar()
    g.add("vector", lambda v: v.tensor_tensor(sMQK[:, 5:6], sMQK[:, 2:3], sMQK[:, 4:5], ALU.mult))
    g.bar()
    g.add("vector", lambda v: v.tensor_scalar(sMcolB[:], sMQK[:, 5:6], -1.0, None, ALU.mult))
    g.bar()
    g.add("tensor", lambda t: t.matmul(pBC[0:1, 0:BHC], sMcolB[:], sIdentB[0:BHC, 0:BHC],
                                       start=True, stop=True))
    g.bar()
    g.add("vector", lambda v: v.tensor_copy(sMrowB[:], pBC[0:1, 0:BHC]))
    g.bar()
    g.add("tensor", lambda t: t.matmul(pBC[:, 0:BHC], sOnesRowB[:], sMrowB[:], start=True, stop=True))
    g.bar()
    g.add("vector", lambda v: v.tensor_copy(sMB128[:], pBC[:, 0:BHC]))
    g.bar()

    for bhi in range(BHC):
        g.add("sync", lambda s, bhi=bhi: s.dma_start(out=sQ8[:], in_=q8[bhi]), dma=True)
        g.add("sync", lambda s, bhi=bhi: s.dma_start(out=sKf[:], in_=kf[bhi]), dma=True)
        g.add("sync", lambda s, bhi=bhi: s.dma_start(
            out=sVa[:, :, 0:64], in_=vt[bhi].rearrange("(jc p) d -> p jc d", p=128)), dma=True)
        g.bar()

        for s_ in range(8):
            jclo, jchi = _slab_window_l1(s_)
            nch = jchi - jclo + 1
            width = nch * 128
            has_past = jclo > 0
            has_fut = jchi < 31

            def eext(v, qt):
                v.tensor_copy(sEdge[:, qt, 0:1], sP[:, qt % 2, VPAST1:VPAST1 + 1])
                return v.tensor_copy(sEdge[:, qt, 1:2], sP[:, qt % 2, VFUT1:VFUT1 + 1])

            def skdma(s, qt, s_, jclo, width):
                S0 = (qt % 2) * WREL1 + CC + 128 * jclo - 512 * s_ - 128 * qt
                src = AP(sP, S0, [[2 * WREL1 - 1, 128], [1, width]])
                return s.dma_start(out=sSkew[:, qt, 0:width], in_=src)

            NKR = WREL1 // 512
            for qt in range(5):
                for k in range((NKR + 1) if qt < 4 else 1):
                    if k < NKR and qt < 4:
                        def pmm(t, qt=qt, k=k, s_=s_):
                            return t.matmul(pSc4[:, k % 2],
                                            sQ8[:, 512 * s_ + 128 * qt: 512 * s_ + 128 * (qt + 1)],
                                            sRel[:, 512 * k:512 * (k + 1)], start=True, stop=True)
                        g.add("tensor", pmm)
                    if k >= 1 and qt < 4:
                        g.add("scalar", lambda sc, qt=qt, k=k: sc.activation(
                            out=sP[:, qt % 2, 512 * (k - 1):512 * k], in_=pSc4[:, (k - 1) % 2],
                            func=AF.Copy))
                    if k == 0 and qt >= 1:
                        g.add("vector", lambda v, qt=qt: eext(v, qt - 1))
                        g.add("sync",
                              lambda s, qt=qt, s_=s_, jclo=jclo, width=width: skdma(
                                  s, qt - 1, s_, jclo, width), dma=True,
                              defer=4 if qt < 4 else 0)
                    g.bar()

            cls_of = {}
            for jc in range(32):
                cls_of[jc] = 0 if jclo <= jc <= jchi else (1 if jc < jclo else 2)
            firsts = {c: min((j for j in range(32) if cls_of[j] == c), default=None) for c in range(3)}
            lasts = {c: max((j for j in range(32) if cls_of[j] == c), default=None) for c in range(3)}

            def content2(t, p0, s_=s_, jclo=jclo, cls_of=cls_of):
                ii = None
                for jc in (2 * p0, 2 * p0 + 1):
                    band = cls_of[jc] == 0
                    ii = t.matmul(pSc4[:, jc % 4], sKf[:, 128 * jc:128 * (jc + 1)],
                                  sQ8[:, 512 * s_:512 * (s_ + 1)], start=True, stop=not band)
                    if band:
                        for qt in range(4):
                            ii = t.matmul(pSc4[:, jc % 4, 128 * qt:128 * (qt + 1)],
                                          sSkew[:, qt, (jc - jclo) * 128:(jc - jclo + 1) * 128],
                                          sIdentB[:], start=False, stop=(qt == 3))
                return ii

            def exp2(sc, p0, bhi=bhi, jclo=jclo, cls_of=cls_of):
                jc0, jc1 = 2 * p0, 2 * p0 + 1
                base = 2 * (p0 % 2)
                tg0 = sATB[:, jc0 - jclo, :] if cls_of[jc0] == 0 else sAT[:, base, :]
                tg1 = sATB[:, jc1 - jclo, :] if cls_of[jc1] == 0 else sAT[:, base + 1, :]
                if cls_of[jc0] == 0 and cls_of[jc1] == 0:
                    return sc.activation(out=sATB[:, jc0 - jclo:jc0 - jclo + 2, :],
                                         in_=pSc4[:, (2 * p0) % 4:(2 * p0) % 4 + 2, :],
                                         func=AF.Exp, bias=sMB128[:, bhi:bhi + 1])
                if cls_of[jc0] != 0 and cls_of[jc1] != 0:
                    return sc.activation(out=sAT[:, base:base + 2, :],
                                         in_=pSc4[:, (2 * p0) % 4:(2 * p0) % 4 + 2, :],
                                         func=AF.Exp, bias=sMB128[:, bhi:bhi + 1])
                sc.activation(out=tg0, in_=pSc4[:, (2 * p0) % 4, :],
                              func=AF.Exp, bias=sMB128[:, bhi:bhi + 1])
                return sc.activation(out=tg1, in_=pSc4[:, (2 * p0) % 4 + 1, :],
                                     func=AF.Exp, bias=sMB128[:, bhi:bhi + 1])

            def pv2(t, p0, jclo=jclo, cls_of=cls_of, firsts=firsts, lasts=lasts):
                ii = None
                base = 2 * (p0 % 2)
                for k, jc in enumerate((2 * p0, 2 * p0 + 1)):
                    c = cls_of[jc]
                    srcb = sATB[:, jc - jclo, :] if c == 0 else sAT[:, base + k, :]
                    ii = t.matmul(pPV[:, c], sVa[:, jc], srcb,
                                  start=(jc == firsts[c]), stop=(jc == lasts[c]))
                return ii

            for st in range(18):
                if st < 16:
                    g.add("tensor", lambda t, st=st, f=content2: f(t, st))
                if 1 <= st <= 16:
                    g.add("scalar", lambda sc, st=st, f=exp2: f(sc, st - 1))
                if st >= 2:
                    g.add("tensor", lambda t, st=st, f=pv2: f(t, st - 2))
                g.bar()

            psl = [pSc4[:, i] for i in range(4)]
            psl1 = [pSc4[0:1, i, :] for i in range(4)]
            psl41 = [pSc4[0:41, i, 0:128] for i in range(4)]
            def csmm1(t, nch=nch):
                ii = None
                for jcw in range(min(4, nch)):
                    ii = t.matmul(psl1[jcw], sOnes128B[:], sATB[:, jcw], start=True, stop=True)
                return ii
            g.add("tensor", csmm1)
            g.bar()

            def csev1(v, nch=nch):
                ii = None
                for jcw in range(min(4, nch)):
                    ii = v.tensor_copy(sCS[0:1, jcw], psl1[jcw])
                return ii
            g.add("vector", csev1)
            g.bar()
            if nch > 4:
                def csmm2(t, nch=nch):
                    ii = None
                    for jcw in range(4, nch):
                        ii = t.matmul(psl1[jcw - 4], sOnes128B[:], sATB[:, jcw], start=True, stop=True)
                    return ii
                g.add("tensor", csmm2)
                g.bar()
            if nch > 4:
                def csev2(v, nch=nch):
                    ii = None
                    for jcw in range(4, nch):
                        ii = v.tensor_copy(sCS[0:1, jcw], psl1[jcw - 4])
                    return ii
                g.add("vector", csev2)
            g.add("vector", lambda v: v.memset(sPFf[:].rearrange("p a b -> p (a b)"), 0.0))
            g.bar()
            layers = {0: [], 1: [], 2: []}
            for qt in range(4):
                c = 4 * s_ + qt
                pl = [(0, jc) for jc in range(jclo, c - 1)]
                fl = [(1, jc) for jc in range(c + 2, jchi + 1)]
                for r, (col, jc) in enumerate(pl):
                    layers[r].append((col, jc, qt))
                for r, (col, jc) in enumerate(fl):
                    layers[r].append((col, jc, qt))
            for r in range(3):
                if not layers[r]:
                    continue

                def lfn(v, items=tuple(layers[r]), jclo=jclo):
                    ii = None
                    for col, jc, qt in items:
                        ii = v.tensor_tensor(
                            sPFf[0:1, col, 128 * qt:128 * (qt + 1)],
                            sPFf[0:1, col, 128 * qt:128 * (qt + 1)],
                            sCS[0:1, jc - jclo, 128 * qt:128 * (qt + 1)], ALU.add)
                    return ii
                g.add("vector", lfn)
                g.bar()
            seq(g, "vector", lambda v: v.tensor_copy(sPFb[:].rearrange("p a b -> p (a b)"),
                                                     sPFf[:].rearrange("p a b -> p (a b)")))
            if s_ in (0, 7):
                seq(g, "vector", lambda v: v.memset(sTR4[:].rearrange("p a b -> p (a b)"), 0.0))

            def trmm(t, s_=s_, jclo=jclo):
                ii = None
                for qt in range(4):
                    c = 4 * s_ + qt
                    for dc, jc in enumerate([c - 1, c, c + 1]):
                        if jc < 0 or jc > 31:
                            continue
                        ii = t.matmul(psl[qt][:, 128 * dc:128 * (dc + 1)],
                                      sATB[:, jc - jclo, 128 * qt:128 * (qt + 1)],
                                      sIdentB[:], start=True, stop=True)
                return ii
            g.add("tensor", trmm)
            g.bar()

            def trev(v, s_=s_):
                ii = None
                for qt in range(4):
                    c = 4 * s_ + qt
                    for dc, jc in enumerate([c - 1, c, c + 1]):
                        if jc < 0 or jc > 31:
                            continue
                        ii = v.tensor_copy(sTR4[:, qt, 128 * dc:128 * (dc + 1)],
                                           psl[qt][:, 128 * dc:128 * (dc + 1)])
                return ii
            g.add("vector", trev)
            g.bar()

            def msks(v):
                v.tensor_tensor(sMsk4[:], sTR4[:], sMk3P4[:], ALU.mult)
                return v.reduce_sum(sExtrP[:], sMsk4[:], AX.X)
            g.add("vector", msks)
            g.bar()

            def msks2(v):
                v.tensor_tensor(sMsk4[:], sTR4[:], sMk3F4[:], ALU.mult)
                return v.reduce_sum(sExtrF[:], sMsk4[:], AX.X)
            g.add("vector", msks2)
            g.bar()
            for qt in range(4):
                g.add("sync", lambda s, qt=qt: s.dma_start(
                    out=sAband4[:, qt, :],
                    in_=AP(sTR4, 384 * qt + 108, [[4 * 384 + 1, 128], [1, 41]])), dma=True)
            g.bar()

            def merge(v):
                v.tensor_tensor(sAband4[:, :, 0], sAband4[:, :, 0], sExtrP[:].opt(), ALU.add)
                return v.tensor_tensor(sAband4[:, :, 40], sAband4[:, :, 40], sExtrF[:].opt(), ALU.add)
            g.add("vector", merge)
            g.bar()

            def a41mm(t):
                ii = None
                for qt in range(4):
                    ii = t.matmul(psl41[qt], sAband4[:, qt, :], sIdentB[:], start=True, stop=True)
                return ii
            g.add("tensor", a41mm)
            g.bar()

            def a41ev(v):
                ii = None
                for qt in range(4):
                    ii = v.tensor_copy(sA41T[:, qt, :], psl41[qt])
                return ii
            g.add("vector", a41ev)
            g.bar()

            def w2mm(t):
                ii = None
                for qt in range(4):
                    ii = t.matmul(pPV[0:64, 0, 128 * qt:128 * (qt + 1)], sRelV[:], sA41T[:, qt, :],
                                  start=False, stop=False)
                ii = t.matmul(pPV[0:64, 0, :], sRelV[0:1, :], sPFb[0:1, 0, :], start=False, stop=False)
                ii = t.matmul(pPV[0:64, 0, :], sRelV40[:], sPFb[0:1, 1, :], start=False, stop=True)
                return ii
            g.add("tensor", w2mm)
            g.bar()

            def etr(t, col, tgt):
                ii = None
                for qt in range(4):
                    ii = t.matmul(tgt[0:1, 128 * qt:128 * (qt + 1)], sEdge[:, qt, col:col + 1],
                                  sIdentB[:], start=(qt == 0), stop=(qt == 3))
                return ii

            if has_past:
                g.add("tensor", lambda t: etr(t, 0, pBC))
            else:
                g.add("vector", lambda v: v.memset(sFar2[:, 0, :], 0.0))
            if not has_fut:
                g.add("vector", lambda v: v.memset(sFar2[:, 1, :], 0.0))
            g.bar()
            if has_fut:
                g.add("tensor", lambda t: etr(t, 1, pSc4[:, 0]))
            if has_past:
                g.add("scalar", lambda sc: sc.activation(out=sERow[:, 0], in_=pBC[0:1, :], func=AF.Exp))
            g.bar()
            if has_past:
                g.add("tensor", lambda t: t.matmul(pBC[0:65, :], sOnesRowB[:, 0:65], sERow[:, 0],
                                                   start=True, stop=True))
            if has_fut:
                g.add("scalar", lambda sc: sc.activation(out=sERow[:, 1], in_=pSc4[0:1, 0, :], func=AF.Exp))
            g.bar()
            if has_fut:
                g.add("tensor", lambda t: t.matmul(pSc4[0:65, 0, :], sOnesRowB[:, 0:65], sERow[:, 1],
                                                   start=True, stop=True))
            if has_past:
                def fm0(v):
                    v.tensor_copy(sBCf[:], pBC[0:65, :])
                    return v.tensor_tensor(sFar2[:, 0, :], pPV[:, 1], sBCf[:], ALU.mult)
                g.add("vector", fm0)
            g.bar()
            def fm1(v, has_fut=has_fut):
                if has_fut:
                    v.tensor_copy(sBCf[:], pSc4[0:65, 0, :])
                    v.tensor_tensor(sFar2[:, 1, :], pPV[:, 2], sBCf[:], ALU.mult)
                return v.tensor_copy(sFarRb[64:65, :, :].rearrange("p a b -> p (a b)"),
                                     sFar2[64:65, :, :].rearrange("p a b -> p (a b)"))
            g.add("vector", fm1)
            g.bar()
            g.add("sync", lambda s: s.dma_start(out=sFarR0[:], in_=sFarRb[64:65, :, :]), dma=True)
            g.bar()

            def w2far(t):
                t.matmul(pPV[0:64, 0, :], sRelV[0:1, :], sFarR0[0:1, 0, :], start=False, stop=False)
                return t.matmul(pPV[0:64, 0, :], sRelV40[:], sFarR0[0:1, 1, :], start=False, stop=True)
            g.add("tensor", w2far)
            g.bar()

            def comb(v):
                v.tensor_copy(sComb[:], pPV[:, 0])
                v.tensor_tensor(sComb[:], sComb[:], sFar2[:, 0, :], ALU.add)
                v.tensor_tensor(sComb[:], sComb[:], sFar2[:, 1, :], ALU.add)
                v.reciprocal(sDrow[:], sComb[64:65, :])
                return v.tensor_copy(sDrowB[:], sDrow[:])
            g.add("vector", comb)
            g.bar()
            g.add("tensor", lambda t: t.matmul(pBC[0:64, :], sOnesRowB[:, 0:64], sDrowB[:],
                                               start=True, stop=True))
            g.bar()

            def fin(v):
                v.tensor_copy(sDB[:], pBC[0:64, :])
                return v.tensor_tensor(sAO[:], sComb[0:64, :], sDB[:], ALU.mult)
            g.add("vector", fin)
            g.bar()
            g.add("sync", lambda s, bhi=bhi, s_=s_: s.dma_start(
                out=ao[bhi, :, 512 * s_:512 * (s_ + 1)], in_=sAO[:]), dma=True)
            g.bar()
    return g.build()


def run_attn_core_l1(q8f, kff, vtf, nrm, rel_k, rel_v, builders):
    if "a2l1" not in builders:
        builders["a2l1"] = build_attn_core_l1()
    relT = f32c(make_relT_l1(rel_k))
    ins = []
    for c in range(NCORES):
        sl = slice(BHC * c, BHC * (c + 1))
        ins.append({
            "q8": q8f[sl], "kf": kff[sl], "vt": vtf[sl],
            "relT": relT,
            "relv": f32c(rel_v),
            "relv40": f32c(rel_v[40:41]),
            "nrm8": f32c(np.transpose(nrm[:, :, sl], (2, 1, 0))),
            "ident": IDENT,
        })
    res = run_bass_kernel_spmd(builders["a2l1"], ins, core_ids=list(range(NCORES)))
    return np.concatenate([r["ao"] for r in res.results], axis=0)




def kernel(**inputs):
    p = {k: np.asarray(v) for k, v in inputs.items()}
    b = _BUILDERS
    x_fm = to_fm(f32c(p["L0_attn_a_x"]))
    q8, kf, vt, nrm = run_attn_prep(x_fm, p["L0_attn_a_Wq"], p["L0_attn_a_bq"],
                                    p["L0_attn_a_Wk"], p["L0_attn_a_bk"],
                                    p["L0_attn_a_Wv"], p["L0_attn_a_bv"], 0.125)
    ao = run_attn_core(q8, kf, vt, nrm, p["L0_attn_a_rel_table"])
    x_fm = run_attn_out(x_fm, ao, p["L0_attn_a_Wo"], p["L0_attn_a_bo"])
    x_fm = run_ffn0(x_fm, p["L0_ffn_Wr"], p["L0_ffn_br"], p["L0_ffn_W_up"],
                    p["L0_ffn_b_up"], p["L0_ffn_W_down"], p["L0_ffn_b_down"], b)
    x_fm = run_attnb(x_fm, p, b)
    x_fm = run_ffn0(x_fm, p["L0_ffn_Wr"], p["L0_ffn_br"], p["L0_ffn_W_up"],
                    p["L0_ffn_b_up"], p["L0_ffn_W_down"], p["L0_ffn_b_down"], b)
    q8, kf, vt, nrm = run_attn_prep(x_fm, p["L1_attn_a_Wq"], p["L1_attn_a_bq"],
                                    p["L1_attn_a_Wk"], p["L1_attn_a_bk"],
                                    p["L1_attn_a_Wv"], p["L1_attn_a_bv"], 0.125)
    ao = run_attn_core_l1(q8, kf, vt, nrm, p["L1_attn_a_rel_k"], p["L1_attn_a_rel_v"], b)
    x_fm = run_attn_out(x_fm, ao, p["L1_attn_a_Wo"], p["L1_attn_a_bo"])
    x_fm = run_ffn1(x_fm, p, b)
    x_fm = run_attnb1(x_fm, p, b)
    x_fm = run_ffn1(x_fm, p, b)
    return from_fm(x_fm)
